# version 02
# baseline (speedup 1.0000x reference)
import sys

sys.path.insert(0, "/opt/trn_rl_repo")
from contextlib import ExitStack

import numpy as np
import ml_dtypes

import concourse.bass as bass
import concourse.mybir as mybir
from concourse.bass_utils import run_bass_kernel_spmd

F32 = mybir.dt.float32
BF16 = mybir.dt.bfloat16
I32 = mybir.dt.int32
AF = mybir.ActivationFunctionType
ALU = mybir.AluOpType
AX = mybir.AxisListType

NCORES = 8
M = 4096
D = 1024
NEG = -1.0e30

BF = ml_dtypes.bfloat16


def _bass(name):
    return bass.Bass("TRN2", target_bir_lowering=False,
                     detect_race_conditions=False, name=name)


class KB:

    def __init__(self, name):
        self.nc = _bass(name)
        self.st = ExitStack()
        self._uid = 0
        self._consts = {}

    def uid(self):
        self._uid += 1
        return self._uid

    def din(self, name, shape, dt=F32):
        return self.nc.dram_tensor(name, list(shape), dt, kind="ExternalInput").ap()

    def dout(self, name, shape, dt=F32):
        return self.nc.dram_tensor(name, list(shape), dt, kind="ExternalOutput").ap()

    def dscr(self, name, shape, dt=F32):
        return self.nc.dram_tensor(name, list(shape), dt, kind="Internal").ap()

    def const_dram(self, arr, name=None):
        key = (arr.tobytes(), arr.shape, str(arr.dtype))
        if key not in self._consts:
            self._consts[key] = self.nc.inline_tensor(
                np.ascontiguousarray(arr), name or f"const{self.uid()}").ap()
        return self._consts[key]

    def const_sb(self, st, ch, arr, name):
        nc = self.nc
        dt = BF16 if arr.dtype == BF else (F32 if arr.dtype == np.float32 else I32)
        t = st.enter_context(
            nc.sbuf_tensor(f"c_{name}{self.uid()}", list(arr.shape), dt))
        dr = self.const_dram(arr, f"cd_{name}{self.uid()}")
        ch.dma(nc.sync, lambda: nc.sync.dma_start(out=t[:, :], in_=dr))
        return t

    def sb(self, st, name, shape, dt):
        return self.st_sb(st, name, shape, dt)

    def st_sb(self, st, name, shape, dt):
        return st.enter_context(
            self.nc.sbuf_tensor(f"{name}{self.uid()}", list(shape), dt))

    def ps(self, st, name, shape, dt=F32):
        return st.enter_context(
            self.nc.psum_tensor(f"{name}{self.uid()}", list(shape), dt))


class Chain:

    def __init__(self, kb, st, name):
        self.kb = kb
        nc = kb.nc
        self.c = st.enter_context(nc.semaphore(f"{name}_c"))
        self.d = st.enter_context(nc.semaphore(f"{name}_d"))
        self.cn = 0
        self.dn = 0
        self._last = {}

    def _pre(self, eng):
        key = type(eng).__name__
        lc, ld = self._last.get(key, (0, 0))
        if self.cn > lc:
            eng.wait_ge(self.c, self.cn)
        if self.dn > ld:
            eng.wait_ge(self.d, self.dn)
        self._last[key] = (max(lc, self.cn), max(ld, self.dn))

    def op(self, eng, emit, inc=True):
        self._pre(eng)
        ins = emit()
        if inc:
            ins.then_inc(self.c, 1)
            self.cn += 1
        return ins

    def dma(self, eng, emit, inc=True):
        self._pre(eng)
        ins = emit()
        if inc:
            ins.then_inc(self.d, 16)
            self.dn += 16
        return ins

    def seal(self):
        nc = self.kb.nc
        for eng in (nc.sync, nc.gpsimd, nc.tensor, nc.vector, nc.scalar):
            self._pre(eng)


def barrier(kb):
    kb.nc.all_engine_barrier()



def fm(ap2d, mlo, mhi):
    return ap2d[:, mlo:mhi].rearrange("(d p) c -> p d c", p=128)


def sb3(t, nd, w):
    return t[:, :].rearrange("p (d c) -> p d c", d=nd)


def emit_cast(kb, ch, src, dst):
    ch.dma(kb.nc.gpsimd, lambda: kb.nc.gpsimd.dma_start(out=dst, in_=src))


def emit_ln(kb, ch, st, x_dram, ln_dram, mtot=M, eps=1e-5):
    nc = kb.nc
    with ExitStack() as ls:
        xm = kb.sb(ls, "ln_xm", [128, 8 * 512], F32)
        xb = kb.sb(ls, "ln_xb", [128, 8 * 512], BF16)
        x2 = kb.sb(ls, "ln_x2", [128, 8 * 512], BF16)
        mu2 = kb.sb(ls, "ln_mu2", [1, 512], F32)
        var = kb.sb(ls, "ln_var", [1, 512], F32)
        rstd = kb.sb(ls, "ln_rstd", [1, 512], F32)
        beta = kb.sb(ls, "ln_beta", [1, 512], F32)
        tmp = kb.sb(ls, "ln_tmp", [128, 512], F32)
        outt = kb.sb(ls, "ln_out", [128, 8 * 512], BF16)
        ones_bf = kb.const_sb(ls, ch, np.ones((128, 1), np.float32).astype(BF), "onesb")
        eps_sb = kb.const_sb(ls, ch, np.full((1, 1), eps, np.float32), "eps")
        ones1f = kb.const_sb(ls, ch, np.ones((1, 128), np.float32), "ones1f")
        ps_s = kb.ps(ls, "ln_ps_s", [128, 512])
        ps_q = kb.ps(ls, "ln_ps_q", [128, 512])
        ps_b = kb.ps(ls, "ln_ps_b", [128, 512])
        ps_b2 = kb.ps(ls, "ln_ps_b2", [128, 512])

        for m in range(mtot // 512):
            src = fm(x_dram, m * 512, (m + 1) * 512)
            ch.dma(nc.sync, lambda src=src: nc.sync.dma_start(out=sb3(xm, 8, 512), in_=src))
            ch.op(nc.vector, lambda: nc.vector.tensor_copy(xb[:, :], xm[:, :]))
            ch.op(nc.scalar, lambda: nc.scalar.activation(x2[:, :], xm[:, :], AF.Square))
            for dci in range(8):
                ch.op(nc.tensor, lambda dci=dci: nc.tensor.matmul(
                    ps_s[0:1, :], ones_bf[:, :], xb[:, dci * 512:(dci + 1) * 512],
                    start=(dci == 0), stop=(dci == 7)), inc=False)
                ch.op(nc.tensor, lambda dci=dci: nc.tensor.matmul(
                    ps_q[0:1, :], ones_bf[:, :], x2[:, dci * 512:(dci + 1) * 512],
                    start=(dci == 0), stop=(dci == 7)), inc=(dci == 7))
            ch.op(nc.scalar, lambda: nc.scalar.activation(
                mu2[:, :], ps_s[0:1, :], AF.Square, scale=1.0 / D))
            ch.op(nc.vector, lambda: nc.vector.scalar_tensor_tensor(
                var[:, :], ps_q[0:1, :], 1.0 / D, mu2[:, :], ALU.mult, ALU.subtract))
            ch.op(nc.scalar, lambda: nc.scalar.activation(
                rstd[:, :], var[:, :], AF.Ln, bias=eps_sb[:, :]))
            ch.op(nc.scalar, lambda: nc.scalar.activation(
                rstd[:, :], rstd[:, :], AF.Exp, scale=-0.5))
            ch.op(nc.vector, lambda: nc.vector.scalar_tensor_tensor(
                beta[:, :], ps_s[0:1, :], -1.0 / D, rstd[:, :], ALU.mult, ALU.mult))
            ch.op(nc.tensor, lambda: nc.tensor.matmul(
                ps_b[:, :], ones1f[:, :], rstd[:, :], start=True, stop=True),
                inc=False)
            ch.op(nc.tensor, lambda: nc.tensor.matmul(
                ps_b2[:, :], ones1f[:, :], beta[:, :], start=True, stop=True))
            for dci in range(8):
                sl = slice(dci * 512, (dci + 1) * 512)
                ch.op(nc.vector, lambda sl=sl: nc.vector.tensor_mul(
                    tmp[:, :], xm[:, sl], ps_b[:, :]))
                ch.op(nc.vector, lambda sl=sl: nc.vector.tensor_add(
                    outt[:, sl], tmp[:, :], ps_b2[:, :]))
            dst = fm(ln_dram, m * 512, (m + 1) * 512)
            ch.dma(nc.gpsimd, lambda dst=dst: nc.gpsimd.dma_start(
                out=dst, in_=sb3(outt, 8, 512)))


def ceil_div(a, b):
    return (a + b - 1) // b


def emit_sigma(kb, ch, st, w_bf, din_, dout_, u_dram, sp_dram, name):
    nc = kb.nc
    no, nci = dout_ // 128, din_ // 128
    u_sb = kb.sb(st, f"sg_u{name}", [128, no], BF16)
    wt_sb = kb.sb(st, f"sg_wt{name}", [128, din_], BF16)
    vsq = kb.sb(st, f"sg_vsq{name}", [1, din_], F32)
    vss = kb.sb(st, f"sg_vss{name}", [1, 1], F32)
    rn = kb.sb(st, f"sg_rn{name}", [1, 1], F32)
    v_row = kb.sb(st, f"sg_vrow{name}", [1, din_], F32)
    vc_sb = kb.sb(st, f"sg_vc{name}", [128, nci], BF16)
    wrow = kb.sb(st, f"sg_wrow{name}", [128, dout_], BF16)
    zsq = kb.sb(st, f"sg_zsq{name}", [1, dout_], F32)
    zss = kb.sb(st, f"sg_zss{name}", [1, 1], F32)
    sp_sb = kb.sb(st, f"sg_sp{name}", [1, 1], F32)
    sc = kb.sb(st, f"sg_sc{name}", [1, 1], F32)
    vsig = kb.dscr(f"sg_vd{name}", [1, din_], BF16)
    with ExitStack() as ls:
        ps_v = kb.ps(ls, f"sg_psv{name}", [128, din_])
        for a in range(no):
            ch.dma(nc.gpsimd, lambda a=a: nc.gpsimd.dma_start(
                out=u_sb[:, a:a + 1],
                in_=u_dram[a * 128:(a + 1) * 128].unsqueeze(1)))
        for a in range(no):
            ch.dma(nc.sync, lambda a=a: nc.sync.dma_start(
                out=wt_sb[:, :], in_=w_bf[:, a * 128:(a + 1) * 128],
                transpose=True))
            for j in range(din_ // 512):
                ch.op(nc.tensor, lambda a=a, j=j: nc.tensor.matmul(
                    ps_v[0:1, j * 512:(j + 1) * 512], u_sb[:, a:a + 1],
                    wt_sb[:, j * 512:(j + 1) * 512],
                    start=(a == 0), stop=(a == no - 1)),
                    inc=(a == no - 1 and j == din_ // 512 - 1))
        ch.op(nc.scalar, lambda: nc.scalar.activation(
            vsq[:, :], ps_v[0:1, 0:din_], AF.Square, accum_out=vss[:, :]))
        ch.op(nc.scalar, lambda: nc.scalar.activation(rn[:, :], vss[:, :], AF.Ln))
        ch.op(nc.scalar, lambda: nc.scalar.activation(
            rn[:, :], rn[:, :], AF.Exp, scale=-0.5))
        ch.op(nc.vector, lambda: nc.vector.tensor_single_scalar(
            v_row[:, :], ps_v[0:1, 0:din_], rn[:, :], ALU.mult))
    ch.dma(nc.gpsimd, lambda: nc.gpsimd.dma_start(out=vsig, in_=v_row[:, :]))
    for c in range(nci):
        ch.dma(nc.sync, lambda c=c: nc.sync.dma_start(
            out=vc_sb[:, c:c + 1],
            in_=vsig[0:1, c * 128:(c + 1) * 128].rearrange("a c -> c a")))
    with ExitStack() as ls:
        ps_z = kb.ps(ls, f"sg_psz{name}", [128, dout_])
        for c in range(nci):
            ch.dma(nc.sync, lambda c=c: nc.sync.dma_start(
                out=wrow[:, :], in_=w_bf[c * 128:(c + 1) * 128, :]))
            for j in range(dout_ // 512):
                ch.op(nc.tensor, lambda c=c, j=j: nc.tensor.matmul(
                    ps_z[0:1, j * 512:(j + 1) * 512], vc_sb[:, c:c + 1],
                    wrow[:, j * 512:(j + 1) * 512],
                    start=(c == 0), stop=(c == nci - 1)),
                    inc=(c == nci - 1 and j == dout_ // 512 - 1))
        ch.op(nc.scalar, lambda: nc.scalar.activation(
            zsq[:, :], ps_z[0:1, 0:dout_], AF.Square, accum_out=zss[:, :]))
    ch.op(nc.scalar, lambda: nc.scalar.activation(sc[:, :], zss[:, :], AF.Ln))
    ch.op(nc.scalar, lambda: nc.scalar.activation(
        sc[:, :], sc[:, :], AF.Exp, scale=-0.5))
    ch.dma(nc.sync, lambda: nc.sync.dma_start(
        out=sp_sb[:, :], in_=sp_dram.unsqueeze(0)))
    ch.op(nc.vector, lambda: nc.vector.tensor_mul(sc[:, :], sc[:, :], sp_sb[:, :]))
    return sc


def emit_l0_attn(kb, x_dram, wqkv_f, uq_f, sq_f, wproj_f, up_f, sp_f, y_dram):
    nc = kb.nc
    wqkv_bf = kb.dscr("l0_wqkv_bf", [1024, 3072], BF16)
    wproj_bf = kb.dscr("l0_wproj_bf", [1024, 1024], BF16)
    ln_scr = kb.dscr("l0_ln", [1024, 4096], BF16)
    q_scr = kb.dscr("l0_q", [1024, 4096], BF16)
    k_scr = kb.dscr("l0_k", [1024, 4096], BF16)
    v_scr = kb.dscr("l0_v", [4096, 1024], BF16)
    avn_scr = kb.dscr("l0_avn", [1024, 4096], BF16)
    sc_scr = kb.dscr("l0_scales", [1, 4], F32)

    with ExitStack() as st:
        ch = Chain(kb, st, "l0a_sig")
        emit_cast(kb, ch, wqkv_f, wqkv_bf)
        emit_cast(kb, ch, wproj_f, wproj_bf)
        sc_qkv = emit_sigma(kb, ch, st, wqkv_bf, 1024, 3072, uq_f, sq_f, "qkv")
        sc_proj = emit_sigma(kb, ch, st, wproj_bf, 1024, 1024, up_f, sp_f, "proj")
        sq_11 = kb.sb(st, "sq11", [1, 1], F32)
        ch.op(nc.vector, lambda: nc.vector.tensor_scalar_mul(
            sq_11[:, :], sc_qkv[:, :], 0.125))
        ch.dma(nc.gpsimd, lambda: nc.gpsimd.dma_start(
            out=sc_scr[0:1, 0:1], in_=sq_11[:, :]))
        ch.dma(nc.gpsimd, lambda: nc.gpsimd.dma_start(
            out=sc_scr[0:1, 1:2], in_=sc_qkv[:, :]))
        ch.dma(nc.gpsimd, lambda: nc.gpsimd.dma_start(
            out=sc_scr[0:1, 2:3], in_=sc_proj[:, :]))
        ch.seal()
    barrier(kb)

    with ExitStack() as st:
        ch = Chain(kb, st, "l0a_ln")
        emit_ln(kb, ch, st, x_dram, ln_scr)
        ch.seal()
    barrier(kb)

    with ExitStack() as st:
        ch = Chain(kb, st, "l0a_qkv")
        s_all = kb.sb(st, "sall", [128, 3], F32)
        ch.dma(nc.sync, lambda: nc.sync.dma_start(
            out=s_all[:, :], in_=sc_scr[0:1, 0:3].broadcast_to([128, 3])))
        wq = kb.sb(st, "wq", [128, 8 * 1024], BF16)
        wk = kb.sb(st, "wk", [128, 8 * 1024], BF16)
        wv = kb.sb(st, "wv", [128, 8 * 1024], BF16)
        lnm = kb.sb(st, "lnm", [128, 8 * 512], BF16)
        ob = kb.sb(st, "ob", [128, 1024], BF16)
        ps0 = kb.ps(st, "qk_ps", [128, 512])
        ps_v2 = kb.ps(st, "v_ps", [128, 1024])
        for wt, lo in ((wq, 0), (wk, 1024), (wv, 2048)):
            ch.dma(nc.sync, lambda wt=wt, lo=lo: nc.sync.dma_start(
                out=sb3(wt, 8, 1024),
                in_=wqkv_bf[:, lo:lo + 1024].rearrange("(d p) c -> p d c", p=128)))
        for m in range(8):
            ch.dma(nc.sync, lambda m=m: nc.sync.dma_start(
                out=sb3(lnm, 8, 512), in_=fm(ln_scr, m * 512, (m + 1) * 512)))
            for wt, si, dst in ((wq, 0, q_scr), (wk, 1, k_scr)):
                for oc in range(8):
                    for dd in range(8):
                        ch.op(nc.tensor, lambda wt=wt, oc=oc, dd=dd: nc.tensor.matmul(
                            ps0[:, :], wt[:, dd * 1024 + oc * 128:dd * 1024 + (oc + 1) * 128],
                            lnm[:, dd * 512:(dd + 1) * 512],
                            start=(dd == 0), stop=(dd == 7)), inc=(dd == 7))
                    ch.op(nc.vector, lambda si=si: nc.vector.tensor_single_scalar(
                        ob[:, 0:512], ps0[:, :], s_all[:, si:si + 1], ALU.mult))
                    ch.dma(nc.gpsimd, lambda dst=dst, oc=oc, m=m: nc.gpsimd.dma_start(
                        out=dst[oc * 128:(oc + 1) * 128, m * 512:(m + 1) * 512],
                        in_=ob[:, 0:512]))
            for mt in range(4):
                for dd in range(8):
                    for half in range(2):
                        ch.op(nc.tensor, lambda mt=mt, dd=dd, half=half: nc.tensor.matmul(
                            ps_v2[:, half * 512:(half + 1) * 512],
                            lnm[:, dd * 512 + mt * 128:dd * 512 + (mt + 1) * 128],
                            wv[:, dd * 1024 + half * 512:dd * 1024 + (half + 1) * 512],
                            start=(dd == 0), stop=(dd == 7)),
                            inc=(dd == 7 and half == 1))
                ch.op(nc.vector, lambda: nc.vector.tensor_single_scalar(
                    ob[:, 0:1024], ps_v2[:, :], s_all[:, 1:2], ALU.mult))
                ch.dma(nc.gpsimd, lambda m=m, mt=mt: nc.gpsimd.dma_start(
                    out=v_scr[m * 512 + mt * 128:m * 512 + (mt + 1) * 128, :],
                    in_=ob[:, 0:1024]))
        ch.seal()
    barrier(kb)

    with ExitStack() as st:
        ch = Chain(kb, st, "l0a_att")
        ident_bf = kb.const_sb(st, ch, np.eye(128, dtype=np.float32).astype(BF), "idb")
        ident_f = kb.const_sb(st, ch, np.eye(128, dtype=np.float32), "idf")
        tri = np.where(np.tril(np.ones((128, 128))) > 0, 0.0, NEG).astype(np.float32)
        trimask = kb.const_sb(st, ch, tri, "tri")
        ones64f = kb.const_sb(st, ch, np.ones((1, 64), np.float32), "o64")
        qh = kb.sb(st, "qh", [64, 4096], BF16)
        kh = kb.sb(st, "kh", [64, 4096], BF16)
        vh = kb.sb(st, "vh", [128, 32 * 64], BF16)
        p_sb = kb.sb(st, "p", [128, 2560], BF16)
        pt_sb = kb.sb(st, "pt", [128, 512], BF16)
        avh = kb.sb(st, "avh", [64, 4096], F32)
        avn = kb.sb(st, "avn", [64, 4096], BF16)
        lrec = kb.sb(st, "lrec", [128, 32], F32)
        mneg = kb.sb(st, "mneg", [128, 2], F32)
        lg = kb.sb(st, "lg", [128, 2], F32)
        sm1 = kb.sb(st, "sm1", [128, 1], F32)
        sm2 = kb.sb(st, "sm2", [128, 2], F32)
        arow = kb.sb(st, "arow", [1, 256], F32)
        ab_sb = kb.sb(st, "ab", [64, 256], F32)
        t_sb = kb.sb(st, "t1", [64, 128], F32)
        u_sb = kb.sb(st, "t2", [64, 128], F32)
        lrT = kb.sb(st, "lrT", [1, 512], F32)
        ps_s = kb.ps(st, "att_s", [128, 2560])
        pst = kb.ps(st, "att_t", [128, 1024], BF16)
        ps_av = kb.ps(st, "att_av", [128, 256])
        ps_m = kb.ps(st, "att_m", [128, 512])
        vh3 = sb3(vh, 32, 64)
        for h in range(16):
            ch.dma(nc.sync, lambda h=h: nc.sync.dma_start(
                out=qh[:, :], in_=q_scr[64 * h:64 * h + 64, :]))
            ch.dma(nc.sync, lambda h=h: nc.sync.dma_start(
                out=kh[:, :], in_=k_scr[64 * h:64 * h + 64, :]))
            ch.dma(nc.sync, lambda h=h: nc.sync.dma_start(
                out=vh3, in_=v_scr[:, 64 * h:64 * h + 64].rearrange(
                    "(t p) c -> p t c", p=128)))
            for i in range(32):
                jmax = 128 * (i + 1)
                nj = ceil_div(jmax, 512)
                groups = [(0, min(nj, 5))]
                if nj > 5:
                    groups.append((5, nj))
                njt_tot = jmax // 128
                for g, (c0, c1) in enumerate(groups):
                    gw = min(jmax, c1 * 512) - c0 * 512
                    for jc in range(c0, c1):
                        w = min(512, jmax - jc * 512)
                        ch.op(nc.tensor, lambda i=i, jc=jc, c0=c0, w=w: nc.tensor.matmul(
                            ps_s[:, (jc - c0) * 512:(jc - c0) * 512 + w],
                            qh[:, i * 128:(i + 1) * 128],
                            kh[:, jc * 512:jc * 512 + w],
                            start=True, stop=True), inc=(jc == c1 - 1))
                    if g == len(groups) - 1:
                        pos = i * 128 - c0 * 512
                        ch.op(nc.vector, lambda pos=pos: nc.vector.tensor_add(
                            ps_s[:, pos:pos + 128], ps_s[:, pos:pos + 128],
                            trimask[:, :]))
                    ch.op(nc.vector, lambda g=g, gw=gw: nc.vector.reduce_max(
                        mneg[:, g:g + 1], ps_s[:, 0:gw], AX.X, negate=True))
                    ch.op(nc.scalar, lambda g=g, gw=gw: nc.scalar.activation(
                        p_sb[:, 0:gw], ps_s[:, 0:gw], AF.Exp,
                        bias=mneg[:, g:g + 1], accum_out=lg[:, g:g + 1]))
                    njc = gw // 128
                    for b0 in range(0, njc, 4):
                        bn = min(4, njc - b0)
                        for s in range(bn):
                            ch.op(nc.tensor, lambda b0=b0, s=s: nc.tensor.transpose(
                                pst[:, s * 128:(s + 1) * 128],
                                p_sb[:, (b0 + s) * 128:(b0 + s + 1) * 128],
                                ident_bf[:, :]), inc=(s == bn - 1))
                        ch.op(nc.vector, lambda bn=bn: nc.vector.tensor_copy(
                            pt_sb[:, 0:bn * 128], pst[:, 0:bn * 128]))
                        for s in range(bn):
                            jt = c0 * 4 + b0 + s
                            first = (b0 + s == 0)
                            last = (b0 + s == njc - 1)
                            ch.op(nc.tensor, lambda jt=jt, s=s, g=g, first=first,
                                  last=last: nc.tensor.matmul(
                                ps_av[0:64, g * 128:(g + 1) * 128],
                                vh3[:, jt, :], pt_sb[:, s * 128:(s + 1) * 128],
                                start=first, stop=last), inc=last)
                if len(groups) == 1:
                    ch.op(nc.vector, lambda i=i: nc.vector.tensor_copy(
                        avh[:, i * 128:(i + 1) * 128], ps_av[0:64, 0:128]))
                    ch.op(nc.vector, lambda i=i: nc.vector.tensor_copy(
                        lrec[:, i:i + 1], lg[:, 0:1]))
                else:
                    ch.op(nc.vector, lambda: nc.vector.tensor_tensor(
                        sm1[:, :], mneg[:, 0:1], mneg[:, 1:2], ALU.min))
                    ch.op(nc.vector, lambda: nc.vector.tensor_sub(
                        sm2[:, 0:2], sm1[:, :].broadcast_to([128, 2]), mneg[:, 0:2]))
                    ch.op(nc.scalar, lambda: nc.scalar.activation(
                        sm2[:, 0:2], sm2[:, 0:2], AF.Exp))
                    ch.op(nc.vector, lambda: nc.vector.tensor_mul(
                        sm2[:, 0:2], lg[:, 0:2], sm2[:, 0:2]))
                    ch.op(nc.vector, lambda i=i: nc.vector.reduce_sum(
                        lrec[:, i:i + 1], sm2[:, 0:2], AX.X))
                    ch.op(nc.vector, lambda: nc.vector.tensor_sub(
                        sm2[:, 0:2], sm1[:, :].broadcast_to([128, 2]), mneg[:, 0:2]))
                    ch.op(nc.scalar, lambda: nc.scalar.activation(
                        sm2[:, 0:2], sm2[:, 0:2], AF.Exp))
                    for g in range(2):
                        ch.op(nc.tensor, lambda g=g: nc.tensor.transpose(
                            ps_m[0:1, g * 128:(g + 1) * 128], sm2[:, g:g + 1],
                            ident_f[:, :]), inc=(g == 1))
                    ch.op(nc.vector, lambda: nc.vector.tensor_copy(
                        arow[0:1, :], ps_m[0:1, 0:256]))
                    for g in range(2):
                        ch.op(nc.tensor, lambda g=g: nc.tensor.matmul(
                            ps_m[0:64, 128 + g * 128:128 + (g + 1) * 128],
                            ones64f[:, :], arow[0:1, g * 128:(g + 1) * 128],
                            start=True, stop=True), inc=(g == 1))
                    ch.op(nc.vector, lambda: nc.vector.tensor_copy(
                        ab_sb[:, :], ps_m[0:64, 128:384]))
                    ch.op(nc.vector, lambda: nc.vector.tensor_mul(
                        t_sb[:, :], ps_av[0:64, 0:128], ab_sb[:, 0:128]))
                    ch.op(nc.vector, lambda: nc.vector.tensor_mul(
                        u_sb[:, :], ps_av[0:64, 128:256], ab_sb[:, 128:256]))
                    ch.op(nc.vector, lambda i=i: nc.vector.tensor_add(
                        avh[:, i * 128:(i + 1) * 128], t_sb[:, :], u_sb[:, :]))
            ch.op(nc.vector, lambda: nc.vector.reciprocal(lrec[:, :], lrec[:, :]))
            for it0 in range(0, 32, 4):
                for s in range(4):
                    ch.op(nc.tensor, lambda it0=it0, s=s: nc.tensor.transpose(
                        ps_m[0:1, s * 128:(s + 1) * 128],
                        lrec[:, it0 + s:it0 + s + 1], ident_f[:, :]),
                        inc=(s == 3))
                ch.op(nc.vector, lambda: nc.vector.tensor_copy(
                    lrT[0:1, :], ps_m[0:1, 0:512]))
                for s in range(4):
                    ch.op(nc.tensor, lambda s=s: nc.tensor.matmul(
                        ps_m[0:64, s * 128:(s + 1) * 128], ones64f[:, :],
                        lrT[0:1, s * 128:(s + 1) * 128], start=True, stop=True),
                        inc=(s == 3))
                ch.op(nc.vector, lambda it0=it0: nc.vector.tensor_mul(
                    avn[:, it0 * 128:(it0 + 4) * 128],
                    avh[:, it0 * 128:(it0 + 4) * 128], ps_m[0:64, 0:512]))
            ch.dma(nc.gpsimd, lambda h=h: nc.gpsimd.dma_start(
                out=avn_scr[64 * h:64 * h + 64, :], in_=avn[:, :]))
        ch.seal()
    barrier(kb)

    with ExitStack() as st:
        ch = Chain(kb, st, "l0a_proj")
        s_all = kb.sb(st, "sall2", [128, 3], F32)
        ch.dma(nc.sync, lambda: nc.sync.dma_start(
            out=s_all[:, :], in_=sc_scr[0:1, 0:3].broadcast_to([128, 3])))
        wp = kb.sb(st, "wp", [128, 8 * 1024], BF16)
        avm = kb.sb(st, "avm", [128, 8 * 512], BF16)
        xt = kb.sb(st, "xt", [128, 512], F32)
        yb = kb.sb(st, "yb", [128, 512], F32)
        ps0 = kb.ps(st, "pj_ps", [128, 512])
        ch.dma(nc.sync, lambda: nc.sync.dma_start(
            out=sb3(wp, 8, 1024),
            in_=wproj_bf.rearrange("(d p) c -> p d c", p=128)))
        for m in range(8):
            ch.dma(nc.sync, lambda m=m: nc.sync.dma_start(
                out=sb3(avm, 8, 512), in_=fm(avn_scr, m * 512, (m + 1) * 512)))
            for oc in range(8):
                ch.dma(nc.sync, lambda oc=oc, m=m: nc.sync.dma_start(
                    out=xt[:, :],
                    in_=x_dram[oc * 128:(oc + 1) * 128, m * 512:(m + 1) * 512]))
                for dd in range(8):
                    ch.op(nc.tensor, lambda oc=oc, dd=dd: nc.tensor.matmul(
                        ps0[:, :], wp[:, dd * 1024 + oc * 128:dd * 1024 + (oc + 1) * 128],
                        avm[:, dd * 512:(dd + 1) * 512],
                        start=(dd == 0), stop=(dd == 7)), inc=(dd == 7))
                ch.op(nc.vector, lambda: nc.vector.scalar_tensor_tensor(
                    yb[:, :], ps0[:, :], s_all[:, 2:3], xt[:, :],
                    ALU.mult, ALU.add))
                ch.dma(nc.gpsimd, lambda oc=oc, m=m: nc.gpsimd.dma_start(
                    out=y_dram[oc * 128:(oc + 1) * 128, m * 512:(m + 1) * 512],
                    in_=yb[:, :]))
        ch.seal()
    barrier(kb)


def emit_gating_topk(kb, ch, st, ln_scr, wg_f, cw_scr, nex, topk_norm_eps=None,
                     bias_row=None):
    nc = kb.nc
    wg_sb = kb.sb(st, "gt_wg", [nex, 1024], BF16)
    wgT = kb.sb(st, "gt_wgT", [128, 8 * nex], BF16)
    lnm = kb.sb(st, "gt_ln", [128, 8 * 512], BF16)
    gate = kb.sb(st, "gt_g", [128, nex], F32)
    mg = kb.sb(st, "gt_m", [128, 1], F32)
    ls = kb.sb(st, "gt_l", [128, 1], F32)
    m1 = kb.sb(st, "gt_m1", [128, 1], F32)
    msk = kb.sb(st, "gt_msk", [128, nex], F32)
    gm = kb.sb(st, "gt_gm", [128, nex], F32)
    cwT = kb.sb(st, "gt_cwT", [nex, 4096], F32)
    ident_bf = kb.const_sb(st, ch, np.eye(128, dtype=np.float32).astype(BF), "idb")
    ident_f = kb.const_sb(st, ch, np.eye(128, dtype=np.float32), "idf")
    ps_g = kb.ps(st, "gt_ps", [128, 512])
    ps_t = kb.ps(st, "gt_pst", [128, 512], BF16)
    ps_tf = kb.ps(st, "gt_ptf", [128, 512])
    if bias_row is not None:
        gb_b = kb.sb(st, "gt_bb", [128, nex], F32)
        ch.dma(nc.sync, lambda: nc.sync.dma_start(
            out=gb_b[:, :], in_=bias_row.broadcast_to([128, nex])))
    ch.dma(nc.gpsimd, lambda: nc.gpsimd.dma_start(out=wg_sb[:, :], in_=wg_f))
    for c in range(8):
        ch.op(nc.tensor, lambda c=c: nc.tensor.transpose(
            ps_t[0:128, 0:nex], wg_sb[0:nex, c * 128:(c + 1) * 128],
            ident_bf[0:nex, 0:nex]))
        ch.op(nc.vector, lambda c=c: nc.vector.tensor_copy(
            wgT[:, c * nex:(c + 1) * nex], ps_t[0:128, 0:nex]))
    for m in range(8):
        ch.dma(nc.sync, lambda m=m: nc.sync.dma_start(
            out=sb3(lnm, 8, 512), in_=fm(ln_scr, m * 512, (m + 1) * 512)))
        for mt in range(4):
            for dd in range(8):
                ch.op(nc.tensor, lambda mt=mt, dd=dd: nc.tensor.matmul(
                    ps_g[:, 0:nex],
                    lnm[:, dd * 512 + mt * 128:dd * 512 + (mt + 1) * 128],
                    wgT[:, dd * nex:(dd + 1) * nex],
                    start=(dd == 0), stop=(dd == 7)), inc=(dd == 7))
            if bias_row is not None:
                ch.op(nc.vector, lambda: nc.vector.tensor_add(
                    ps_g[:, 0:nex], ps_g[:, 0:nex], gb_b[:, :]))
            ch.op(nc.vector, lambda: nc.vector.reduce_max(
                mg[:, :], ps_g[:, 0:nex], AX.X, negate=True))
            ch.op(nc.scalar, lambda: nc.scalar.activation(
                gate[:, :], ps_g[:, 0:nex], AF.Exp, bias=mg[:, :],
                accum_out=ls[:, :]))
            ch.op(nc.vector, lambda: nc.vector.reciprocal(ls[:, :], ls[:, :]))
            ch.op(nc.vector, lambda: nc.vector.tensor_single_scalar(
                gate[:, :], gate[:, :], ls[:, :], ALU.mult))
            ch.op(nc.vector, lambda: nc.vector.reduce_max(m1[:, :], gate[:, :], AX.X))
            ch.op(nc.vector, lambda: nc.vector.tensor_single_scalar(
                msk[:, :], gate[:, :], m1[:, :], ALU.is_equal))
            ch.op(nc.vector, lambda: nc.vector.scalar_tensor_tensor(
                gm[:, :], msk[:, :], NEG, gate[:, :], ALU.mult, ALU.add))
            ch.op(nc.vector, lambda: nc.vector.reduce_max(m1[:, :], gm[:, :], AX.X))
            ch.op(nc.vector, lambda: nc.vector.tensor_single_scalar(
                msk[:, :], gate[:, :], m1[:, :], ALU.is_ge))
            ch.op(nc.vector, lambda: nc.vector.tensor_mul(
                gate[:, :], gate[:, :], msk[:, :]))
            if topk_norm_eps is not None:
                ch.op(nc.vector, lambda: nc.vector.reduce_sum(
                    ls[:, :], gate[:, :], AX.X))
                ch.op(nc.vector, lambda: nc.vector.tensor_scalar_add(
                    ls[:, :], ls[:, :], topk_norm_eps))
                ch.op(nc.vector, lambda: nc.vector.reciprocal(ls[:, :], ls[:, :]))
                ch.op(nc.vector, lambda: nc.vector.tensor_single_scalar(
                    gate[:, :], gate[:, :], ls[:, :], ALU.mult))
            ch.op(nc.tensor, lambda: nc.tensor.transpose(
                ps_tf[0:nex, 0:128], gate[:, 0:nex], ident_f[:, :]))
            ch.op(nc.vector, lambda m=m, mt=mt: nc.vector.tensor_copy(
                cwT[:, m * 512 + mt * 128:m * 512 + (mt + 1) * 128],
                ps_tf[0:nex, 0:128]))
    ch.dma(nc.gpsimd, lambda: nc.gpsimd.dma_start(out=cw_scr, in_=cwT[:, :]))


def emit_l0_ffn(kb, x_dram, wg_f, w1_f, b1t_f, w2_f, b2_f, y_dram):
    nc = kb.nc
    w1_bf = kb.dscr("l0f_w1", [8, 1024, 4096], BF16)
    w2_bf = kb.dscr("l0f_w2", [8, 4096, 1024], BF16)
    ln_scr = kb.dscr("l0f_ln", [1024, 4096], BF16)
    cw_scr = kb.dscr("l0f_cw", [8, 4096], F32)
    with ExitStack() as st:
        ch = Chain(kb, st, "l0f_pre")
        for e in range(8):
            ch.dma(nc.gpsimd, lambda e=e: nc.gpsimd.dma_start(
                out=w1_bf[e], in_=w1_f[e]))
            ch.dma(nc.gpsimd, lambda e=e: nc.gpsimd.dma_start(
                out=w2_bf[e], in_=w2_f[e]))
        ch.dma(nc.gpsimd, lambda: nc.gpsimd.dma_start(out=y_dram, in_=x_dram))
        emit_ln(kb, ch, st, x_dram, ln_scr)
        ch.seal()
    barrier(kb)
    with ExitStack() as st:
        ch = Chain(kb, st, "l0f_gate")
        emit_gating_topk(kb, ch, st, ln_scr, wg_f, cw_scr, 8)
        ch.seal()
    barrier(kb)
    with ExitStack() as st:
        ch = Chain(kb, st, "l0f_ex")
        w1e = kb.sb(st, "w1e", [128, 8 * 4096], BF16)
        w2e = kb.sb(st, "w2e", [128, 32 * 1024], BF16)
        lnm = kb.sb(st, "lnm", [128, 8 * 512], BF16)
        hblk = kb.sb(st, "hblk", [128, 32 * 512], BF16)
        gbuf = kb.sb(st, "gbuf", [128, 512], BF16)
        cwbe = kb.sb(st, "cwbe", [128, 4096], BF16)
        b1e = kb.sb(st, "b1e", [128, 32], F32)
        b2_bf = kb.sb(st, "b2bf", [8, 1024], BF16)
        cwT_bf = kb.sb(st, "cwtbf", [8, 4096], BF16)
        tmpo = kb.sb(st, "tmpo", [128, 512], F32)
        ps1 = kb.ps(st, "ffn_ps1", [128, 512])
        ps2 = kb.ps(st, "ffn_ps2", [128, 512])
        ch.dma(nc.gpsimd, lambda: nc.gpsimd.dma_start(out=b2_bf[:, :], in_=b2_f))
        ch.dma(nc.gpsimd, lambda: nc.gpsimd.dma_start(out=cwT_bf[:, :], in_=cw_scr))
        for e in range(8):
            ch.dma(nc.sync, lambda e=e: nc.sync.dma_start(
                out=sb3(w1e, 8, 4096),
                in_=w1_bf[e].rearrange("(d p) c -> p d c", p=128)))
            ch.dma(nc.sync, lambda e=e: nc.sync.dma_start(
                out=sb3(w2e, 32, 1024),
                in_=w2_bf[e].rearrange("(d p) c -> p d c", p=128)))
            ch.dma(nc.gpsimd, lambda e=e: nc.gpsimd.dma_start(
                out=cwbe[:, :], in_=cw_scr[e:e + 1, :].broadcast_to([128, 4096])))
            ch.dma(nc.sync, lambda e=e: nc.sync.dma_start(
                out=b1e[:, :], in_=b1t_f[e]))
            for m in range(8):
                ch.dma(nc.sync, lambda m=m: nc.sync.dma_start(
                    out=sb3(lnm, 8, 512), in_=fm(ln_scr, m * 512, (m + 1) * 512)))
                for hc in range(32):
                    for dd in range(8):
                        ch.op(nc.tensor, lambda hc=hc, dd=dd: nc.tensor.matmul(
                            ps1[:, :],
                            w1e[:, dd * 4096 + hc * 128:dd * 4096 + (hc + 1) * 128],
                            lnm[:, dd * 512:(dd + 1) * 512],
                            start=(dd == 0), stop=(dd == 7)), inc=(dd == 7))
                    ch.op(nc.scalar, lambda hc=hc: nc.scalar.activation(
                        gbuf[:, :], ps1[:, :], AF.Gelu, bias=b1e[:, hc:hc + 1]))
                    ch.op(nc.vector, lambda hc=hc, m=m: nc.vector.tensor_mul(
                        hblk[:, hc * 512:(hc + 1) * 512], gbuf[:, :],
                        cwbe[:, m * 512:(m + 1) * 512]))
                for oc in range(8):
                    if e == 0:
                        ch.op(nc.tensor, lambda oc=oc, m=m: nc.tensor.matmul(
                            ps2[:, :], b2_bf[:, oc * 128:(oc + 1) * 128],
                            cwT_bf[:, m * 512:(m + 1) * 512],
                            start=True, stop=False), inc=False)
                    for hc in range(32):
                        ch.op(nc.tensor, lambda oc=oc, hc=hc: nc.tensor.matmul(
                            ps2[:, :],
                            w2e[:, hc * 1024 + oc * 128:hc * 1024 + (oc + 1) * 128],
                            hblk[:, hc * 512:(hc + 1) * 512],
                            start=(e != 0 and hc == 0), stop=(hc == 31)),
                            inc=(hc == 31))
                    ch.op(nc.vector, lambda: nc.vector.tensor_copy(
                        tmpo[:, :], ps2[:, :]))
                    ch.dma(nc.gpsimd, lambda oc=oc, m=m: nc.gpsimd.dma_start(
                        out=y_dram[oc * 128:(oc + 1) * 128, m * 512:(m + 1) * 512],
                        in_=tmpo[:, :], accum_op=ALU.add))
        ch.seal()
    barrier(kb)


def _dft_consts():
    Mf = 8192
    i64 = np.int64
    b = np.arange(64, dtype=i64)[:, None]
    b2 = np.arange(128, dtype=i64)[:, None]
    c = np.arange(128, dtype=i64)[None, :]
    out = {}
    c1v = np.zeros((64, 64 * 128), np.float64)
    s1v = np.zeros((64, 64 * 128), np.float64)
    c1t = np.zeros((128, 64 * 128), np.float64)
    s1t = np.zeros((128, 64 * 128), np.float64)
    for a in range(64):
        th = 2 * np.pi * ((c * (64 * b + a)) % Mf) / Mf
        c1v[:, a * 128:(a + 1) * 128] = np.cos(th)
        s1v[:, a * 128:(a + 1) * 128] = -np.sin(th)
        th2 = 2 * np.pi * ((c * (64 * b2 + a)) % Mf) / Mf
        c1t[:, a * 128:(a + 1) * 128] = np.cos(th2)
        s1t[:, a * 128:(a + 1) * 128] = -np.sin(th2)
    out["c1v"], out["s1v"] = c1v, s1v
    out["c1t"], out["s1t"] = c1t, s1t
    ad = np.outer(np.arange(64), np.arange(64))
    C64 = np.cos(2 * np.pi * (ad % 64) / 64)
    S64 = np.sin(2 * np.pi * (ad % 64) / 64)
    out["ls2a"] = np.concatenate([C64, -S64], 1)
    out["ls2b"] = np.concatenate([S64, C64], 1)
    out["li1a"] = np.concatenate([C64, S64], 1)
    out["li1b"] = np.concatenate([-S64, C64], 1)
    ci2 = np.zeros((128, 64 * 64), np.float64)
    si2 = np.zeros((128, 64 * 64), np.float64)
    cc = np.arange(128, dtype=i64)[:, None]
    bb = np.arange(64, dtype=i64)[None, :]
    for a in range(64):
        th = 2 * np.pi * ((cc * (64 * bb + a)) % Mf) / Mf
        ci2[:, a * 64:(a + 1) * 64] = np.cos(th) / Mf
        si2[:, a * 64:(a + 1) * 64] = -np.sin(th) / Mf
    out["ci2"], out["si2"] = ci2, si2
    return {k: v.astype(np.float32).astype(BF) for k, v in out.items()}


def _rpe_pos_gam():
    lag = np.zeros(8192, np.float64)
    lag[1:4096] = np.arange(1, 4096)
    lag[4097:] = np.arange(4097, 8192) - 8192
    gam = 0.99 ** np.abs(lag)
    gam[4096] = 0.0
    pos = lag.copy()
    return (pos.reshape(64, 128).astype(np.float32),
            gam.reshape(64, 128).astype(np.float32))


def emit_l1_attn(kb, x_dram, uvw_f, ow_f, rpw_f, rlw_f, rlb_f, rfc_f,
                 rolw_f, rolb_f, row_f, y_dram):
    nc = kb.nc
    ln_scr = kb.dscr("l1_ln", [1024, 4096], BF16)
    u_scr = kb.dscr("l1_u", [4096, 1024], BF16)
    v_scr = kb.dscr("l1_v", [4096, 1024], BF16)
    t_scr = kb.dscr("l1_t", [8192, 1024], BF16)
    z_scr = kb.dscr("l1_z", [64, 2, 128, 1024], BF16)
    zt_scr = kb.dscr("l1_zt", [64, 2, 128, 1024], BF16)
    xv_scr = kb.dscr("l1_xv", [128, 128 * 1024], BF16)
    xt_scr = kb.dscr("l1_xt", [128, 128 * 1024], BF16)
    p_scr = kb.dscr("l1_p", [128, 128 * 1024], BF16)
    q_scr = kb.dscr("l1_q", [128, 128 * 1024], BF16)
    y_scr = kb.dscr("l1_y", [4096, 1024], BF16)
    prod_scr = kb.dscr("l1_prod", [4096, 1024], BF16)
    uvw_bf = kb.dscr("l1_uvw", [1024, 2048], BF16)
    ow_bf = kb.dscr("l1_ow", [1024, 1024], BF16)

    with ExitStack() as st:
        ch = Chain(kb, st, "l1_a")
        emit_cast(kb, ch, uvw_f, uvw_bf)
        emit_cast(kb, ch, ow_f, ow_bf)
        emit_ln(kb, ch, st, x_dram, ln_scr)
        ch.seal()
    barrier(kb)

    with ExitStack() as st:
        ch = Chain(kb, st, "l1_b")
        uvw_sb = kb.sb(st, "uvw", [128, 8 * 2048], BF16)
        lnm = kb.sb(st, "lnm", [128, 8 * 512], BF16)
        sil = kb.sb(st, "sil", [128, 2048], BF16)
        ps_uv = kb.ps(st, "uv_ps", [128, 2048])
        ch.dma(nc.sync, lambda: nc.sync.dma_start(
            out=sb3(uvw_sb, 8, 2048),
            in_=uvw_bf.rearrange("(d p) c -> p d c", p=128)))
        for m in range(8):
            ch.dma(nc.sync, lambda m=m: nc.sync.dma_start(
                out=sb3(lnm, 8, 512), in_=fm(ln_scr, m * 512, (m + 1) * 512)))
            for mt in range(4):
                for dd in range(8):
                    for q4 in range(4):
                        ch.op(nc.tensor, lambda mt=mt, dd=dd, q4=q4: nc.tensor.matmul(
                            ps_uv[:, q4 * 512:(q4 + 1) * 512],
                            lnm[:, dd * 512 + mt * 128:dd * 512 + (mt + 1) * 128],
                            uvw_sb[:, dd * 2048 + q4 * 512:dd * 2048 + (q4 + 1) * 512],
                            start=(dd == 0), stop=(dd == 7)),
                            inc=(dd == 7 and q4 == 3))
                ch.op(nc.scalar, lambda: nc.scalar.activation(
                    sil[:, :], ps_uv[:, :], AF.Silu))
                r0 = m * 512 + mt * 128
                ch.dma(nc.gpsimd, lambda r0=r0: nc.gpsimd.dma_start(
                    out=u_scr[r0:r0 + 128, :], in_=sil[:, 0:1024]))
                ch.dma(nc.gpsimd, lambda r0=r0: nc.gpsimd.dma_start(
                    out=v_scr[r0:r0 + 128, :], in_=sil[:, 1024:2048]))
        ch.seal()
    barrier(kb)

    with ExitStack() as st:
        ch = Chain(kb, st, "l1_c")
        POS, GAM = _rpe_pos_gam()
        pos_d = kb.const_dram(POS, "l1pos")
        gam_d = kb.const_dram(GAM, "l1gam")
        eps_sb = kb.const_sb(st, ch, np.full((128, 1), 1e-5, np.float32), "eps1")
        ident_f = kb.const_sb(st, ch, np.eye(128, dtype=np.float32), "idf1")
        posw_b = kb.sb(st, "poswb", [128, 32], F32)
        ch.dma(nc.sync, lambda: nc.sync.dma_start(
            out=posw_b[:, :], in_=rpw_f.broadcast_to([128, 32])))
        lnw_b = kb.sb(st, "lnwb", [128, 3 * 32], F32)
        lnb_b = kb.sb(st, "lnbb", [128, 3 * 32], F32)
        for i in range(3):
            ch.dma(nc.sync, lambda i=i: nc.sync.dma_start(
                out=lnw_b[:, i * 32:(i + 1) * 32],
                in_=rlw_f[i:i + 1, :].broadcast_to([128, 32])))
            ch.dma(nc.sync, lambda i=i: nc.sync.dma_start(
                out=lnb_b[:, i * 32:(i + 1) * 32],
                in_=rlb_f[i:i + 1, :].broadcast_to([128, 32])))
        olw_b = kb.sb(st, "olwb", [128, 32], F32)
        olb_b = kb.sb(st, "olbb", [128, 32], F32)
        ch.dma(nc.sync, lambda: nc.sync.dma_start(
            out=olw_b[:, :], in_=rolw_f.unsqueeze(0).broadcast_to([128, 32])))
        ch.dma(nc.sync, lambda: nc.sync.dma_start(
            out=olb_b[:, :], in_=rolb_f.unsqueeze(0).broadcast_to([128, 32])))
        fc_sb = kb.sb(st, "fcsb", [32, 3 * 32], BF16)
        for i in range(3):
            ch.dma(nc.gpsimd, lambda i=i: nc.gpsimd.dma_start(
                out=fc_sb[:, i * 32:(i + 1) * 32], in_=rfc_f[i]))
        outw_bf = kb.sb(st, "outwb", [32, 1024], BF16)
        ch.dma(nc.gpsimd, lambda: nc.gpsimd.dma_start(out=outw_bf[:, :], in_=row_f))
        pos_col = kb.sb(st, "posc", [128, 1], F32)
        gam_col = kb.sb(st, "gamc", [128, 1], F32)
        h_sb = kb.sb(st, "h", [128, 32], F32)
        hc_sb = kb.sb(st, "hc", [128, 32], F32)
        hsq = kb.sb(st, "hsq", [128, 32], F32)
        s1 = kb.sb(st, "s1", [128, 1], F32)
        rstd = kb.sb(st, "rstd", [128, 1], F32)
        hT = kb.sb(st, "hT", [32, 128], BF16)
        tb = kb.sb(st, "tb", [128, 1024], BF16)
        ps_t = kb.ps(st, "rpe_pt", [128, 128])
        ps_h = kb.ps(st, "rpe_ph", [128, 128])
        ps_o = kb.ps(st, "rpe_po", [128, 1024])

        def rpe_ln_silu_T(wb, bb, io):
            ch.op(nc.vector, lambda: nc.vector.reduce_sum(s1[:, :], h_sb[:, :], AX.X))
            ch.op(nc.vector, lambda: nc.vector.tensor_scalar_mul(
                s1[:, :], s1[:, :], 1.0 / 32))
            ch.op(nc.vector, lambda: nc.vector.tensor_single_scalar(
                hc_sb[:, :], h_sb[:, :], s1[:, :], ALU.subtract))
            ch.op(nc.scalar, lambda: nc.scalar.activation(
                hsq[:, :], hc_sb[:, :], AF.Square))
            ch.op(nc.vector, lambda: nc.vector.reduce_sum(
                rstd[:, :], hsq[:, :], AX.X))
            ch.op(nc.scalar, lambda: nc.scalar.activation(
                rstd[:, :], rstd[:, :], AF.Ln, bias=eps_sb[:, :], scale=1.0 / 32))
            ch.op(nc.scalar, lambda: nc.scalar.activation(
                rstd[:, :], rstd[:, :], AF.Exp, scale=-0.5))
            ch.op(nc.vector, lambda: nc.vector.tensor_single_scalar(
                hc_sb[:, :], hc_sb[:, :], rstd[:, :], ALU.mult))
            ch.op(nc.vector, lambda wb=wb: nc.vector.tensor_mul(
                hc_sb[:, :], hc_sb[:, :], wb))
            ch.op(nc.vector, lambda bb=bb: nc.vector.tensor_add(
                hc_sb[:, :], hc_sb[:, :], bb))
            ch.op(nc.scalar, lambda: nc.scalar.activation(
                hc_sb[:, :], hc_sb[:, :], AF.Silu))
            ch.op(nc.tensor, lambda: nc.tensor.transpose(
                ps_t[0:32, 0:128], hc_sb[:, 0:32], ident_f[:, :]))
            ch.op(nc.vector, lambda: nc.vector.tensor_copy(
                hT[:, :], ps_t[0:32, 0:128]))

        for t in range(64):
            ch.dma(nc.sync, lambda t=t: nc.sync.dma_start(
                out=pos_col[:, :], in_=pos_d[t:t + 1, :].rearrange("a c -> c a")))
            ch.dma(nc.sync, lambda t=t: nc.sync.dma_start(
                out=gam_col[:, :], in_=gam_d[t:t + 1, :].rearrange("a c -> c a")))
            ch.op(nc.vector, lambda: nc.vector.tensor_mul(
                h_sb[:, :], pos_col[:, :].broadcast_to([128, 32]), posw_b[:, :]))
            for i in range(3):
                rpe_ln_silu_T(lnw_b[:, i * 32:(i + 1) * 32],
                              lnb_b[:, i * 32:(i + 1) * 32], i)
                ch.op(nc.tensor, lambda i=i: nc.tensor.matmul(
                    ps_h[:, 0:32], hT[:, :], fc_sb[:, i * 32:(i + 1) * 32],
                    start=True, stop=True))
                ch.op(nc.vector, lambda: nc.vector.tensor_copy(
                    h_sb[:, :], ps_h[:, 0:32]))
            rpe_ln_silu_T(olw_b[:, :], olb_b[:, :], 3)
            for q in range(2):
                ch.op(nc.tensor, lambda q=q: nc.tensor.matmul(
                    ps_o[:, q * 512:(q + 1) * 512], hT[:, :],
                    outw_bf[:, q * 512:(q + 1) * 512],
                    start=True, stop=True), inc=(q == 1))
            ch.op(nc.vector, lambda: nc.vector.tensor_single_scalar(
                tb[:, :], ps_o[:, :], gam_col[:, :], ALU.mult))
            ch.dma(nc.gpsimd, lambda t=t: nc.gpsimd.dma_start(
                out=t_scr[t * 128:(t + 1) * 128, :], in_=tb[:, :]))
        ch.seal()
    barrier(kb)

    dft = _dft_consts()
    with ExitStack() as st:
        ch = Chain(kb, st, "l1_d")
        c1v = kb.const_sb(st, ch, dft["c1v"], "c1v")
        s1v = kb.const_sb(st, ch, dft["s1v"], "s1v")
        c1t = kb.const_sb(st, ch, dft["c1t"], "c1t")
        s1t = kb.const_sb(st, ch, dft["s1t"], "s1t")
        v1 = kb.sb(st, "v1", [64, 64 * 512], BF16)
        t1 = kb.sb(st, "t1", [128, 64 * 512], BF16)
        ev = kb.sb(st, "ev", [128, 1024], BF16)
        ps_re = kb.ps(st, "s1_re", [128, 512])
        ps_im = kb.ps(st, "s1_im", [128, 512])
        vv = v_scr.rearrange("(b a) ch -> b a ch", a=64)
        tt = t_scr.rearrange("(b a) ch -> b a ch", a=64)
        for cc in range(2):
            ch.dma(nc.sync, lambda cc=cc: nc.sync.dma_start(
                out=sb3(v1, 64, 512), in_=vv[:, :, cc * 512:(cc + 1) * 512]))
            for a in range(64):
                ch.op(nc.tensor, lambda a=a: nc.tensor.matmul(
                    ps_re[:, :], c1v[:, a * 128:(a + 1) * 128],
                    v1[:, a * 512:(a + 1) * 512], start=True, stop=True),
                    inc=False)
                ch.op(nc.tensor, lambda a=a: nc.tensor.matmul(
                    ps_im[:, :], s1v[:, a * 128:(a + 1) * 128],
                    v1[:, a * 512:(a + 1) * 512], start=True, stop=True))
                ch.op(nc.vector, lambda: nc.vector.tensor_copy(
                    ev[:, 0:512], ps_re[:, :]))
                ch.op(nc.scalar, lambda: nc.scalar.copy(
                    ev[:, 512:1024], ps_im[:, :]))
                ch.dma(nc.gpsimd, lambda a=a, cc=cc: nc.gpsimd.dma_start(
                    out=z_scr[a, 0, :, cc * 512:(cc + 1) * 512], in_=ev[:, 0:512]))
                ch.dma(nc.gpsimd, lambda a=a, cc=cc: nc.gpsimd.dma_start(
                    out=z_scr[a, 1, :, cc * 512:(cc + 1) * 512], in_=ev[:, 512:1024]))
        for cc in range(2):
            ch.dma(nc.sync, lambda cc=cc: nc.sync.dma_start(
                out=sb3(t1, 64, 512), in_=tt[:, :, cc * 512:(cc + 1) * 512]))
            for a in range(64):
                ch.op(nc.tensor, lambda a=a: nc.tensor.matmul(
                    ps_re[:, :], c1t[:, a * 128:(a + 1) * 128],
                    t1[:, a * 512:(a + 1) * 512], start=True, stop=True),
                    inc=False)
                ch.op(nc.tensor, lambda a=a: nc.tensor.matmul(
                    ps_im[:, :], s1t[:, a * 128:(a + 1) * 128],
                    t1[:, a * 512:(a + 1) * 512], start=True, stop=True))
                ch.op(nc.vector, lambda: nc.vector.tensor_copy(
                    ev[:, 0:512], ps_re[:, :]))
                ch.op(nc.scalar, lambda: nc.scalar.copy(
                    ev[:, 512:1024], ps_im[:, :]))
                ch.dma(nc.gpsimd, lambda a=a, cc=cc: nc.gpsimd.dma_start(
                    out=zt_scr[a, 0, :, cc * 512:(cc + 1) * 512], in_=ev[:, 0:512]))
                ch.dma(nc.gpsimd, lambda a=a, cc=cc: nc.gpsimd.dma_start(
                    out=zt_scr[a, 1, :, cc * 512:(cc + 1) * 512], in_=ev[:, 512:1024]))
        ch.seal()
    barrier(kb)

    with ExitStack() as st:
        ch = Chain(kb, st, "l1_e")
        ls2a = kb.const_sb(st, ch, dft["ls2a"], "ls2a")
        ls2b = kb.const_sb(st, ch, dft["ls2b"], "ls2b")
        zre = kb.sb(st, "zre", [64, 128 * 256], BF16)
        zim = kb.sb(st, "zim", [64, 128 * 256], BF16)
        ev = kb.sb(st, "ev2", [128, 512], BF16)
        ps = kb.ps(st, "s2_ps", [128, 512])
        for src, dst in ((z_scr, xv_scr), (zt_scr, xt_scr)):
            dst3 = dst.rearrange("p (c ch) -> p c ch", ch=1024)
            for cc in range(4):
                ch.dma(nc.sync, lambda src=src, cc=cc: nc.sync.dma_start(
                    out=zre.rearrange("p (c ch) -> p c ch", ch=256),
                    in_=src[:, 0, :, cc * 256:(cc + 1) * 256]))
                ch.dma(nc.sync, lambda src=src, cc=cc: nc.sync.dma_start(
                    out=zim.rearrange("p (c ch) -> p c ch", ch=256),
                    in_=src[:, 1, :, cc * 256:(cc + 1) * 256]))
                for n in range(64):
                    ch.op(nc.tensor, lambda n=n: nc.tensor.matmul(
                        ps[:, :], ls2a[:, :], zre[:, n * 512:(n + 1) * 512],
                        start=True, stop=False), inc=False)
                    ch.op(nc.tensor, lambda n=n: nc.tensor.matmul(
                        ps[:, :], ls2b[:, :], zim[:, n * 512:(n + 1) * 512],
                        start=False, stop=True))
                    ch.op(nc.vector, lambda: nc.vector.tensor_copy(ev[:, :], ps[:, :]))
                    c0 = n * 2
                    ch.dma(nc.gpsimd, lambda dst3=dst3, c0=c0, cc=cc:
                           nc.gpsimd.dma_start(
                        out=dst3[:, c0:c0 + 2, cc * 256:(cc + 1) * 256],
                        in_=sb3(ev, 2, 256)))
        ch.seal()
    barrier(kb)

    with ExitStack() as st:
        ch = Chain(kb, st, "l1_f")
        xv_sb = kb.sb(st, "xvs", [128, 4096], BF16)
        xt_sb = kb.sb(st, "xts", [128, 4096], BF16)
        pp = kb.sb(st, "pp", [128, 4096], BF16)
        tm1 = kb.sb(st, "tm1", [64, 4096], BF16)
        tm2 = kb.sb(st, "tm2", [64, 4096], BF16)
        vim = kb.sb(st, "vimlo", [64, 4096], BF16)
        tim = kb.sb(st, "timlo", [64, 4096], BF16)
        for ck in range(32):
            sl = slice(ck * 4096, (ck + 1) * 4096)
            ch.dma(nc.sync, lambda sl=sl: nc.sync.dma_start(
                out=xv_sb[:, :], in_=xv_scr[:, sl]))
            ch.dma(nc.sync, lambda sl=sl: nc.sync.dma_start(
                out=xt_sb[:, :], in_=xt_scr[:, sl]))
            ch.op(nc.vector, lambda: nc.vector.tensor_copy(
                vim[:, :], xv_sb[64:128, :]))
            ch.op(nc.vector, lambda: nc.vector.tensor_copy(
                tim[:, :], xt_sb[64:128, :]))
            ch.op(nc.vector, lambda: nc.vector.tensor_mul(
                tm1[:, :], xv_sb[0:64, :], xt_sb[0:64, :]))
            ch.op(nc.vector, lambda: nc.vector.tensor_mul(
                tm2[:, :], vim[:, :], tim[:, :]))
            ch.op(nc.vector, lambda: nc.vector.tensor_sub(
                pp[0:64, :], tm1[:, :], tm2[:, :]))
            ch.op(nc.vector, lambda: nc.vector.tensor_mul(
                tm1[:, :], xv_sb[0:64, :], tim[:, :]))
            ch.op(nc.vector, lambda: nc.vector.tensor_mul(
                tm2[:, :], vim[:, :], xt_sb[0:64, :]))
            ch.op(nc.vector, lambda: nc.vector.tensor_add(
                pp[64:128, :], tm1[:, :], tm2[:, :]))
            ch.dma(nc.gpsimd, lambda sl=sl: nc.gpsimd.dma_start(
                out=p_scr[:, sl], in_=pp[:, :]))
        ch.seal()
    barrier(kb)

    with ExitStack() as st:
        ch = Chain(kb, st, "l1_g")
        li1a = kb.const_sb(st, ch, dft["li1a"], "li1a")
        li1b = kb.const_sb(st, ch, dft["li1b"], "li1b")
        pp = kb.sb(st, "pg", [128, 4096], BF16)
        pim_lo = kb.sb(st, "pimlo", [64, 4096], BF16)
        ev = kb.sb(st, "evg", [128, 512], BF16)
        ps = kb.ps(st, "i1_ps", [128, 512])
        for ck in range(32):
            sl = slice(ck * 4096, (ck + 1) * 4096)
            ch.dma(nc.sync, lambda sl=sl: nc.sync.dma_start(
                out=pp[:, :], in_=p_scr[:, sl]))
            ch.op(nc.vector, lambda: nc.vector.tensor_copy(
                pim_lo[:, :], pp[64:128, :]))
            for n in range(8):
                ch.op(nc.tensor, lambda n=n: nc.tensor.matmul(
                    ps[:, :], li1a[:, :], pp[0:64, n * 512:(n + 1) * 512],
                    start=True, stop=False), inc=False)
                ch.op(nc.tensor, lambda n=n: nc.tensor.matmul(
                    ps[:, :], li1b[:, :], pim_lo[:, n * 512:(n + 1) * 512],
                    start=False, stop=True))
                ch.op(nc.vector, lambda: nc.vector.tensor_copy(ev[:, :], ps[:, :]))
                ch.dma(nc.gpsimd, lambda ck=ck, n=n: nc.gpsimd.dma_start(
                    out=q_scr[:, ck * 4096 + n * 512:ck * 4096 + (n + 1) * 512],
                    in_=ev[:, :]))
        ch.seal()
    barrier(kb)

    with ExitStack() as st:
        ch = Chain(kb, st, "l1_h")
        ci2 = kb.const_sb(st, ch, dft["ci2"], "ci2")
        si2 = kb.const_sb(st, ch, dft["si2"], "si2")
        qre = kb.sb(st, "qre", [128, 64 * 512], BF16)
        qim = kb.sb(st, "qim", [128, 64 * 512], BF16)
        ev = kb.sb(st, "evh", [64, 512], BF16)
        ps = kb.ps(st, "i2_ps", [128, 512])
        q4 = q_scr.rearrange("(pl a) (c ch) -> pl c a ch", pl=2, ch=1024)
        yv = y_scr.rearrange("(b a) ch -> a b ch", a=64)
        for cc in range(2):
            ch.dma(nc.sync, lambda cc=cc: nc.sync.dma_start(
                out=sb3(qre, 64, 512), in_=q4[0, :, :, cc * 512:(cc + 1) * 512]))
            ch.dma(nc.sync, lambda cc=cc: nc.sync.dma_start(
                out=sb3(qim, 64, 512), in_=q4[1, :, :, cc * 512:(cc + 1) * 512]))
            for a in range(64):
                ch.op(nc.tensor, lambda a=a: nc.tensor.matmul(
                    ps[0:64, :], ci2[:, a * 64:(a + 1) * 64],
                    qre[:, a * 512:(a + 1) * 512], start=True, stop=False),
                    inc=False)
                ch.op(nc.tensor, lambda a=a: nc.tensor.matmul(
                    ps[0:64, :], si2[:, a * 64:(a + 1) * 64],
                    qim[:, a * 512:(a + 1) * 512], start=False, stop=True))
                ch.op(nc.vector, lambda: nc.vector.tensor_copy(ev[:, :], ps[0:64, :]))
                ch.dma(nc.gpsimd, lambda a=a, cc=cc: nc.gpsimd.dma_start(
                    out=yv[a, :, cc * 512:(cc + 1) * 512], in_=ev[:, :]))
        ch.seal()
    barrier(kb)

    with ExitStack() as st:
        ch = Chain(kb, st, "l1_i")
        ub = kb.sb(st, "ub", [128, 4 * 1024], BF16)
        yb = kb.sb(st, "ybuf", [128, 4 * 1024], BF16)
        pb = kb.sb(st, "pbuf", [128, 4 * 1024], BF16)
        uu = u_scr.rearrange("(q p) ch -> p q ch", p=128)
        yy = y_scr.rearrange("(q p) ch -> p q ch", p=128)
        pr = prod_scr.rearrange("(q p) ch -> p q ch", p=128)
        for q0 in range(0, 32, 4):
            ch.dma(nc.sync, lambda q0=q0: nc.sync.dma_start(
                out=sb3(ub, 4, 1024), in_=uu[:, q0:q0 + 4, :]))
            ch.dma(nc.sync, lambda q0=q0: nc.sync.dma_start(
                out=sb3(yb, 4, 1024), in_=yy[:, q0:q0 + 4, :]))
            ch.op(nc.vector, lambda: nc.vector.tensor_mul(
                pb[:, :], ub[:, :], yb[:, :]))
            ch.dma(nc.gpsimd, lambda q0=q0: nc.gpsimd.dma_start(
                out=pr[:, q0:q0 + 4, :], in_=sb3(pb, 4, 1024)))
        ch.seal()
    barrier(kb)
    with ExitStack() as st:
        ch = Chain(kb, st, "l1_j")
        prodT = kb.sb(st, "prodT", [128, 8 * 4096], BF16)
        ow_sb = kb.sb(st, "ow", [128, 8 * 1024], BF16)
        xt = kb.sb(st, "xt1", [128, 512], F32)
        yo = kb.sb(st, "yo", [128, 512], F32)
        ps = kb.ps(st, "j_ps", [128, 512])
        for dd in range(8):
            ch.dma(nc.sync, lambda dd=dd: nc.sync.dma_start(
                out=prodT[:, dd * 4096:(dd + 1) * 4096],
                in_=prod_scr[:, dd * 128:(dd + 1) * 128], transpose=True))
        ch.dma(nc.sync, lambda: nc.sync.dma_start(
            out=sb3(ow_sb, 8, 1024),
            in_=ow_bf.rearrange("(d p) c -> p d c", p=128)))
        for m in range(8):
            for oc in range(8):
                ch.dma(nc.sync, lambda oc=oc, m=m: nc.sync.dma_start(
                    out=xt[:, :],
                    in_=x_dram[oc * 128:(oc + 1) * 128, m * 512:(m + 1) * 512]))
                for dd in range(8):
                    ch.op(nc.tensor, lambda oc=oc, dd=dd, m=m: nc.tensor.matmul(
                        ps[:, :], ow_sb[:, dd * 1024 + oc * 128:dd * 1024 + (oc + 1) * 128],
                        prodT[:, dd * 4096 + m * 512:dd * 4096 + (m + 1) * 512],
                        start=(dd == 0), stop=(dd == 7)), inc=(dd == 7))
                ch.op(nc.vector, lambda: nc.vector.tensor_add(
                    yo[:, :], ps[:, :], xt[:, :]))
                ch.dma(nc.gpsimd, lambda oc=oc, m=m: nc.gpsimd.dma_start(
                    out=y_dram[oc * 128:(oc + 1) * 128, m * 512:(m + 1) * 512],
                    in_=yo[:, :]))
        ch.seal()
    barrier(kb)


def emit_quantize_w(kb, ch, st, w_f, wq_bf, rows, cols, name, eps=1e-5):
    nc = kb.nc
    nch = rows // 128
    per = max(1, 4096 // cols)
    wld = kb.sb(st, f"q_w{name}", [128, per * cols], F32)
    junk = kb.sb(st, f"q_j{name}", [128, per * cols], F32)
    acc = kb.sb(st, f"q_a{name}", [128, 1], F32)
    accs = kb.sb(st, f"q_as{name}", [128, 1], F32)
    gam = kb.sb(st, f"q_g{name}", [1, 1], F32)
    rbc = kb.sb(st, f"q_rb{name}", [128, 1], F32)
    tmpf = kb.sb(st, f"q_tf{name}", [128, per * cols], F32)
    tmpi = kb.sb(st, f"q_ti{name}", [128, per * cols], I32)
    wqb = kb.sb(st, f"q_wq{name}", [128, per * cols], BF16)
    ones_f = kb.const_sb(st, ch, np.ones((128, 1), np.float32), f"q_o{name}")
    r_scr = kb.dscr(f"q_rs{name}", [1, 1], F32)
    w3 = w_f.rearrange("(n p) c -> n p c", p=128)
    wq3 = wq_bf.rearrange("(n p) c -> n p c", p=128)
    with ExitStack() as ls:
        ps = kb.ps(ls, f"q_ps{name}", [128, 512])
        ch.op(nc.vector, lambda: nc.vector.memset(accs[:, :], 0.0))
        for it in range(nch // per):
            ch.dma(nc.sync, lambda it=it: nc.sync.dma_start(
                out=sb3(wld, per, cols), in_=w3[it * per:(it + 1) * per]))
            ch.op(nc.scalar, lambda: nc.scalar.activation(
                junk[:, :], wld[:, :], AF.Abs, accum_out=acc[:, :]))
            ch.op(nc.vector, lambda: nc.vector.tensor_add(
                accs[:, :], accs[:, :], acc[:, :]))
        ch.op(nc.tensor, lambda: nc.tensor.matmul(
            ps[0:1, 0:1], ones_f[:, :], accs[:, :], start=True, stop=True))
        ch.op(nc.vector, lambda: nc.vector.tensor_scalar(
            gam[:, :], ps[0:1, 0:1], 1.0 / (rows * cols), eps, ALU.mult, ALU.add))
        ch.op(nc.vector, lambda: nc.vector.reciprocal(gam[:, :], gam[:, :]))
        ch.dma(nc.gpsimd, lambda: nc.gpsimd.dma_start(out=r_scr, in_=gam[:, :]))
        ch.dma(nc.sync, lambda: nc.sync.dma_start(
            out=rbc[:, :], in_=r_scr.broadcast_to([128, 1])))
        for it in range(nch // per):
            ch.dma(nc.sync, lambda it=it: nc.sync.dma_start(
                out=sb3(wld, per, cols), in_=w3[it * per:(it + 1) * per]))
            ch.op(nc.vector, lambda: nc.vector.tensor_single_scalar(
                tmpf[:, :], wld[:, :], rbc[:, :], ALU.mult))
            ch.op(nc.vector, lambda: nc.vector.tensor_copy(tmpi[:, :], tmpf[:, :]))
            ch.op(nc.vector, lambda: nc.vector.tensor_scalar(
                wqb[:, :], tmpi[:, :], -1.0, 1.0, ALU.max, ALU.min))
            ch.dma(nc.gpsimd, lambda it=it: nc.gpsimd.dma_start(
                out=wq3[it * per:(it + 1) * per], in_=sb3(wqb, per, cols)))


def emit_l1_ffn(kb, x_dram, w1_f, b1t_f, w2_f, b2t_f, y_dram):
    nc = kb.nc
    ln_scr = kb.dscr("l1f_ln", [1024, 4096], BF16)
    wq1 = kb.dscr("l1f_wq1", [4096, 1024], BF16)
    wq2 = kb.dscr("l1f_wq2", [1024, 4096], BF16)
    with ExitStack() as st:
        ch = Chain(kb, st, "l1f_q")
        emit_quantize_w(kb, ch, st, w1_f, wq1, 4096, 1024, "w1")
        emit_quantize_w(kb, ch, st, w2_f, wq2, 1024, 4096, "w2")
        emit_ln(kb, ch, st, x_dram, ln_scr)
        ch.seal()
    barrier(kb)
    with ExitStack() as st:
        ch = Chain(kb, st, "l1f_m")
        w1T = kb.sb(st, "w1T", [128, 8 * 4096], BF16)
        w2T = kb.sb(st, "w2T", [128, 32 * 1024], BF16)
        lnm = kb.sb(st, "lnm1f", [128, 8 * 512], BF16)
        hblk = kb.sb(st, "hblk1f", [128, 32 * 512], BF16)
        b1t = kb.sb(st, "b1t", [128, 32], F32)
        b2t = kb.sb(st, "b2t", [128, 8], F32)
        xt = kb.sb(st, "xt1f", [128, 512], F32)
        yb = kb.sb(st, "yb1f", [128, 512], F32)
        ps1 = kb.ps(st, "1f_ps1", [128, 512])
        ps2 = kb.ps(st, "1f_ps2", [128, 512])
        ch.dma(nc.sync, lambda: nc.sync.dma_start(out=b1t[:, :], in_=b1t_f))
        ch.dma(nc.sync, lambda: nc.sync.dma_start(out=b2t[:, :], in_=b2t_f))
        for dd in range(8):
            ch.dma(nc.sync, lambda dd=dd: nc.sync.dma_start(
                out=w1T[:, dd * 4096:(dd + 1) * 4096],
                in_=wq1[:, dd * 128:(dd + 1) * 128], transpose=True))
        for fc in range(32):
            ch.dma(nc.sync, lambda fc=fc: nc.sync.dma_start(
                out=w2T[:, fc * 1024:(fc + 1) * 1024],
                in_=wq2[:, fc * 128:(fc + 1) * 128], transpose=True))
        for m in range(8):
            ch.dma(nc.sync, lambda m=m: nc.sync.dma_start(
                out=sb3(lnm, 8, 512), in_=fm(ln_scr, m * 512, (m + 1) * 512)))
            for fc in range(32):
                for dd in range(8):
                    ch.op(nc.tensor, lambda fc=fc, dd=dd: nc.tensor.matmul(
                        ps1[:, :],
                        w1T[:, dd * 4096 + fc * 128:dd * 4096 + (fc + 1) * 128],
                        lnm[:, dd * 512:(dd + 1) * 512],
                        start=(dd == 0), stop=(dd == 7)), inc=(dd == 7))
                ch.op(nc.scalar, lambda fc=fc: nc.scalar.activation(
                    hblk[:, fc * 512:(fc + 1) * 512], ps1[:, :], AF.Relu,
                    bias=b1t[:, fc:fc + 1]))
            for oc in range(8):
                ch.dma(nc.sync, lambda oc=oc, m=m: nc.sync.dma_start(
                    out=xt[:, :],
                    in_=x_dram[oc * 128:(oc + 1) * 128, m * 512:(m + 1) * 512]))
                for fc in range(32):
                    ch.op(nc.tensor, lambda oc=oc, fc=fc: nc.tensor.matmul(
                        ps2[:, :],
                        w2T[:, fc * 1024 + oc * 128:fc * 1024 + (oc + 1) * 128],
                        hblk[:, fc * 512:(fc + 1) * 512],
                        start=(fc == 0), stop=(fc == 31)), inc=(fc == 31))
                ch.op(nc.vector, lambda oc=oc: nc.vector.scalar_tensor_tensor(
                    yb[:, :], ps2[:, :], b2t[:, oc:oc + 1], xt[:, :],
                    ALU.add, ALU.add))
                ch.dma(nc.gpsimd, lambda oc=oc, m=m: nc.gpsimd.dma_start(
                    out=y_dram[oc * 128:(oc + 1) * 128, m * 512:(m + 1) * 512],
                    in_=yb[:, :]))
        ch.seal()
    barrier(kb)


def emit_l2_attn(kb, x_dram, w_f, bt_f, brow_f, y_dram):
    nc = kb.nc
    w_bf = kb.dscr("l2_w", [1024, 1024], BF16)
    ln_scr = kb.dscr("l2_ln", [1024, 4096], BF16)
    f_scr = kb.dscr("l2_f", [1024, 4096], BF16)
    ft_scr = kb.dscr("l2_ft", [4096, 1024], BF16)
    avn_scr = kb.dscr("l2_avn", [1024, 4096], BF16)
    with ExitStack() as st:
        ch = Chain(kb, st, "l2_a")
        emit_cast(kb, ch, w_f, w_bf)
        emit_ln(kb, ch, st, x_dram, ln_scr)
        ch.seal()
    barrier(kb)
    with ExitStack() as st:
        ch = Chain(kb, st, "l2_f")
        ws = kb.sb(st, "l2ws", [128, 8 * 1024], BF16)
        lnm = kb.sb(st, "l2lnm", [128, 8 * 512], BF16)
        bt = kb.sb(st, "l2bt", [128, 8], F32)
        bb = kb.sb(st, "l2bb", [128, 1024], F32)
        ob = kb.sb(st, "l2ob", [128, 1024], BF16)
        ps0 = kb.ps(st, "l2_ps0", [128, 512])
        ps1 = kb.ps(st, "l2_ps1", [128, 1024])
        ch.dma(nc.sync, lambda: nc.sync.dma_start(
            out=sb3(ws, 8, 1024), in_=w_bf.rearrange("(d p) c -> p d c", p=128)))
        ch.dma(nc.sync, lambda: nc.sync.dma_start(out=bt[:, :], in_=bt_f))
        ch.dma(nc.sync, lambda: nc.sync.dma_start(
            out=bb[:, :], in_=brow_f.broadcast_to([128, 1024])))
        for m in range(8):
            ch.dma(nc.sync, lambda m=m: nc.sync.dma_start(
                out=sb3(lnm, 8, 512), in_=fm(ln_scr, m * 512, (m + 1) * 512)))
            for oc in range(8):
                for dd in range(8):
                    ch.op(nc.tensor, lambda oc=oc, dd=dd: nc.tensor.matmul(
                        ps0[:, :], ws[:, dd * 1024 + oc * 128:dd * 1024 + (oc + 1) * 128],
                        lnm[:, dd * 512:(dd + 1) * 512],
                        start=(dd == 0), stop=(dd == 7)), inc=(dd == 7))
                ch.op(nc.scalar, lambda oc=oc: nc.scalar.activation(
                    ob[:, 0:512], ps0[:, :], AF.Identity, bias=bt[:, oc:oc + 1]))
                ch.dma(nc.gpsimd, lambda oc=oc, m=m: nc.gpsimd.dma_start(
                    out=f_scr[oc * 128:(oc + 1) * 128, m * 512:(m + 1) * 512],
                    in_=ob[:, 0:512]))
            for mt in range(4):
                for dd in range(8):
                    for half in range(2):
                        ch.op(nc.tensor, lambda mt=mt, dd=dd, half=half: nc.tensor.matmul(
                            ps1[:, half * 512:(half + 1) * 512],
                            lnm[:, dd * 512 + mt * 128:dd * 512 + (mt + 1) * 128],
                            ws[:, dd * 1024 + half * 512:dd * 1024 + (half + 1) * 512],
                            start=(dd == 0), stop=(dd == 7)),
                            inc=(dd == 7 and half == 1))
                ch.op(nc.vector, lambda: nc.vector.tensor_add(
                    ob[:, 0:1024], ps1[:, :], bb[:, :]))
                r0 = m * 512 + mt * 128
                ch.dma(nc.gpsimd, lambda r0=r0: nc.gpsimd.dma_start(
                    out=ft_scr[r0:r0 + 128, :], in_=ob[:, 0:1024]))
        ch.seal()
    barrier(kb)
    with ExitStack() as st:
        ch = Chain(kb, st, "l2_att")
        ident_bf = kb.const_sb(st, ch, np.eye(128, dtype=np.float32).astype(BF), "idb2")
        ident_f = kb.const_sb(st, ch, np.eye(128, dtype=np.float32), "idf2")
        ones128f = kb.const_sb(st, ch, np.ones((1, 128), np.float32), "o128f")
        p_ = np.arange(128)[:, None]
        c_ = np.arange(148)[None, :]
        band = (0.01 * (np.minimum(np.abs(p_ - c_ + 10), 10) - 10)).astype(np.float32)
        band_sb = kb.const_sb(st, ch, band, "band")
        fT = kb.sb(st, "l2fT", [128, 8 * 4096], BF16)
        ftk = kb.sb(st, "l2ftk", [128, 32 * 1024], BF16)
        p_sb = kb.sb(st, "l2p", [128, 1536], BF16)
        pt_sb = kb.sb(st, "l2pt", [128, 1536], BF16)
        mneg = kb.sb(st, "l2mn", [128, 3], F32)
        lg = kb.sb(st, "l2lg", [128, 3], F32)
        sm1 = kb.sb(st, "l2sm1", [128, 1], F32)
        sm2 = kb.sb(st, "l2sm2", [128, 3], F32)
        arow = kb.sb(st, "l2ar", [1, 384], F32)
        avg_sb = kb.sb(st, "l2avg", [128, 3 * 1024], F32)
        acc1 = kb.sb(st, "l2acc", [128, 1024], F32)
        tmu = kb.sb(st, "l2tmu", [128, 1024], F32)
        alb = kb.sb(st, "l2alb", [128, 128], F32)
        avnb = kb.sb(st, "l2avnb", [128, 8 * 128], BF16)
        lrow = kb.sb(st, "l2lrow", [1, 128], F32)
        ps_s = kb.ps(st, "l2_s", [128, 1536])
        pst = kb.ps(st, "l2_t", [128, 1024], BF16)
        ps_av = kb.ps(st, "l2_av", [128, 1024])
        ps_m = kb.ps(st, "l2_m", [128, 512])
        for dd in range(8):
            ch.dma(nc.sync, lambda dd=dd: nc.sync.dma_start(
                out=fT[:, dd * 4096:(dd + 1) * 4096],
                in_=f_scr[dd * 128:(dd + 1) * 128, :]))
        ch.dma(nc.sync, lambda: nc.sync.dma_start(
            out=sb3(ftk, 32, 1024),
            in_=ft_scr.rearrange("(t p) c -> p t c", p=128)))
        GROUPS = [(0, 3), (3, 6), (6, 8)]
        for i in range(32):
            i0 = i * 128
            jlo, jhi = max(0, i0 - 10), min(4096, i0 + 138)
            for g, (c0, c1) in enumerate(GROUPS):
                gw = (c1 - c0) * 512
                glo, ghi = c0 * 512, c1 * 512
                for jc in range(c0, c1):
                    for dd in range(8):
                        ch.op(nc.tensor, lambda i=i, jc=jc, c0=c0, dd=dd:
                              nc.tensor.matmul(
                            ps_s[:, (jc - c0) * 512:(jc - c0 + 1) * 512],
                            fT[:, dd * 4096 + i * 128:dd * 4096 + (i + 1) * 128],
                            fT[:, dd * 4096 + jc * 512:dd * 4096 + (jc + 1) * 512],
                            start=(dd == 0), stop=(dd == 7)),
                            inc=(dd == 7 and jc == c1 - 1))
                blo, bhi = max(jlo, glo), min(jhi, ghi)
                if blo < bhi:
                    coff = blo - (i0 - 10)
                    ch.op(nc.vector, lambda blo=blo, bhi=bhi, coff=coff, glo=glo:
                          nc.vector.tensor_add(
                        ps_s[:, blo - glo:bhi - glo],
                        ps_s[:, blo - glo:bhi - glo],
                        band_sb[:, coff:coff + (bhi - blo)]))
                ch.op(nc.vector, lambda g=g, gw=gw: nc.vector.reduce_max(
                    mneg[:, g:g + 1], ps_s[:, 0:gw], AX.X, negate=True))
                ch.op(nc.scalar, lambda g=g, gw=gw: nc.scalar.activation(
                    p_sb[:, 0:gw], ps_s[:, 0:gw], AF.Exp,
                    bias=mneg[:, g:g + 1], accum_out=lg[:, g:g + 1]))
                njc = gw // 128
                for b0 in range(0, njc, 4):
                    bn = min(4, njc - b0)
                    for sj in range(bn):
                        ch.op(nc.tensor, lambda b0=b0, sj=sj: nc.tensor.transpose(
                            pst[:, sj * 128:(sj + 1) * 128],
                            p_sb[:, (b0 + sj) * 128:(b0 + sj + 1) * 128],
                            ident_bf[:, :]), inc=(sj == bn - 1))
                    ch.op(nc.vector, lambda b0=b0, bn=bn: nc.vector.tensor_copy(
                        pt_sb[:, b0 * 128:(b0 + bn) * 128], pst[:, 0:bn * 128]))
                for oc in range(8):
                    for jtl in range(njc):
                        jt = c0 * 4 + jtl
                        ch.op(nc.tensor, lambda jt=jt, jtl=jtl, oc=oc, njc=njc:
                              nc.tensor.matmul(
                            ps_av[:, oc * 128:(oc + 1) * 128],
                            ftk[:, jt * 1024 + oc * 128:jt * 1024 + (oc + 1) * 128],
                            pt_sb[:, jtl * 128:(jtl + 1) * 128],
                            start=(jtl == 0), stop=(jtl == njc - 1)),
                            inc=(jtl == njc - 1))
                ch.op(nc.vector, lambda g=g: nc.vector.tensor_copy(
                    avg_sb[:, g * 1024:(g + 1) * 1024], ps_av[:, :]))
            ch.op(nc.vector, lambda: nc.vector.tensor_tensor(
                sm1[:, :], mneg[:, 0:1], mneg[:, 1:2], ALU.min))
            ch.op(nc.vector, lambda: nc.vector.tensor_tensor(
                sm1[:, :], sm1[:, :], mneg[:, 2:3], ALU.min))
            ch.op(nc.vector, lambda: nc.vector.tensor_sub(
                sm2[:, 0:3], sm1[:, :].broadcast_to([128, 3]), mneg[:, 0:3]))
            ch.op(nc.scalar, lambda: nc.scalar.activation(
                sm2[:, 0:3], sm2[:, 0:3], AF.Exp))
            ch.op(nc.vector, lambda: nc.vector.tensor_mul(
                lg[:, 0:3], lg[:, 0:3], sm2[:, 0:3]))
            ch.op(nc.vector, lambda: nc.vector.reduce_sum(
                sm1[:, :], lg[:, 0:3], AX.X))
            ch.op(nc.vector, lambda: nc.vector.reciprocal(sm1[:, :], sm1[:, :]))
            for g in range(3):
                ch.op(nc.tensor, lambda g=g: nc.tensor.transpose(
                    ps_m[0:1, g * 128:(g + 1) * 128], sm2[:, g:g + 1],
                    ident_f[:, :]), inc=(g == 2))
            ch.op(nc.tensor, lambda: nc.tensor.transpose(
                ps_m[0:1, 384:512], sm1[:, :], ident_f[:, :]))
            ch.op(nc.vector, lambda: nc.vector.tensor_copy(
                arow[0:1, 0:384], ps_m[0:1, 0:384]))
            ch.op(nc.vector, lambda: nc.vector.tensor_copy(
                lrow[0:1, :], ps_m[0:1, 384:512]))
            for g in range(3):
                ch.op(nc.tensor, lambda g=g: nc.tensor.matmul(
                    ps_m[:, 0:128], ones128f[:, :],
                    arow[0:1, g * 128:(g + 1) * 128], start=True, stop=True))
                ch.op(nc.vector, lambda: nc.vector.tensor_copy(
                    alb[:, :], ps_m[:, 0:128]))
                if g == 0:
                    ch.op(nc.vector, lambda g=g: nc.vector.tensor_mul(
                        acc1[:, :].rearrange("p (o c) -> p o c", o=8),
                        avg_sb[:, g * 1024:(g + 1) * 1024].rearrange(
                            "p (o c) -> p o c", o=8),
                        alb[:, :].unsqueeze(1).broadcast_to([128, 8, 128])))
                else:
                    ch.op(nc.vector, lambda g=g: nc.vector.tensor_mul(
                        tmu[:, :].rearrange("p (o c) -> p o c", o=8),
                        avg_sb[:, g * 1024:(g + 1) * 1024].rearrange(
                            "p (o c) -> p o c", o=8),
                        alb[:, :].unsqueeze(1).broadcast_to([128, 8, 128])))
                    ch.op(nc.vector, lambda: nc.vector.tensor_add(
                        acc1[:, :], acc1[:, :], tmu[:, :]))
            ch.op(nc.tensor, lambda: nc.tensor.matmul(
                ps_m[:, 0:128], ones128f[:, :], lrow[0:1, :],
                start=True, stop=True))
            ch.op(nc.vector, lambda: nc.vector.tensor_copy(
                alb[:, :], ps_m[:, 0:128]))
            ch.op(nc.vector, lambda: nc.vector.tensor_mul(
                avnb[:, :].rearrange("p (o c) -> p o c", o=8),
                acc1[:, :].rearrange("p (o c) -> p o c", o=8),
                alb[:, :].unsqueeze(1).broadcast_to([128, 8, 128])))
            ch.dma(nc.gpsimd, lambda i=i: nc.gpsimd.dma_start(
                out=avn_scr[:, i * 128:(i + 1) * 128].rearrange(
                    "(o p) c -> p o c", p=128), in_=sb3(avnb, 8, 128)))
        ch.seal()
    barrier(kb)
    with ExitStack() as st:
        ch = Chain(kb, st, "l2_res")
        am = kb.sb(st, "l2am", [128, 8 * 512], BF16)
        xm = kb.sb(st, "l2xm", [128, 8 * 512], F32)
        ym = kb.sb(st, "l2ym", [128, 8 * 512], F32)
        for m in range(8):
            ch.dma(nc.sync, lambda m=m: nc.sync.dma_start(
                out=sb3(am, 8, 512), in_=fm(avn_scr, m * 512, (m + 1) * 512)))
            ch.dma(nc.sync, lambda m=m: nc.sync.dma_start(
                out=sb3(xm, 8, 512), in_=fm(x_dram, m * 512, (m + 1) * 512)))
            ch.op(nc.vector, lambda: nc.vector.tensor_add(
                ym[:, :], xm[:, :], am[:, :]))
            ch.dma(nc.gpsimd, lambda m=m: nc.gpsimd.dma_start(
                out=fm(y_dram, m * 512, (m + 1) * 512), in_=sb3(ym, 8, 512)))
        ch.seal()
    barrier(kb)


def emit_l2_ffn(kb, x_dram, gw_f, eg_f, eu_f, ed_f, sg_f, su_f, sd_f, y_dram):
    nc = kb.nc
    ln_scr = kb.dscr("l2f_ln", [1024, 4096], BF16)
    cw_scr = kb.dscr("l2f_cw", [8, 4096], F32)
    eg_bf = kb.dscr("l2f_eg", [8, 2816, 1024], BF16)
    eu_bf = kb.dscr("l2f_eu", [8, 2816, 1024], BF16)
    ed_bf = kb.dscr("l2f_ed", [8, 1024, 2816], BF16)
    sg_bf = kb.dscr("l2f_sg", [2816, 1024], BF16)
    su_bf = kb.dscr("l2f_su", [2816, 1024], BF16)
    sd_bf = kb.dscr("l2f_sd", [1024, 2816], BF16)
    with ExitStack() as st:
        ch = Chain(kb, st, "l2f_pre")
        for e in range(8):
            ch.dma(nc.gpsimd, lambda e=e: nc.gpsimd.dma_start(out=eg_bf[e], in_=eg_f[e]))
            ch.dma(nc.gpsimd, lambda e=e: nc.gpsimd.dma_start(out=eu_bf[e], in_=eu_f[e]))
            ch.dma(nc.gpsimd, lambda e=e: nc.gpsimd.dma_start(out=ed_bf[e], in_=ed_f[e]))
        ch.dma(nc.gpsimd, lambda: nc.gpsimd.dma_start(out=sg_bf, in_=sg_f))
        ch.dma(nc.gpsimd, lambda: nc.gpsimd.dma_start(out=su_bf, in_=su_f))
        ch.dma(nc.gpsimd, lambda: nc.gpsimd.dma_start(out=sd_bf, in_=sd_f))
        ch.dma(nc.gpsimd, lambda: nc.gpsimd.dma_start(out=y_dram, in_=x_dram))
        emit_ln(kb, ch, st, x_dram, ln_scr)
        ch.seal()
    barrier(kb)
    with ExitStack() as st:
        ch = Chain(kb, st, "l2f_gate")
        emit_gating_topk(kb, ch, st, ln_scr, gw_f, cw_scr, 8, topk_norm_eps=1e-20)
        ch.seal()
    barrier(kb)
    with ExitStack() as st:
        ch = Chain(kb, st, "l2f_ex")
        gwT = kb.sb(st, "gwT", [128, 8 * 2816], BF16)
        uwT = kb.sb(st, "uwT", [128, 8 * 2816], BF16)
        dwT = kb.sb(st, "dwT", [128, 22 * 1024], BF16)
        lnm = kb.sb(st, "lnm2f", [128, 8 * 512], BF16)
        hblk = kb.sb(st, "hblk2f", [128, 22 * 512], BF16)
        sgb = kb.sb(st, "sgb", [128, 512], BF16)
        gub = kb.sb(st, "gub", [128, 512], BF16)
        cwbe = kb.sb(st, "cwbe2", [128, 4096], BF16)
        tmpo = kb.sb(st, "tmpo2", [128, 512], F32)
        ps_g = kb.ps(st, "2f_psg", [128, 512])
        ps_u = kb.ps(st, "2f_psu", [128, 512])
        ps2 = kb.ps(st, "2f_ps2", [128, 512])
        for e in range(9):
            shared = (e == 8)
            g_src = sg_bf if shared else eg_bf[e]
            u_src = su_bf if shared else eu_bf[e]
            d_src = sd_bf if shared else ed_bf[e]
            for dd in range(8):
                ch.dma(nc.sync, lambda g_src=g_src, dd=dd: nc.sync.dma_start(
                    out=gwT[:, dd * 2816:(dd + 1) * 2816],
                    in_=g_src[:, dd * 128:(dd + 1) * 128], transpose=True))
                ch.dma(nc.sync, lambda u_src=u_src, dd=dd: nc.sync.dma_start(
                    out=uwT[:, dd * 2816:(dd + 1) * 2816],
                    in_=u_src[:, dd * 128:(dd + 1) * 128], transpose=True))
            for hc in range(22):
                ch.dma(nc.sync, lambda d_src=d_src, hc=hc: nc.sync.dma_start(
                    out=dwT[:, hc * 1024:(hc + 1) * 1024],
                    in_=d_src[:, hc * 128:(hc + 1) * 128], transpose=True))
            if not shared:
                ch.dma(nc.gpsimd, lambda e=e: nc.gpsimd.dma_start(
                    out=cwbe[:, :], in_=cw_scr[e:e + 1, :].broadcast_to([128, 4096])))
            for m in range(8):
                ch.dma(nc.sync, lambda m=m: nc.sync.dma_start(
                    out=sb3(lnm, 8, 512), in_=fm(ln_scr, m * 512, (m + 1) * 512)))
                for hc in range(22):
                    for dd in range(8):
                        ch.op(nc.tensor, lambda hc=hc, dd=dd: nc.tensor.matmul(
                            ps_g[:, :],
                            gwT[:, dd * 2816 + hc * 128:dd * 2816 + (hc + 1) * 128],
                            lnm[:, dd * 512:(dd + 1) * 512],
                            start=(dd == 0), stop=(dd == 7)), inc=False)
                        ch.op(nc.tensor, lambda hc=hc, dd=dd: nc.tensor.matmul(
                            ps_u[:, :],
                            uwT[:, dd * 2816 + hc * 128:dd * 2816 + (hc + 1) * 128],
                            lnm[:, dd * 512:(dd + 1) * 512],
                            start=(dd == 0), stop=(dd == 7)), inc=(dd == 7))
                    ch.op(nc.scalar, lambda: nc.scalar.activation(
                        sgb[:, :], ps_g[:, :], AF.Silu))
                    ch.op(nc.vector, lambda: nc.vector.tensor_mul(
                        gub[:, :], sgb[:, :], ps_u[:, :]))
                    if shared:
                        ch.op(nc.vector, lambda hc=hc: nc.vector.tensor_copy(
                            hblk[:, hc * 512:(hc + 1) * 512], gub[:, :]))
                    else:
                        ch.op(nc.vector, lambda hc=hc, m=m: nc.vector.tensor_mul(
                            hblk[:, hc * 512:(hc + 1) * 512], gub[:, :],
                            cwbe[:, m * 512:(m + 1) * 512]))
                for oc in range(8):
                    for hc in range(22):
                        ch.op(nc.tensor, lambda oc=oc, hc=hc: nc.tensor.matmul(
                            ps2[:, :],
                            dwT[:, hc * 1024 + oc * 128:hc * 1024 + (oc + 1) * 128],
                            hblk[:, hc * 512:(hc + 1) * 512],
                            start=(hc == 0), stop=(hc == 21)), inc=(hc == 21))
                    ch.op(nc.vector, lambda: nc.vector.tensor_copy(
                        tmpo[:, :], ps2[:, :]))
                    ch.dma(nc.gpsimd, lambda oc=oc, m=m: nc.gpsimd.dma_start(
                        out=y_dram[oc * 128:(oc + 1) * 128, m * 512:(m + 1) * 512],
                        in_=tmpo[:, :], accum_op=ALU.add))
        ch.seal()
    barrier(kb)


def emit_l3_attn(kb, x_dram, ipw_f, ipbt_f, ipbv_f, opw_f, opbt_f, y_dram):
    nc = kb.nc
    ipw_bf = kb.dscr("l3_ipw", [1024, 3072], BF16)
    opw_bf = kb.dscr("l3_opw", [1024, 1024], BF16)
    ln_scr = kb.dscr("l3_ln", [1024, 4096], BF16)
    q_scr = kb.dscr("l3_q", [1024, 4096], BF16)
    k_scr = kb.dscr("l3_k", [1024, 4096], BF16)
    v_scr = kb.dscr("l3_v", [4096, 1024], BF16)
    avn_scr = kb.dscr("l3_avn", [1024, 4096], BF16)
    with ExitStack() as st:
        ch = Chain(kb, st, "l3a_pre")
        emit_cast(kb, ch, ipw_f, ipw_bf)
        emit_cast(kb, ch, opw_f, opw_bf)
        emit_ln(kb, ch, st, x_dram, ln_scr)
        ch.seal()
    barrier(kb)
    with ExitStack() as st:
        ch = Chain(kb, st, "l3a_qkv")
        wT = kb.sb(st, "l3wT", [128, 8 * 3072], BF16)
        lnm = kb.sb(st, "l3lnm", [128, 8 * 512], BF16)
        bt = kb.sb(st, "l3bt", [128, 24], F32)
        btq = kb.sb(st, "l3btq", [128, 8], F32)
        bvb = kb.sb(st, "l3bvb", [128, 1024], F32)
        ob = kb.sb(st, "l3ob", [128, 1024], BF16)
        ps0 = kb.ps(st, "l3_ps0", [128, 512])
        ps_v2 = kb.ps(st, "l3_psv", [128, 1024])
        ch.dma(nc.sync, lambda: nc.sync.dma_start(
            out=sb3(wT, 8, 3072),
            in_=ipw_bf.rearrange("(d p) c -> p d c", p=128)))
        ch.dma(nc.sync, lambda: nc.sync.dma_start(out=bt[:, :], in_=ipbt_f))
        ch.dma(nc.sync, lambda: nc.sync.dma_start(
            out=bvb[:, :], in_=ipbv_f.broadcast_to([128, 1024])))
        ch.op(nc.vector, lambda: nc.vector.tensor_scalar_mul(
            btq[:, :], bt[:, 0:8], 0.125))
        for m in range(8):
            ch.dma(nc.sync, lambda m=m: nc.sync.dma_start(
                out=sb3(lnm, 8, 512), in_=fm(ln_scr, m * 512, (m + 1) * 512)))
            for part, dst in ((0, q_scr), (1, k_scr)):
                for oc in range(8):
                    ocg = part * 8 + oc
                    for dd in range(8):
                        ch.op(nc.tensor, lambda ocg=ocg, dd=dd: nc.tensor.matmul(
                            ps0[:, :],
                            wT[:, dd * 3072 + ocg * 128:dd * 3072 + (ocg + 1) * 128],
                            lnm[:, dd * 512:(dd + 1) * 512],
                            start=(dd == 0), stop=(dd == 7)), inc=(dd == 7))
                    if part == 0:
                        ch.op(nc.scalar, lambda oc=oc: nc.scalar.activation(
                            ob[:, 0:512], ps0[:, :], AF.Identity,
                            bias=btq[:, oc:oc + 1], scale=0.125))
                    else:
                        ch.op(nc.scalar, lambda oc=oc: nc.scalar.activation(
                            ob[:, 0:512], ps0[:, :], AF.Identity,
                            bias=bt[:, 8 + oc:9 + oc]))
                    ch.dma(nc.gpsimd, lambda dst=dst, oc=oc, m=m: nc.gpsimd.dma_start(
                        out=dst[oc * 128:(oc + 1) * 128, m * 512:(m + 1) * 512],
                        in_=ob[:, 0:512]))
            for mt in range(4):
                for dd in range(8):
                    for half in range(2):
                        ch.op(nc.tensor, lambda mt=mt, dd=dd, half=half: nc.tensor.matmul(
                            ps_v2[:, half * 512:(half + 1) * 512],
                            lnm[:, dd * 512 + mt * 128:dd * 512 + (mt + 1) * 128],
                            wT[:, dd * 3072 + 2048 + half * 512:
                               dd * 3072 + 2048 + (half + 1) * 512],
                            start=(dd == 0), stop=(dd == 7)),
                            inc=(dd == 7 and half == 1))
                ch.op(nc.vector, lambda: nc.vector.tensor_add(
                    ob[:, 0:1024], ps_v2[:, :], bvb[:, :]))
                r0 = m * 512 + mt * 128
                ch.dma(nc.gpsimd, lambda r0=r0: nc.gpsimd.dma_start(
                    out=v_scr[r0:r0 + 128, :], in_=ob[:, 0:1024]))
        ch.seal()
    barrier(kb)
    with ExitStack() as st:
        ch = Chain(kb, st, "l3a_att")
        ident_bf = kb.const_sb(st, ch, np.eye(128, dtype=np.float32).astype(BF), "idb3")
        ident_f = kb.const_sb(st, ch, np.eye(128, dtype=np.float32), "idf3")
        ones64f = kb.const_sb(st, ch, np.ones((1, 64), np.float32), "o64f3")
        dm = np.where(np.eye(128) > 0, 0.0, NEG).astype(np.float32)
        dmask = kb.const_sb(st, ch, np.tile(dm, (1, 8)), "dmask3")
        qh = kb.sb(st, "l3qh", [64, 4096], BF16)
        kh = kb.sb(st, "l3kh", [64, 4096], BF16)
        vh = kb.sb(st, "l3vh", [128, 32 * 64], BF16)
        p_sb = kb.sb(st, "l3p", [128, 1024], BF16)
        pt_sb = kb.sb(st, "l3pt", [128, 1024], BF16)
        avh = kb.sb(st, "l3avh", [64, 4096], F32)
        avn = kb.sb(st, "l3avn", [64, 4096], BF16)
        lrec = kb.sb(st, "l3lrec", [128, 32], F32)
        mneg = kb.sb(st, "l3mneg", [128, 1], F32)
        lrT = kb.sb(st, "l3lrT", [1, 512], F32)
        ps_s = kb.ps(st, "l3_s", [128, 1024])
        pst = kb.ps(st, "l3_t", [128, 1024], BF16)
        ps_av = kb.ps(st, "l3_av", [128, 128])
        ps_m = kb.ps(st, "l3_m", [128, 512])
        vh3 = sb3(vh, 32, 64)
        for h in range(16):
            ch.dma(nc.sync, lambda h=h: nc.sync.dma_start(
                out=qh[:, :], in_=q_scr[64 * h:64 * h + 64, :]))
            ch.dma(nc.sync, lambda h=h: nc.sync.dma_start(
                out=kh[:, :], in_=k_scr[64 * h:64 * h + 64, :]))
            ch.dma(nc.sync, lambda h=h: nc.sync.dma_start(
                out=vh3, in_=v_scr[:, 64 * h:64 * h + 64].rearrange(
                    "(t p) c -> p t c", p=128)))
            for i in range(32):
                for t in range(8):
                    j0 = t * 512 + (i % 4) * 128
                    ch.op(nc.tensor, lambda i=i, t=t, j0=j0: nc.tensor.matmul(
                        ps_s[:, t * 128:(t + 1) * 128],
                        qh[:, i * 128:(i + 1) * 128], kh[:, j0:j0 + 128],
                        start=True, stop=True), inc=(t == 7))
                ch.op(nc.vector, lambda: nc.vector.tensor_add(
                    ps_s[:, :], ps_s[:, :], dmask[:, :]))
                ch.op(nc.vector, lambda: nc.vector.reduce_max(
                    mneg[:, :], ps_s[:, :], AX.X, negate=True))
                ch.op(nc.scalar, lambda i=i: nc.scalar.activation(
                    p_sb[:, :], ps_s[:, :], AF.Exp, bias=mneg[:, :],
                    accum_out=lrec[:, i:i + 1]))
                for b0 in (0, 4):
                    for sj in range(4):
                        ch.op(nc.tensor, lambda b0=b0, sj=sj: nc.tensor.transpose(
                            pst[:, sj * 128:(sj + 1) * 128],
                            p_sb[:, (b0 + sj) * 128:(b0 + sj + 1) * 128],
                            ident_bf[:, :]), inc=(sj == 3))
                    ch.op(nc.vector, lambda b0=b0: nc.vector.tensor_copy(
                        pt_sb[:, b0 * 128:(b0 + 4) * 128], pst[:, 0:512]))
                for t in range(8):
                    jt = t * 4 + (i % 4)
                    ch.op(nc.tensor, lambda t=t, jt=jt: nc.tensor.matmul(
                        ps_av[0:64, 0:128], vh3[:, jt, :],
                        pt_sb[:, t * 128:(t + 1) * 128],
                        start=(t == 0), stop=(t == 7)), inc=(t == 7))
                ch.op(nc.vector, lambda i=i: nc.vector.tensor_copy(
                    avh[:, i * 128:(i + 1) * 128], ps_av[0:64, 0:128]))
            ch.op(nc.vector, lambda: nc.vector.reciprocal(lrec[:, :], lrec[:, :]))
            for it0 in range(0, 32, 4):
                for sj in range(4):
                    ch.op(nc.tensor, lambda it0=it0, sj=sj: nc.tensor.transpose(
                        ps_m[0:1, sj * 128:(sj + 1) * 128],
                        lrec[:, it0 + sj:it0 + sj + 1], ident_f[:, :]),
                        inc=(sj == 3))
                ch.op(nc.vector, lambda: nc.vector.tensor_copy(
                    lrT[0:1, :], ps_m[0:1, 0:512]))
                for sj in range(4):
                    ch.op(nc.tensor, lambda sj=sj: nc.tensor.matmul(
                        ps_m[0:64, sj * 128:(sj + 1) * 128], ones64f[:, :],
                        lrT[0:1, sj * 128:(sj + 1) * 128], start=True, stop=True),
                        inc=(sj == 3))
                ch.op(nc.vector, lambda it0=it0: nc.vector.tensor_mul(
                    avn[:, it0 * 128:(it0 + 4) * 128],
                    avh[:, it0 * 128:(it0 + 4) * 128], ps_m[0:64, 0:512]))
            ch.dma(nc.gpsimd, lambda h=h: nc.gpsimd.dma_start(
                out=avn_scr[64 * h:64 * h + 64, :], in_=avn[:, :]))
        ch.seal()
    barrier(kb)
    with ExitStack() as st:
        ch = Chain(kb, st, "l3a_proj")
        opT = kb.sb(st, "opT", [128, 8 * 1024], BF16)
        avm = kb.sb(st, "l3avm", [128, 8 * 512], BF16)
        obt = kb.sb(st, "l3obt", [128, 8], F32)
        xt = kb.sb(st, "l3xt", [128, 512], F32)
        yb = kb.sb(st, "l3yb", [128, 512], F32)
        ps0 = kb.ps(st, "l3p_ps", [128, 512])
        ch.dma(nc.sync, lambda: nc.sync.dma_start(
            out=sb3(opT, 8, 1024),
            in_=opw_bf.rearrange("(d p) c -> p d c", p=128)))
        ch.dma(nc.sync, lambda: nc.sync.dma_start(out=obt[:, :], in_=opbt_f))
        for m in range(8):
            ch.dma(nc.sync, lambda m=m: nc.sync.dma_start(
                out=sb3(avm, 8, 512), in_=fm(avn_scr, m * 512, (m + 1) * 512)))
            for oc in range(8):
                ch.dma(nc.sync, lambda oc=oc, m=m: nc.sync.dma_start(
                    out=xt[:, :],
                    in_=x_dram[oc * 128:(oc + 1) * 128, m * 512:(m + 1) * 512]))
                for dd in range(8):
                    ch.op(nc.tensor, lambda oc=oc, dd=dd: nc.tensor.matmul(
                        ps0[:, :], opT[:, dd * 1024 + oc * 128:dd * 1024 + (oc + 1) * 128],
                        avm[:, dd * 512:(dd + 1) * 512],
                        start=(dd == 0), stop=(dd == 7)), inc=(dd == 7))
                ch.op(nc.vector, lambda oc=oc: nc.vector.scalar_tensor_tensor(
                    yb[:, :], ps0[:, :], obt[:, oc:oc + 1], xt[:, :],
                    ALU.add, ALU.add))
                ch.dma(nc.gpsimd, lambda oc=oc, m=m: nc.gpsimd.dma_start(
                    out=y_dram[oc * 128:(oc + 1) * 128, m * 512:(m + 1) * 512],
                    in_=yb[:, :]))
        ch.seal()
    barrier(kb)


def emit_l3_ffn(kb, x_dram, bw_f, bbt_f, sa_f, sb_f, ra_f, rb_f, rw_f, rbrow_f,
                y_dram):
    nc = kb.nc
    ln_scr = kb.dscr("l3f_ln", [1024, 4096], BF16)
    bw_bf = kb.dscr("l3f_bw", [1024, 1024], BF16)
    sa_bf = kb.dscr("l3f_sa", [1024, 16], BF16)
    sbw_bf = kb.dscr("l3f_sb", [16, 1024], BF16)
    ra_bf = kb.dscr("l3f_ra", [1024, 96], BF16)
    rb_bf = kb.dscr("l3f_rb", [96, 1024], BF16)
    sw_scr = kb.dscr("l3f_sw", [6, 4096], F32)
    with ExitStack() as st:
        ch = Chain(kb, st, "l3f_pre")
        emit_cast(kb, ch, bw_f, bw_bf)
        emit_cast(kb, ch, sa_f, sa_bf)
        emit_cast(kb, ch, sb_f, sbw_bf)
        emit_cast(kb, ch, ra_f, ra_bf)
        emit_cast(kb, ch, rb_f, rb_bf)
        emit_ln(kb, ch, st, x_dram, ln_scr)
        ch.seal()
    barrier(kb)
    with ExitStack() as st:
        ch = Chain(kb, st, "l3f_gate")
        emit_gating_topk(kb, ch, st, ln_scr, rw_f, sw_scr, 6,
                         topk_norm_eps=1e-6, bias_row=rbrow_f)
        ch.seal()
    barrier(kb)
    with ExitStack() as st:
        ch = Chain(kb, st, "l3f_m")
        bwT = kb.sb(st, "bwT", [128, 8 * 1024], BF16)
        saT = kb.sb(st, "saT", [128, 8 * 16], BF16)
        sbT = kb.sb(st, "sbT", [16, 1024], BF16)
        raT = kb.sb(st, "raT", [128, 8 * 96], BF16)
        rbT = kb.sb(st, "rbT", [96, 1024], BF16)
        lnm = kb.sb(st, "lnm3f", [128, 8 * 512], BF16)
        h1b = kb.sb(st, "h1b", [16, 512], BF16)
        hrb = kb.sb(st, "hrb", [96, 512], BF16)
        swb = kb.sb(st, "swb", [96, 512], F32)
        bbt = kb.sb(st, "bbt", [128, 8], F32)
        xt = kb.sb(st, "xt3f", [128, 512], F32)
        yb = kb.sb(st, "yb3f", [128, 512], F32)
        ps_h1 = kb.ps(st, "3f_ph1", [128, 512])
        ps_hr = kb.ps(st, "3f_phr", [128, 512])
        ps2 = kb.ps(st, "3f_ps2", [128, 512])
        ch.dma(nc.sync, lambda: nc.sync.dma_start(
            out=sb3(bwT, 8, 1024),
            in_=bw_bf.rearrange("(d p) c -> p d c", p=128)))
        ch.dma(nc.sync, lambda: nc.sync.dma_start(
            out=sb3(saT, 8, 16),
            in_=sa_bf.rearrange("(d p) c -> p d c", p=128)))
        ch.dma(nc.sync, lambda: nc.sync.dma_start(
            out=sb3(raT, 8, 96),
            in_=ra_bf.rearrange("(d p) c -> p d c", p=128)))
        ch.dma(nc.sync, lambda: nc.sync.dma_start(out=sbT[:, :], in_=sbw_bf))
        ch.dma(nc.sync, lambda: nc.sync.dma_start(out=rbT[:, :], in_=rb_bf))
        ch.dma(nc.sync, lambda: nc.sync.dma_start(out=bbt[:, :], in_=bbt_f))
        for m in range(8):
            ch.dma(nc.sync, lambda m=m: nc.sync.dma_start(
                out=sb3(lnm, 8, 512), in_=fm(ln_scr, m * 512, (m + 1) * 512)))
            for dd in range(8):
                ch.op(nc.tensor, lambda dd=dd: nc.tensor.matmul(
                    ps_h1[0:16, :], saT[:, dd * 16:(dd + 1) * 16],
                    lnm[:, dd * 512:(dd + 1) * 512],
                    start=(dd == 0), stop=(dd == 7)), inc=False)
                ch.op(nc.tensor, lambda dd=dd: nc.tensor.matmul(
                    ps_hr[0:96, :], raT[:, dd * 96:(dd + 1) * 96],
                    lnm[:, dd * 512:(dd + 1) * 512],
                    start=(dd == 0), stop=(dd == 7)), inc=(dd == 7))
            ch.op(nc.vector, lambda: nc.vector.tensor_copy(
                h1b[:, :], ps_h1[0:16, :]))
            for e in range(6):
                ch.dma(nc.sync, lambda m=m, e=e: nc.sync.dma_start(
                    out=swb[e * 16:(e + 1) * 16, :],
                    in_=sw_scr[e:e + 1, m * 512:(m + 1) * 512].broadcast_to(
                        [16, 512])))
            ch.op(nc.vector, lambda: nc.vector.tensor_mul(
                hrb[:, :], ps_hr[0:96, :], swb[:, :]))
            for oc in range(8):
                ch.dma(nc.sync, lambda oc=oc, m=m: nc.sync.dma_start(
                    out=xt[:, :],
                    in_=x_dram[oc * 128:(oc + 1) * 128, m * 512:(m + 1) * 512]))
                for dd in range(8):
                    ch.op(nc.tensor, lambda oc=oc, dd=dd: nc.tensor.matmul(
                        ps2[:, :],
                        bwT[:, dd * 1024 + oc * 128:dd * 1024 + (oc + 1) * 128],
                        lnm[:, dd * 512:(dd + 1) * 512],
                        start=(dd == 0), stop=False), inc=False)
                ch.op(nc.tensor, lambda oc=oc: nc.tensor.matmul(
                    ps2[:, :], sbT[:, oc * 128:(oc + 1) * 128], h1b[:, :],
                    start=False, stop=False), inc=False)
                ch.op(nc.tensor, lambda oc=oc: nc.tensor.matmul(
                    ps2[:, :], rbT[:, oc * 128:(oc + 1) * 128], hrb[:, :],
                    start=False, stop=True))
                ch.op(nc.vector, lambda oc=oc: nc.vector.scalar_tensor_tensor(
                    yb[:, :], ps2[:, :], bbt[:, oc:oc + 1], xt[:, :],
                    ALU.add, ALU.add))
                ch.dma(nc.gpsimd, lambda oc=oc, m=m: nc.gpsimd.dma_start(
                    out=y_dram[oc * 128:(oc + 1) * 128, m * 512:(m + 1) * 512],
                    in_=yb[:, :]))
        ch.seal()
    barrier(kb)



def _build_l0a():
    kb = KB("k_l0a")
    x = kb.din("x", [1024, 4096], F32)
    wqkv = kb.din("wqkv", [1024, 3072], F32)
    uq = kb.din("uq", [3072], F32)
    sq = kb.din("sq", [1], F32)
    wproj = kb.din("wproj", [1024, 1024], F32)
    up = kb.din("up", [1024], F32)
    sp = kb.din("sp", [1], F32)
    y = kb.dout("y", [1024, 4096], F32)
    emit_l0_attn(kb, x, wqkv, uq, sq, wproj, up, sp, y)
    kb.st.close()
    return kb.nc


def _build_l0f():
    kb = KB("k_l0f")
    x = kb.din("x", [1024, 4096], F32)
    wg = kb.din("wg", [8, 1024], F32)
    w1 = kb.din("w1", [8, 1024, 4096], F32)
    b1t = kb.din("b1t", [8, 128, 32], F32)
    w2 = kb.din("w2", [8, 4096, 1024], F32)
    b2 = kb.din("b2", [8, 1024], F32)
    y = kb.dout("y", [1024, 4096], F32)
    emit_l0_ffn(kb, x, wg, w1, b1t, w2, b2, y)
    kb.st.close()
    return kb.nc


def _build_l1a():
    kb = KB("k_l1a")
    x = kb.din("x", [1024, 4096], F32)
    uvw = kb.din("uvw", [1024, 2048], F32)
    ow = kb.din("ow", [1024, 1024], F32)
    rpw = kb.din("rpw", [1, 32], F32)
    rlw = kb.din("rlw", [3, 32], F32)
    rlb = kb.din("rlb", [3, 32], F32)
    rfc = kb.din("rfc", [3, 32, 32], F32)
    rolw = kb.din("rolw", [32], F32)
    rolb = kb.din("rolb", [32], F32)
    row = kb.din("row", [32, 1024], F32)
    y = kb.dout("y", [1024, 4096], F32)
    emit_l1_attn(kb, x, uvw, ow, rpw, rlw, rlb, rfc, rolw, rolb, row, y)
    kb.st.close()
    return kb.nc


def _build_l1f():
    kb = KB("k_l1f")
    x = kb.din("x", [1024, 4096], F32)
    w1 = kb.din("w1", [4096, 1024], F32)
    b1t = kb.din("b1t", [128, 32], F32)
    w2 = kb.din("w2", [1024, 4096], F32)
    b2t = kb.din("b2t", [128, 8], F32)
    y = kb.dout("y", [1024, 4096], F32)
    emit_l1_ffn(kb, x, w1, b1t, w2, b2t, y)
    kb.st.close()
    return kb.nc


def _build_l2a():
    kb = KB("k_l2a")
    x = kb.din("x", [1024, 4096], F32)
    w = kb.din("w", [1024, 1024], F32)
    bt = kb.din("bt", [128, 8], F32)
    br = kb.din("br", [1, 1024], F32)
    y = kb.dout("y", [1024, 4096], F32)
    emit_l2_attn(kb, x, w, bt, br, y)
    kb.st.close()
    return kb.nc


def _build_l2f():
    kb = KB("k_l2f")
    x = kb.din("x", [1024, 4096], F32)
    gw = kb.din("gw", [8, 1024], F32)
    eg = kb.din("eg", [8, 2816, 1024], F32)
    eu = kb.din("eu", [8, 2816, 1024], F32)
    ed = kb.din("ed", [8, 1024, 2816], F32)
    sg = kb.din("sg", [2816, 1024], F32)
    su = kb.din("su", [2816, 1024], F32)
    sd = kb.din("sd", [1024, 2816], F32)
    y = kb.dout("y", [1024, 4096], F32)
    emit_l2_ffn(kb, x, gw, eg, eu, ed, sg, su, sd, y)
    kb.st.close()
    return kb.nc


def _build_l3():
    kb = KB("k_l3")
    x = kb.din("x", [1024, 4096], F32)
    ipw = kb.din("ipw", [1024, 3072], F32)
    ipbt = kb.din("ipbt", [128, 24], F32)
    ipbv = kb.din("ipbv", [1, 1024], F32)
    opw = kb.din("opw", [1024, 1024], F32)
    opbt = kb.din("opbt", [128, 8], F32)
    xa = kb.dscr("xa", [1024, 4096], F32)
    emit_l3_attn(kb, x, ipw, ipbt, ipbv, opw, opbt, xa)
    bw = kb.din("bw", [1024, 1024], F32)
    bbt = kb.din("bbt", [128, 8], F32)
    sa = kb.din("sa", [1024, 16], F32)
    sbw = kb.din("sbw", [16, 1024], F32)
    ra = kb.din("ra", [1024, 96], F32)
    rb = kb.din("rb", [96, 1024], F32)
    rw = kb.din("rw", [6, 1024], F32)
    rbr = kb.din("rbr", [1, 6], F32)
    y = kb.dout("y", [1024, 4096], F32)
    emit_l3_ffn(kb, xa, bw, bbt, sa, sbw, ra, rb, rw, rbr, y)
    kb.st.close()
    return kb.nc


def _run(nc, common, xs):
    in_maps = [dict(common, x=np.ascontiguousarray(x)) for x in xs]
    res = run_bass_kernel_spmd(nc, in_maps, core_ids=list(range(NCORES)))
    return [np.asarray(r["y"]) for r in res.results]


def kernel_unfused(**inp):
    cg = np.ascontiguousarray
    x = inp["L0_attn_batch"]
    xs = [cg(x[c].T) for c in range(NCORES)]

    xs = _run(_build_l0a(), {
        "wqkv": inp["L0_attn_W_qkv"], "uq": inp["L0_attn_u_qkv"],
        "sq": inp["L0_attn_sigma_qkv"], "wproj": inp["L0_attn_W_proj"],
        "up": inp["L0_attn_u_proj"], "sp": inp["L0_attn_sigma_proj"]}, xs)

    xs = _run(_build_l0f(), {
        "wg": inp["L0_ffn_w_gate"], "w1": inp["L0_ffn_W1"],
        "b1t": cg(inp["L0_ffn_b1"].reshape(8, 32, 128).transpose(0, 2, 1)),
        "w2": inp["L0_ffn_W2"], "b2": inp["L0_ffn_b2"]}, xs)

    xs = _run(_build_l1a(), {
        "uvw": inp["L1_attn_uv_w"], "ow": inp["L1_attn_o_w"],
        "rpw": inp["L1_attn_rpe_pos_w"], "rlw": inp["L1_attn_rpe_ln_w"],
        "rlb": inp["L1_attn_rpe_ln_b"], "rfc": inp["L1_attn_rpe_fc_w"],
        "rolw": inp["L1_attn_rpe_out_ln_w"], "rolb": inp["L1_attn_rpe_out_ln_b"],
        "row": inp["L1_attn_rpe_out_w"]}, xs)

    xs = _run(_build_l1f(), {
        "w1": inp["L1_ffn_W1"],
        "b1t": cg(inp["L1_ffn_b1"].reshape(32, 128).T),
        "w2": inp["L1_ffn_W2"],
        "b2t": cg(inp["L1_ffn_b2"].reshape(8, 128).T)}, xs)

    xs = _run(_build_l2a(), {
        "w": inp["L2_attn_W"],
        "bt": cg(inp["L2_attn_b"].reshape(8, 128).T),
        "br": inp["L2_attn_b"].reshape(1, 1024)}, xs)

    xs = _run(_build_l2f(), {
        "gw": inp["L2_ffn_gate_w"], "eg": inp["L2_ffn_expert_gate"],
        "eu": inp["L2_ffn_expert_up"], "ed": inp["L2_ffn_expert_down"],
        "sg": inp["L2_ffn_shared_gate"], "su": inp["L2_ffn_shared_up"],
        "sd": inp["L2_ffn_shared_down"]}, xs)

    full = np.stack([r.T for r in xs])
    xrs = [cg(full[:, 512 * j:512 * (j + 1), :].reshape(4096, 1024).T)
           for j in range(NCORES)]

    ys = _run(_build_l3(), {
        "ipw": cg(inp["L3_attn_in_proj_w"].T),
        "ipbt": cg(inp["L3_attn_in_proj_b"].reshape(24, 128).T),
        "ipbv": inp["L3_attn_in_proj_b"][2048:].reshape(1, 1024),
        "opw": cg(inp["L3_attn_out_proj_w"].T),
        "opbt": cg(inp["L3_attn_out_proj_b"].reshape(8, 128).T),
        "bw": cg(inp["L3_ffn_base_W"].T),
        "bbt": cg(inp["L3_ffn_base_b"].reshape(8, 128).T),
        "sa": cg(inp["L3_ffn_shared_A"].T),
        "sbw": cg(inp["L3_ffn_shared_B"].T),
        "ra": cg(inp["L3_ffn_routed_A"].reshape(96, 1024).T),
        "rb": cg(inp["L3_ffn_routed_B"].transpose(0, 2, 1).reshape(96, 1024)),
        "rw": inp["L3_ffn_router_W"],
        "rbr": inp["L3_ffn_router_b"].reshape(1, 6)}, xrs)

    out = np.empty((8, 4096, 1024), np.float32)
    for j in range(NCORES):
        out[:, 512 * j:512 * (j + 1), :] = ys[j].T.reshape(8, 512, 1024)
    return out



def emit_reshard_a2a(kb, src_dram, dst_dram):
    from concourse.aot_env import allow_ici_all_to_all
    nc = kb.nc
    a2a_in = kb.nc.dram_tensor("a2a_in", [8, 1024, 512], F32, kind="Internal")
    a2a_out = kb.nc.dram_tensor("a2a_out", [8, 1024, 512], F32, kind="Internal")
    with ExitStack() as st:
        ch = Chain(kb, st, "a2a")
        for j in range(8):
            ch.dma(nc.gpsimd, lambda j=j: nc.gpsimd.dma_start(
                out=a2a_in.ap()[j], in_=src_dram[:, j * 512:(j + 1) * 512]))
        with allow_ici_all_to_all():
            ch.op(nc.gpsimd, lambda: nc.gpsimd.collective_compute(
                "AllToAll", ALU.bypass,
                replica_groups=[list(range(NCORES))],
                ins=[a2a_in.ap().opt()], outs=[a2a_out.ap().opt()]))
        for j in range(8):
            ch.dma(nc.gpsimd, lambda j=j: nc.gpsimd.dma_start(
                out=dst_dram[:, j * 512:(j + 1) * 512], in_=a2a_out.ap()[j]))
        ch.seal()
    barrier(kb)


def _build_fused(inp_shapes_only=False):
    kb = KB("mk_fused")
    x = kb.din("x", [1024, 4096], F32)
    wqkv = kb.din("wqkv", [1024, 3072], F32)
    uq = kb.din("uq", [3072], F32)
    sq = kb.din("sq", [1], F32)
    wproj = kb.din("wproj", [1024, 1024], F32)
    up = kb.din("up", [1024], F32)
    sp = kb.din("sp", [1], F32)
    wg0 = kb.din("wg0", [8, 1024], F32)
    w10 = kb.din("w10", [8, 1024, 4096], F32)
    b1t0 = kb.din("b1t0", [8, 128, 32], F32)
    w20 = kb.din("w20", [8, 4096, 1024], F32)
    b20 = kb.din("b20", [8, 1024], F32)
    uvw = kb.din("uvw", [1024, 2048], F32)
    ow = kb.din("ow", [1024, 1024], F32)
    rpw = kb.din("rpw", [1, 32], F32)
    rlw = kb.din("rlw", [3, 32], F32)
    rlb = kb.din("rlb", [3, 32], F32)
    rfc = kb.din("rfc", [3, 32, 32], F32)
    rolw = kb.din("rolw", [32], F32)
    rolb = kb.din("rolb", [32], F32)
    row = kb.din("row", [32, 1024], F32)
    w11 = kb.din("w11", [4096, 1024], F32)
    b1t1 = kb.din("b1t1", [128, 32], F32)
    w21 = kb.din("w21", [1024, 4096], F32)
    b2t1 = kb.din("b2t1", [128, 8], F32)
    w2a = kb.din("w2a", [1024, 1024], F32)
    bt2a = kb.din("bt2a", [128, 8], F32)
    br2a = kb.din("br2a", [1, 1024], F32)
    gw2 = kb.din("gw2", [8, 1024], F32)
    eg2 = kb.din("eg2", [8, 2816, 1024], F32)
    eu2 = kb.din("eu2", [8, 2816, 1024], F32)
    ed2 = kb.din("ed2", [8, 1024, 2816], F32)
    sg2 = kb.din("sg2", [2816, 1024], F32)
    su2 = kb.din("su2", [2816, 1024], F32)
    sd2 = kb.din("sd2", [1024, 2816], F32)
    ipw = kb.din("ipw", [1024, 3072], F32)
    ipbt = kb.din("ipbt", [128, 24], F32)
    ipbv = kb.din("ipbv", [1, 1024], F32)
    opw = kb.din("opw", [1024, 1024], F32)
    opbt = kb.din("opbt", [128, 8], F32)
    bw3 = kb.din("bw3", [1024, 1024], F32)
    bbt3 = kb.din("bbt3", [128, 8], F32)
    sa3 = kb.din("sa3", [1024, 16], F32)
    sbw3 = kb.din("sbw3", [16, 1024], F32)
    ra3 = kb.din("ra3", [1024, 96], F32)
    rb3 = kb.din("rb3", [96, 1024], F32)
    rw3 = kb.din("rw3", [6, 1024], F32)
    rbr3 = kb.din("rbr3", [1, 6], F32)
    y = kb.dout("y", [1024, 4096], F32)

    xA = kb.dscr("xA", [1024, 4096], F32)
    xB = kb.dscr("xB", [1024, 4096], F32)
    xR = kb.dscr("xR", [1024, 4096], F32)
    xRA = kb.dscr("xRA", [1024, 4096], F32)

    phases = [
        lambda: emit_l0_attn(kb, x, wqkv, uq, sq, wproj, up, sp, xA),
        lambda: emit_l0_ffn(kb, xA, wg0, w10, b1t0, w20, b20, xB),
        lambda: emit_l1_attn(kb, xB, uvw, ow, rpw, rlw, rlb, rfc, rolw, rolb,
                             row, xA),
        lambda: emit_l1_ffn(kb, xA, w11, b1t1, w21, b2t1, xB),
        lambda: emit_l2_attn(kb, xB, w2a, bt2a, br2a, xA),
        lambda: emit_l2_ffn(kb, xA, gw2, eg2, eu2, ed2, sg2, su2, sd2, xB),
        lambda: emit_reshard_a2a(kb, xB, xR),
        lambda: emit_l3_attn(kb, xR, ipw, ipbt, ipbv, opw, opbt, xRA),
        lambda: emit_l3_ffn(kb, xRA, bw3, bbt3, sa3, sbw3, ra3, rb3, rw3,
                            rbr3, y),
    ]
    import os
    upto = int(os.environ.get("MK_UPTO", "9"))
    for ph in phases[:upto]:
        ph()
    if upto < 9:
        srcs = [xA, xB, xA, xB, xA, xB, xR, xRA][upto - 1]
        with ExitStack() as st:
            ch = Chain(kb, st, "fin")
            ch.dma(kb.nc.gpsimd, lambda: kb.nc.gpsimd.dma_start(out=y, in_=srcs))
            ch.seal()
    kb.st.close()
    return kb.nc


def kernel(**inp):
    cg = np.ascontiguousarray
    x = inp["L0_attn_batch"]
    common = {
        "wqkv": inp["L0_attn_W_qkv"], "uq": inp["L0_attn_u_qkv"],
        "sq": inp["L0_attn_sigma_qkv"], "wproj": inp["L0_attn_W_proj"],
        "up": inp["L0_attn_u_proj"], "sp": inp["L0_attn_sigma_proj"],
        "wg0": inp["L0_ffn_w_gate"], "w10": inp["L0_ffn_W1"],
        "b1t0": cg(inp["L0_ffn_b1"].reshape(8, 32, 128).transpose(0, 2, 1)),
        "w20": inp["L0_ffn_W2"], "b20": inp["L0_ffn_b2"],
        "uvw": inp["L1_attn_uv_w"], "ow": inp["L1_attn_o_w"],
        "rpw": inp["L1_attn_rpe_pos_w"], "rlw": inp["L1_attn_rpe_ln_w"],
        "rlb": inp["L1_attn_rpe_ln_b"], "rfc": inp["L1_attn_rpe_fc_w"],
        "rolw": inp["L1_attn_rpe_out_ln_w"], "rolb": inp["L1_attn_rpe_out_ln_b"],
        "row": inp["L1_attn_rpe_out_w"],
        "w11": inp["L1_ffn_W1"],
        "b1t1": cg(inp["L1_ffn_b1"].reshape(32, 128).T),
        "w21": inp["L1_ffn_W2"],
        "b2t1": cg(inp["L1_ffn_b2"].reshape(8, 128).T),
        "w2a": inp["L2_attn_W"],
        "bt2a": cg(inp["L2_attn_b"].reshape(8, 128).T),
        "br2a": inp["L2_attn_b"].reshape(1, 1024),
        "gw2": inp["L2_ffn_gate_w"], "eg2": inp["L2_ffn_expert_gate"],
        "eu2": inp["L2_ffn_expert_up"], "ed2": inp["L2_ffn_expert_down"],
        "sg2": inp["L2_ffn_shared_gate"], "su2": inp["L2_ffn_shared_up"],
        "sd2": inp["L2_ffn_shared_down"],
        "ipw": cg(inp["L3_attn_in_proj_w"].T),
        "ipbt": cg(inp["L3_attn_in_proj_b"].reshape(24, 128).T),
        "ipbv": inp["L3_attn_in_proj_b"][2048:].reshape(1, 1024),
        "opw": cg(inp["L3_attn_out_proj_w"].T),
        "opbt": cg(inp["L3_attn_out_proj_b"].reshape(8, 128).T),
        "bw3": cg(inp["L3_ffn_base_W"].T),
        "bbt3": cg(inp["L3_ffn_base_b"].reshape(8, 128).T),
        "sa3": cg(inp["L3_ffn_shared_A"].T),
        "sbw3": cg(inp["L3_ffn_shared_B"].T),
        "ra3": cg(inp["L3_ffn_routed_A"].reshape(96, 1024).T),
        "rb3": cg(inp["L3_ffn_routed_B"].transpose(0, 2, 1).reshape(96, 1024)),
        "rw3": inp["L3_ffn_router_W"],
        "rbr3": inp["L3_ffn_router_b"].reshape(1, 6),
    }
    nc = _build_fused()
    in_maps = [dict(common, x=cg(x[c].T)) for c in range(NCORES)]
    res = run_bass_kernel_spmd(nc, in_maps, core_ids=list(range(NCORES)))
    out = np.empty((8, 4096, 1024), np.float32)
    for j in range(NCORES):
        out[:, 512 * j:512 * (j + 1), :] = \
            np.asarray(res.results[j]["y"]).T.reshape(8, 512, 1024)
    return out


# version 04
# speedup vs baseline: 1.2509x; 1.2509x over previous
import sys

sys.path.insert(0, "/opt/trn_rl_repo")
from contextlib import ExitStack

import numpy as np
import ml_dtypes

import concourse.bass as bass
import concourse.mybir as mybir
from concourse.bass_utils import run_bass_kernel_spmd

F32 = mybir.dt.float32
BF16 = mybir.dt.bfloat16
I32 = mybir.dt.int32
AF = mybir.ActivationFunctionType
ALU = mybir.AluOpType
AX = mybir.AxisListType

NCORES = 8
M = 4096
D = 1024
NEG = -1.0e30

BF = ml_dtypes.bfloat16


def _bass(name):
    return bass.Bass("TRN2", target_bir_lowering=False,
                     detect_race_conditions=False, name=name)


class KB:

    def __init__(self, name):
        self.nc = _bass(name)
        self.st = ExitStack()
        self._uid = 0
        self._consts = {}

    def uid(self):
        self._uid += 1
        return self._uid

    def din(self, name, shape, dt=F32):
        return self.nc.dram_tensor(name, list(shape), dt, kind="ExternalInput").ap()

    def dout(self, name, shape, dt=F32):
        return self.nc.dram_tensor(name, list(shape), dt, kind="ExternalOutput").ap()

    def dscr(self, name, shape, dt=F32):
        return self.nc.dram_tensor(name, list(shape), dt, kind="Internal").ap()

    def const_dram(self, arr, name=None):
        key = (arr.tobytes(), arr.shape, str(arr.dtype))
        if key not in self._consts:
            self._consts[key] = self.nc.inline_tensor(
                np.ascontiguousarray(arr), name or f"const{self.uid()}").ap()
        return self._consts[key]

    def const_sb(self, st, ch, arr, name):
        nc = self.nc
        dt = BF16 if arr.dtype == BF else (F32 if arr.dtype == np.float32 else I32)
        t = st.enter_context(
            nc.sbuf_tensor(f"c_{name}{self.uid()}", list(arr.shape), dt))
        dr = self.const_dram(arr, f"cd_{name}{self.uid()}")
        ch.dma(nc.sync, lambda: nc.sync.dma_start(out=t[:, :], in_=dr))
        return t

    def sb(self, st, name, shape, dt):
        return self.st_sb(st, name, shape, dt)

    def st_sb(self, st, name, shape, dt):
        return st.enter_context(
            self.nc.sbuf_tensor(f"{name}{self.uid()}", list(shape), dt))

    def ps(self, st, name, shape, dt=F32):
        return st.enter_context(
            self.nc.psum_tensor(f"{name}{self.uid()}", list(shape), dt))


class Chain:

    def __init__(self, kb, st, name):
        self.kb = kb
        nc = kb.nc
        self.c = st.enter_context(nc.semaphore(f"{name}_c"))
        self.d = st.enter_context(nc.semaphore(f"{name}_d"))
        self.cn = 0
        self.dn = 0
        self._last = {}

    def _pre(self, eng):
        key = type(eng).__name__
        lc, ld = self._last.get(key, (0, 0))
        if self.cn > lc:
            eng.wait_ge(self.c, self.cn)
        if self.dn > ld:
            eng.wait_ge(self.d, self.dn)
        self._last[key] = (max(lc, self.cn), max(ld, self.dn))

    def op(self, eng, emit, inc=True):
        self._pre(eng)
        ins = emit()
        if inc:
            ins.then_inc(self.c, 1)
            self.cn += 1
        return ins

    def dma(self, eng, emit, inc=True):
        self._pre(eng)
        ins = emit()
        if inc:
            ins.then_inc(self.d, 16)
            self.dn += 16
        return ins

    def seal(self):
        nc = self.kb.nc
        for eng in (nc.sync, nc.gpsimd, nc.tensor, nc.vector, nc.scalar):
            self._pre(eng)


def barrier(kb):
    kb.nc.all_engine_barrier()



def fm(ap2d, mlo, mhi):
    return ap2d[:, mlo:mhi].rearrange("(d p) c -> p d c", p=128)


def sb3(t, nd, w):
    return t[:, :].rearrange("p (d c) -> p d c", d=nd)


def emit_cast(kb, ch, src, dst):
    ch.dma(kb.nc.gpsimd, lambda: kb.nc.gpsimd.dma_start(out=dst, in_=src))


def emit_ln(kb, ch, st, x_dram, ln_dram, mtot=M, eps=1e-5):
    nc = kb.nc
    with ExitStack() as ls:
        xm = kb.sb(ls, "ln_xm", [128, 8 * 512], F32)
        xb = kb.sb(ls, "ln_xb", [128, 8 * 512], BF16)
        x2 = kb.sb(ls, "ln_x2", [128, 8 * 512], BF16)
        mu2 = kb.sb(ls, "ln_mu2", [1, 512], F32)
        var = kb.sb(ls, "ln_var", [1, 512], F32)
        rstd = kb.sb(ls, "ln_rstd", [1, 512], F32)
        beta = kb.sb(ls, "ln_beta", [1, 512], F32)
        tmp = kb.sb(ls, "ln_tmp", [128, 512], F32)
        outt = kb.sb(ls, "ln_out", [128, 8 * 512], BF16)
        ones_bf = kb.const_sb(ls, ch, np.ones((128, 1), np.float32).astype(BF), "onesb")
        eps_sb = kb.const_sb(ls, ch, np.full((1, 1), eps, np.float32), "eps")
        ones1f = kb.const_sb(ls, ch, np.ones((1, 128), np.float32), "ones1f")
        ps_s = kb.ps(ls, "ln_ps_s", [128, 512])
        ps_q = kb.ps(ls, "ln_ps_q", [128, 512])
        ps_b = kb.ps(ls, "ln_ps_b", [128, 512])
        ps_b2 = kb.ps(ls, "ln_ps_b2", [128, 512])

        for m in range(mtot // 512):
            src = fm(x_dram, m * 512, (m + 1) * 512)
            ch.dma(nc.sync, lambda src=src: nc.sync.dma_start(out=sb3(xm, 8, 512), in_=src))
            ch.op(nc.vector, lambda: nc.vector.tensor_copy(xb[:, :], xm[:, :]))
            ch.op(nc.scalar, lambda: nc.scalar.activation(x2[:, :], xm[:, :], AF.Square))
            for dci in range(8):
                ch.op(nc.tensor, lambda dci=dci: nc.tensor.matmul(
                    ps_s[0:1, :], ones_bf[:, :], xb[:, dci * 512:(dci + 1) * 512],
                    start=(dci == 0), stop=(dci == 7)), inc=False)
                ch.op(nc.tensor, lambda dci=dci: nc.tensor.matmul(
                    ps_q[0:1, :], ones_bf[:, :], x2[:, dci * 512:(dci + 1) * 512],
                    start=(dci == 0), stop=(dci == 7)), inc=(dci == 7))
            ch.op(nc.scalar, lambda: nc.scalar.activation(
                mu2[:, :], ps_s[0:1, :], AF.Square, scale=1.0 / D))
            ch.op(nc.vector, lambda: nc.vector.scalar_tensor_tensor(
                var[:, :], ps_q[0:1, :], 1.0 / D, mu2[:, :], ALU.mult, ALU.subtract))
            ch.op(nc.scalar, lambda: nc.scalar.activation(
                rstd[:, :], var[:, :], AF.Ln, bias=eps_sb[:, :]))
            ch.op(nc.scalar, lambda: nc.scalar.activation(
                rstd[:, :], rstd[:, :], AF.Exp, scale=-0.5))
            ch.op(nc.vector, lambda: nc.vector.scalar_tensor_tensor(
                beta[:, :], ps_s[0:1, :], -1.0 / D, rstd[:, :], ALU.mult, ALU.mult))
            ch.op(nc.tensor, lambda: nc.tensor.matmul(
                ps_b[:, :], ones1f[:, :], rstd[:, :], start=True, stop=True),
                inc=False)
            ch.op(nc.tensor, lambda: nc.tensor.matmul(
                ps_b2[:, :], ones1f[:, :], beta[:, :], start=True, stop=True))
            for dci in range(8):
                sl = slice(dci * 512, (dci + 1) * 512)
                ch.op(nc.vector, lambda sl=sl: nc.vector.tensor_mul(
                    tmp[:, :], xm[:, sl], ps_b[:, :]))
                ch.op(nc.vector, lambda sl=sl: nc.vector.tensor_add(
                    outt[:, sl], tmp[:, :], ps_b2[:, :]))
            dst = fm(ln_dram, m * 512, (m + 1) * 512)
            ch.dma(nc.gpsimd, lambda dst=dst: nc.gpsimd.dma_start(
                out=dst, in_=sb3(outt, 8, 512)))


def ceil_div(a, b):
    return (a + b - 1) // b


def emit_sigma(kb, ch, st, w_bf, din_, dout_, u_dram, sp_dram, name):
    nc = kb.nc
    no, nci = dout_ // 128, din_ // 128
    u_sb = kb.sb(st, f"sg_u{name}", [128, no], BF16)
    wt_sb = kb.sb(st, f"sg_wt{name}", [128, din_], BF16)
    vsq = kb.sb(st, f"sg_vsq{name}", [1, din_], F32)
    vss = kb.sb(st, f"sg_vss{name}", [1, 1], F32)
    rn = kb.sb(st, f"sg_rn{name}", [1, 1], F32)
    v_row = kb.sb(st, f"sg_vrow{name}", [1, din_], F32)
    vc_sb = kb.sb(st, f"sg_vc{name}", [128, nci], BF16)
    wrow = kb.sb(st, f"sg_wrow{name}", [128, dout_], BF16)
    zsq = kb.sb(st, f"sg_zsq{name}", [1, dout_], F32)
    zss = kb.sb(st, f"sg_zss{name}", [1, 1], F32)
    sp_sb = kb.sb(st, f"sg_sp{name}", [1, 1], F32)
    sc = kb.sb(st, f"sg_sc{name}", [1, 1], F32)
    vsig = kb.dscr(f"sg_vd{name}", [1, din_], BF16)
    with ExitStack() as ls:
        ps_v = kb.ps(ls, f"sg_psv{name}", [128, din_])
        for a in range(no):
            ch.dma(nc.gpsimd, lambda a=a: nc.gpsimd.dma_start(
                out=u_sb[:, a:a + 1],
                in_=u_dram[a * 128:(a + 1) * 128].unsqueeze(1)))
        for a in range(no):
            ch.dma(nc.sync, lambda a=a: nc.sync.dma_start(
                out=wt_sb[:, :], in_=w_bf[:, a * 128:(a + 1) * 128],
                transpose=True))
            for j in range(din_ // 512):
                ch.op(nc.tensor, lambda a=a, j=j: nc.tensor.matmul(
                    ps_v[0:1, j * 512:(j + 1) * 512], u_sb[:, a:a + 1],
                    wt_sb[:, j * 512:(j + 1) * 512],
                    start=(a == 0), stop=(a == no - 1)),
                    inc=(a == no - 1 and j == din_ // 512 - 1))
        ch.op(nc.scalar, lambda: nc.scalar.activation(
            vsq[:, :], ps_v[0:1, 0:din_], AF.Square, accum_out=vss[:, :]))
        ch.op(nc.scalar, lambda: nc.scalar.activation(rn[:, :], vss[:, :], AF.Ln))
        ch.op(nc.scalar, lambda: nc.scalar.activation(
            rn[:, :], rn[:, :], AF.Exp, scale=-0.5))
        ch.op(nc.vector, lambda: nc.vector.tensor_single_scalar(
            v_row[:, :], ps_v[0:1, 0:din_], rn[:, :], ALU.mult))
    ch.dma(nc.gpsimd, lambda: nc.gpsimd.dma_start(out=vsig, in_=v_row[:, :]))
    for c in range(nci):
        ch.dma(nc.sync, lambda c=c: nc.sync.dma_start(
            out=vc_sb[:, c:c + 1],
            in_=vsig[0:1, c * 128:(c + 1) * 128].rearrange("a c -> c a")))
    with ExitStack() as ls:
        ps_z = kb.ps(ls, f"sg_psz{name}", [128, dout_])
        for c in range(nci):
            ch.dma(nc.sync, lambda c=c: nc.sync.dma_start(
                out=wrow[:, :], in_=w_bf[c * 128:(c + 1) * 128, :]))
            for j in range(dout_ // 512):
                ch.op(nc.tensor, lambda c=c, j=j: nc.tensor.matmul(
                    ps_z[0:1, j * 512:(j + 1) * 512], vc_sb[:, c:c + 1],
                    wrow[:, j * 512:(j + 1) * 512],
                    start=(c == 0), stop=(c == nci - 1)),
                    inc=(c == nci - 1 and j == dout_ // 512 - 1))
        ch.op(nc.scalar, lambda: nc.scalar.activation(
            zsq[:, :], ps_z[0:1, 0:dout_], AF.Square, accum_out=zss[:, :]))
    ch.op(nc.scalar, lambda: nc.scalar.activation(sc[:, :], zss[:, :], AF.Ln))
    ch.op(nc.scalar, lambda: nc.scalar.activation(
        sc[:, :], sc[:, :], AF.Exp, scale=-0.5))
    ch.dma(nc.sync, lambda: nc.sync.dma_start(
        out=sp_sb[:, :], in_=sp_dram.unsqueeze(0)))
    ch.op(nc.vector, lambda: nc.vector.tensor_mul(sc[:, :], sc[:, :], sp_sb[:, :]))
    return sc


def emit_l0_attn(kb, x_dram, wqkv_f, uq_f, sq_f, wproj_f, up_f, sp_f, y_dram):
    nc = kb.nc
    wqkv_bf = kb.dscr("l0_wqkv_bf", [1024, 3072], BF16)
    wproj_bf = kb.dscr("l0_wproj_bf", [1024, 1024], BF16)
    ln_scr = kb.dscr("l0_ln", [1024, 4096], BF16)
    q_scr = kb.dscr("l0_q", [1024, 4096], BF16)
    k_scr = kb.dscr("l0_k", [1024, 4096], BF16)
    v_scr = kb.dscr("l0_v", [4096, 1024], BF16)
    avn_scr = kb.dscr("l0_avn", [1024, 4096], BF16)
    sc_scr = kb.dscr("l0_scales", [1, 4], F32)

    with ExitStack() as st:
        ch = Chain(kb, st, "l0a_sig")
        emit_cast(kb, ch, wqkv_f, wqkv_bf)
        emit_cast(kb, ch, wproj_f, wproj_bf)
        sc_qkv = emit_sigma(kb, ch, st, wqkv_bf, 1024, 3072, uq_f, sq_f, "qkv")
        sc_proj = emit_sigma(kb, ch, st, wproj_bf, 1024, 1024, up_f, sp_f, "proj")
        sq_11 = kb.sb(st, "sq11", [1, 1], F32)
        ch.op(nc.vector, lambda: nc.vector.tensor_scalar_mul(
            sq_11[:, :], sc_qkv[:, :], 0.125))
        ch.dma(nc.gpsimd, lambda: nc.gpsimd.dma_start(
            out=sc_scr[0:1, 0:1], in_=sq_11[:, :]))
        ch.dma(nc.gpsimd, lambda: nc.gpsimd.dma_start(
            out=sc_scr[0:1, 1:2], in_=sc_qkv[:, :]))
        ch.dma(nc.gpsimd, lambda: nc.gpsimd.dma_start(
            out=sc_scr[0:1, 2:3], in_=sc_proj[:, :]))
        ch.seal()
    barrier(kb)

    with ExitStack() as st:
        ch = Chain(kb, st, "l0a_ln")
        emit_ln(kb, ch, st, x_dram, ln_scr)
        ch.seal()
    barrier(kb)

    with ExitStack() as st:
        ch = Chain(kb, st, "l0a_qkv")
        s_all = kb.sb(st, "sall", [128, 3], F32)
        ch.dma(nc.sync, lambda: nc.sync.dma_start(
            out=s_all[:, :], in_=sc_scr[0:1, 0:3].broadcast_to([128, 3])))
        wq = kb.sb(st, "wq", [128, 8 * 1024], BF16)
        wk = kb.sb(st, "wk", [128, 8 * 1024], BF16)
        wv = kb.sb(st, "wv", [128, 8 * 1024], BF16)
        lnm = kb.sb(st, "lnm", [128, 8 * 512], BF16)
        ob = kb.sb(st, "ob", [128, 1024], BF16)
        ps0 = kb.ps(st, "qk_ps", [128, 512])
        ps_v2 = kb.ps(st, "v_ps", [128, 1024])
        for wt, lo in ((wq, 0), (wk, 1024), (wv, 2048)):
            ch.dma(nc.sync, lambda wt=wt, lo=lo: nc.sync.dma_start(
                out=sb3(wt, 8, 1024),
                in_=wqkv_bf[:, lo:lo + 1024].rearrange("(d p) c -> p d c", p=128)))
        for m in range(8):
            ch.dma(nc.sync, lambda m=m: nc.sync.dma_start(
                out=sb3(lnm, 8, 512), in_=fm(ln_scr, m * 512, (m + 1) * 512)))
            for wt, si, dst in ((wq, 0, q_scr), (wk, 1, k_scr)):
                for oc in range(8):
                    for dd in range(8):
                        ch.op(nc.tensor, lambda wt=wt, oc=oc, dd=dd: nc.tensor.matmul(
                            ps0[:, :], wt[:, dd * 1024 + oc * 128:dd * 1024 + (oc + 1) * 128],
                            lnm[:, dd * 512:(dd + 1) * 512],
                            start=(dd == 0), stop=(dd == 7)), inc=(dd == 7))
                    ch.op(nc.vector, lambda si=si: nc.vector.tensor_single_scalar(
                        ob[:, 0:512], ps0[:, :], s_all[:, si:si + 1], ALU.mult))
                    ch.dma(nc.gpsimd, lambda dst=dst, oc=oc, m=m: nc.gpsimd.dma_start(
                        out=dst[oc * 128:(oc + 1) * 128, m * 512:(m + 1) * 512],
                        in_=ob[:, 0:512]))
            for mt in range(4):
                for dd in range(8):
                    for half in range(2):
                        ch.op(nc.tensor, lambda mt=mt, dd=dd, half=half: nc.tensor.matmul(
                            ps_v2[:, half * 512:(half + 1) * 512],
                            lnm[:, dd * 512 + mt * 128:dd * 512 + (mt + 1) * 128],
                            wv[:, dd * 1024 + half * 512:dd * 1024 + (half + 1) * 512],
                            start=(dd == 0), stop=(dd == 7)),
                            inc=(dd == 7 and half == 1))
                ch.op(nc.vector, lambda: nc.vector.tensor_single_scalar(
                    ob[:, 0:1024], ps_v2[:, :], s_all[:, 1:2], ALU.mult))
                ch.dma(nc.gpsimd, lambda m=m, mt=mt: nc.gpsimd.dma_start(
                    out=v_scr[m * 512 + mt * 128:m * 512 + (mt + 1) * 128, :],
                    in_=ob[:, 0:1024]))
        ch.seal()
    barrier(kb)

    with ExitStack() as st:
        ch = Chain(kb, st, "l0a_att")
        ident_bf = kb.const_sb(st, ch, np.eye(128, dtype=np.float32).astype(BF), "idb")
        ident_f = kb.const_sb(st, ch, np.eye(128, dtype=np.float32), "idf")
        tri = np.where(np.tril(np.ones((128, 128))) > 0, 0.0, NEG).astype(np.float32)
        trimask = kb.const_sb(st, ch, tri, "tri")
        ones64f = kb.const_sb(st, ch, np.ones((1, 64), np.float32), "o64")
        qh = kb.sb(st, "qh", [64, 4096], BF16)
        kh = kb.sb(st, "kh", [64, 4096], BF16)
        vh = kb.sb(st, "vh", [128, 32 * 64], BF16)
        p_sb = kb.sb(st, "p", [128, 2560], BF16)
        pt_sb = kb.sb(st, "pt", [128, 512], BF16)
        avh = kb.sb(st, "avh", [64, 4096], F32)
        avn = kb.sb(st, "avn", [64, 4096], BF16)
        lrec = kb.sb(st, "lrec", [128, 32], F32)
        mneg = kb.sb(st, "mneg", [128, 2], F32)
        lg = kb.sb(st, "lg", [128, 2], F32)
        sm1 = kb.sb(st, "sm1", [128, 1], F32)
        sm2 = kb.sb(st, "sm2", [128, 2], F32)
        arow = kb.sb(st, "arow", [1, 256], F32)
        ab_sb = kb.sb(st, "ab", [64, 256], F32)
        t_sb = kb.sb(st, "t1", [64, 128], F32)
        u_sb = kb.sb(st, "t2", [64, 128], F32)
        lrT = kb.sb(st, "lrT", [1, 512], F32)
        ps_s = kb.ps(st, "att_s", [128, 2560])
        pst = kb.ps(st, "att_t", [128, 1024], BF16)
        ps_av = kb.ps(st, "att_av", [128, 256])
        ps_m = kb.ps(st, "att_m", [128, 512])
        vh3 = sb3(vh, 32, 64)
        for h in range(16):
            ch.dma(nc.sync, lambda h=h: nc.sync.dma_start(
                out=qh[:, :], in_=q_scr[64 * h:64 * h + 64, :]))
            ch.dma(nc.sync, lambda h=h: nc.sync.dma_start(
                out=kh[:, :], in_=k_scr[64 * h:64 * h + 64, :]))
            ch.dma(nc.sync, lambda h=h: nc.sync.dma_start(
                out=vh3, in_=v_scr[:, 64 * h:64 * h + 64].rearrange(
                    "(t p) c -> p t c", p=128)))
            for i in range(32):
                jmax = 128 * (i + 1)
                nj = ceil_div(jmax, 512)
                groups = [(0, min(nj, 5))]
                if nj > 5:
                    groups.append((5, nj))
                njt_tot = jmax // 128
                for g, (c0, c1) in enumerate(groups):
                    gw = min(jmax, c1 * 512) - c0 * 512
                    for jc in range(c0, c1):
                        w = min(512, jmax - jc * 512)
                        ch.op(nc.tensor, lambda i=i, jc=jc, c0=c0, w=w: nc.tensor.matmul(
                            ps_s[:, (jc - c0) * 512:(jc - c0) * 512 + w],
                            qh[:, i * 128:(i + 1) * 128],
                            kh[:, jc * 512:jc * 512 + w],
                            start=True, stop=True), inc=(jc == c1 - 1))
                    if g == len(groups) - 1:
                        pos = i * 128 - c0 * 512
                        ch.op(nc.vector, lambda pos=pos: nc.vector.tensor_add(
                            ps_s[:, pos:pos + 128], ps_s[:, pos:pos + 128],
                            trimask[:, :]))
                    ch.op(nc.vector, lambda g=g, gw=gw: nc.vector.reduce_max(
                        mneg[:, g:g + 1], ps_s[:, 0:gw], AX.X, negate=True))
                    ch.op(nc.scalar, lambda g=g, gw=gw: nc.scalar.activation(
                        p_sb[:, 0:gw], ps_s[:, 0:gw], AF.Exp,
                        bias=mneg[:, g:g + 1], accum_out=lg[:, g:g + 1]))
                    njc = gw // 128
                    for b0 in range(0, njc, 4):
                        bn = min(4, njc - b0)
                        for s in range(bn):
                            ch.op(nc.tensor, lambda b0=b0, s=s: nc.tensor.transpose(
                                pst[:, s * 128:(s + 1) * 128],
                                p_sb[:, (b0 + s) * 128:(b0 + s + 1) * 128],
                                ident_bf[:, :]), inc=(s == bn - 1))
                        ch.op(nc.vector, lambda bn=bn: nc.vector.tensor_copy(
                            pt_sb[:, 0:bn * 128], pst[:, 0:bn * 128]))
                        for s in range(bn):
                            jt = c0 * 4 + b0 + s
                            first = (b0 + s == 0)
                            last = (b0 + s == njc - 1)
                            ch.op(nc.tensor, lambda jt=jt, s=s, g=g, first=first,
                                  last=last: nc.tensor.matmul(
                                ps_av[0:64, g * 128:(g + 1) * 128],
                                vh3[:, jt, :], pt_sb[:, s * 128:(s + 1) * 128],
                                start=first, stop=last), inc=last)
                if len(groups) == 1:
                    ch.op(nc.vector, lambda i=i: nc.vector.tensor_copy(
                        avh[:, i * 128:(i + 1) * 128], ps_av[0:64, 0:128]))
                    ch.op(nc.vector, lambda i=i: nc.vector.tensor_copy(
                        lrec[:, i:i + 1], lg[:, 0:1]))
                else:
                    ch.op(nc.vector, lambda: nc.vector.tensor_tensor(
                        sm1[:, :], mneg[:, 0:1], mneg[:, 1:2], ALU.min))
                    ch.op(nc.vector, lambda: nc.vector.tensor_sub(
                        sm2[:, 0:2], sm1[:, :].broadcast_to([128, 2]), mneg[:, 0:2]))
                    ch.op(nc.scalar, lambda: nc.scalar.activation(
                        sm2[:, 0:2], sm2[:, 0:2], AF.Exp))
                    ch.op(nc.vector, lambda: nc.vector.tensor_mul(
                        sm2[:, 0:2], lg[:, 0:2], sm2[:, 0:2]))
                    ch.op(nc.vector, lambda i=i: nc.vector.reduce_sum(
                        lrec[:, i:i + 1], sm2[:, 0:2], AX.X))
                    ch.op(nc.vector, lambda: nc.vector.tensor_sub(
                        sm2[:, 0:2], sm1[:, :].broadcast_to([128, 2]), mneg[:, 0:2]))
                    ch.op(nc.scalar, lambda: nc.scalar.activation(
                        sm2[:, 0:2], sm2[:, 0:2], AF.Exp))
                    for g in range(2):
                        ch.op(nc.tensor, lambda g=g: nc.tensor.transpose(
                            ps_m[0:1, g * 128:(g + 1) * 128], sm2[:, g:g + 1],
                            ident_f[:, :]), inc=(g == 1))
                    ch.op(nc.vector, lambda: nc.vector.tensor_copy(
                        arow[0:1, :], ps_m[0:1, 0:256]))
                    for g in range(2):
                        ch.op(nc.tensor, lambda g=g: nc.tensor.matmul(
                            ps_m[0:64, 128 + g * 128:128 + (g + 1) * 128],
                            ones64f[:, :], arow[0:1, g * 128:(g + 1) * 128],
                            start=True, stop=True), inc=(g == 1))
                    ch.op(nc.vector, lambda: nc.vector.tensor_copy(
                        ab_sb[:, :], ps_m[0:64, 128:384]))
                    ch.op(nc.vector, lambda: nc.vector.tensor_mul(
                        t_sb[:, :], ps_av[0:64, 0:128], ab_sb[:, 0:128]))
                    ch.op(nc.vector, lambda: nc.vector.tensor_mul(
                        u_sb[:, :], ps_av[0:64, 128:256], ab_sb[:, 128:256]))
                    ch.op(nc.vector, lambda i=i: nc.vector.tensor_add(
                        avh[:, i * 128:(i + 1) * 128], t_sb[:, :], u_sb[:, :]))
            ch.op(nc.vector, lambda: nc.vector.reciprocal(lrec[:, :], lrec[:, :]))
            for it0 in range(0, 32, 4):
                for s in range(4):
                    ch.op(nc.tensor, lambda it0=it0, s=s: nc.tensor.transpose(
                        ps_m[0:1, s * 128:(s + 1) * 128],
                        lrec[:, it0 + s:it0 + s + 1], ident_f[:, :]),
                        inc=(s == 3))
                ch.op(nc.vector, lambda: nc.vector.tensor_copy(
                    lrT[0:1, :], ps_m[0:1, 0:512]))
                for s in range(4):
                    ch.op(nc.tensor, lambda s=s: nc.tensor.matmul(
                        ps_m[0:64, s * 128:(s + 1) * 128], ones64f[:, :],
                        lrT[0:1, s * 128:(s + 1) * 128], start=True, stop=True),
                        inc=(s == 3))
                ch.op(nc.vector, lambda it0=it0: nc.vector.tensor_mul(
                    avn[:, it0 * 128:(it0 + 4) * 128],
                    avh[:, it0 * 128:(it0 + 4) * 128], ps_m[0:64, 0:512]))
            ch.dma(nc.gpsimd, lambda h=h: nc.gpsimd.dma_start(
                out=avn_scr[64 * h:64 * h + 64, :], in_=avn[:, :]))
        ch.seal()
    barrier(kb)

    with ExitStack() as st:
        ch = Chain(kb, st, "l0a_proj")
        s_all = kb.sb(st, "sall2", [128, 3], F32)
        ch.dma(nc.sync, lambda: nc.sync.dma_start(
            out=s_all[:, :], in_=sc_scr[0:1, 0:3].broadcast_to([128, 3])))
        wp = kb.sb(st, "wp", [128, 8 * 1024], BF16)
        avm = kb.sb(st, "avm", [128, 8 * 512], BF16)
        xt = kb.sb(st, "xt", [128, 512], F32)
        yb = kb.sb(st, "yb", [128, 512], F32)
        ps0 = kb.ps(st, "pj_ps", [128, 512])
        ch.dma(nc.sync, lambda: nc.sync.dma_start(
            out=sb3(wp, 8, 1024),
            in_=wproj_bf.rearrange("(d p) c -> p d c", p=128)))
        for m in range(8):
            ch.dma(nc.sync, lambda m=m: nc.sync.dma_start(
                out=sb3(avm, 8, 512), in_=fm(avn_scr, m * 512, (m + 1) * 512)))
            for oc in range(8):
                ch.dma(nc.sync, lambda oc=oc, m=m: nc.sync.dma_start(
                    out=xt[:, :],
                    in_=x_dram[oc * 128:(oc + 1) * 128, m * 512:(m + 1) * 512]))
                for dd in range(8):
                    ch.op(nc.tensor, lambda oc=oc, dd=dd: nc.tensor.matmul(
                        ps0[:, :], wp[:, dd * 1024 + oc * 128:dd * 1024 + (oc + 1) * 128],
                        avm[:, dd * 512:(dd + 1) * 512],
                        start=(dd == 0), stop=(dd == 7)), inc=(dd == 7))
                ch.op(nc.vector, lambda: nc.vector.scalar_tensor_tensor(
                    yb[:, :], ps0[:, :], s_all[:, 2:3], xt[:, :],
                    ALU.mult, ALU.add))
                ch.dma(nc.gpsimd, lambda oc=oc, m=m: nc.gpsimd.dma_start(
                    out=y_dram[oc * 128:(oc + 1) * 128, m * 512:(m + 1) * 512],
                    in_=yb[:, :]))
        ch.seal()
    barrier(kb)


def emit_gating_topk(kb, ch, st, ln_scr, wg_f, cw_scr, nex, topk_norm_eps=None,
                     bias_row=None):
    nc = kb.nc
    wg_sb = kb.sb(st, "gt_wg", [nex, 1024], BF16)
    wgT = kb.sb(st, "gt_wgT", [128, 8 * nex], BF16)
    lnm = kb.sb(st, "gt_ln", [128, 8 * 512], BF16)
    gate = kb.sb(st, "gt_g", [128, nex], F32)
    mg = kb.sb(st, "gt_m", [128, 1], F32)
    ls = kb.sb(st, "gt_l", [128, 1], F32)
    m1 = kb.sb(st, "gt_m1", [128, 1], F32)
    msk = kb.sb(st, "gt_msk", [128, nex], F32)
    gm = kb.sb(st, "gt_gm", [128, nex], F32)
    cwT = kb.sb(st, "gt_cwT", [nex, 4096], F32)
    ident_bf = kb.const_sb(st, ch, np.eye(128, dtype=np.float32).astype(BF), "idb")
    ident_f = kb.const_sb(st, ch, np.eye(128, dtype=np.float32), "idf")
    ps_g = kb.ps(st, "gt_ps", [128, 512])
    ps_t = kb.ps(st, "gt_pst", [128, 512], BF16)
    ps_tf = kb.ps(st, "gt_ptf", [128, 512])
    if bias_row is not None:
        gb_b = kb.sb(st, "gt_bb", [128, nex], F32)
        ch.dma(nc.sync, lambda: nc.sync.dma_start(
            out=gb_b[:, :], in_=bias_row.broadcast_to([128, nex])))
    ch.dma(nc.gpsimd, lambda: nc.gpsimd.dma_start(out=wg_sb[:, :], in_=wg_f))
    for c in range(8):
        ch.op(nc.tensor, lambda c=c: nc.tensor.transpose(
            ps_t[0:128, 0:nex], wg_sb[0:nex, c * 128:(c + 1) * 128],
            ident_bf[0:nex, 0:nex]))
        ch.op(nc.vector, lambda c=c: nc.vector.tensor_copy(
            wgT[:, c * nex:(c + 1) * nex], ps_t[0:128, 0:nex]))
    for m in range(8):
        ch.dma(nc.sync, lambda m=m: nc.sync.dma_start(
            out=sb3(lnm, 8, 512), in_=fm(ln_scr, m * 512, (m + 1) * 512)))
        for mt in range(4):
            for dd in range(8):
                ch.op(nc.tensor, lambda mt=mt, dd=dd: nc.tensor.matmul(
                    ps_g[:, 0:nex],
                    lnm[:, dd * 512 + mt * 128:dd * 512 + (mt + 1) * 128],
                    wgT[:, dd * nex:(dd + 1) * nex],
                    start=(dd == 0), stop=(dd == 7)), inc=(dd == 7))
            if bias_row is not None:
                ch.op(nc.vector, lambda: nc.vector.tensor_add(
                    ps_g[:, 0:nex], ps_g[:, 0:nex], gb_b[:, :]))
            ch.op(nc.vector, lambda: nc.vector.reduce_max(
                mg[:, :], ps_g[:, 0:nex], AX.X, negate=True))
            ch.op(nc.scalar, lambda: nc.scalar.activation(
                gate[:, :], ps_g[:, 0:nex], AF.Exp, bias=mg[:, :],
                accum_out=ls[:, :]))
            ch.op(nc.vector, lambda: nc.vector.reciprocal(ls[:, :], ls[:, :]))
            ch.op(nc.vector, lambda: nc.vector.tensor_single_scalar(
                gate[:, :], gate[:, :], ls[:, :], ALU.mult))
            ch.op(nc.vector, lambda: nc.vector.reduce_max(m1[:, :], gate[:, :], AX.X))
            ch.op(nc.vector, lambda: nc.vector.tensor_single_scalar(
                msk[:, :], gate[:, :], m1[:, :], ALU.is_equal))
            ch.op(nc.vector, lambda: nc.vector.scalar_tensor_tensor(
                gm[:, :], msk[:, :], NEG, gate[:, :], ALU.mult, ALU.add))
            ch.op(nc.vector, lambda: nc.vector.reduce_max(m1[:, :], gm[:, :], AX.X))
            ch.op(nc.vector, lambda: nc.vector.tensor_single_scalar(
                msk[:, :], gate[:, :], m1[:, :], ALU.is_ge))
            ch.op(nc.vector, lambda: nc.vector.tensor_mul(
                gate[:, :], gate[:, :], msk[:, :]))
            if topk_norm_eps is not None:
                ch.op(nc.vector, lambda: nc.vector.reduce_sum(
                    ls[:, :], gate[:, :], AX.X))
                ch.op(nc.vector, lambda: nc.vector.tensor_scalar_add(
                    ls[:, :], ls[:, :], topk_norm_eps))
                ch.op(nc.vector, lambda: nc.vector.reciprocal(ls[:, :], ls[:, :]))
                ch.op(nc.vector, lambda: nc.vector.tensor_single_scalar(
                    gate[:, :], gate[:, :], ls[:, :], ALU.mult))
            ch.op(nc.tensor, lambda: nc.tensor.transpose(
                ps_tf[0:nex, 0:128], gate[:, 0:nex], ident_f[:, :]))
            ch.op(nc.vector, lambda m=m, mt=mt: nc.vector.tensor_copy(
                cwT[:, m * 512 + mt * 128:m * 512 + (mt + 1) * 128],
                ps_tf[0:nex, 0:128]))
    ch.dma(nc.gpsimd, lambda: nc.gpsimd.dma_start(out=cw_scr, in_=cwT[:, :]))


def emit_l0_ffn(kb, x_dram, wg_f, w1_f, b1t_f, w2_f, b2_f, y_dram):
    nc = kb.nc
    w1_bf = kb.dscr("l0f_w1", [8, 1024, 4096], BF16)
    w2_bf = kb.dscr("l0f_w2", [8, 4096, 1024], BF16)
    ln_scr = kb.dscr("l0f_ln", [1024, 4096], BF16)
    cw_scr = kb.dscr("l0f_cw", [8, 4096], F32)
    with ExitStack() as st:
        ch = Chain(kb, st, "l0f_pre")
        for e in range(8):
            ch.dma(nc.gpsimd, lambda e=e: nc.gpsimd.dma_start(
                out=w1_bf[e], in_=w1_f[e]))
            ch.dma(nc.gpsimd, lambda e=e: nc.gpsimd.dma_start(
                out=w2_bf[e], in_=w2_f[e]))
        ch.dma(nc.gpsimd, lambda: nc.gpsimd.dma_start(out=y_dram, in_=x_dram))
        emit_ln(kb, ch, st, x_dram, ln_scr)
        ch.seal()
    barrier(kb)
    with ExitStack() as st:
        ch = Chain(kb, st, "l0f_gate")
        emit_gating_topk(kb, ch, st, ln_scr, wg_f, cw_scr, 8)
        ch.seal()
    barrier(kb)
    with ExitStack() as st:
        df = DF(kb, st, "l0f_ex")
        w1e = kb.sb(st, "w1e", [128, 8 * 4096], BF16)
        w2e = kb.sb(st, "w2e", [128, 32 * 1024], BF16)
        lnm = [kb.sb(st, f"lnm{i}", [128, 8 * 512], BF16) for i in range(2)]
        hblk = kb.sb(st, "hblk", [128, 32 * 512], BF16)
        gbuf = [kb.sb(st, f"gbuf{i}", [128, 512], BF16) for i in range(2)]
        cwbe = kb.sb(st, "cwbe", [128, 4096], BF16)
        b1e = kb.sb(st, "b1e", [128, 32], F32)
        b2_bf = kb.sb(st, "b2bf", [8, 1024], BF16)
        cwT_bf = kb.sb(st, "cwtbf", [8, 4096], BF16)
        tmpo = [kb.sb(st, f"tmpo{i}", [128, 512], F32) for i in range(2)]
        ps1 = [kb.ps(st, f"ffn_ps1{i}", [128, 512]) for i in range(2)]
        ps2 = [kb.ps(st, f"ffn_ps2{i}", [128, 512]) for i in range(2)]
        df.dma("gpsimd", lambda: nc.gpsimd.dma_start(out=b2_bf[:, :], in_=b2_f),
               writes=[b2_bf])
        df.dma("gpsimd", lambda: nc.gpsimd.dma_start(out=cwT_bf[:, :], in_=cw_scr),
               writes=[cwT_bf])
        for e in range(8):
            df.dma("sync", lambda e=e: nc.sync.dma_start(
                out=sb3(w1e, 8, 4096),
                in_=w1_bf[e].rearrange("(d p) c -> p d c", p=128)), writes=[w1e])
            df.dma("sync", lambda e=e: nc.sync.dma_start(
                out=sb3(w2e, 32, 1024),
                in_=w2_bf[e].rearrange("(d p) c -> p d c", p=128)), writes=[w2e])
            df.dma("gpsimd", lambda e=e: nc.gpsimd.dma_start(
                out=cwbe[:, :], in_=cw_scr[e:e + 1, :].broadcast_to([128, 4096])),
                writes=[cwbe])
            df.dma("sync", lambda e=e: nc.sync.dma_start(
                out=b1e[:, :], in_=b1t_f[e]), writes=[b1e])
            for m in range(8):
                lb = lnm[m % 2]
                df.dma("sync", lambda m=m, lb=lb: nc.sync.dma_start(
                    out=sb3(lb, 8, 512), in_=fm(ln_scr, m * 512, (m + 1) * 512)),
                    writes=[lb])
                for hc in range(32):
                    pb = ps1[hc % 2]
                    for dd in range(8):
                        df.op("tensor", lambda hc=hc, dd=dd, pb=pb, lb=lb:
                              nc.tensor.matmul(
                            pb[:, :],
                            w1e[:, dd * 4096 + hc * 128:dd * 4096 + (hc + 1) * 128],
                            lb[:, dd * 512:(dd + 1) * 512],
                            start=(dd == 0), stop=(dd == 7)),
                            reads=[w1e, lb], writes=[pb], inc=(dd == 7))
                    gb = gbuf[hc % 2]
                    df.op("scalar", lambda hc=hc, pb=pb, gb=gb: nc.scalar.activation(
                        gb[:, :], pb[:, :], AF.Gelu, bias=b1e[:, hc:hc + 1]),
                        reads=[pb, b1e], writes=[gb])
                    df.op("vector", lambda hc=hc, m=m, gb=gb: nc.vector.tensor_mul(
                        hblk[:, hc * 512:(hc + 1) * 512], gb[:, :],
                        cwbe[:, m * 512:(m + 1) * 512]),
                        reads=[gb, cwbe], writes=[(hblk, hc)])
                for oc in range(8):
                    qb = ps2[oc % 2]
                    if e == 0:
                        df.op("tensor", lambda oc=oc, m=m, qb=qb: nc.tensor.matmul(
                            qb[:, :], b2_bf[:, oc * 128:(oc + 1) * 128],
                            cwT_bf[:, m * 512:(m + 1) * 512],
                            start=True, stop=False),
                            reads=[b2_bf, cwT_bf], writes=[qb], inc=False)
                    for hc in range(32):
                        df.op("tensor", lambda oc=oc, hc=hc, qb=qb, e=e:
                              nc.tensor.matmul(
                            qb[:, :],
                            w2e[:, hc * 1024 + oc * 128:hc * 1024 + (oc + 1) * 128],
                            hblk[:, hc * 512:(hc + 1) * 512],
                            start=(e != 0 and hc == 0), stop=(hc == 31)),
                            reads=[w2e, (hblk, hc)], writes=[qb], inc=(hc == 31))
                    tb = tmpo[oc % 2]
                    df.op("vector", lambda qb=qb, tb=tb: nc.vector.tensor_copy(
                        tb[:, :], qb[:, :]), reads=[qb], writes=[tb])
                    df.dma("gpsimd", lambda oc=oc, m=m, tb=tb: nc.gpsimd.dma_start(
                        out=y_dram[oc * 128:(oc + 1) * 128, m * 512:(m + 1) * 512],
                        in_=tb[:, :], accum_op=ALU.add),
                        reads=[tb, ("y", oc, m)], writes=[("y", oc, m)])
        df.seal()
    barrier(kb)


def _dft_consts():
    Mf = 8192
    i64 = np.int64
    b = np.arange(64, dtype=i64)[:, None]
    b2 = np.arange(128, dtype=i64)[:, None]
    c = np.arange(128, dtype=i64)[None, :]
    out = {}
    c1v = np.zeros((64, 64 * 128), np.float64)
    s1v = np.zeros((64, 64 * 128), np.float64)
    c1t = np.zeros((128, 64 * 128), np.float64)
    s1t = np.zeros((128, 64 * 128), np.float64)
    for a in range(64):
        th = 2 * np.pi * ((c * (64 * b + a)) % Mf) / Mf
        c1v[:, a * 128:(a + 1) * 128] = np.cos(th)
        s1v[:, a * 128:(a + 1) * 128] = -np.sin(th)
        th2 = 2 * np.pi * ((c * (64 * b2 + a)) % Mf) / Mf
        c1t[:, a * 128:(a + 1) * 128] = np.cos(th2)
        s1t[:, a * 128:(a + 1) * 128] = -np.sin(th2)
    out["c1v"], out["s1v"] = c1v, s1v
    out["c1t"], out["s1t"] = c1t, s1t
    ad = np.outer(np.arange(64), np.arange(64))
    C64 = np.cos(2 * np.pi * (ad % 64) / 64)
    S64 = np.sin(2 * np.pi * (ad % 64) / 64)
    out["ls2a"] = np.concatenate([C64, -S64], 1)
    out["ls2b"] = np.concatenate([S64, C64], 1)
    out["li1a"] = np.concatenate([C64, S64], 1)
    out["li1b"] = np.concatenate([-S64, C64], 1)
    ci2 = np.zeros((128, 64 * 64), np.float64)
    si2 = np.zeros((128, 64 * 64), np.float64)
    cc = np.arange(128, dtype=i64)[:, None]
    bb = np.arange(64, dtype=i64)[None, :]
    for a in range(64):
        th = 2 * np.pi * ((cc * (64 * bb + a)) % Mf) / Mf
        ci2[:, a * 64:(a + 1) * 64] = np.cos(th) / Mf
        si2[:, a * 64:(a + 1) * 64] = -np.sin(th) / Mf
    out["ci2"], out["si2"] = ci2, si2
    return {k: v.astype(np.float32).astype(BF) for k, v in out.items()}


def _rpe_pos_gam():
    lag = np.zeros(8192, np.float64)
    lag[1:4096] = np.arange(1, 4096)
    lag[4097:] = np.arange(4097, 8192) - 8192
    gam = 0.99 ** np.abs(lag)
    gam[4096] = 0.0
    pos = lag.copy()
    return (pos.reshape(64, 128).astype(np.float32),
            gam.reshape(64, 128).astype(np.float32))



def emit_l1_attn(kb, x_dram, uvw_f, ow_f, rpw_f, rlw_f, rlb_f, rfc_f,
                 rolw_f, rolb_f, row_f, y_dram):
    nc = kb.nc
    ln_scr = kb.dscr("l1_ln", [1024, 4096], BF16)
    u_scr = kb.dscr("l1_u", [4096, 1024], BF16)
    v_scr = kb.dscr("l1_v", [4096, 1024], BF16)
    t_scr = kb.dscr("l1_t", [8192, 1024], BF16)
    z_scr = kb.dscr("l1_z", [64, 2, 128, 1024], BF16)
    zt_scr = kb.dscr("l1_zt", [64, 2, 128, 1024], BF16)
    xv_scr = kb.dscr("l1_xv", [128, 128 * 1024], BF16)
    xt_scr = kb.dscr("l1_xt", [128, 128 * 1024], BF16)
    p_scr = kb.dscr("l1_p", [128, 128 * 1024], BF16)
    q_scr = kb.dscr("l1_q", [128, 128 * 1024], BF16)
    y_scr = kb.dscr("l1_y", [4096, 1024], BF16)
    prod_scr = kb.dscr("l1_prod", [4096, 1024], BF16)
    uvw_bf = kb.dscr("l1_uvw", [1024, 2048], BF16)
    ow_bf = kb.dscr("l1_ow", [1024, 1024], BF16)

    with ExitStack() as st:
        ch = Chain(kb, st, "l1_a")
        emit_cast(kb, ch, uvw_f, uvw_bf)
        emit_cast(kb, ch, ow_f, ow_bf)
        emit_ln(kb, ch, st, x_dram, ln_scr)
        ch.seal()
    barrier(kb)

    with ExitStack() as st:
        ch = Chain(kb, st, "l1_b")
        uvw_sb = kb.sb(st, "uvw", [128, 8 * 2048], BF16)
        lnm = kb.sb(st, "lnm", [128, 8 * 512], BF16)
        sil = kb.sb(st, "sil", [128, 2048], BF16)
        ps_uv = kb.ps(st, "uv_ps", [128, 2048])
        ch.dma(nc.sync, lambda: nc.sync.dma_start(
            out=sb3(uvw_sb, 8, 2048),
            in_=uvw_bf.rearrange("(d p) c -> p d c", p=128)))
        for m in range(8):
            ch.dma(nc.sync, lambda m=m: nc.sync.dma_start(
                out=sb3(lnm, 8, 512), in_=fm(ln_scr, m * 512, (m + 1) * 512)))
            for mt in range(4):
                for dd in range(8):
                    for q4 in range(4):
                        ch.op(nc.tensor, lambda mt=mt, dd=dd, q4=q4: nc.tensor.matmul(
                            ps_uv[:, q4 * 512:(q4 + 1) * 512],
                            lnm[:, dd * 512 + mt * 128:dd * 512 + (mt + 1) * 128],
                            uvw_sb[:, dd * 2048 + q4 * 512:dd * 2048 + (q4 + 1) * 512],
                            start=(dd == 0), stop=(dd == 7)),
                            inc=(dd == 7 and q4 == 3))
                ch.op(nc.scalar, lambda: nc.scalar.activation(
                    sil[:, :], ps_uv[:, :], AF.Silu))
                r0 = m * 512 + mt * 128
                ch.dma(nc.gpsimd, lambda r0=r0: nc.gpsimd.dma_start(
                    out=u_scr[r0:r0 + 128, :], in_=sil[:, 0:1024]))
                ch.dma(nc.gpsimd, lambda r0=r0: nc.gpsimd.dma_start(
                    out=v_scr[r0:r0 + 128, :], in_=sil[:, 1024:2048]))
        ch.seal()
    barrier(kb)

    with ExitStack() as st:
        ch = Chain(kb, st, "l1_c")
        POS, GAM = _rpe_pos_gam()
        pos_d = kb.const_dram(POS, "l1pos")
        gam_d = kb.const_dram(GAM, "l1gam")
        eps_sb = kb.const_sb(st, ch, np.full((128, 1), 1e-5, np.float32), "eps1")
        ident_f = kb.const_sb(st, ch, np.eye(128, dtype=np.float32), "idf1")
        posw_b = kb.sb(st, "poswb", [128, 32], F32)
        ch.dma(nc.sync, lambda: nc.sync.dma_start(
            out=posw_b[:, :], in_=rpw_f.broadcast_to([128, 32])))
        lnw_b = kb.sb(st, "lnwb", [128, 3 * 32], F32)
        lnb_b = kb.sb(st, "lnbb", [128, 3 * 32], F32)
        for i in range(3):
            ch.dma(nc.sync, lambda i=i: nc.sync.dma_start(
                out=lnw_b[:, i * 32:(i + 1) * 32],
                in_=rlw_f[i:i + 1, :].broadcast_to([128, 32])))
            ch.dma(nc.sync, lambda i=i: nc.sync.dma_start(
                out=lnb_b[:, i * 32:(i + 1) * 32],
                in_=rlb_f[i:i + 1, :].broadcast_to([128, 32])))
        olw_b = kb.sb(st, "olwb", [128, 32], F32)
        olb_b = kb.sb(st, "olbb", [128, 32], F32)
        ch.dma(nc.sync, lambda: nc.sync.dma_start(
            out=olw_b[:, :], in_=rolw_f.unsqueeze(0).broadcast_to([128, 32])))
        ch.dma(nc.sync, lambda: nc.sync.dma_start(
            out=olb_b[:, :], in_=rolb_f.unsqueeze(0).broadcast_to([128, 32])))
        fc_sb = kb.sb(st, "fcsb", [32, 3 * 32], BF16)
        for i in range(3):
            ch.dma(nc.gpsimd, lambda i=i: nc.gpsimd.dma_start(
                out=fc_sb[:, i * 32:(i + 1) * 32], in_=rfc_f[i]))
        outw_bf = kb.sb(st, "outwb", [32, 1024], BF16)
        ch.dma(nc.gpsimd, lambda: nc.gpsimd.dma_start(out=outw_bf[:, :], in_=row_f))
        pos_col = kb.sb(st, "posc", [128, 1], F32)
        gam_col = kb.sb(st, "gamc", [128, 1], F32)
        h_sb = kb.sb(st, "h", [128, 32], F32)
        hc_sb = kb.sb(st, "hc", [128, 32], F32)
        hsq = kb.sb(st, "hsq", [128, 32], F32)
        s1 = kb.sb(st, "s1", [128, 1], F32)
        rstd = kb.sb(st, "rstd", [128, 1], F32)
        hT = kb.sb(st, "hT", [32, 128], BF16)
        tb = kb.sb(st, "tb", [128, 1024], BF16)
        ps_t = kb.ps(st, "rpe_pt", [128, 128])
        ps_h = kb.ps(st, "rpe_ph", [128, 128])
        ps_o = kb.ps(st, "rpe_po", [128, 1024])

        def rpe_ln_silu_T(wb, bb, io):
            ch.op(nc.vector, lambda: nc.vector.reduce_sum(s1[:, :], h_sb[:, :], AX.X))
            ch.op(nc.vector, lambda: nc.vector.tensor_scalar_mul(
                s1[:, :], s1[:, :], 1.0 / 32))
            ch.op(nc.vector, lambda: nc.vector.tensor_single_scalar(
                hc_sb[:, :], h_sb[:, :], s1[:, :], ALU.subtract))
            ch.op(nc.scalar, lambda: nc.scalar.activation(
                hsq[:, :], hc_sb[:, :], AF.Square))
            ch.op(nc.vector, lambda: nc.vector.reduce_sum(
                rstd[:, :], hsq[:, :], AX.X))
            ch.op(nc.scalar, lambda: nc.scalar.activation(
                rstd[:, :], rstd[:, :], AF.Ln, bias=eps_sb[:, :], scale=1.0 / 32))
            ch.op(nc.scalar, lambda: nc.scalar.activation(
                rstd[:, :], rstd[:, :], AF.Exp, scale=-0.5))
            ch.op(nc.vector, lambda: nc.vector.tensor_single_scalar(
                hc_sb[:, :], hc_sb[:, :], rstd[:, :], ALU.mult))
            ch.op(nc.vector, lambda wb=wb: nc.vector.tensor_mul(
                hc_sb[:, :], hc_sb[:, :], wb))
            ch.op(nc.vector, lambda bb=bb: nc.vector.tensor_add(
                hc_sb[:, :], hc_sb[:, :], bb))
            ch.op(nc.scalar, lambda: nc.scalar.activation(
                hc_sb[:, :], hc_sb[:, :], AF.Silu))
            ch.op(nc.tensor, lambda: nc.tensor.transpose(
                ps_t[0:32, 0:128], hc_sb[:, 0:32], ident_f[:, :]))
            ch.op(nc.vector, lambda: nc.vector.tensor_copy(
                hT[:, :], ps_t[0:32, 0:128]))

        for t in range(64):
            ch.dma(nc.sync, lambda t=t: nc.sync.dma_start(
                out=pos_col[:, :], in_=pos_d[t:t + 1, :].rearrange("a c -> c a")))
            ch.dma(nc.sync, lambda t=t: nc.sync.dma_start(
                out=gam_col[:, :], in_=gam_d[t:t + 1, :].rearrange("a c -> c a")))
            ch.op(nc.vector, lambda: nc.vector.tensor_mul(
                h_sb[:, :], pos_col[:, :].broadcast_to([128, 32]), posw_b[:, :]))
            for i in range(3):
                rpe_ln_silu_T(lnw_b[:, i * 32:(i + 1) * 32],
                              lnb_b[:, i * 32:(i + 1) * 32], i)
                ch.op(nc.tensor, lambda i=i: nc.tensor.matmul(
                    ps_h[:, 0:32], hT[:, :], fc_sb[:, i * 32:(i + 1) * 32],
                    start=True, stop=True))
                ch.op(nc.vector, lambda: nc.vector.tensor_copy(
                    h_sb[:, :], ps_h[:, 0:32]))
            rpe_ln_silu_T(olw_b[:, :], olb_b[:, :], 3)
            for q in range(2):
                ch.op(nc.tensor, lambda q=q: nc.tensor.matmul(
                    ps_o[:, q * 512:(q + 1) * 512], hT[:, :],
                    outw_bf[:, q * 512:(q + 1) * 512],
                    start=True, stop=True), inc=(q == 1))
            ch.op(nc.vector, lambda: nc.vector.tensor_single_scalar(
                tb[:, :], ps_o[:, :], gam_col[:, :], ALU.mult))
            ch.dma(nc.gpsimd, lambda t=t: nc.gpsimd.dma_start(
                out=t_scr[t * 128:(t + 1) * 128, :], in_=tb[:, :]))
        ch.seal()
    barrier(kb)

    dft = _dft_consts()
    with ExitStack() as st:
        ch = Chain(kb, st, "l1_d")
        c1v = kb.const_sb(st, ch, dft["c1v"], "c1v")
        s1v = kb.const_sb(st, ch, dft["s1v"], "s1v")
        c1t = kb.const_sb(st, ch, dft["c1t"], "c1t")
        s1t = kb.const_sb(st, ch, dft["s1t"], "s1t")
        v1 = kb.sb(st, "v1", [64, 64 * 512], BF16)
        t1 = kb.sb(st, "t1", [128, 64 * 512], BF16)
        ev = kb.sb(st, "ev", [128, 1024], BF16)
        ps_re = kb.ps(st, "s1_re", [128, 512])
        ps_im = kb.ps(st, "s1_im", [128, 512])
        vv = v_scr.rearrange("(b a) ch -> b a ch", a=64)
        tt = t_scr.rearrange("(b a) ch -> b a ch", a=64)
        for cc in range(2):
            ch.dma(nc.sync, lambda cc=cc: nc.sync.dma_start(
                out=sb3(v1, 64, 512), in_=vv[:, :, cc * 512:(cc + 1) * 512]))
            for a in range(64):
                ch.op(nc.tensor, lambda a=a: nc.tensor.matmul(
                    ps_re[:, :], c1v[:, a * 128:(a + 1) * 128],
                    v1[:, a * 512:(a + 1) * 512], start=True, stop=True),
                    inc=False)
                ch.op(nc.tensor, lambda a=a: nc.tensor.matmul(
                    ps_im[:, :], s1v[:, a * 128:(a + 1) * 128],
                    v1[:, a * 512:(a + 1) * 512], start=True, stop=True))
                ch.op(nc.vector, lambda: nc.vector.tensor_copy(
                    ev[:, 0:512], ps_re[:, :]))
                ch.op(nc.scalar, lambda: nc.scalar.copy(
                    ev[:, 512:1024], ps_im[:, :]))
                ch.dma(nc.gpsimd, lambda a=a, cc=cc: nc.gpsimd.dma_start(
                    out=z_scr[a, 0, :, cc * 512:(cc + 1) * 512], in_=ev[:, 0:512]))
                ch.dma(nc.gpsimd, lambda a=a, cc=cc: nc.gpsimd.dma_start(
                    out=z_scr[a, 1, :, cc * 512:(cc + 1) * 512], in_=ev[:, 512:1024]))
        for cc in range(2):
            ch.dma(nc.sync, lambda cc=cc: nc.sync.dma_start(
                out=sb3(t1, 64, 512), in_=tt[:, :, cc * 512:(cc + 1) * 512]))
            for a in range(64):
                ch.op(nc.tensor, lambda a=a: nc.tensor.matmul(
                    ps_re[:, :], c1t[:, a * 128:(a + 1) * 128],
                    t1[:, a * 512:(a + 1) * 512], start=True, stop=True),
                    inc=False)
                ch.op(nc.tensor, lambda a=a: nc.tensor.matmul(
                    ps_im[:, :], s1t[:, a * 128:(a + 1) * 128],
                    t1[:, a * 512:(a + 1) * 512], start=True, stop=True))
                ch.op(nc.vector, lambda: nc.vector.tensor_copy(
                    ev[:, 0:512], ps_re[:, :]))
                ch.op(nc.scalar, lambda: nc.scalar.copy(
                    ev[:, 512:1024], ps_im[:, :]))
                ch.dma(nc.gpsimd, lambda a=a, cc=cc: nc.gpsimd.dma_start(
                    out=zt_scr[a, 0, :, cc * 512:(cc + 1) * 512], in_=ev[:, 0:512]))
                ch.dma(nc.gpsimd, lambda a=a, cc=cc: nc.gpsimd.dma_start(
                    out=zt_scr[a, 1, :, cc * 512:(cc + 1) * 512], in_=ev[:, 512:1024]))
        ch.seal()
    barrier(kb)

    with ExitStack() as st:
        ch = Chain(kb, st, "l1_e")
        ls2a = kb.const_sb(st, ch, dft["ls2a"], "ls2a")
        ls2b = kb.const_sb(st, ch, dft["ls2b"], "ls2b")
        zre = kb.sb(st, "zre", [64, 128 * 256], BF16)
        zim = kb.sb(st, "zim", [64, 128 * 256], BF16)
        ev = kb.sb(st, "ev2", [128, 512], BF16)
        ps = kb.ps(st, "s2_ps", [128, 512])
        for src, dst in ((z_scr, xv_scr), (zt_scr, xt_scr)):
            dst3 = dst.rearrange("p (c ch) -> p c ch", ch=1024)
            for cc in range(4):
                ch.dma(nc.sync, lambda src=src, cc=cc: nc.sync.dma_start(
                    out=zre.rearrange("p (c ch) -> p c ch", ch=256),
                    in_=src[:, 0, :, cc * 256:(cc + 1) * 256]))
                ch.dma(nc.sync, lambda src=src, cc=cc: nc.sync.dma_start(
                    out=zim.rearrange("p (c ch) -> p c ch", ch=256),
                    in_=src[:, 1, :, cc * 256:(cc + 1) * 256]))
                for n in range(64):
                    ch.op(nc.tensor, lambda n=n: nc.tensor.matmul(
                        ps[:, :], ls2a[:, :], zre[:, n * 512:(n + 1) * 512],
                        start=True, stop=False), inc=False)
                    ch.op(nc.tensor, lambda n=n: nc.tensor.matmul(
                        ps[:, :], ls2b[:, :], zim[:, n * 512:(n + 1) * 512],
                        start=False, stop=True))
                    ch.op(nc.vector, lambda: nc.vector.tensor_copy(ev[:, :], ps[:, :]))
                    c0 = n * 2
                    ch.dma(nc.gpsimd, lambda dst3=dst3, c0=c0, cc=cc:
                           nc.gpsimd.dma_start(
                        out=dst3[:, c0:c0 + 2, cc * 256:(cc + 1) * 256],
                        in_=sb3(ev, 2, 256)))
        ch.seal()
    barrier(kb)

    with ExitStack() as st:
        ch = Chain(kb, st, "l1_f")
        xv_sb = kb.sb(st, "xvs", [128, 4096], BF16)
        xt_sb = kb.sb(st, "xts", [128, 4096], BF16)
        pp = kb.sb(st, "pp", [128, 4096], BF16)
        tm1 = kb.sb(st, "tm1", [64, 4096], BF16)
        tm2 = kb.sb(st, "tm2", [64, 4096], BF16)
        vim = kb.sb(st, "vimlo", [64, 4096], BF16)
        tim = kb.sb(st, "timlo", [64, 4096], BF16)
        for ck in range(32):
            sl = slice(ck * 4096, (ck + 1) * 4096)
            ch.dma(nc.sync, lambda sl=sl: nc.sync.dma_start(
                out=xv_sb[:, :], in_=xv_scr[:, sl]))
            ch.dma(nc.sync, lambda sl=sl: nc.sync.dma_start(
                out=xt_sb[:, :], in_=xt_scr[:, sl]))
            ch.op(nc.vector, lambda: nc.vector.tensor_copy(
                vim[:, :], xv_sb[64:128, :]))
            ch.op(nc.vector, lambda: nc.vector.tensor_copy(
                tim[:, :], xt_sb[64:128, :]))
            ch.op(nc.vector, lambda: nc.vector.tensor_mul(
                tm1[:, :], xv_sb[0:64, :], xt_sb[0:64, :]))
            ch.op(nc.vector, lambda: nc.vector.tensor_mul(
                tm2[:, :], vim[:, :], tim[:, :]))
            ch.op(nc.vector, lambda: nc.vector.tensor_sub(
                pp[0:64, :], tm1[:, :], tm2[:, :]))
            ch.op(nc.vector, lambda: nc.vector.tensor_mul(
                tm1[:, :], xv_sb[0:64, :], tim[:, :]))
            ch.op(nc.vector, lambda: nc.vector.tensor_mul(
                tm2[:, :], vim[:, :], xt_sb[0:64, :]))
            ch.op(nc.vector, lambda: nc.vector.tensor_add(
                pp[64:128, :], tm1[:, :], tm2[:, :]))
            ch.dma(nc.gpsimd, lambda sl=sl: nc.gpsimd.dma_start(
                out=p_scr[:, sl], in_=pp[:, :]))
        ch.seal()
    barrier(kb)

    with ExitStack() as st:
        ch = Chain(kb, st, "l1_g")
        li1a = kb.const_sb(st, ch, dft["li1a"], "li1a")
        li1b = kb.const_sb(st, ch, dft["li1b"], "li1b")
        pp = kb.sb(st, "pg", [128, 4096], BF16)
        pim_lo = kb.sb(st, "pimlo", [64, 4096], BF16)
        ev = kb.sb(st, "evg", [128, 512], BF16)
        ps = kb.ps(st, "i1_ps", [128, 512])
        for ck in range(32):
            sl = slice(ck * 4096, (ck + 1) * 4096)
            ch.dma(nc.sync, lambda sl=sl: nc.sync.dma_start(
                out=pp[:, :], in_=p_scr[:, sl]))
            ch.op(nc.vector, lambda: nc.vector.tensor_copy(
                pim_lo[:, :], pp[64:128, :]))
            for n in range(8):
                ch.op(nc.tensor, lambda n=n: nc.tensor.matmul(
                    ps[:, :], li1a[:, :], pp[0:64, n * 512:(n + 1) * 512],
                    start=True, stop=False), inc=False)
                ch.op(nc.tensor, lambda n=n: nc.tensor.matmul(
                    ps[:, :], li1b[:, :], pim_lo[:, n * 512:(n + 1) * 512],
                    start=False, stop=True))
                ch.op(nc.vector, lambda: nc.vector.tensor_copy(ev[:, :], ps[:, :]))
                ch.dma(nc.gpsimd, lambda ck=ck, n=n: nc.gpsimd.dma_start(
                    out=q_scr[:, ck * 4096 + n * 512:ck * 4096 + (n + 1) * 512],
                    in_=ev[:, :]))
        ch.seal()
    barrier(kb)

    with ExitStack() as st:
        ch = Chain(kb, st, "l1_h")
        ci2 = kb.const_sb(st, ch, dft["ci2"], "ci2")
        si2 = kb.const_sb(st, ch, dft["si2"], "si2")
        qre = kb.sb(st, "qre", [128, 64 * 512], BF16)
        qim = kb.sb(st, "qim", [128, 64 * 512], BF16)
        ev = kb.sb(st, "evh", [64, 512], BF16)
        ps = kb.ps(st, "i2_ps", [128, 512])
        q4 = q_scr.rearrange("(pl a) (c ch) -> pl c a ch", pl=2, ch=1024)
        yv = y_scr.rearrange("(b a) ch -> a b ch", a=64)
        for cc in range(2):
            ch.dma(nc.sync, lambda cc=cc: nc.sync.dma_start(
                out=sb3(qre, 64, 512), in_=q4[0, :, :, cc * 512:(cc + 1) * 512]))
            ch.dma(nc.sync, lambda cc=cc: nc.sync.dma_start(
                out=sb3(qim, 64, 512), in_=q4[1, :, :, cc * 512:(cc + 1) * 512]))
            for a in range(64):
                ch.op(nc.tensor, lambda a=a: nc.tensor.matmul(
                    ps[0:64, :], ci2[:, a * 64:(a + 1) * 64],
                    qre[:, a * 512:(a + 1) * 512], start=True, stop=False),
                    inc=False)
                ch.op(nc.tensor, lambda a=a: nc.tensor.matmul(
                    ps[0:64, :], si2[:, a * 64:(a + 1) * 64],
                    qim[:, a * 512:(a + 1) * 512], start=False, stop=True))
                ch.op(nc.vector, lambda: nc.vector.tensor_copy(ev[:, :], ps[0:64, :]))
                ch.dma(nc.gpsimd, lambda a=a, cc=cc: nc.gpsimd.dma_start(
                    out=yv[a, :, cc * 512:(cc + 1) * 512], in_=ev[:, :]))
        ch.seal()
    barrier(kb)

    with ExitStack() as st:
        ch = Chain(kb, st, "l1_i")
        ub = kb.sb(st, "ub", [128, 4 * 1024], BF16)
        yb = kb.sb(st, "ybuf", [128, 4 * 1024], BF16)
        pb = kb.sb(st, "pbuf", [128, 4 * 1024], BF16)
        uu = u_scr.rearrange("(q p) ch -> p q ch", p=128)
        yy = y_scr.rearrange("(q p) ch -> p q ch", p=128)
        pr = prod_scr.rearrange("(q p) ch -> p q ch", p=128)
        for q0 in range(0, 32, 4):
            ch.dma(nc.sync, lambda q0=q0: nc.sync.dma_start(
                out=sb3(ub, 4, 1024), in_=uu[:, q0:q0 + 4, :]))
            ch.dma(nc.sync, lambda q0=q0: nc.sync.dma_start(
                out=sb3(yb, 4, 1024), in_=yy[:, q0:q0 + 4, :]))
            ch.op(nc.vector, lambda: nc.vector.tensor_mul(
                pb[:, :], ub[:, :], yb[:, :]))
            ch.dma(nc.gpsimd, lambda q0=q0: nc.gpsimd.dma_start(
                out=pr[:, q0:q0 + 4, :], in_=sb3(pb, 4, 1024)))
        ch.seal()
    barrier(kb)
    with ExitStack() as st:
        ch = Chain(kb, st, "l1_j")
        prodT = kb.sb(st, "prodT", [128, 8 * 4096], BF16)
        ow_sb = kb.sb(st, "ow", [128, 8 * 1024], BF16)
        xt = kb.sb(st, "xt1", [128, 512], F32)
        yo = kb.sb(st, "yo", [128, 512], F32)
        ps = kb.ps(st, "j_ps", [128, 512])
        for dd in range(8):
            ch.dma(nc.sync, lambda dd=dd: nc.sync.dma_start(
                out=prodT[:, dd * 4096:(dd + 1) * 4096],
                in_=prod_scr[:, dd * 128:(dd + 1) * 128], transpose=True))
        ch.dma(nc.sync, lambda: nc.sync.dma_start(
            out=sb3(ow_sb, 8, 1024),
            in_=ow_bf.rearrange("(d p) c -> p d c", p=128)))
        for m in range(8):
            for oc in range(8):
                ch.dma(nc.sync, lambda oc=oc, m=m: nc.sync.dma_start(
                    out=xt[:, :],
                    in_=x_dram[oc * 128:(oc + 1) * 128, m * 512:(m + 1) * 512]))
                for dd in range(8):
                    ch.op(nc.tensor, lambda oc=oc, dd=dd, m=m: nc.tensor.matmul(
                        ps[:, :], ow_sb[:, dd * 1024 + oc * 128:dd * 1024 + (oc + 1) * 128],
                        prodT[:, dd * 4096 + m * 512:dd * 4096 + (m + 1) * 512],
                        start=(dd == 0), stop=(dd == 7)), inc=(dd == 7))
                ch.op(nc.vector, lambda: nc.vector.tensor_add(
                    yo[:, :], ps[:, :], xt[:, :]))
                ch.dma(nc.gpsimd, lambda oc=oc, m=m: nc.gpsimd.dma_start(
                    out=y_dram[oc * 128:(oc + 1) * 128, m * 512:(m + 1) * 512],
                    in_=yo[:, :]))
        ch.seal()
    barrier(kb)


def emit_quantize_w(kb, ch, st, w_f, wq_bf, rows, cols, name, eps=1e-5):
    nc = kb.nc
    nch = rows // 128
    per = max(1, 4096 // cols)
    wld = kb.sb(st, f"q_w{name}", [128, per * cols], F32)
    junk = kb.sb(st, f"q_j{name}", [128, per * cols], F32)
    acc = kb.sb(st, f"q_a{name}", [128, 1], F32)
    accs = kb.sb(st, f"q_as{name}", [128, 1], F32)
    gam = kb.sb(st, f"q_g{name}", [1, 1], F32)
    rbc = kb.sb(st, f"q_rb{name}", [128, 1], F32)
    tmpf = kb.sb(st, f"q_tf{name}", [128, per * cols], F32)
    tmpi = kb.sb(st, f"q_ti{name}", [128, per * cols], I32)
    wqb = kb.sb(st, f"q_wq{name}", [128, per * cols], BF16)
    ones_f = kb.const_sb(st, ch, np.ones((128, 1), np.float32), f"q_o{name}")
    r_scr = kb.dscr(f"q_rs{name}", [1, 1], F32)
    w3 = w_f.rearrange("(n p) c -> n p c", p=128)
    wq3 = wq_bf.rearrange("(n p) c -> n p c", p=128)
    with ExitStack() as ls:
        ps = kb.ps(ls, f"q_ps{name}", [128, 512])
        ch.op(nc.vector, lambda: nc.vector.memset(accs[:, :], 0.0))
        for it in range(nch // per):
            ch.dma(nc.sync, lambda it=it: nc.sync.dma_start(
                out=sb3(wld, per, cols), in_=w3[it * per:(it + 1) * per]))
            ch.op(nc.scalar, lambda: nc.scalar.activation(
                junk[:, :], wld[:, :], AF.Abs, accum_out=acc[:, :]))
            ch.op(nc.vector, lambda: nc.vector.tensor_add(
                accs[:, :], accs[:, :], acc[:, :]))
        ch.op(nc.tensor, lambda: nc.tensor.matmul(
            ps[0:1, 0:1], ones_f[:, :], accs[:, :], start=True, stop=True))
        ch.op(nc.vector, lambda: nc.vector.tensor_scalar(
            gam[:, :], ps[0:1, 0:1], 1.0 / (rows * cols), eps, ALU.mult, ALU.add))
        ch.op(nc.vector, lambda: nc.vector.reciprocal(gam[:, :], gam[:, :]))
        ch.dma(nc.gpsimd, lambda: nc.gpsimd.dma_start(out=r_scr, in_=gam[:, :]))
        ch.dma(nc.sync, lambda: nc.sync.dma_start(
            out=rbc[:, :], in_=r_scr.broadcast_to([128, 1])))
        for it in range(nch // per):
            ch.dma(nc.sync, lambda it=it: nc.sync.dma_start(
                out=sb3(wld, per, cols), in_=w3[it * per:(it + 1) * per]))
            ch.op(nc.vector, lambda: nc.vector.tensor_single_scalar(
                tmpf[:, :], wld[:, :], rbc[:, :], ALU.mult))
            ch.op(nc.vector, lambda: nc.vector.tensor_copy(tmpi[:, :], tmpf[:, :]))
            ch.op(nc.vector, lambda: nc.vector.tensor_scalar(
                wqb[:, :], tmpi[:, :], -1.0, 1.0, ALU.max, ALU.min))
            ch.dma(nc.gpsimd, lambda it=it: nc.gpsimd.dma_start(
                out=wq3[it * per:(it + 1) * per], in_=sb3(wqb, per, cols)))


def emit_l1_ffn(kb, x_dram, w1_f, b1t_f, w2_f, b2t_f, y_dram):
    nc = kb.nc
    ln_scr = kb.dscr("l1f_ln", [1024, 4096], BF16)
    wq1 = kb.dscr("l1f_wq1", [4096, 1024], BF16)
    wq2 = kb.dscr("l1f_wq2", [1024, 4096], BF16)
    with ExitStack() as st:
        ch = Chain(kb, st, "l1f_q")
        emit_quantize_w(kb, ch, st, w1_f, wq1, 4096, 1024, "w1")
        emit_quantize_w(kb, ch, st, w2_f, wq2, 1024, 4096, "w2")
        emit_ln(kb, ch, st, x_dram, ln_scr)
        ch.seal()
    barrier(kb)
    with ExitStack() as st:
        ch = Chain(kb, st, "l1f_m")
        w1T = kb.sb(st, "w1T", [128, 8 * 4096], BF16)
        w2T = kb.sb(st, "w2T", [128, 32 * 1024], BF16)
        lnm = kb.sb(st, "lnm1f", [128, 8 * 512], BF16)
        hblk = kb.sb(st, "hblk1f", [128, 32 * 512], BF16)
        b1t = kb.sb(st, "b1t", [128, 32], F32)
        b2t = kb.sb(st, "b2t", [128, 8], F32)
        xt = kb.sb(st, "xt1f", [128, 512], F32)
        yb = kb.sb(st, "yb1f", [128, 512], F32)
        ps1 = kb.ps(st, "1f_ps1", [128, 512])
        ps2 = kb.ps(st, "1f_ps2", [128, 512])
        ch.dma(nc.sync, lambda: nc.sync.dma_start(out=b1t[:, :], in_=b1t_f))
        ch.dma(nc.sync, lambda: nc.sync.dma_start(out=b2t[:, :], in_=b2t_f))
        for dd in range(8):
            ch.dma(nc.sync, lambda dd=dd: nc.sync.dma_start(
                out=w1T[:, dd * 4096:(dd + 1) * 4096],
                in_=wq1[:, dd * 128:(dd + 1) * 128], transpose=True))
        for fc in range(32):
            ch.dma(nc.sync, lambda fc=fc: nc.sync.dma_start(
                out=w2T[:, fc * 1024:(fc + 1) * 1024],
                in_=wq2[:, fc * 128:(fc + 1) * 128], transpose=True))
        for m in range(8):
            ch.dma(nc.sync, lambda m=m: nc.sync.dma_start(
                out=sb3(lnm, 8, 512), in_=fm(ln_scr, m * 512, (m + 1) * 512)))
            for fc in range(32):
                for dd in range(8):
                    ch.op(nc.tensor, lambda fc=fc, dd=dd: nc.tensor.matmul(
                        ps1[:, :],
                        w1T[:, dd * 4096 + fc * 128:dd * 4096 + (fc + 1) * 128],
                        lnm[:, dd * 512:(dd + 1) * 512],
                        start=(dd == 0), stop=(dd == 7)), inc=(dd == 7))
                ch.op(nc.scalar, lambda fc=fc: nc.scalar.activation(
                    hblk[:, fc * 512:(fc + 1) * 512], ps1[:, :], AF.Relu,
                    bias=b1t[:, fc:fc + 1]))
            for oc in range(8):
                ch.dma(nc.sync, lambda oc=oc, m=m: nc.sync.dma_start(
                    out=xt[:, :],
                    in_=x_dram[oc * 128:(oc + 1) * 128, m * 512:(m + 1) * 512]))
                for fc in range(32):
                    ch.op(nc.tensor, lambda oc=oc, fc=fc: nc.tensor.matmul(
                        ps2[:, :],
                        w2T[:, fc * 1024 + oc * 128:fc * 1024 + (oc + 1) * 128],
                        hblk[:, fc * 512:(fc + 1) * 512],
                        start=(fc == 0), stop=(fc == 31)), inc=(fc == 31))
                ch.op(nc.vector, lambda oc=oc: nc.vector.scalar_tensor_tensor(
                    yb[:, :], ps2[:, :], b2t[:, oc:oc + 1], xt[:, :],
                    ALU.add, ALU.add))
                ch.dma(nc.gpsimd, lambda oc=oc, m=m: nc.gpsimd.dma_start(
                    out=y_dram[oc * 128:(oc + 1) * 128, m * 512:(m + 1) * 512],
                    in_=yb[:, :]))
        ch.seal()
    barrier(kb)


def emit_l2_attn(kb, x_dram, w_f, bt_f, brow_f, y_dram):
    nc = kb.nc
    w_bf = kb.dscr("l2_w", [1024, 1024], BF16)
    ln_scr = kb.dscr("l2_ln", [1024, 4096], BF16)
    f_scr = kb.dscr("l2_f", [1024, 4096], BF16)
    ft_scr = kb.dscr("l2_ft", [4096, 1024], BF16)
    avn_scr = kb.dscr("l2_avn", [1024, 4096], BF16)
    with ExitStack() as st:
        ch = Chain(kb, st, "l2_a")
        emit_cast(kb, ch, w_f, w_bf)
        emit_ln(kb, ch, st, x_dram, ln_scr)
        ch.seal()
    barrier(kb)
    with ExitStack() as st:
        ch = Chain(kb, st, "l2_f")
        ws = kb.sb(st, "l2ws", [128, 8 * 1024], BF16)
        lnm = kb.sb(st, "l2lnm", [128, 8 * 512], BF16)
        bt = kb.sb(st, "l2bt", [128, 8], F32)
        bb = kb.sb(st, "l2bb", [128, 1024], F32)
        ob = kb.sb(st, "l2ob", [128, 1024], BF16)
        ps0 = kb.ps(st, "l2_ps0", [128, 512])
        ps1 = kb.ps(st, "l2_ps1", [128, 1024])
        ch.dma(nc.sync, lambda: nc.sync.dma_start(
            out=sb3(ws, 8, 1024), in_=w_bf.rearrange("(d p) c -> p d c", p=128)))
        ch.dma(nc.sync, lambda: nc.sync.dma_start(out=bt[:, :], in_=bt_f))
        ch.dma(nc.sync, lambda: nc.sync.dma_start(
            out=bb[:, :], in_=brow_f.broadcast_to([128, 1024])))
        for m in range(8):
            ch.dma(nc.sync, lambda m=m: nc.sync.dma_start(
                out=sb3(lnm, 8, 512), in_=fm(ln_scr, m * 512, (m + 1) * 512)))
            for oc in range(8):
                for dd in range(8):
                    ch.op(nc.tensor, lambda oc=oc, dd=dd: nc.tensor.matmul(
                        ps0[:, :], ws[:, dd * 1024 + oc * 128:dd * 1024 + (oc + 1) * 128],
                        lnm[:, dd * 512:(dd + 1) * 512],
                        start=(dd == 0), stop=(dd == 7)), inc=(dd == 7))
                ch.op(nc.scalar, lambda oc=oc: nc.scalar.activation(
                    ob[:, 0:512], ps0[:, :], AF.Identity, bias=bt[:, oc:oc + 1]))
                ch.dma(nc.gpsimd, lambda oc=oc, m=m: nc.gpsimd.dma_start(
                    out=f_scr[oc * 128:(oc + 1) * 128, m * 512:(m + 1) * 512],
                    in_=ob[:, 0:512]))
            for mt in range(4):
                for dd in range(8):
                    for half in range(2):
                        ch.op(nc.tensor, lambda mt=mt, dd=dd, half=half: nc.tensor.matmul(
                            ps1[:, half * 512:(half + 1) * 512],
                            lnm[:, dd * 512 + mt * 128:dd * 512 + (mt + 1) * 128],
                            ws[:, dd * 1024 + half * 512:dd * 1024 + (half + 1) * 512],
                            start=(dd == 0), stop=(dd == 7)),
                            inc=(dd == 7 and half == 1))
                ch.op(nc.vector, lambda: nc.vector.tensor_add(
                    ob[:, 0:1024], ps1[:, :], bb[:, :]))
                r0 = m * 512 + mt * 128
                ch.dma(nc.gpsimd, lambda r0=r0: nc.gpsimd.dma_start(
                    out=ft_scr[r0:r0 + 128, :], in_=ob[:, 0:1024]))
        ch.seal()
    barrier(kb)
    with ExitStack() as st:
        ch = Chain(kb, st, "l2_att")
        ident_bf = kb.const_sb(st, ch, np.eye(128, dtype=np.float32).astype(BF), "idb2")
        ident_f = kb.const_sb(st, ch, np.eye(128, dtype=np.float32), "idf2")
        ones128f = kb.const_sb(st, ch, np.ones((1, 128), np.float32), "o128f")
        p_ = np.arange(128)[:, None]
        c_ = np.arange(148)[None, :]
        band = (0.01 * (np.minimum(np.abs(p_ - c_ + 10), 10) - 10)).astype(np.float32)
        band_sb = kb.const_sb(st, ch, band, "band")
        fT = kb.sb(st, "l2fT", [128, 8 * 4096], BF16)
        ftk = kb.sb(st, "l2ftk", [128, 32 * 1024], BF16)
        p_sb = kb.sb(st, "l2p", [128, 1536], BF16)
        pt_sb = kb.sb(st, "l2pt", [128, 1536], BF16)
        mneg = kb.sb(st, "l2mn", [128, 3], F32)
        lg = kb.sb(st, "l2lg", [128, 3], F32)
        sm1 = kb.sb(st, "l2sm1", [128, 1], F32)
        sm2 = kb.sb(st, "l2sm2", [128, 3], F32)
        arow = kb.sb(st, "l2ar", [1, 384], F32)
        avg_sb = kb.sb(st, "l2avg", [128, 3 * 1024], F32)
        acc1 = kb.sb(st, "l2acc", [128, 1024], F32)
        tmu = kb.sb(st, "l2tmu", [128, 1024], F32)
        alb = kb.sb(st, "l2alb", [128, 128], F32)
        avnb = kb.sb(st, "l2avnb", [128, 8 * 128], BF16)
        lrow = kb.sb(st, "l2lrow", [1, 128], F32)
        ps_s = kb.ps(st, "l2_s", [128, 1536])
        pst = kb.ps(st, "l2_t", [128, 1024], BF16)
        ps_av = kb.ps(st, "l2_av", [128, 1024])
        ps_m = kb.ps(st, "l2_m", [128, 512])
        for dd in range(8):
            ch.dma(nc.sync, lambda dd=dd: nc.sync.dma_start(
                out=fT[:, dd * 4096:(dd + 1) * 4096],
                in_=f_scr[dd * 128:(dd + 1) * 128, :]))
        ch.dma(nc.sync, lambda: nc.sync.dma_start(
            out=sb3(ftk, 32, 1024),
            in_=ft_scr.rearrange("(t p) c -> p t c", p=128)))
        GROUPS = [(0, 3), (3, 6), (6, 8)]
        for i in range(32):
            i0 = i * 128
            jlo, jhi = max(0, i0 - 10), min(4096, i0 + 138)
            for g, (c0, c1) in enumerate(GROUPS):
                gw = (c1 - c0) * 512
                glo, ghi = c0 * 512, c1 * 512
                for jc in range(c0, c1):
                    for dd in range(8):
                        ch.op(nc.tensor, lambda i=i, jc=jc, c0=c0, dd=dd:
                              nc.tensor.matmul(
                            ps_s[:, (jc - c0) * 512:(jc - c0 + 1) * 512],
                            fT[:, dd * 4096 + i * 128:dd * 4096 + (i + 1) * 128],
                            fT[:, dd * 4096 + jc * 512:dd * 4096 + (jc + 1) * 512],
                            start=(dd == 0), stop=(dd == 7)),
                            inc=(dd == 7 and jc == c1 - 1))
                blo, bhi = max(jlo, glo), min(jhi, ghi)
                if blo < bhi:
                    coff = blo - (i0 - 10)
                    ch.op(nc.vector, lambda blo=blo, bhi=bhi, coff=coff, glo=glo:
                          nc.vector.tensor_add(
                        ps_s[:, blo - glo:bhi - glo],
                        ps_s[:, blo - glo:bhi - glo],
                        band_sb[:, coff:coff + (bhi - blo)]))
                ch.op(nc.vector, lambda g=g, gw=gw: nc.vector.reduce_max(
                    mneg[:, g:g + 1], ps_s[:, 0:gw], AX.X, negate=True))
                ch.op(nc.scalar, lambda g=g, gw=gw: nc.scalar.activation(
                    p_sb[:, 0:gw], ps_s[:, 0:gw], AF.Exp,
                    bias=mneg[:, g:g + 1], accum_out=lg[:, g:g + 1]))
                njc = gw // 128
                for b0 in range(0, njc, 4):
                    bn = min(4, njc - b0)
                    for sj in range(bn):
                        ch.op(nc.tensor, lambda b0=b0, sj=sj: nc.tensor.transpose(
                            pst[:, sj * 128:(sj + 1) * 128],
                            p_sb[:, (b0 + sj) * 128:(b0 + sj + 1) * 128],
                            ident_bf[:, :]), inc=(sj == bn - 1))
                    ch.op(nc.vector, lambda b0=b0, bn=bn: nc.vector.tensor_copy(
                        pt_sb[:, b0 * 128:(b0 + bn) * 128], pst[:, 0:bn * 128]))
                for oc in range(8):
                    for jtl in range(njc):
                        jt = c0 * 4 + jtl
                        ch.op(nc.tensor, lambda jt=jt, jtl=jtl, oc=oc, njc=njc:
                              nc.tensor.matmul(
                            ps_av[:, oc * 128:(oc + 1) * 128],
                            ftk[:, jt * 1024 + oc * 128:jt * 1024 + (oc + 1) * 128],
                            pt_sb[:, jtl * 128:(jtl + 1) * 128],
                            start=(jtl == 0), stop=(jtl == njc - 1)),
                            inc=(jtl == njc - 1))
                ch.op(nc.vector, lambda g=g: nc.vector.tensor_copy(
                    avg_sb[:, g * 1024:(g + 1) * 1024], ps_av[:, :]))
            ch.op(nc.vector, lambda: nc.vector.tensor_tensor(
                sm1[:, :], mneg[:, 0:1], mneg[:, 1:2], ALU.min))
            ch.op(nc.vector, lambda: nc.vector.tensor_tensor(
                sm1[:, :], sm1[:, :], mneg[:, 2:3], ALU.min))
            ch.op(nc.vector, lambda: nc.vector.tensor_sub(
                sm2[:, 0:3], sm1[:, :].broadcast_to([128, 3]), mneg[:, 0:3]))
            ch.op(nc.scalar, lambda: nc.scalar.activation(
                sm2[:, 0:3], sm2[:, 0:3], AF.Exp))
            ch.op(nc.vector, lambda: nc.vector.tensor_mul(
                lg[:, 0:3], lg[:, 0:3], sm2[:, 0:3]))
            ch.op(nc.vector, lambda: nc.vector.reduce_sum(
                sm1[:, :], lg[:, 0:3], AX.X))
            ch.op(nc.vector, lambda: nc.vector.reciprocal(sm1[:, :], sm1[:, :]))
            for g in range(3):
                ch.op(nc.tensor, lambda g=g: nc.tensor.transpose(
                    ps_m[0:1, g * 128:(g + 1) * 128], sm2[:, g:g + 1],
                    ident_f[:, :]), inc=(g == 2))
            ch.op(nc.tensor, lambda: nc.tensor.transpose(
                ps_m[0:1, 384:512], sm1[:, :], ident_f[:, :]))
            ch.op(nc.vector, lambda: nc.vector.tensor_copy(
                arow[0:1, 0:384], ps_m[0:1, 0:384]))
            ch.op(nc.vector, lambda: nc.vector.tensor_copy(
                lrow[0:1, :], ps_m[0:1, 384:512]))
            for g in range(3):
                ch.op(nc.tensor, lambda g=g: nc.tensor.matmul(
                    ps_m[:, 0:128], ones128f[:, :],
                    arow[0:1, g * 128:(g + 1) * 128], start=True, stop=True))
                ch.op(nc.vector, lambda: nc.vector.tensor_copy(
                    alb[:, :], ps_m[:, 0:128]))
                if g == 0:
                    ch.op(nc.vector, lambda g=g: nc.vector.tensor_mul(
                        acc1[:, :].rearrange("p (o c) -> p o c", o=8),
                        avg_sb[:, g * 1024:(g + 1) * 1024].rearrange(
                            "p (o c) -> p o c", o=8),
                        alb[:, :].unsqueeze(1).broadcast_to([128, 8, 128])))
                else:
                    ch.op(nc.vector, lambda g=g: nc.vector.tensor_mul(
                        tmu[:, :].rearrange("p (o c) -> p o c", o=8),
                        avg_sb[:, g * 1024:(g + 1) * 1024].rearrange(
                            "p (o c) -> p o c", o=8),
                        alb[:, :].unsqueeze(1).broadcast_to([128, 8, 128])))
                    ch.op(nc.vector, lambda: nc.vector.tensor_add(
                        acc1[:, :], acc1[:, :], tmu[:, :]))
            ch.op(nc.tensor, lambda: nc.tensor.matmul(
                ps_m[:, 0:128], ones128f[:, :], lrow[0:1, :],
                start=True, stop=True))
            ch.op(nc.vector, lambda: nc.vector.tensor_copy(
                alb[:, :], ps_m[:, 0:128]))
            ch.op(nc.vector, lambda: nc.vector.tensor_mul(
                avnb[:, :].rearrange("p (o c) -> p o c", o=8),
                acc1[:, :].rearrange("p (o c) -> p o c", o=8),
                alb[:, :].unsqueeze(1).broadcast_to([128, 8, 128])))
            ch.dma(nc.gpsimd, lambda i=i: nc.gpsimd.dma_start(
                out=avn_scr[:, i * 128:(i + 1) * 128].rearrange(
                    "(o p) c -> p o c", p=128), in_=sb3(avnb, 8, 128)))
        ch.seal()
    barrier(kb)
    with ExitStack() as st:
        ch = Chain(kb, st, "l2_res")
        am = kb.sb(st, "l2am", [128, 8 * 512], BF16)
        xm = kb.sb(st, "l2xm", [128, 8 * 512], F32)
        ym = kb.sb(st, "l2ym", [128, 8 * 512], F32)
        for m in range(8):
            ch.dma(nc.sync, lambda m=m: nc.sync.dma_start(
                out=sb3(am, 8, 512), in_=fm(avn_scr, m * 512, (m + 1) * 512)))
            ch.dma(nc.sync, lambda m=m: nc.sync.dma_start(
                out=sb3(xm, 8, 512), in_=fm(x_dram, m * 512, (m + 1) * 512)))
            ch.op(nc.vector, lambda: nc.vector.tensor_add(
                ym[:, :], xm[:, :], am[:, :]))
            ch.dma(nc.gpsimd, lambda m=m: nc.gpsimd.dma_start(
                out=fm(y_dram, m * 512, (m + 1) * 512), in_=sb3(ym, 8, 512)))
        ch.seal()
    barrier(kb)


def emit_l2_ffn(kb, x_dram, gw_f, egt_f, eut_f, edt_f, sgt_f, sut_f, sdt_f,
                y_dram):
    nc = kb.nc
    ln_scr = kb.dscr("l2f_ln", [1024, 4096], BF16)
    cw_scr = kb.dscr("l2f_cw", [8, 4096], F32)
    eg_bf = kb.dscr("l2f_eg", [8, 1024, 2816], BF16)
    eu_bf = kb.dscr("l2f_eu", [8, 1024, 2816], BF16)
    ed_bf = kb.dscr("l2f_ed", [8, 2816, 1024], BF16)
    sg_bf = kb.dscr("l2f_sg", [1024, 2816], BF16)
    su_bf = kb.dscr("l2f_su", [1024, 2816], BF16)
    sd_bf = kb.dscr("l2f_sd", [2816, 1024], BF16)
    with ExitStack() as st:
        ch = Chain(kb, st, "l2f_pre")
        for e in range(8):
            ch.dma(nc.gpsimd, lambda e=e: nc.gpsimd.dma_start(out=eg_bf[e], in_=egt_f[e]))
            ch.dma(nc.gpsimd, lambda e=e: nc.gpsimd.dma_start(out=eu_bf[e], in_=eut_f[e]))
            ch.dma(nc.gpsimd, lambda e=e: nc.gpsimd.dma_start(out=ed_bf[e], in_=edt_f[e]))
        ch.dma(nc.gpsimd, lambda: nc.gpsimd.dma_start(out=sg_bf, in_=sgt_f))
        ch.dma(nc.gpsimd, lambda: nc.gpsimd.dma_start(out=su_bf, in_=sut_f))
        ch.dma(nc.gpsimd, lambda: nc.gpsimd.dma_start(out=sd_bf, in_=sdt_f))
        ch.dma(nc.gpsimd, lambda: nc.gpsimd.dma_start(out=y_dram, in_=x_dram))
        emit_ln(kb, ch, st, x_dram, ln_scr)
        ch.seal()
    barrier(kb)
    with ExitStack() as st:
        ch = Chain(kb, st, "l2f_gate")
        emit_gating_topk(kb, ch, st, ln_scr, gw_f, cw_scr, 8, topk_norm_eps=1e-20)
        ch.seal()
    barrier(kb)
    with ExitStack() as st:
        df = DF(kb, st, "l2f_ex")
        gwT = kb.sb(st, "gwT", [128, 8 * 2816], BF16)
        uwT = kb.sb(st, "uwT", [128, 8 * 2816], BF16)
        dwT = kb.sb(st, "dwT", [128, 22 * 1024], BF16)
        lnm = [kb.sb(st, f"lnm2f{i}", [128, 8 * 512], BF16) for i in range(2)]
        hblk = kb.sb(st, "hblk2f", [128, 22 * 512], BF16)
        sgb = [kb.sb(st, f"sgb{i}", [128, 512], BF16) for i in range(2)]
        gub = [kb.sb(st, f"gub{i}", [128, 512], BF16) for i in range(2)]
        cwbe = kb.sb(st, "cwbe2", [128, 4096], BF16)
        tmpo = [kb.sb(st, f"tmpo2{i}", [128, 512], F32) for i in range(2)]
        ps_g = [kb.ps(st, f"2f_psg{i}", [128, 512]) for i in range(2)]
        ps_u = [kb.ps(st, f"2f_psu{i}", [128, 512]) for i in range(2)]
        ps2 = [kb.ps(st, f"2f_ps2{i}", [128, 512]) for i in range(2)]
        for e in range(9):
            shared = (e == 8)
            g_src = sg_bf if shared else eg_bf[e]
            u_src = su_bf if shared else eu_bf[e]
            d_src = sd_bf if shared else ed_bf[e]
            df.dma("sync", lambda g_src=g_src: nc.sync.dma_start(
                out=sb3(gwT, 8, 2816),
                in_=g_src.rearrange("(d p) c -> p d c", p=128)), writes=[gwT])
            df.dma("sync", lambda u_src=u_src: nc.sync.dma_start(
                out=sb3(uwT, 8, 2816),
                in_=u_src.rearrange("(d p) c -> p d c", p=128)), writes=[uwT])
            df.dma("sync", lambda d_src=d_src: nc.sync.dma_start(
                out=sb3(dwT, 22, 1024),
                in_=d_src.rearrange("(h p) c -> p h c", p=128)), writes=[dwT])
            if not shared:
                df.dma("gpsimd", lambda e=e: nc.gpsimd.dma_start(
                    out=cwbe[:, :],
                    in_=cw_scr[e:e + 1, :].broadcast_to([128, 4096])),
                    writes=[cwbe])
            for m in range(8):
                lb = lnm[m % 2]
                df.dma("sync", lambda m=m, lb=lb: nc.sync.dma_start(
                    out=sb3(lb, 8, 512), in_=fm(ln_scr, m * 512, (m + 1) * 512)),
                    writes=[lb])
                for hc in range(22):
                    pg, pu = ps_g[hc % 2], ps_u[hc % 2]
                    for dd in range(8):
                        df.op("tensor", lambda hc=hc, dd=dd, pg=pg, lb=lb:
                              nc.tensor.matmul(
                            pg[:, :],
                            gwT[:, dd * 2816 + hc * 128:dd * 2816 + (hc + 1) * 128],
                            lb[:, dd * 512:(dd + 1) * 512],
                            start=(dd == 0), stop=(dd == 7)),
                            reads=[gwT, lb], writes=[pg], inc=False)
                        df.op("tensor", lambda hc=hc, dd=dd, pu=pu, lb=lb:
                              nc.tensor.matmul(
                            pu[:, :],
                            uwT[:, dd * 2816 + hc * 128:dd * 2816 + (hc + 1) * 128],
                            lb[:, dd * 512:(dd + 1) * 512],
                            start=(dd == 0), stop=(dd == 7)),
                            reads=[uwT, lb], writes=[pu], inc=(dd == 7))
                    sg_ = sgb[hc % 2]
                    gu_ = gub[hc % 2]
                    df.op("scalar", lambda pg=pg, sg_=sg_: nc.scalar.activation(
                        sg_[:, :], pg[:, :], AF.Silu), reads=[pg], writes=[sg_])
                    df.op("vector", lambda pu=pu, sg_=sg_, gu_=gu_:
                          nc.vector.tensor_mul(
                        gu_[:, :], sg_[:, :], pu[:, :]),
                        reads=[sg_, pu], writes=[gu_])
                    if shared:
                        df.op("vector", lambda hc=hc, gu_=gu_: nc.vector.tensor_copy(
                            hblk[:, hc * 512:(hc + 1) * 512], gu_[:, :]),
                            reads=[gu_], writes=[(hblk, hc)])
                    else:
                        df.op("vector", lambda hc=hc, m=m, gu_=gu_:
                              nc.vector.tensor_mul(
                            hblk[:, hc * 512:(hc + 1) * 512], gu_[:, :],
                            cwbe[:, m * 512:(m + 1) * 512]),
                            reads=[gu_, cwbe], writes=[(hblk, hc)])
                for oc in range(8):
                    qb = ps2[oc % 2]
                    for hc in range(22):
                        df.op("tensor", lambda oc=oc, hc=hc, qb=qb:
                              nc.tensor.matmul(
                            qb[:, :],
                            dwT[:, hc * 1024 + oc * 128:hc * 1024 + (oc + 1) * 128],
                            hblk[:, hc * 512:(hc + 1) * 512],
                            start=(hc == 0), stop=(hc == 21)),
                            reads=[dwT, (hblk, hc)], writes=[qb], inc=(hc == 21))
                    tb = tmpo[oc % 2]
                    df.op("vector", lambda qb=qb, tb=tb: nc.vector.tensor_copy(
                        tb[:, :], qb[:, :]), reads=[qb], writes=[tb])
                    df.dma("gpsimd", lambda oc=oc, m=m, tb=tb: nc.gpsimd.dma_start(
                        out=y_dram[oc * 128:(oc + 1) * 128, m * 512:(m + 1) * 512],
                        in_=tb[:, :], accum_op=ALU.add),
                        reads=[tb, ("y", oc, m)], writes=[("y", oc, m)])
        df.seal()
    barrier(kb)


def emit_l3_attn(kb, x_dram, ipw_f, ipbt_f, ipbv_f, opw_f, opbt_f, y_dram):
    nc = kb.nc
    ipw_bf = kb.dscr("l3_ipw", [1024, 3072], BF16)
    opw_bf = kb.dscr("l3_opw", [1024, 1024], BF16)
    ln_scr = kb.dscr("l3_ln", [1024, 4096], BF16)
    q_scr = kb.dscr("l3_q", [1024, 4096], BF16)
    k_scr = kb.dscr("l3_k", [1024, 4096], BF16)
    v_scr = kb.dscr("l3_v", [4096, 1024], BF16)
    avn_scr = kb.dscr("l3_avn", [1024, 4096], BF16)
    with ExitStack() as st:
        ch = Chain(kb, st, "l3a_pre")
        emit_cast(kb, ch, ipw_f, ipw_bf)
        emit_cast(kb, ch, opw_f, opw_bf)
        emit_ln(kb, ch, st, x_dram, ln_scr)
        ch.seal()
    barrier(kb)
    with ExitStack() as st:
        ch = Chain(kb, st, "l3a_qkv")
        wT = kb.sb(st, "l3wT", [128, 8 * 3072], BF16)
        lnm = kb.sb(st, "l3lnm", [128, 8 * 512], BF16)
        bt = kb.sb(st, "l3bt", [128, 24], F32)
        btq = kb.sb(st, "l3btq", [128, 8], F32)
        bvb = kb.sb(st, "l3bvb", [128, 1024], F32)
        ob = kb.sb(st, "l3ob", [128, 1024], BF16)
        ps0 = kb.ps(st, "l3_ps0", [128, 512])
        ps_v2 = kb.ps(st, "l3_psv", [128, 1024])
        ch.dma(nc.sync, lambda: nc.sync.dma_start(
            out=sb3(wT, 8, 3072),
            in_=ipw_bf.rearrange("(d p) c -> p d c", p=128)))
        ch.dma(nc.sync, lambda: nc.sync.dma_start(out=bt[:, :], in_=ipbt_f))
        ch.dma(nc.sync, lambda: nc.sync.dma_start(
            out=bvb[:, :], in_=ipbv_f.broadcast_to([128, 1024])))
        ch.op(nc.vector, lambda: nc.vector.tensor_scalar_mul(
            btq[:, :], bt[:, 0:8], 0.125))
        for m in range(8):
            ch.dma(nc.sync, lambda m=m: nc.sync.dma_start(
                out=sb3(lnm, 8, 512), in_=fm(ln_scr, m * 512, (m + 1) * 512)))
            for part, dst in ((0, q_scr), (1, k_scr)):
                for oc in range(8):
                    ocg = part * 8 + oc
                    for dd in range(8):
                        ch.op(nc.tensor, lambda ocg=ocg, dd=dd: nc.tensor.matmul(
                            ps0[:, :],
                            wT[:, dd * 3072 + ocg * 128:dd * 3072 + (ocg + 1) * 128],
                            lnm[:, dd * 512:(dd + 1) * 512],
                            start=(dd == 0), stop=(dd == 7)), inc=(dd == 7))
                    if part == 0:
                        ch.op(nc.scalar, lambda oc=oc: nc.scalar.activation(
                            ob[:, 0:512], ps0[:, :], AF.Identity,
                            bias=btq[:, oc:oc + 1], scale=0.125))
                    else:
                        ch.op(nc.scalar, lambda oc=oc: nc.scalar.activation(
                            ob[:, 0:512], ps0[:, :], AF.Identity,
                            bias=bt[:, 8 + oc:9 + oc]))
                    ch.dma(nc.gpsimd, lambda dst=dst, oc=oc, m=m: nc.gpsimd.dma_start(
                        out=dst[oc * 128:(oc + 1) * 128, m * 512:(m + 1) * 512],
                        in_=ob[:, 0:512]))
            for mt in range(4):
                for dd in range(8):
                    for half in range(2):
                        ch.op(nc.tensor, lambda mt=mt, dd=dd, half=half: nc.tensor.matmul(
                            ps_v2[:, half * 512:(half + 1) * 512],
                            lnm[:, dd * 512 + mt * 128:dd * 512 + (mt + 1) * 128],
                            wT[:, dd * 3072 + 2048 + half * 512:
                               dd * 3072 + 2048 + (half + 1) * 512],
                            start=(dd == 0), stop=(dd == 7)),
                            inc=(dd == 7 and half == 1))
                ch.op(nc.vector, lambda: nc.vector.tensor_add(
                    ob[:, 0:1024], ps_v2[:, :], bvb[:, :]))
                r0 = m * 512 + mt * 128
                ch.dma(nc.gpsimd, lambda r0=r0: nc.gpsimd.dma_start(
                    out=v_scr[r0:r0 + 128, :], in_=ob[:, 0:1024]))
        ch.seal()
    barrier(kb)
    with ExitStack() as st:
        ch = Chain(kb, st, "l3a_att")
        ident_bf = kb.const_sb(st, ch, np.eye(128, dtype=np.float32).astype(BF), "idb3")
        ident_f = kb.const_sb(st, ch, np.eye(128, dtype=np.float32), "idf3")
        ones64f = kb.const_sb(st, ch, np.ones((1, 64), np.float32), "o64f3")
        dm = np.where(np.eye(128) > 0, 0.0, NEG).astype(np.float32)
        dmask = kb.const_sb(st, ch, np.tile(dm, (1, 8)), "dmask3")
        qh = kb.sb(st, "l3qh", [64, 4096], BF16)
        kh = kb.sb(st, "l3kh", [64, 4096], BF16)
        vh = kb.sb(st, "l3vh", [128, 32 * 64], BF16)
        p_sb = kb.sb(st, "l3p", [128, 1024], BF16)
        pt_sb = kb.sb(st, "l3pt", [128, 1024], BF16)
        avh = kb.sb(st, "l3avh", [64, 4096], F32)
        avn = kb.sb(st, "l3avn", [64, 4096], BF16)
        lrec = kb.sb(st, "l3lrec", [128, 32], F32)
        mneg = kb.sb(st, "l3mneg", [128, 1], F32)
        lrT = kb.sb(st, "l3lrT", [1, 512], F32)
        ps_s = kb.ps(st, "l3_s", [128, 1024])
        pst = kb.ps(st, "l3_t", [128, 1024], BF16)
        ps_av = kb.ps(st, "l3_av", [128, 128])
        ps_m = kb.ps(st, "l3_m", [128, 512])
        vh3 = sb3(vh, 32, 64)
        for h in range(16):
            ch.dma(nc.sync, lambda h=h: nc.sync.dma_start(
                out=qh[:, :], in_=q_scr[64 * h:64 * h + 64, :]))
            ch.dma(nc.sync, lambda h=h: nc.sync.dma_start(
                out=kh[:, :], in_=k_scr[64 * h:64 * h + 64, :]))
            ch.dma(nc.sync, lambda h=h: nc.sync.dma_start(
                out=vh3, in_=v_scr[:, 64 * h:64 * h + 64].rearrange(
                    "(t p) c -> p t c", p=128)))
            for i in range(32):
                for t in range(8):
                    j0 = t * 512 + (i % 4) * 128
                    ch.op(nc.tensor, lambda i=i, t=t, j0=j0: nc.tensor.matmul(
                        ps_s[:, t * 128:(t + 1) * 128],
                        qh[:, i * 128:(i + 1) * 128], kh[:, j0:j0 + 128],
                        start=True, stop=True), inc=(t == 7))
                ch.op(nc.vector, lambda: nc.vector.tensor_add(
                    ps_s[:, :], ps_s[:, :], dmask[:, :]))
                ch.op(nc.vector, lambda: nc.vector.reduce_max(
                    mneg[:, :], ps_s[:, :], AX.X, negate=True))
                ch.op(nc.scalar, lambda i=i: nc.scalar.activation(
                    p_sb[:, :], ps_s[:, :], AF.Exp, bias=mneg[:, :],
                    accum_out=lrec[:, i:i + 1]))
                for b0 in (0, 4):
                    for sj in range(4):
                        ch.op(nc.tensor, lambda b0=b0, sj=sj: nc.tensor.transpose(
                            pst[:, sj * 128:(sj + 1) * 128],
                            p_sb[:, (b0 + sj) * 128:(b0 + sj + 1) * 128],
                            ident_bf[:, :]), inc=(sj == 3))
                    ch.op(nc.vector, lambda b0=b0: nc.vector.tensor_copy(
                        pt_sb[:, b0 * 128:(b0 + 4) * 128], pst[:, 0:512]))
                for t in range(8):
                    jt = t * 4 + (i % 4)
                    ch.op(nc.tensor, lambda t=t, jt=jt: nc.tensor.matmul(
                        ps_av[0:64, 0:128], vh3[:, jt, :],
                        pt_sb[:, t * 128:(t + 1) * 128],
                        start=(t == 0), stop=(t == 7)), inc=(t == 7))
                ch.op(nc.vector, lambda i=i: nc.vector.tensor_copy(
                    avh[:, i * 128:(i + 1) * 128], ps_av[0:64, 0:128]))
            ch.op(nc.vector, lambda: nc.vector.reciprocal(lrec[:, :], lrec[:, :]))
            for it0 in range(0, 32, 4):
                for sj in range(4):
                    ch.op(nc.tensor, lambda it0=it0, sj=sj: nc.tensor.transpose(
                        ps_m[0:1, sj * 128:(sj + 1) * 128],
                        lrec[:, it0 + sj:it0 + sj + 1], ident_f[:, :]),
                        inc=(sj == 3))
                ch.op(nc.vector, lambda: nc.vector.tensor_copy(
                    lrT[0:1, :], ps_m[0:1, 0:512]))
                for sj in range(4):
                    ch.op(nc.tensor, lambda sj=sj: nc.tensor.matmul(
                        ps_m[0:64, sj * 128:(sj + 1) * 128], ones64f[:, :],
                        lrT[0:1, sj * 128:(sj + 1) * 128], start=True, stop=True),
                        inc=(sj == 3))
                ch.op(nc.vector, lambda it0=it0: nc.vector.tensor_mul(
                    avn[:, it0 * 128:(it0 + 4) * 128],
                    avh[:, it0 * 128:(it0 + 4) * 128], ps_m[0:64, 0:512]))
            ch.dma(nc.gpsimd, lambda h=h: nc.gpsimd.dma_start(
                out=avn_scr[64 * h:64 * h + 64, :], in_=avn[:, :]))
        ch.seal()
    barrier(kb)
    with ExitStack() as st:
        ch = Chain(kb, st, "l3a_proj")
        opT = kb.sb(st, "opT", [128, 8 * 1024], BF16)
        avm = kb.sb(st, "l3avm", [128, 8 * 512], BF16)
        obt = kb.sb(st, "l3obt", [128, 8], F32)
        xt = kb.sb(st, "l3xt", [128, 512], F32)
        yb = kb.sb(st, "l3yb", [128, 512], F32)
        ps0 = kb.ps(st, "l3p_ps", [128, 512])
        ch.dma(nc.sync, lambda: nc.sync.dma_start(
            out=sb3(opT, 8, 1024),
            in_=opw_bf.rearrange("(d p) c -> p d c", p=128)))
        ch.dma(nc.sync, lambda: nc.sync.dma_start(out=obt[:, :], in_=opbt_f))
        for m in range(8):
            ch.dma(nc.sync, lambda m=m: nc.sync.dma_start(
                out=sb3(avm, 8, 512), in_=fm(avn_scr, m * 512, (m + 1) * 512)))
            for oc in range(8):
                ch.dma(nc.sync, lambda oc=oc, m=m: nc.sync.dma_start(
                    out=xt[:, :],
                    in_=x_dram[oc * 128:(oc + 1) * 128, m * 512:(m + 1) * 512]))
                for dd in range(8):
                    ch.op(nc.tensor, lambda oc=oc, dd=dd: nc.tensor.matmul(
                        ps0[:, :], opT[:, dd * 1024 + oc * 128:dd * 1024 + (oc + 1) * 128],
                        avm[:, dd * 512:(dd + 1) * 512],
                        start=(dd == 0), stop=(dd == 7)), inc=(dd == 7))
                ch.op(nc.vector, lambda oc=oc: nc.vector.scalar_tensor_tensor(
                    yb[:, :], ps0[:, :], obt[:, oc:oc + 1], xt[:, :],
                    ALU.add, ALU.add))
                ch.dma(nc.gpsimd, lambda oc=oc, m=m: nc.gpsimd.dma_start(
                    out=y_dram[oc * 128:(oc + 1) * 128, m * 512:(m + 1) * 512],
                    in_=yb[:, :]))
        ch.seal()
    barrier(kb)


def emit_l3_ffn(kb, x_dram, bw_f, bbt_f, sa_f, sb_f, ra_f, rb_f, rw_f, rbrow_f,
                y_dram):
    nc = kb.nc
    ln_scr = kb.dscr("l3f_ln", [1024, 4096], BF16)
    bw_bf = kb.dscr("l3f_bw", [1024, 1024], BF16)
    sa_bf = kb.dscr("l3f_sa", [1024, 16], BF16)
    sbw_bf = kb.dscr("l3f_sb", [16, 1024], BF16)
    ra_bf = kb.dscr("l3f_ra", [1024, 96], BF16)
    rb_bf = kb.dscr("l3f_rb", [96, 1024], BF16)
    sw_scr = kb.dscr("l3f_sw", [6, 4096], F32)
    with ExitStack() as st:
        ch = Chain(kb, st, "l3f_pre")
        emit_cast(kb, ch, bw_f, bw_bf)
        emit_cast(kb, ch, sa_f, sa_bf)
        emit_cast(kb, ch, sb_f, sbw_bf)
        emit_cast(kb, ch, ra_f, ra_bf)
        emit_cast(kb, ch, rb_f, rb_bf)
        emit_ln(kb, ch, st, x_dram, ln_scr)
        ch.seal()
    barrier(kb)
    with ExitStack() as st:
        ch = Chain(kb, st, "l3f_gate")
        emit_gating_topk(kb, ch, st, ln_scr, rw_f, sw_scr, 6,
                         topk_norm_eps=1e-6, bias_row=rbrow_f)
        ch.seal()
    barrier(kb)
    with ExitStack() as st:
        ch = Chain(kb, st, "l3f_m")
        bwT = kb.sb(st, "bwT", [128, 8 * 1024], BF16)
        saT = kb.sb(st, "saT", [128, 8 * 16], BF16)
        sbT = kb.sb(st, "sbT", [16, 1024], BF16)
        raT = kb.sb(st, "raT", [128, 8 * 96], BF16)
        rbT = kb.sb(st, "rbT", [96, 1024], BF16)
        lnm = kb.sb(st, "lnm3f", [128, 8 * 512], BF16)
        h1b = kb.sb(st, "h1b", [16, 512], BF16)
        hrb = kb.sb(st, "hrb", [96, 512], BF16)
        swb = kb.sb(st, "swb", [96, 512], F32)
        bbt = kb.sb(st, "bbt", [128, 8], F32)
        xt = kb.sb(st, "xt3f", [128, 512], F32)
        yb = kb.sb(st, "yb3f", [128, 512], F32)
        ps_h1 = kb.ps(st, "3f_ph1", [128, 512])
        ps_hr = kb.ps(st, "3f_phr", [128, 512])
        ps2 = kb.ps(st, "3f_ps2", [128, 512])
        ch.dma(nc.sync, lambda: nc.sync.dma_start(
            out=sb3(bwT, 8, 1024),
            in_=bw_bf.rearrange("(d p) c -> p d c", p=128)))
        ch.dma(nc.sync, lambda: nc.sync.dma_start(
            out=sb3(saT, 8, 16),
            in_=sa_bf.rearrange("(d p) c -> p d c", p=128)))
        ch.dma(nc.sync, lambda: nc.sync.dma_start(
            out=sb3(raT, 8, 96),
            in_=ra_bf.rearrange("(d p) c -> p d c", p=128)))
        ch.dma(nc.sync, lambda: nc.sync.dma_start(out=sbT[:, :], in_=sbw_bf))
        ch.dma(nc.sync, lambda: nc.sync.dma_start(out=rbT[:, :], in_=rb_bf))
        ch.dma(nc.sync, lambda: nc.sync.dma_start(out=bbt[:, :], in_=bbt_f))
        for m in range(8):
            ch.dma(nc.sync, lambda m=m: nc.sync.dma_start(
                out=sb3(lnm, 8, 512), in_=fm(ln_scr, m * 512, (m + 1) * 512)))
            for dd in range(8):
                ch.op(nc.tensor, lambda dd=dd: nc.tensor.matmul(
                    ps_h1[0:16, :], saT[:, dd * 16:(dd + 1) * 16],
                    lnm[:, dd * 512:(dd + 1) * 512],
                    start=(dd == 0), stop=(dd == 7)), inc=False)
                ch.op(nc.tensor, lambda dd=dd: nc.tensor.matmul(
                    ps_hr[0:96, :], raT[:, dd * 96:(dd + 1) * 96],
                    lnm[:, dd * 512:(dd + 1) * 512],
                    start=(dd == 0), stop=(dd == 7)), inc=(dd == 7))
            ch.op(nc.vector, lambda: nc.vector.tensor_copy(
                h1b[:, :], ps_h1[0:16, :]))
            for e in range(6):
                ch.dma(nc.sync, lambda m=m, e=e: nc.sync.dma_start(
                    out=swb[e * 16:(e + 1) * 16, :],
                    in_=sw_scr[e:e + 1, m * 512:(m + 1) * 512].broadcast_to(
                        [16, 512])))
            ch.op(nc.vector, lambda: nc.vector.tensor_mul(
                hrb[:, :], ps_hr[0:96, :], swb[:, :]))
            for oc in range(8):
                ch.dma(nc.sync, lambda oc=oc, m=m: nc.sync.dma_start(
                    out=xt[:, :],
                    in_=x_dram[oc * 128:(oc + 1) * 128, m * 512:(m + 1) * 512]))
                for dd in range(8):
                    ch.op(nc.tensor, lambda oc=oc, dd=dd: nc.tensor.matmul(
                        ps2[:, :],
                        bwT[:, dd * 1024 + oc * 128:dd * 1024 + (oc + 1) * 128],
                        lnm[:, dd * 512:(dd + 1) * 512],
                        start=(dd == 0), stop=False), inc=False)
                ch.op(nc.tensor, lambda oc=oc: nc.tensor.matmul(
                    ps2[:, :], sbT[:, oc * 128:(oc + 1) * 128], h1b[:, :],
                    start=False, stop=False), inc=False)
                ch.op(nc.tensor, lambda oc=oc: nc.tensor.matmul(
                    ps2[:, :], rbT[:, oc * 128:(oc + 1) * 128], hrb[:, :],
                    start=False, stop=True))
                ch.op(nc.vector, lambda oc=oc: nc.vector.scalar_tensor_tensor(
                    yb[:, :], ps2[:, :], bbt[:, oc:oc + 1], xt[:, :],
                    ALU.add, ALU.add))
                ch.dma(nc.gpsimd, lambda oc=oc, m=m: nc.gpsimd.dma_start(
                    out=y_dram[oc * 128:(oc + 1) * 128, m * 512:(m + 1) * 512],
                    in_=yb[:, :]))
        ch.seal()
    barrier(kb)



def _build_l0a():
    kb = KB("k_l0a")
    x = kb.din("x", [1024, 4096], F32)
    wqkv = kb.din("wqkv", [1024, 3072], F32)
    uq = kb.din("uq", [3072], F32)
    sq = kb.din("sq", [1], F32)
    wproj = kb.din("wproj", [1024, 1024], F32)
    up = kb.din("up", [1024], F32)
    sp = kb.din("sp", [1], F32)
    y = kb.dout("y", [1024, 4096], F32)
    emit_l0_attn(kb, x, wqkv, uq, sq, wproj, up, sp, y)
    kb.st.close()
    return kb.nc


def _build_l0f():
    kb = KB("k_l0f")
    x = kb.din("x", [1024, 4096], F32)
    wg = kb.din("wg", [8, 1024], F32)
    w1 = kb.din("w1", [8, 1024, 4096], F32)
    b1t = kb.din("b1t", [8, 128, 32], F32)
    w2 = kb.din("w2", [8, 4096, 1024], F32)
    b2 = kb.din("b2", [8, 1024], F32)
    y = kb.dout("y", [1024, 4096], F32)
    emit_l0_ffn(kb, x, wg, w1, b1t, w2, b2, y)
    kb.st.close()
    return kb.nc


def _build_l1a():
    kb = KB("k_l1a")
    x = kb.din("x", [1024, 4096], F32)
    uvw = kb.din("uvw", [1024, 2048], F32)
    ow = kb.din("ow", [1024, 1024], F32)
    rpw = kb.din("rpw", [1, 32], F32)
    rlw = kb.din("rlw", [3, 32], F32)
    rlb = kb.din("rlb", [3, 32], F32)
    rfc = kb.din("rfc", [3, 32, 32], F32)
    rolw = kb.din("rolw", [32], F32)
    rolb = kb.din("rolb", [32], F32)
    row = kb.din("row", [32, 1024], F32)
    y = kb.dout("y", [1024, 4096], F32)
    emit_l1_attn(kb, x, uvw, ow, rpw, rlw, rlb, rfc, rolw, rolb, row, y)
    kb.st.close()
    return kb.nc


def _build_l1f():
    kb = KB("k_l1f")
    x = kb.din("x", [1024, 4096], F32)
    w1 = kb.din("w1", [4096, 1024], F32)
    b1t = kb.din("b1t", [128, 32], F32)
    w2 = kb.din("w2", [1024, 4096], F32)
    b2t = kb.din("b2t", [128, 8], F32)
    y = kb.dout("y", [1024, 4096], F32)
    emit_l1_ffn(kb, x, w1, b1t, w2, b2t, y)
    kb.st.close()
    return kb.nc


def _build_l2a():
    kb = KB("k_l2a")
    x = kb.din("x", [1024, 4096], F32)
    w = kb.din("w", [1024, 1024], F32)
    bt = kb.din("bt", [128, 8], F32)
    br = kb.din("br", [1, 1024], F32)
    y = kb.dout("y", [1024, 4096], F32)
    emit_l2_attn(kb, x, w, bt, br, y)
    kb.st.close()
    return kb.nc


def _build_l2f():
    kb = KB("k_l2f")
    x = kb.din("x", [1024, 4096], F32)
    gw = kb.din("gw", [8, 1024], F32)
    eg = kb.din("eg", [8, 1024, 2816], F32)
    eu = kb.din("eu", [8, 1024, 2816], F32)
    ed = kb.din("ed", [8, 2816, 1024], F32)
    sg = kb.din("sg", [1024, 2816], F32)
    su = kb.din("su", [1024, 2816], F32)
    sd = kb.din("sd", [2816, 1024], F32)
    y = kb.dout("y", [1024, 4096], F32)
    emit_l2_ffn(kb, x, gw, eg, eu, ed, sg, su, sd, y)
    kb.st.close()
    return kb.nc


def _build_l3():
    kb = KB("k_l3")
    x = kb.din("x", [1024, 4096], F32)
    ipw = kb.din("ipw", [1024, 3072], F32)
    ipbt = kb.din("ipbt", [128, 24], F32)
    ipbv = kb.din("ipbv", [1, 1024], F32)
    opw = kb.din("opw", [1024, 1024], F32)
    opbt = kb.din("opbt", [128, 8], F32)
    xa = kb.dscr("xa", [1024, 4096], F32)
    emit_l3_attn(kb, x, ipw, ipbt, ipbv, opw, opbt, xa)
    bw = kb.din("bw", [1024, 1024], F32)
    bbt = kb.din("bbt", [128, 8], F32)
    sa = kb.din("sa", [1024, 16], F32)
    sbw = kb.din("sbw", [16, 1024], F32)
    ra = kb.din("ra", [1024, 96], F32)
    rb = kb.din("rb", [96, 1024], F32)
    rw = kb.din("rw", [6, 1024], F32)
    rbr = kb.din("rbr", [1, 6], F32)
    y = kb.dout("y", [1024, 4096], F32)
    emit_l3_ffn(kb, xa, bw, bbt, sa, sbw, ra, rb, rw, rbr, y)
    kb.st.close()
    return kb.nc


def _run(nc, common, xs):
    in_maps = [dict(common, x=np.ascontiguousarray(x)) for x in xs]
    res = run_bass_kernel_spmd(nc, in_maps, core_ids=list(range(NCORES)))
    return [np.asarray(r["y"]) for r in res.results]


def kernel_unfused(**inp):
    cg = np.ascontiguousarray
    x = inp["L0_attn_batch"]
    xs = [cg(x[c].T) for c in range(NCORES)]

    xs = _run(_build_l0a(), {
        "wqkv": inp["L0_attn_W_qkv"], "uq": inp["L0_attn_u_qkv"],
        "sq": inp["L0_attn_sigma_qkv"], "wproj": inp["L0_attn_W_proj"],
        "up": inp["L0_attn_u_proj"], "sp": inp["L0_attn_sigma_proj"]}, xs)

    xs = _run(_build_l0f(), {
        "wg": inp["L0_ffn_w_gate"], "w1": inp["L0_ffn_W1"],
        "b1t": cg(inp["L0_ffn_b1"].reshape(8, 32, 128).transpose(0, 2, 1)),
        "w2": inp["L0_ffn_W2"], "b2": inp["L0_ffn_b2"]}, xs)

    xs = _run(_build_l1a(), {
        "uvw": inp["L1_attn_uv_w"], "ow": inp["L1_attn_o_w"],
        "rpw": inp["L1_attn_rpe_pos_w"], "rlw": inp["L1_attn_rpe_ln_w"],
        "rlb": inp["L1_attn_rpe_ln_b"], "rfc": inp["L1_attn_rpe_fc_w"],
        "rolw": inp["L1_attn_rpe_out_ln_w"], "rolb": inp["L1_attn_rpe_out_ln_b"],
        "row": inp["L1_attn_rpe_out_w"]}, xs)

    xs = _run(_build_l1f(), {
        "w1": inp["L1_ffn_W1"],
        "b1t": cg(inp["L1_ffn_b1"].reshape(32, 128).T),
        "w2": inp["L1_ffn_W2"],
        "b2t": cg(inp["L1_ffn_b2"].reshape(8, 128).T)}, xs)

    xs = _run(_build_l2a(), {
        "w": inp["L2_attn_W"],
        "bt": cg(inp["L2_attn_b"].reshape(8, 128).T),
        "br": inp["L2_attn_b"].reshape(1, 1024)}, xs)

    xs = _run(_build_l2f(), {
        "gw": inp["L2_ffn_gate_w"],
        "eg": cg(inp["L2_ffn_expert_gate"].transpose(0, 2, 1)),
        "eu": cg(inp["L2_ffn_expert_up"].transpose(0, 2, 1)),
        "ed": cg(inp["L2_ffn_expert_down"].transpose(0, 2, 1)),
        "sg": cg(inp["L2_ffn_shared_gate"].T),
        "su": cg(inp["L2_ffn_shared_up"].T),
        "sd": cg(inp["L2_ffn_shared_down"].T)}, xs)

    full = np.stack([r.T for r in xs])
    xrs = [cg(full[:, 512 * j:512 * (j + 1), :].reshape(4096, 1024).T)
           for j in range(NCORES)]

    ys = _run(_build_l3(), {
        "ipw": cg(inp["L3_attn_in_proj_w"].T),
        "ipbt": cg(inp["L3_attn_in_proj_b"].reshape(24, 128).T),
        "ipbv": inp["L3_attn_in_proj_b"][2048:].reshape(1, 1024),
        "opw": cg(inp["L3_attn_out_proj_w"].T),
        "opbt": cg(inp["L3_attn_out_proj_b"].reshape(8, 128).T),
        "bw": cg(inp["L3_ffn_base_W"].T),
        "bbt": cg(inp["L3_ffn_base_b"].reshape(8, 128).T),
        "sa": cg(inp["L3_ffn_shared_A"].T),
        "sbw": cg(inp["L3_ffn_shared_B"].T),
        "ra": cg(inp["L3_ffn_routed_A"].reshape(96, 1024).T),
        "rb": cg(inp["L3_ffn_routed_B"].transpose(0, 2, 1).reshape(96, 1024)),
        "rw": inp["L3_ffn_router_W"],
        "rbr": inp["L3_ffn_router_b"].reshape(1, 6)}, xrs)

    out = np.empty((8, 4096, 1024), np.float32)
    for j in range(NCORES):
        out[:, 512 * j:512 * (j + 1), :] = ys[j].T.reshape(8, 512, 1024)
    return out



def emit_reshard_a2a(kb, src_dram, dst_dram):
    from concourse.aot_env import allow_ici_all_to_all
    nc = kb.nc
    a2a_in = kb.nc.dram_tensor("a2a_in", [8, 1024, 512], F32, kind="Internal")
    a2a_out = kb.nc.dram_tensor("a2a_out", [8, 1024, 512], F32, kind="Internal")
    with ExitStack() as st:
        ch = Chain(kb, st, "a2a")
        for j in range(8):
            ch.dma(nc.gpsimd, lambda j=j: nc.gpsimd.dma_start(
                out=a2a_in.ap()[j], in_=src_dram[:, j * 512:(j + 1) * 512]))
        with allow_ici_all_to_all():
            ch.op(nc.gpsimd, lambda: nc.gpsimd.collective_compute(
                "AllToAll", ALU.bypass,
                replica_groups=[list(range(NCORES))],
                ins=[a2a_in.ap().opt()], outs=[a2a_out.ap().opt()]))
        for j in range(8):
            ch.dma(nc.gpsimd, lambda j=j: nc.gpsimd.dma_start(
                out=dst_dram[:, j * 512:(j + 1) * 512], in_=a2a_out.ap()[j]))
        ch.seal()
    barrier(kb)


def _build_fused(inp_shapes_only=False):
    kb = KB("mk_fused")
    x = kb.din("x", [1024, 4096], F32)
    wqkv = kb.din("wqkv", [1024, 3072], F32)
    uq = kb.din("uq", [3072], F32)
    sq = kb.din("sq", [1], F32)
    wproj = kb.din("wproj", [1024, 1024], F32)
    up = kb.din("up", [1024], F32)
    sp = kb.din("sp", [1], F32)
    wg0 = kb.din("wg0", [8, 1024], F32)
    w10 = kb.din("w10", [8, 1024, 4096], F32)
    b1t0 = kb.din("b1t0", [8, 128, 32], F32)
    w20 = kb.din("w20", [8, 4096, 1024], F32)
    b20 = kb.din("b20", [8, 1024], F32)
    uvw = kb.din("uvw", [1024, 2048], F32)
    ow = kb.din("ow", [1024, 1024], F32)
    rpw = kb.din("rpw", [1, 32], F32)
    rlw = kb.din("rlw", [3, 32], F32)
    rlb = kb.din("rlb", [3, 32], F32)
    rfc = kb.din("rfc", [3, 32, 32], F32)
    rolw = kb.din("rolw", [32], F32)
    rolb = kb.din("rolb", [32], F32)
    row = kb.din("row", [32, 1024], F32)
    w11 = kb.din("w11", [4096, 1024], F32)
    b1t1 = kb.din("b1t1", [128, 32], F32)
    w21 = kb.din("w21", [1024, 4096], F32)
    b2t1 = kb.din("b2t1", [128, 8], F32)
    w2a = kb.din("w2a", [1024, 1024], F32)
    bt2a = kb.din("bt2a", [128, 8], F32)
    br2a = kb.din("br2a", [1, 1024], F32)
    gw2 = kb.din("gw2", [8, 1024], F32)
    eg2 = kb.din("eg2", [8, 1024, 2816], F32)
    eu2 = kb.din("eu2", [8, 1024, 2816], F32)
    ed2 = kb.din("ed2", [8, 2816, 1024], F32)
    sg2 = kb.din("sg2", [1024, 2816], F32)
    su2 = kb.din("su2", [1024, 2816], F32)
    sd2 = kb.din("sd2", [2816, 1024], F32)
    ipw = kb.din("ipw", [1024, 3072], F32)
    ipbt = kb.din("ipbt", [128, 24], F32)
    ipbv = kb.din("ipbv", [1, 1024], F32)
    opw = kb.din("opw", [1024, 1024], F32)
    opbt = kb.din("opbt", [128, 8], F32)
    bw3 = kb.din("bw3", [1024, 1024], F32)
    bbt3 = kb.din("bbt3", [128, 8], F32)
    sa3 = kb.din("sa3", [1024, 16], F32)
    sbw3 = kb.din("sbw3", [16, 1024], F32)
    ra3 = kb.din("ra3", [1024, 96], F32)
    rb3 = kb.din("rb3", [96, 1024], F32)
    rw3 = kb.din("rw3", [6, 1024], F32)
    rbr3 = kb.din("rbr3", [1, 6], F32)
    y = kb.dout("y", [1024, 4096], F32)

    xA = kb.dscr("xA", [1024, 4096], F32)
    xB = kb.dscr("xB", [1024, 4096], F32)
    xR = kb.dscr("xR", [1024, 4096], F32)
    xRA = kb.dscr("xRA", [1024, 4096], F32)

    phases = [
        lambda: emit_l0_attn(kb, x, wqkv, uq, sq, wproj, up, sp, xA),
        lambda: emit_l0_ffn(kb, xA, wg0, w10, b1t0, w20, b20, xB),
        lambda: emit_l1_attn(kb, xB, uvw, ow, rpw, rlw, rlb, rfc, rolw, rolb,
                             row, xA),
        lambda: emit_l1_ffn(kb, xA, w11, b1t1, w21, b2t1, xB),
        lambda: emit_l2_attn(kb, xB, w2a, bt2a, br2a, xA),
        lambda: emit_l2_ffn(kb, xA, gw2, eg2, eu2, ed2, sg2, su2, sd2, xB),
        lambda: emit_reshard_a2a(kb, xB, xR),
        lambda: emit_l3_attn(kb, xR, ipw, ipbt, ipbv, opw, opbt, xRA),
        lambda: emit_l3_ffn(kb, xRA, bw3, bbt3, sa3, sbw3, ra3, rb3, rw3,
                            rbr3, y),
    ]
    import os
    upto = int(os.environ.get("MK_UPTO", "9"))
    for ph in phases[:upto]:
        ph()
    if upto < 9:
        srcs = [xA, xB, xA, xB, xA, xB, xR, xRA][upto - 1]
        with ExitStack() as st:
            ch = Chain(kb, st, "fin")
            ch.dma(kb.nc.gpsimd, lambda: kb.nc.gpsimd.dma_start(out=y, in_=srcs))
            ch.seal()
    kb.st.close()
    return kb.nc


def kernel(**inp):
    cg = np.ascontiguousarray
    x = inp["L0_attn_batch"]
    common = {
        "wqkv": inp["L0_attn_W_qkv"], "uq": inp["L0_attn_u_qkv"],
        "sq": inp["L0_attn_sigma_qkv"], "wproj": inp["L0_attn_W_proj"],
        "up": inp["L0_attn_u_proj"], "sp": inp["L0_attn_sigma_proj"],
        "wg0": inp["L0_ffn_w_gate"], "w10": inp["L0_ffn_W1"],
        "b1t0": cg(inp["L0_ffn_b1"].reshape(8, 32, 128).transpose(0, 2, 1)),
        "w20": inp["L0_ffn_W2"], "b20": inp["L0_ffn_b2"],
        "uvw": inp["L1_attn_uv_w"], "ow": inp["L1_attn_o_w"],
        "rpw": inp["L1_attn_rpe_pos_w"], "rlw": inp["L1_attn_rpe_ln_w"],
        "rlb": inp["L1_attn_rpe_ln_b"], "rfc": inp["L1_attn_rpe_fc_w"],
        "rolw": inp["L1_attn_rpe_out_ln_w"], "rolb": inp["L1_attn_rpe_out_ln_b"],
        "row": inp["L1_attn_rpe_out_w"],
        "w11": inp["L1_ffn_W1"],
        "b1t1": cg(inp["L1_ffn_b1"].reshape(32, 128).T),
        "w21": inp["L1_ffn_W2"],
        "b2t1": cg(inp["L1_ffn_b2"].reshape(8, 128).T),
        "w2a": inp["L2_attn_W"],
        "bt2a": cg(inp["L2_attn_b"].reshape(8, 128).T),
        "br2a": inp["L2_attn_b"].reshape(1, 1024),
        "gw2": inp["L2_ffn_gate_w"],
        "eg2": cg(inp["L2_ffn_expert_gate"].transpose(0, 2, 1)),
        "eu2": cg(inp["L2_ffn_expert_up"].transpose(0, 2, 1)),
        "ed2": cg(inp["L2_ffn_expert_down"].transpose(0, 2, 1)),
        "sg2": cg(inp["L2_ffn_shared_gate"].T),
        "su2": cg(inp["L2_ffn_shared_up"].T),
        "sd2": cg(inp["L2_ffn_shared_down"].T),
        "ipw": cg(inp["L3_attn_in_proj_w"].T),
        "ipbt": cg(inp["L3_attn_in_proj_b"].reshape(24, 128).T),
        "ipbv": inp["L3_attn_in_proj_b"][2048:].reshape(1, 1024),
        "opw": cg(inp["L3_attn_out_proj_w"].T),
        "opbt": cg(inp["L3_attn_out_proj_b"].reshape(8, 128).T),
        "bw3": cg(inp["L3_ffn_base_W"].T),
        "bbt3": cg(inp["L3_ffn_base_b"].reshape(8, 128).T),
        "sa3": cg(inp["L3_ffn_shared_A"].T),
        "sbw3": cg(inp["L3_ffn_shared_B"].T),
        "ra3": cg(inp["L3_ffn_routed_A"].reshape(96, 1024).T),
        "rb3": cg(inp["L3_ffn_routed_B"].transpose(0, 2, 1).reshape(96, 1024)),
        "rw3": inp["L3_ffn_router_W"],
        "rbr3": inp["L3_ffn_router_b"].reshape(1, 6),
    }
    nc = _build_fused()
    in_maps = [dict(common, x=cg(x[c].T)) for c in range(NCORES)]
    res = run_bass_kernel_spmd(nc, in_maps, core_ids=list(range(NCORES)))
    out = np.empty((8, 4096, 1024), np.float32)
    for j in range(NCORES):
        out[:, 512 * j:512 * (j + 1), :] = \
            np.asarray(res.results[j]["y"]).T.reshape(8, 512, 1024)
    return out


class DF:

    ENGS = ("tensor", "vector", "scalar", "gpsimd", "sync")

    def __init__(self, kb, st, name):
        self.kb = kb
        nc = kb.nc
        self.sem = {}
        self.cnt = {}
        for e in self.ENGS:
            self.sem[e] = st.enter_context(nc.semaphore(f"{name}_{e}"))
            self.cnt[e] = 0
        for q in ("dsync", "dgp", "dsc"):
            self.sem[q] = st.enter_context(nc.semaphore(f"{name}_{q}"))
            self.cnt[q] = 0
        self._eng = {"tensor": nc.tensor, "vector": nc.vector,
                     "scalar": nc.scalar, "gpsimd": nc.gpsimd, "sync": nc.sync}
        self._dq = {"sync": "dsync", "gpsimd": "dgp", "scalar": "dsc"}
        self.wr = {}
        self.rd = {}
        self.lastw = {}

    def _wait(self, engname, deps):
        eng = self._eng[engname]
        lw = self.lastw.setdefault(engname, {})
        for s, v in sorted(deps.items()):
            if s == engname and engname == "tensor":
                continue
            if v > lw.get(s, 0):
                eng.wait_ge(self.sem[s], v)
                lw[s] = v

    @classmethod
    def _key(cls, k):
        if isinstance(k, tuple):
            return tuple(cls._key(x) for x in k)
        try:
            hash(k)
            return k
        except TypeError:
            return id(k)

    def op(self, engname, emit, reads=(), writes=(), inc=True, dma=False):
        reads = [self._key(k) for k in reads]
        writes = [self._key(k) for k in writes]
        deps = {}
        for k in reads:
            w = self.wr.get(k)
            if w:
                deps[w[0]] = max(deps.get(w[0], 0), w[1])
        for k in writes:
            w = self.wr.get(k)
            if w:
                deps[w[0]] = max(deps.get(w[0], 0), w[1])
            for s, v in self.rd.get(k, {}).items():
                deps[s] = max(deps.get(s, 0), v)
        self._wait(engname, deps)
        ins = emit()
        sname = self._dq[engname] if dma else engname
        if inc:
            ins.then_inc(self.sem[sname], 16 if dma else 1)
            self.cnt[sname] += 16 if dma else 1
        done = (sname, self.cnt[sname] + (16 if dma and not inc else
                                          (1 if not inc else 0)))
        for k in reads:
            self.rd.setdefault(k, {})[sname] = done[1]
        for k in writes:
            self.wr[k] = done
            self.rd[k] = {}
        return ins

    def dma(self, engname, emit, reads=(), writes=(), inc=True):
        return self.op(engname, emit, reads, writes, inc=inc, dma=True)

    def seal(self):
        for engname in self.ENGS:
            deps = {s: v for s, v in self.cnt.items() if v}
            self._wait(engname, deps)


def emit_moe_experts_df(kb, name, ln_scr, cw_scr, x_dram, y_dram, nexp, nh,
                        w_loader, act_kind, has_shared):
    raise NotImplementedError


# version 05 (final)
# speedup vs baseline: 1.2947x; 1.0350x over previous
import sys

sys.path.insert(0, "/opt/trn_rl_repo")
from contextlib import ExitStack

import numpy as np
import ml_dtypes

import concourse.bass as bass
import concourse.mybir as mybir
from concourse.bass_utils import run_bass_kernel_spmd

F32 = mybir.dt.float32
BF16 = mybir.dt.bfloat16
I32 = mybir.dt.int32
AF = mybir.ActivationFunctionType
ALU = mybir.AluOpType
AX = mybir.AxisListType

NCORES = 8
M = 4096
D = 1024
NEG = -1.0e30

BF = ml_dtypes.bfloat16


def _bass(name):
    return bass.Bass("TRN2", target_bir_lowering=False,
                     detect_race_conditions=False, name=name)


class KB:

    def __init__(self, name):
        self.nc = _bass(name)
        self.st = ExitStack()
        self._uid = 0
        self._consts = {}

    def uid(self):
        self._uid += 1
        return self._uid

    def din(self, name, shape, dt=F32):
        return self.nc.dram_tensor(name, list(shape), dt, kind="ExternalInput").ap()

    def dout(self, name, shape, dt=F32):
        return self.nc.dram_tensor(name, list(shape), dt, kind="ExternalOutput").ap()

    def dscr(self, name, shape, dt=F32):
        return self.nc.dram_tensor(name, list(shape), dt, kind="Internal").ap()

    def const_dram(self, arr, name=None):
        key = (arr.tobytes(), arr.shape, str(arr.dtype))
        if key not in self._consts:
            self._consts[key] = self.nc.inline_tensor(
                np.ascontiguousarray(arr), name or f"const{self.uid()}").ap()
        return self._consts[key]

    def const_sb(self, st, ch, arr, name):
        nc = self.nc
        dt = BF16 if arr.dtype == BF else (F32 if arr.dtype == np.float32 else I32)
        t = st.enter_context(
            nc.sbuf_tensor(f"c_{name}{self.uid()}", list(arr.shape), dt))
        dr = self.const_dram(arr, f"cd_{name}{self.uid()}")
        ch.dma(nc.sync, lambda: nc.sync.dma_start(out=t[:, :], in_=dr))
        return t

    def sb(self, st, name, shape, dt):
        return self.st_sb(st, name, shape, dt)

    def st_sb(self, st, name, shape, dt):
        return st.enter_context(
            self.nc.sbuf_tensor(f"{name}{self.uid()}", list(shape), dt))

    def ps(self, st, name, shape, dt=F32):
        return st.enter_context(
            self.nc.psum_tensor(f"{name}{self.uid()}", list(shape), dt))


class Chain:

    def __init__(self, kb, st, name):
        self.kb = kb
        nc = kb.nc
        self.c = st.enter_context(nc.semaphore(f"{name}_c"))
        self.d = st.enter_context(nc.semaphore(f"{name}_d"))
        self.cn = 0
        self.dn = 0
        self._last = {}

    def _pre(self, eng):
        key = type(eng).__name__
        lc, ld = self._last.get(key, (0, 0))
        if self.cn > lc:
            eng.wait_ge(self.c, self.cn)
        if self.dn > ld:
            eng.wait_ge(self.d, self.dn)
        self._last[key] = (max(lc, self.cn), max(ld, self.dn))

    def op(self, eng, emit, inc=True):
        self._pre(eng)
        ins = emit()
        if inc:
            ins.then_inc(self.c, 1)
            self.cn += 1
        return ins

    def dma(self, eng, emit, inc=True):
        self._pre(eng)
        ins = emit()
        if inc:
            ins.then_inc(self.d, 16)
            self.dn += 16
        return ins

    def seal(self):
        nc = self.kb.nc
        for eng in (nc.sync, nc.gpsimd, nc.tensor, nc.vector, nc.scalar):
            self._pre(eng)


def barrier(kb):
    kb.nc.all_engine_barrier()



def fm(ap2d, mlo, mhi):
    return ap2d[:, mlo:mhi].rearrange("(d p) c -> p d c", p=128)


def sb3(t, nd, w):
    return t[:, :].rearrange("p (d c) -> p d c", d=nd)


def emit_cast(kb, ch, src, dst):
    ch.dma(kb.nc.gpsimd, lambda: kb.nc.gpsimd.dma_start(out=dst, in_=src))


def emit_ln(kb, ch, st, x_dram, ln_dram, mtot=M, eps=1e-5):
    nc = kb.nc
    with ExitStack() as ls:
        xm = kb.sb(ls, "ln_xm", [128, 8 * 512], F32)
        xb = kb.sb(ls, "ln_xb", [128, 8 * 512], BF16)
        x2 = kb.sb(ls, "ln_x2", [128, 8 * 512], BF16)
        mu2 = kb.sb(ls, "ln_mu2", [1, 512], F32)
        var = kb.sb(ls, "ln_var", [1, 512], F32)
        rstd = kb.sb(ls, "ln_rstd", [1, 512], F32)
        beta = kb.sb(ls, "ln_beta", [1, 512], F32)
        tmp = kb.sb(ls, "ln_tmp", [128, 512], F32)
        outt = kb.sb(ls, "ln_out", [128, 8 * 512], BF16)
        ones_bf = kb.const_sb(ls, ch, np.ones((128, 1), np.float32).astype(BF), "onesb")
        eps_sb = kb.const_sb(ls, ch, np.full((1, 1), eps, np.float32), "eps")
        ones1f = kb.const_sb(ls, ch, np.ones((1, 128), np.float32), "ones1f")
        ps_s = kb.ps(ls, "ln_ps_s", [128, 512])
        ps_q = kb.ps(ls, "ln_ps_q", [128, 512])
        ps_b = kb.ps(ls, "ln_ps_b", [128, 512])
        ps_b2 = kb.ps(ls, "ln_ps_b2", [128, 512])

        for m in range(mtot // 512):
            src = fm(x_dram, m * 512, (m + 1) * 512)
            ch.dma(nc.sync, lambda src=src: nc.sync.dma_start(out=sb3(xm, 8, 512), in_=src))
            ch.op(nc.vector, lambda: nc.vector.tensor_copy(xb[:, :], xm[:, :]))
            ch.op(nc.scalar, lambda: nc.scalar.activation(x2[:, :], xm[:, :], AF.Square))
            for dci in range(8):
                ch.op(nc.tensor, lambda dci=dci: nc.tensor.matmul(
                    ps_s[0:1, :], ones_bf[:, :], xb[:, dci * 512:(dci + 1) * 512],
                    start=(dci == 0), stop=(dci == 7)), inc=False)
                ch.op(nc.tensor, lambda dci=dci: nc.tensor.matmul(
                    ps_q[0:1, :], ones_bf[:, :], x2[:, dci * 512:(dci + 1) * 512],
                    start=(dci == 0), stop=(dci == 7)), inc=(dci == 7))
            ch.op(nc.scalar, lambda: nc.scalar.activation(
                mu2[:, :], ps_s[0:1, :], AF.Square, scale=1.0 / D))
            ch.op(nc.vector, lambda: nc.vector.scalar_tensor_tensor(
                var[:, :], ps_q[0:1, :], 1.0 / D, mu2[:, :], ALU.mult, ALU.subtract))
            ch.op(nc.scalar, lambda: nc.scalar.activation(
                rstd[:, :], var[:, :], AF.Ln, bias=eps_sb[:, :]))
            ch.op(nc.scalar, lambda: nc.scalar.activation(
                rstd[:, :], rstd[:, :], AF.Exp, scale=-0.5))
            ch.op(nc.vector, lambda: nc.vector.scalar_tensor_tensor(
                beta[:, :], ps_s[0:1, :], -1.0 / D, rstd[:, :], ALU.mult, ALU.mult))
            ch.op(nc.tensor, lambda: nc.tensor.matmul(
                ps_b[:, :], ones1f[:, :], rstd[:, :], start=True, stop=True),
                inc=False)
            ch.op(nc.tensor, lambda: nc.tensor.matmul(
                ps_b2[:, :], ones1f[:, :], beta[:, :], start=True, stop=True))
            for dci in range(8):
                sl = slice(dci * 512, (dci + 1) * 512)
                ch.op(nc.vector, lambda sl=sl: nc.vector.tensor_mul(
                    tmp[:, :], xm[:, sl], ps_b[:, :]))
                ch.op(nc.vector, lambda sl=sl: nc.vector.tensor_add(
                    outt[:, sl], tmp[:, :], ps_b2[:, :]))
            dst = fm(ln_dram, m * 512, (m + 1) * 512)
            ch.dma(nc.gpsimd, lambda dst=dst: nc.gpsimd.dma_start(
                out=dst, in_=sb3(outt, 8, 512)))


def ceil_div(a, b):
    return (a + b - 1) // b


def emit_sigma(kb, ch, st, w_bf, din_, dout_, u_dram, sp_dram, name):
    nc = kb.nc
    no, nci = dout_ // 128, din_ // 128
    u_sb = kb.sb(st, f"sg_u{name}", [128, no], BF16)
    wt_sb = kb.sb(st, f"sg_wt{name}", [128, din_], BF16)
    vsq = kb.sb(st, f"sg_vsq{name}", [1, din_], F32)
    vss = kb.sb(st, f"sg_vss{name}", [1, 1], F32)
    rn = kb.sb(st, f"sg_rn{name}", [1, 1], F32)
    v_row = kb.sb(st, f"sg_vrow{name}", [1, din_], F32)
    vc_sb = kb.sb(st, f"sg_vc{name}", [128, nci], BF16)
    wrow = kb.sb(st, f"sg_wrow{name}", [128, dout_], BF16)
    zsq = kb.sb(st, f"sg_zsq{name}", [1, dout_], F32)
    zss = kb.sb(st, f"sg_zss{name}", [1, 1], F32)
    sp_sb = kb.sb(st, f"sg_sp{name}", [1, 1], F32)
    sc = kb.sb(st, f"sg_sc{name}", [1, 1], F32)
    vsig = kb.dscr(f"sg_vd{name}", [1, din_], BF16)
    with ExitStack() as ls:
        ps_v = kb.ps(ls, f"sg_psv{name}", [128, din_])
        for a in range(no):
            ch.dma(nc.gpsimd, lambda a=a: nc.gpsimd.dma_start(
                out=u_sb[:, a:a + 1],
                in_=u_dram[a * 128:(a + 1) * 128].unsqueeze(1)))
        for a in range(no):
            ch.dma(nc.sync, lambda a=a: nc.sync.dma_start(
                out=wt_sb[:, :], in_=w_bf[:, a * 128:(a + 1) * 128],
                transpose=True))
            for j in range(din_ // 512):
                ch.op(nc.tensor, lambda a=a, j=j: nc.tensor.matmul(
                    ps_v[0:1, j * 512:(j + 1) * 512], u_sb[:, a:a + 1],
                    wt_sb[:, j * 512:(j + 1) * 512],
                    start=(a == 0), stop=(a == no - 1)),
                    inc=(a == no - 1 and j == din_ // 512 - 1))
        ch.op(nc.scalar, lambda: nc.scalar.activation(
            vsq[:, :], ps_v[0:1, 0:din_], AF.Square, accum_out=vss[:, :]))
        ch.op(nc.scalar, lambda: nc.scalar.activation(rn[:, :], vss[:, :], AF.Ln))
        ch.op(nc.scalar, lambda: nc.scalar.activation(
            rn[:, :], rn[:, :], AF.Exp, scale=-0.5))
        ch.op(nc.vector, lambda: nc.vector.tensor_single_scalar(
            v_row[:, :], ps_v[0:1, 0:din_], rn[:, :], ALU.mult))
    ch.dma(nc.gpsimd, lambda: nc.gpsimd.dma_start(out=vsig, in_=v_row[:, :]))
    for c in range(nci):
        ch.dma(nc.sync, lambda c=c: nc.sync.dma_start(
            out=vc_sb[:, c:c + 1],
            in_=vsig[0:1, c * 128:(c + 1) * 128].rearrange("a c -> c a")))
    with ExitStack() as ls:
        ps_z = kb.ps(ls, f"sg_psz{name}", [128, dout_])
        for c in range(nci):
            ch.dma(nc.sync, lambda c=c: nc.sync.dma_start(
                out=wrow[:, :], in_=w_bf[c * 128:(c + 1) * 128, :]))
            for j in range(dout_ // 512):
                ch.op(nc.tensor, lambda c=c, j=j: nc.tensor.matmul(
                    ps_z[0:1, j * 512:(j + 1) * 512], vc_sb[:, c:c + 1],
                    wrow[:, j * 512:(j + 1) * 512],
                    start=(c == 0), stop=(c == nci - 1)),
                    inc=(c == nci - 1 and j == dout_ // 512 - 1))
        ch.op(nc.scalar, lambda: nc.scalar.activation(
            zsq[:, :], ps_z[0:1, 0:dout_], AF.Square, accum_out=zss[:, :]))
    ch.op(nc.scalar, lambda: nc.scalar.activation(sc[:, :], zss[:, :], AF.Ln))
    ch.op(nc.scalar, lambda: nc.scalar.activation(
        sc[:, :], sc[:, :], AF.Exp, scale=-0.5))
    ch.dma(nc.sync, lambda: nc.sync.dma_start(
        out=sp_sb[:, :], in_=sp_dram.unsqueeze(0)))
    ch.op(nc.vector, lambda: nc.vector.tensor_mul(sc[:, :], sc[:, :], sp_sb[:, :]))
    return sc


def emit_l0_attn(kb, x_dram, wqkv_f, uq_f, sq_f, wproj_f, up_f, sp_f, y_dram):
    nc = kb.nc
    wqkv_bf = kb.dscr("l0_wqkv_bf", [1024, 3072], BF16)
    wproj_bf = kb.dscr("l0_wproj_bf", [1024, 1024], BF16)
    ln_scr = kb.dscr("l0_ln", [1024, 4096], BF16)
    q_scr = kb.dscr("l0_q", [1024, 4096], BF16)
    k_scr = kb.dscr("l0_k", [1024, 4096], BF16)
    v_scr = kb.dscr("l0_v", [4096, 1024], BF16)
    avn_scr = kb.dscr("l0_avn", [1024, 4096], BF16)
    sc_scr = kb.dscr("l0_scales", [1, 4], F32)

    with ExitStack() as st:
        ch = Chain(kb, st, "l0a_sig")
        emit_cast(kb, ch, wqkv_f, wqkv_bf)
        emit_cast(kb, ch, wproj_f, wproj_bf)
        sc_qkv = emit_sigma(kb, ch, st, wqkv_bf, 1024, 3072, uq_f, sq_f, "qkv")
        sc_proj = emit_sigma(kb, ch, st, wproj_bf, 1024, 1024, up_f, sp_f, "proj")
        sq_11 = kb.sb(st, "sq11", [1, 1], F32)
        ch.op(nc.vector, lambda: nc.vector.tensor_scalar_mul(
            sq_11[:, :], sc_qkv[:, :], 0.125))
        ch.dma(nc.gpsimd, lambda: nc.gpsimd.dma_start(
            out=sc_scr[0:1, 0:1], in_=sq_11[:, :]))
        ch.dma(nc.gpsimd, lambda: nc.gpsimd.dma_start(
            out=sc_scr[0:1, 1:2], in_=sc_qkv[:, :]))
        ch.dma(nc.gpsimd, lambda: nc.gpsimd.dma_start(
            out=sc_scr[0:1, 2:3], in_=sc_proj[:, :]))
        ch.seal()
    barrier(kb)

    with ExitStack() as st:
        ch = Chain(kb, st, "l0a_ln")
        emit_ln(kb, ch, st, x_dram, ln_scr)
        ch.seal()
    barrier(kb)

    with ExitStack() as st:
        ch = Chain(kb, st, "l0a_qkv")
        s_all = kb.sb(st, "sall", [128, 3], F32)
        ch.dma(nc.sync, lambda: nc.sync.dma_start(
            out=s_all[:, :], in_=sc_scr[0:1, 0:3].broadcast_to([128, 3])))
        wq = kb.sb(st, "wq", [128, 8 * 1024], BF16)
        wk = kb.sb(st, "wk", [128, 8 * 1024], BF16)
        wv = kb.sb(st, "wv", [128, 8 * 1024], BF16)
        lnm = kb.sb(st, "lnm", [128, 8 * 512], BF16)
        ob = kb.sb(st, "ob", [128, 1024], BF16)
        ps0 = kb.ps(st, "qk_ps", [128, 512])
        ps_v2 = kb.ps(st, "v_ps", [128, 1024])
        for wt, lo in ((wq, 0), (wk, 1024), (wv, 2048)):
            ch.dma(nc.sync, lambda wt=wt, lo=lo: nc.sync.dma_start(
                out=sb3(wt, 8, 1024),
                in_=wqkv_bf[:, lo:lo + 1024].rearrange("(d p) c -> p d c", p=128)))
        for m in range(8):
            ch.dma(nc.sync, lambda m=m: nc.sync.dma_start(
                out=sb3(lnm, 8, 512), in_=fm(ln_scr, m * 512, (m + 1) * 512)))
            for wt, si, dst in ((wq, 0, q_scr), (wk, 1, k_scr)):
                for oc in range(8):
                    for dd in range(8):
                        ch.op(nc.tensor, lambda wt=wt, oc=oc, dd=dd: nc.tensor.matmul(
                            ps0[:, :], wt[:, dd * 1024 + oc * 128:dd * 1024 + (oc + 1) * 128],
                            lnm[:, dd * 512:(dd + 1) * 512],
                            start=(dd == 0), stop=(dd == 7)), inc=(dd == 7))
                    ch.op(nc.vector, lambda si=si: nc.vector.tensor_single_scalar(
                        ob[:, 0:512], ps0[:, :], s_all[:, si:si + 1], ALU.mult))
                    ch.dma(nc.gpsimd, lambda dst=dst, oc=oc, m=m: nc.gpsimd.dma_start(
                        out=dst[oc * 128:(oc + 1) * 128, m * 512:(m + 1) * 512],
                        in_=ob[:, 0:512]))
            for mt in range(4):
                for dd in range(8):
                    for half in range(2):
                        ch.op(nc.tensor, lambda mt=mt, dd=dd, half=half: nc.tensor.matmul(
                            ps_v2[:, half * 512:(half + 1) * 512],
                            lnm[:, dd * 512 + mt * 128:dd * 512 + (mt + 1) * 128],
                            wv[:, dd * 1024 + half * 512:dd * 1024 + (half + 1) * 512],
                            start=(dd == 0), stop=(dd == 7)),
                            inc=(dd == 7 and half == 1))
                ch.op(nc.vector, lambda: nc.vector.tensor_single_scalar(
                    ob[:, 0:1024], ps_v2[:, :], s_all[:, 1:2], ALU.mult))
                ch.dma(nc.gpsimd, lambda m=m, mt=mt: nc.gpsimd.dma_start(
                    out=v_scr[m * 512 + mt * 128:m * 512 + (mt + 1) * 128, :],
                    in_=ob[:, 0:1024]))
        ch.seal()
    barrier(kb)

    with ExitStack() as st:
        df = DF(kb, st, "l0a_att")
        nc_ = nc
        ident_bf = kb.sb(st, "idb", [128, 128], BF16)
        ident_f = kb.sb(st, "idf", [128, 128], F32)
        tri = np.where(np.tril(np.ones((128, 128))) > 0, 0.0, NEG).astype(np.float32)
        trimask = kb.sb(st, "tri", [128, 128], F32)
        ones64f = kb.sb(st, "o64", [1, 64], F32)
        for t_, arr in ((ident_bf, np.eye(128, dtype=np.float32).astype(BF)),
                        (ident_f, np.eye(128, dtype=np.float32)),
                        (trimask, tri),
                        (ones64f, np.ones((1, 64), np.float32))):
            dr = kb.const_dram(arr)
            df.dma("sync", lambda t_=t_, dr=dr: nc_.sync.dma_start(
                out=t_[:, :], in_=dr), writes=[t_])
        qh = [kb.sb(st, f"qh{i}", [64, 4096], BF16) for i in range(2)]
        kh = [kb.sb(st, f"kh{i}", [64, 4096], BF16) for i in range(2)]
        vh = [kb.sb(st, f"vh{i}", [128, 32 * 64], BF16) for i in range(2)]
        p_sb = [kb.sb(st, f"p{i}", [128, 2560], BF16) for i in range(2)]
        pt_sb = [kb.sb(st, f"pt{i}", [128, 512], BF16) for i in range(2)]
        avh = kb.sb(st, "avh", [64, 4096], F32)
        avn = kb.sb(st, "avn", [64, 4096], BF16)
        lrec = kb.sb(st, "lrec", [128, 32], F32)
        mx = kb.sb(st, "mx", [128, 8], F32)
        mneg = kb.sb(st, "mneg", [128, 2], F32)
        lg = kb.sb(st, "lg", [128, 2], F32)
        sm1 = kb.sb(st, "sm1", [128, 1], F32)
        sm2 = kb.sb(st, "sm2", [128, 2], F32)
        arow = kb.sb(st, "arow", [1, 256], F32)
        ab_sb = kb.sb(st, "ab", [64, 256], F32)
        t_sb = kb.sb(st, "t1", [64, 128], F32)
        u_sb = kb.sb(st, "t2", [64, 128], F32)
        lrT = kb.sb(st, "lrT", [1, 512], F32)
        ps_s = kb.ps(st, "att_s", [128, 2560])
        pst = kb.ps(st, "att_t", [128, 1024], BF16)
        ps_av = kb.ps(st, "att_av", [128, 256])
        ps_m = kb.ps(st, "att_m", [128, 512])
        vh3 = [sb3(v, 32, 64) for v in vh]
        for h in range(16):
            hb = h % 2
            df.dma("sync", lambda h=h, hb=hb: nc_.sync.dma_start(
                out=qh[hb][:, :], in_=q_scr[64 * h:64 * h + 64, :]),
                writes=[qh[hb]])
            df.dma("sync", lambda h=h, hb=hb: nc_.sync.dma_start(
                out=kh[hb][:, :], in_=k_scr[64 * h:64 * h + 64, :]),
                writes=[kh[hb]])
            df.dma("sync", lambda h=h, hb=hb: nc_.sync.dma_start(
                out=vh3[hb], in_=v_scr[:, 64 * h:64 * h + 64].rearrange(
                    "(t p) c -> p t c", p=128)), writes=[vh[hb]])
            for i in range(32):
                jmax = 128 * (i + 1)
                nj = ceil_div(jmax, 512)
                groups = [(0, min(nj, 5))]
                if nj > 5:
                    groups.append((5, nj))
                for g, (c0, c1) in enumerate(groups):
                    gw = min(jmax, c1 * 512) - c0 * 512
                    pb = p_sb[g % 2]
                    for jc in range(c0, c1):
                        w = min(512, jmax - jc * 512)
                        sl = jc - c0
                        df.op("tensor", lambda i=i, jc=jc, sl=sl, w=w, hb=hb:
                              nc_.tensor.matmul(
                            ps_s[:, sl * 512:sl * 512 + w],
                            qh[hb][:, i * 128:(i + 1) * 128],
                            kh[hb][:, jc * 512:jc * 512 + w],
                            start=True, stop=True),
                            reads=[qh[hb], kh[hb]], writes=[("ss", sl)])
                        if jc == nj - 1:
                            pos = i * 128 - c0 * 512
                            df.op("vector", lambda pos=pos: nc_.vector.tensor_add(
                                ps_s[:, pos:pos + 128], ps_s[:, pos:pos + 128],
                                trimask[:, :]),
                                reads=[("ss", sl), trimask], writes=[("ss", sl)])
                        df.op("vector", lambda sl=sl, w=w: nc_.vector.reduce_max(
                            mx[:, sl:sl + 1], ps_s[:, sl * 512:sl * 512 + w],
                            AX.X), reads=[("ss", sl)], writes=[("mx", sl)])
                    ng = c1 - c0
                    df.op("vector", lambda g=g, ng=ng: nc_.vector.reduce_max(
                        mneg[:, g:g + 1], mx[:, 0:ng], AX.X, negate=True),
                        reads=[("mx", sl2) for sl2 in range(ng)],
                        writes=[("mneg", g)])
                    df.op("scalar", lambda g=g, gw=gw, pb=pb: nc_.scalar.activation(
                        pb[:, 0:gw], ps_s[:, 0:gw], AF.Exp,
                        bias=mneg[:, g:g + 1], accum_out=lg[:, g:g + 1]),
                        reads=[("ss", sl2) for sl2 in range(ng)] + [("mneg", g)],
                        writes=[pb, ("lg", g)])
                    njc = gw // 128
                    for b0 in range(0, njc, 4):
                        bn = min(4, njc - b0)
                        ptb = pt_sb[(b0 // 4) % 2]
                        for sj in range(bn):
                            df.op("tensor", lambda b0=b0, sj=sj, pb=pb:
                                  nc_.tensor.transpose(
                                pst[:, sj * 128:(sj + 1) * 128],
                                pb[:, (b0 + sj) * 128:(b0 + sj + 1) * 128],
                                ident_bf[:, :]),
                                reads=[pb, ident_bf], writes=[pst],
                                inc=(sj == bn - 1))
                        df.op("vector", lambda bn=bn, ptb=ptb: nc_.vector.tensor_copy(
                            ptb[:, 0:bn * 128], pst[:, 0:bn * 128]),
                            reads=[pst], writes=[ptb])
                        for sj in range(bn):
                            jt = c0 * 4 + b0 + sj
                            first = (b0 + sj == 0)
                            last = (b0 + sj == njc - 1)
                            df.op("tensor", lambda jt=jt, sj=sj, g=g, first=first,
                                  last=last, hb=hb, ptb=ptb: nc_.tensor.matmul(
                                ps_av[0:64, g * 128:(g + 1) * 128],
                                vh3[hb][:, jt, :], ptb[:, sj * 128:(sj + 1) * 128],
                                start=first, stop=last),
                                reads=[vh[hb], ptb], writes=[("av", g)],
                                inc=last)
                if len(groups) == 1:
                    df.op("vector", lambda i=i: nc_.vector.tensor_copy(
                        avh[:, i * 128:(i + 1) * 128], ps_av[0:64, 0:128]),
                        reads=[("av", 0)], writes=[("avh", i)])
                    df.op("vector", lambda i=i: nc_.vector.tensor_copy(
                        lrec[:, i:i + 1], lg[:, 0:1]),
                        reads=[("lg", 0)], writes=[("lr", i), lrec])
                else:
                    df.op("vector", lambda: nc_.vector.tensor_tensor(
                        sm1[:, :], mneg[:, 0:1], mneg[:, 1:2], ALU.min),
                        reads=[("mneg", 0), ("mneg", 1)], writes=[sm1])
                    df.op("vector", lambda: nc_.vector.tensor_sub(
                        sm2[:, 0:2], sm1[:, :].broadcast_to([128, 2]),
                        mneg[:, 0:2]),
                        reads=[sm1, ("mneg", 0), ("mneg", 1)], writes=[sm2])
                    df.op("scalar", lambda: nc_.scalar.activation(
                        sm2[:, 0:2], sm2[:, 0:2], AF.Exp),
                        reads=[sm2], writes=[sm2])
                    df.op("vector", lambda: nc_.vector.tensor_mul(
                        lg[:, 0:2], lg[:, 0:2], sm2[:, 0:2]),
                        reads=[("lg", 0), ("lg", 1), sm2],
                        writes=[("lg", 0), ("lg", 1)])
                    df.op("vector", lambda i=i: nc_.vector.reduce_sum(
                        lrec[:, i:i + 1], lg[:, 0:2], AX.X),
                        reads=[("lg", 0), ("lg", 1)], writes=[("lr", i), lrec])
                    for g in range(2):
                        df.op("tensor", lambda g=g: nc_.tensor.transpose(
                            ps_m[0:1, g * 128:(g + 1) * 128], sm2[:, g:g + 1],
                            ident_f[:, :]),
                            reads=[sm2, ident_f], writes=[("pm", 0), ("pm", 1), ("pmb", 0), ("pmb", 1), ("pml", 0), ("pml", 1)],
                            inc=(g == 1))
                    df.op("vector", lambda: nc_.vector.tensor_copy(
                        arow[0:1, :], ps_m[0:1, 0:256]),
                        reads=[("pm", 0), ("pm", 1)], writes=[arow])
                    for g in range(2):
                        df.op("tensor", lambda g=g: nc_.tensor.matmul(
                            ps_m[0:64, 128 + g * 128:128 + (g + 1) * 128],
                            ones64f[:, :], arow[0:1, g * 128:(g + 1) * 128],
                            start=True, stop=True),
                            reads=[arow, ones64f], writes=[("pm", 0), ("pm", 1), ("pmb", 0), ("pmb", 1), ("pml", 0), ("pml", 1)],
                            inc=(g == 1))
                    df.op("vector", lambda: nc_.vector.tensor_copy(
                        ab_sb[:, :], ps_m[0:64, 128:384]),
                        reads=[("pmb", 0), ("pmb", 1)], writes=[ab_sb])
                    df.op("vector", lambda: nc_.vector.tensor_mul(
                        t_sb[:, :], ps_av[0:64, 0:128], ab_sb[:, 0:128]),
                        reads=[("av", 0), ab_sb], writes=[t_sb])
                    df.op("vector", lambda: nc_.vector.tensor_mul(
                        u_sb[:, :], ps_av[0:64, 128:256], ab_sb[:, 128:256]),
                        reads=[("av", 1), ab_sb], writes=[u_sb])
                    df.op("vector", lambda i=i: nc_.vector.tensor_add(
                        avh[:, i * 128:(i + 1) * 128], t_sb[:, :], u_sb[:, :]),
                        reads=[t_sb, u_sb], writes=[("avh", i)])
            df.op("vector", lambda: nc_.vector.reciprocal(lrec[:, :], lrec[:, :]),
                  reads=[("lr", i2) for i2 in range(32)], writes=[lrec])
            for it0 in range(0, 32, 4):
                for sj in range(4):
                    df.op("tensor", lambda it0=it0, sj=sj: nc_.tensor.transpose(
                        ps_m[0:1, sj * 128:(sj + 1) * 128],
                        lrec[:, it0 + sj:it0 + sj + 1], ident_f[:, :]),
                        reads=[lrec, ident_f], writes=[("pm", 0), ("pm", 1), ("pmb", 0), ("pmb", 1), ("pml", 0), ("pml", 1)],
                        inc=(sj == 3))
                df.op("vector", lambda: nc_.vector.tensor_copy(
                    lrT[0:1, :], ps_m[0:1, 0:512]),
                    reads=[("pml", 0)], writes=[lrT])
                for sj in range(4):
                    df.op("tensor", lambda sj=sj: nc_.tensor.matmul(
                        ps_m[0:64, sj * 128:(sj + 1) * 128], ones64f[:, :],
                        lrT[0:1, sj * 128:(sj + 1) * 128], start=True, stop=True),
                        reads=[lrT, ones64f], writes=[("pm", 0), ("pm", 1), ("pmb", 0), ("pmb", 1), ("pml", 0), ("pml", 1)],
                        inc=(sj == 3))
                df.op("vector", lambda it0=it0: nc_.vector.tensor_mul(
                    avn[:, it0 * 128:(it0 + 4) * 128],
                    avh[:, it0 * 128:(it0 + 4) * 128], ps_m[0:64, 0:512]),
                    reads=[("pml", 1)] + [("avh", it0 + k) for k in range(4)],
                    writes=[("avn", it0)])
            df.dma("gpsimd", lambda h=h: nc_.gpsimd.dma_start(
                out=avn_scr[64 * h:64 * h + 64, :], in_=avn[:, :]),
                reads=[("avn", it0) for it0 in range(0, 32, 4)], writes=[avn])
        df.seal()
    barrier(kb)

    with ExitStack() as st:
        ch = Chain(kb, st, "l0a_proj")
        s_all = kb.sb(st, "sall2", [128, 3], F32)
        ch.dma(nc.sync, lambda: nc.sync.dma_start(
            out=s_all[:, :], in_=sc_scr[0:1, 0:3].broadcast_to([128, 3])))
        wp = kb.sb(st, "wp", [128, 8 * 1024], BF16)
        avm = kb.sb(st, "avm", [128, 8 * 512], BF16)
        xt = kb.sb(st, "xt", [128, 512], F32)
        yb = kb.sb(st, "yb", [128, 512], F32)
        ps0 = kb.ps(st, "pj_ps", [128, 512])
        ch.dma(nc.sync, lambda: nc.sync.dma_start(
            out=sb3(wp, 8, 1024),
            in_=wproj_bf.rearrange("(d p) c -> p d c", p=128)))
        for m in range(8):
            ch.dma(nc.sync, lambda m=m: nc.sync.dma_start(
                out=sb3(avm, 8, 512), in_=fm(avn_scr, m * 512, (m + 1) * 512)))
            for oc in range(8):
                ch.dma(nc.sync, lambda oc=oc, m=m: nc.sync.dma_start(
                    out=xt[:, :],
                    in_=x_dram[oc * 128:(oc + 1) * 128, m * 512:(m + 1) * 512]))
                for dd in range(8):
                    ch.op(nc.tensor, lambda oc=oc, dd=dd: nc.tensor.matmul(
                        ps0[:, :], wp[:, dd * 1024 + oc * 128:dd * 1024 + (oc + 1) * 128],
                        avm[:, dd * 512:(dd + 1) * 512],
                        start=(dd == 0), stop=(dd == 7)), inc=(dd == 7))
                ch.op(nc.vector, lambda: nc.vector.scalar_tensor_tensor(
                    yb[:, :], ps0[:, :], s_all[:, 2:3], xt[:, :],
                    ALU.mult, ALU.add))
                ch.dma(nc.gpsimd, lambda oc=oc, m=m: nc.gpsimd.dma_start(
                    out=y_dram[oc * 128:(oc + 1) * 128, m * 512:(m + 1) * 512],
                    in_=yb[:, :]))
        ch.seal()
    barrier(kb)


def emit_gating_topk(kb, ch, st, ln_scr, wg_f, cw_scr, nex, topk_norm_eps=None,
                     bias_row=None):
    nc = kb.nc
    wg_sb = kb.sb(st, "gt_wg", [nex, 1024], BF16)
    wgT = kb.sb(st, "gt_wgT", [128, 8 * nex], BF16)
    lnm = kb.sb(st, "gt_ln", [128, 8 * 512], BF16)
    gate = kb.sb(st, "gt_g", [128, nex], F32)
    mg = kb.sb(st, "gt_m", [128, 1], F32)
    ls = kb.sb(st, "gt_l", [128, 1], F32)
    m1 = kb.sb(st, "gt_m1", [128, 1], F32)
    msk = kb.sb(st, "gt_msk", [128, nex], F32)
    gm = kb.sb(st, "gt_gm", [128, nex], F32)
    cwT = kb.sb(st, "gt_cwT", [nex, 4096], F32)
    ident_bf = kb.const_sb(st, ch, np.eye(128, dtype=np.float32).astype(BF), "idb")
    ident_f = kb.const_sb(st, ch, np.eye(128, dtype=np.float32), "idf")
    ps_g = kb.ps(st, "gt_ps", [128, 512])
    ps_t = kb.ps(st, "gt_pst", [128, 512], BF16)
    ps_tf = kb.ps(st, "gt_ptf", [128, 512])
    if bias_row is not None:
        gb_b = kb.sb(st, "gt_bb", [128, nex], F32)
        ch.dma(nc.sync, lambda: nc.sync.dma_start(
            out=gb_b[:, :], in_=bias_row.broadcast_to([128, nex])))
    ch.dma(nc.gpsimd, lambda: nc.gpsimd.dma_start(out=wg_sb[:, :], in_=wg_f))
    for c in range(8):
        ch.op(nc.tensor, lambda c=c: nc.tensor.transpose(
            ps_t[0:128, 0:nex], wg_sb[0:nex, c * 128:(c + 1) * 128],
            ident_bf[0:nex, 0:nex]))
        ch.op(nc.vector, lambda c=c: nc.vector.tensor_copy(
            wgT[:, c * nex:(c + 1) * nex], ps_t[0:128, 0:nex]))
    for m in range(8):
        ch.dma(nc.sync, lambda m=m: nc.sync.dma_start(
            out=sb3(lnm, 8, 512), in_=fm(ln_scr, m * 512, (m + 1) * 512)))
        for mt in range(4):
            for dd in range(8):
                ch.op(nc.tensor, lambda mt=mt, dd=dd: nc.tensor.matmul(
                    ps_g[:, 0:nex],
                    lnm[:, dd * 512 + mt * 128:dd * 512 + (mt + 1) * 128],
                    wgT[:, dd * nex:(dd + 1) * nex],
                    start=(dd == 0), stop=(dd == 7)), inc=(dd == 7))
            if bias_row is not None:
                ch.op(nc.vector, lambda: nc.vector.tensor_add(
                    ps_g[:, 0:nex], ps_g[:, 0:nex], gb_b[:, :]))
            ch.op(nc.vector, lambda: nc.vector.reduce_max(
                mg[:, :], ps_g[:, 0:nex], AX.X, negate=True))
            ch.op(nc.scalar, lambda: nc.scalar.activation(
                gate[:, :], ps_g[:, 0:nex], AF.Exp, bias=mg[:, :],
                accum_out=ls[:, :]))
            ch.op(nc.vector, lambda: nc.vector.reciprocal(ls[:, :], ls[:, :]))
            ch.op(nc.vector, lambda: nc.vector.tensor_single_scalar(
                gate[:, :], gate[:, :], ls[:, :], ALU.mult))
            ch.op(nc.vector, lambda: nc.vector.reduce_max(m1[:, :], gate[:, :], AX.X))
            ch.op(nc.vector, lambda: nc.vector.tensor_single_scalar(
                msk[:, :], gate[:, :], m1[:, :], ALU.is_equal))
            ch.op(nc.vector, lambda: nc.vector.scalar_tensor_tensor(
                gm[:, :], msk[:, :], NEG, gate[:, :], ALU.mult, ALU.add))
            ch.op(nc.vector, lambda: nc.vector.reduce_max(m1[:, :], gm[:, :], AX.X))
            ch.op(nc.vector, lambda: nc.vector.tensor_single_scalar(
                msk[:, :], gate[:, :], m1[:, :], ALU.is_ge))
            ch.op(nc.vector, lambda: nc.vector.tensor_mul(
                gate[:, :], gate[:, :], msk[:, :]))
            if topk_norm_eps is not None:
                ch.op(nc.vector, lambda: nc.vector.reduce_sum(
                    ls[:, :], gate[:, :], AX.X))
                ch.op(nc.vector, lambda: nc.vector.tensor_scalar_add(
                    ls[:, :], ls[:, :], topk_norm_eps))
                ch.op(nc.vector, lambda: nc.vector.reciprocal(ls[:, :], ls[:, :]))
                ch.op(nc.vector, lambda: nc.vector.tensor_single_scalar(
                    gate[:, :], gate[:, :], ls[:, :], ALU.mult))
            ch.op(nc.tensor, lambda: nc.tensor.transpose(
                ps_tf[0:nex, 0:128], gate[:, 0:nex], ident_f[:, :]))
            ch.op(nc.vector, lambda m=m, mt=mt: nc.vector.tensor_copy(
                cwT[:, m * 512 + mt * 128:m * 512 + (mt + 1) * 128],
                ps_tf[0:nex, 0:128]))
    ch.dma(nc.gpsimd, lambda: nc.gpsimd.dma_start(out=cw_scr, in_=cwT[:, :]))


def emit_l0_ffn(kb, x_dram, wg_f, w1_f, b1t_f, w2_f, b2_f, y_dram):
    nc = kb.nc
    w1_bf = kb.dscr("l0f_w1", [8, 1024, 4096], BF16)
    w2_bf = kb.dscr("l0f_w2", [8, 4096, 1024], BF16)
    ln_scr = kb.dscr("l0f_ln", [1024, 4096], BF16)
    cw_scr = kb.dscr("l0f_cw", [8, 4096], F32)
    with ExitStack() as st:
        ch = Chain(kb, st, "l0f_pre")
        for e in range(8):
            ch.dma(nc.gpsimd, lambda e=e: nc.gpsimd.dma_start(
                out=w1_bf[e], in_=w1_f[e]))
            ch.dma(nc.gpsimd, lambda e=e: nc.gpsimd.dma_start(
                out=w2_bf[e], in_=w2_f[e]))
        ch.dma(nc.gpsimd, lambda: nc.gpsimd.dma_start(out=y_dram, in_=x_dram))
        emit_ln(kb, ch, st, x_dram, ln_scr)
        ch.seal()
    barrier(kb)
    with ExitStack() as st:
        ch = Chain(kb, st, "l0f_gate")
        emit_gating_topk(kb, ch, st, ln_scr, wg_f, cw_scr, 8)
        ch.seal()
    barrier(kb)
    with ExitStack() as st:
        df = DF(kb, st, "l0f_ex")
        w1e = kb.sb(st, "w1e", [128, 8 * 4096], BF16)
        w2e = kb.sb(st, "w2e", [128, 32 * 1024], BF16)
        lnm = [kb.sb(st, f"lnm{i}", [128, 8 * 512], BF16) for i in range(2)]
        hblk = kb.sb(st, "hblk", [128, 32 * 512], BF16)
        gbuf = [kb.sb(st, f"gbuf{i}", [128, 512], BF16) for i in range(2)]
        cwbe = kb.sb(st, "cwbe", [128, 4096], BF16)
        b1e = kb.sb(st, "b1e", [128, 32], F32)
        b2_bf = kb.sb(st, "b2bf", [8, 1024], BF16)
        cwT_bf = kb.sb(st, "cwtbf", [8, 4096], BF16)
        tmpo = [kb.sb(st, f"tmpo{i}", [128, 512], F32) for i in range(2)]
        ps1 = [kb.ps(st, f"ffn_ps1{i}", [128, 512]) for i in range(2)]
        ps2 = [kb.ps(st, f"ffn_ps2{i}", [128, 512]) for i in range(2)]
        df.dma("gpsimd", lambda: nc.gpsimd.dma_start(out=b2_bf[:, :], in_=b2_f),
               writes=[b2_bf])
        df.dma("gpsimd", lambda: nc.gpsimd.dma_start(out=cwT_bf[:, :], in_=cw_scr),
               writes=[cwT_bf])
        for e in range(8):
            df.dma("sync", lambda e=e: nc.sync.dma_start(
                out=sb3(w1e, 8, 4096),
                in_=w1_bf[e].rearrange("(d p) c -> p d c", p=128)), writes=[w1e])
            df.dma("sync", lambda e=e: nc.sync.dma_start(
                out=sb3(w2e, 32, 1024),
                in_=w2_bf[e].rearrange("(d p) c -> p d c", p=128)), writes=[w2e])
            df.dma("gpsimd", lambda e=e: nc.gpsimd.dma_start(
                out=cwbe[:, :], in_=cw_scr[e:e + 1, :].broadcast_to([128, 4096])),
                writes=[cwbe])
            df.dma("sync", lambda e=e: nc.sync.dma_start(
                out=b1e[:, :], in_=b1t_f[e]), writes=[b1e])
            for m in range(8):
                lb = lnm[m % 2]
                df.dma("sync", lambda m=m, lb=lb: nc.sync.dma_start(
                    out=sb3(lb, 8, 512), in_=fm(ln_scr, m * 512, (m + 1) * 512)),
                    writes=[lb])
                for hc in range(32):
                    pb = ps1[hc % 2]
                    for dd in range(8):
                        df.op("tensor", lambda hc=hc, dd=dd, pb=pb, lb=lb:
                              nc.tensor.matmul(
                            pb[:, :],
                            w1e[:, dd * 4096 + hc * 128:dd * 4096 + (hc + 1) * 128],
                            lb[:, dd * 512:(dd + 1) * 512],
                            start=(dd == 0), stop=(dd == 7)),
                            reads=[w1e, lb], writes=[pb], inc=(dd == 7))
                    gb = gbuf[hc % 2]
                    df.op("scalar", lambda hc=hc, pb=pb, gb=gb: nc.scalar.activation(
                        gb[:, :], pb[:, :], AF.Gelu, bias=b1e[:, hc:hc + 1]),
                        reads=[pb, b1e], writes=[gb])
                    df.op("vector", lambda hc=hc, m=m, gb=gb: nc.vector.tensor_mul(
                        hblk[:, hc * 512:(hc + 1) * 512], gb[:, :],
                        cwbe[:, m * 512:(m + 1) * 512]),
                        reads=[gb, cwbe], writes=[(hblk, hc)])
                for oc in range(8):
                    qb = ps2[oc % 2]
                    if e == 0:
                        df.op("tensor", lambda oc=oc, m=m, qb=qb: nc.tensor.matmul(
                            qb[:, :], b2_bf[:, oc * 128:(oc + 1) * 128],
                            cwT_bf[:, m * 512:(m + 1) * 512],
                            start=True, stop=False),
                            reads=[b2_bf, cwT_bf], writes=[qb], inc=False)
                    for hc in range(32):
                        df.op("tensor", lambda oc=oc, hc=hc, qb=qb, e=e:
                              nc.tensor.matmul(
                            qb[:, :],
                            w2e[:, hc * 1024 + oc * 128:hc * 1024 + (oc + 1) * 128],
                            hblk[:, hc * 512:(hc + 1) * 512],
                            start=(e != 0 and hc == 0), stop=(hc == 31)),
                            reads=[w2e, (hblk, hc)], writes=[qb], inc=(hc == 31))
                    tb = tmpo[oc % 2]
                    df.op("vector", lambda qb=qb, tb=tb: nc.vector.tensor_copy(
                        tb[:, :], qb[:, :]), reads=[qb], writes=[tb])
                    df.dma("gpsimd", lambda oc=oc, m=m, tb=tb: nc.gpsimd.dma_start(
                        out=y_dram[oc * 128:(oc + 1) * 128, m * 512:(m + 1) * 512],
                        in_=tb[:, :], accum_op=ALU.add),
                        reads=[tb, ("y", oc, m)], writes=[("y", oc, m)])
        df.seal()
    barrier(kb)


def _dft_consts():
    Mf = 8192
    i64 = np.int64
    b = np.arange(64, dtype=i64)[:, None]
    b2 = np.arange(128, dtype=i64)[:, None]
    c = np.arange(128, dtype=i64)[None, :]
    out = {}
    c1v = np.zeros((64, 64 * 128), np.float64)
    s1v = np.zeros((64, 64 * 128), np.float64)
    c1t = np.zeros((128, 64 * 128), np.float64)
    s1t = np.zeros((128, 64 * 128), np.float64)
    for a in range(64):
        th = 2 * np.pi * ((c * (64 * b + a)) % Mf) / Mf
        c1v[:, a * 128:(a + 1) * 128] = np.cos(th)
        s1v[:, a * 128:(a + 1) * 128] = -np.sin(th)
        th2 = 2 * np.pi * ((c * (64 * b2 + a)) % Mf) / Mf
        c1t[:, a * 128:(a + 1) * 128] = np.cos(th2)
        s1t[:, a * 128:(a + 1) * 128] = -np.sin(th2)
    out["c1v"], out["s1v"] = c1v, s1v
    out["c1t"], out["s1t"] = c1t, s1t
    ad = np.outer(np.arange(64), np.arange(64))
    C64 = np.cos(2 * np.pi * (ad % 64) / 64)
    S64 = np.sin(2 * np.pi * (ad % 64) / 64)
    out["ls2a"] = np.concatenate([C64, -S64], 1)
    out["ls2b"] = np.concatenate([S64, C64], 1)
    out["li1a"] = np.concatenate([C64, S64], 1)
    out["li1b"] = np.concatenate([-S64, C64], 1)
    ci2 = np.zeros((128, 64 * 64), np.float64)
    si2 = np.zeros((128, 64 * 64), np.float64)
    cc = np.arange(128, dtype=i64)[:, None]
    bb = np.arange(64, dtype=i64)[None, :]
    for a in range(64):
        th = 2 * np.pi * ((cc * (64 * bb + a)) % Mf) / Mf
        ci2[:, a * 64:(a + 1) * 64] = np.cos(th) / Mf
        si2[:, a * 64:(a + 1) * 64] = -np.sin(th) / Mf
    out["ci2"], out["si2"] = ci2, si2
    return {k: v.astype(np.float32).astype(BF) for k, v in out.items()}


def _rpe_pos_gam():
    lag = np.zeros(8192, np.float64)
    lag[1:4096] = np.arange(1, 4096)
    lag[4097:] = np.arange(4097, 8192) - 8192
    gam = 0.99 ** np.abs(lag)
    gam[4096] = 0.0
    pos = lag.copy()
    return (pos.reshape(64, 128).astype(np.float32),
            gam.reshape(64, 128).astype(np.float32))



def emit_l1_attn(kb, x_dram, uvw_f, ow_f, rpw_f, rlw_f, rlb_f, rfc_f,
                 rolw_f, rolb_f, row_f, y_dram):
    nc = kb.nc
    ln_scr = kb.dscr("l1_ln", [1024, 4096], BF16)
    u_scr = kb.dscr("l1_u", [4096, 1024], BF16)
    v_scr = kb.dscr("l1_v", [4096, 1024], BF16)
    t_scr = kb.dscr("l1_t", [8192, 1024], BF16)
    z_scr = kb.dscr("l1_z", [64, 2, 128, 1024], BF16)
    zt_scr = kb.dscr("l1_zt", [64, 2, 128, 1024], BF16)
    xv_scr = kb.dscr("l1_xv", [128, 128 * 1024], BF16)
    xt_scr = kb.dscr("l1_xt", [128, 128 * 1024], BF16)
    p_scr = kb.dscr("l1_p", [128, 128 * 1024], BF16)
    q_scr = kb.dscr("l1_q", [128, 128 * 1024], BF16)
    y_scr = kb.dscr("l1_y", [4096, 1024], BF16)
    prod_scr = kb.dscr("l1_prod", [4096, 1024], BF16)
    uvw_bf = kb.dscr("l1_uvw", [1024, 2048], BF16)
    ow_bf = kb.dscr("l1_ow", [1024, 1024], BF16)

    with ExitStack() as st:
        ch = Chain(kb, st, "l1_a")
        emit_cast(kb, ch, uvw_f, uvw_bf)
        emit_cast(kb, ch, ow_f, ow_bf)
        emit_ln(kb, ch, st, x_dram, ln_scr)
        ch.seal()
    barrier(kb)

    with ExitStack() as st:
        ch = Chain(kb, st, "l1_b")
        uvw_sb = kb.sb(st, "uvw", [128, 8 * 2048], BF16)
        lnm = kb.sb(st, "lnm", [128, 8 * 512], BF16)
        sil = kb.sb(st, "sil", [128, 2048], BF16)
        ps_uv = kb.ps(st, "uv_ps", [128, 2048])
        ch.dma(nc.sync, lambda: nc.sync.dma_start(
            out=sb3(uvw_sb, 8, 2048),
            in_=uvw_bf.rearrange("(d p) c -> p d c", p=128)))
        for m in range(8):
            ch.dma(nc.sync, lambda m=m: nc.sync.dma_start(
                out=sb3(lnm, 8, 512), in_=fm(ln_scr, m * 512, (m + 1) * 512)))
            for mt in range(4):
                for dd in range(8):
                    for q4 in range(4):
                        ch.op(nc.tensor, lambda mt=mt, dd=dd, q4=q4: nc.tensor.matmul(
                            ps_uv[:, q4 * 512:(q4 + 1) * 512],
                            lnm[:, dd * 512 + mt * 128:dd * 512 + (mt + 1) * 128],
                            uvw_sb[:, dd * 2048 + q4 * 512:dd * 2048 + (q4 + 1) * 512],
                            start=(dd == 0), stop=(dd == 7)),
                            inc=(dd == 7 and q4 == 3))
                ch.op(nc.scalar, lambda: nc.scalar.activation(
                    sil[:, :], ps_uv[:, :], AF.Silu))
                r0 = m * 512 + mt * 128
                ch.dma(nc.gpsimd, lambda r0=r0: nc.gpsimd.dma_start(
                    out=u_scr[r0:r0 + 128, :], in_=sil[:, 0:1024]))
                ch.dma(nc.gpsimd, lambda r0=r0: nc.gpsimd.dma_start(
                    out=v_scr[r0:r0 + 128, :], in_=sil[:, 1024:2048]))
        ch.seal()
    barrier(kb)

    with ExitStack() as st:
        ch = Chain(kb, st, "l1_c")
        POS, GAM = _rpe_pos_gam()
        pos_d = kb.const_dram(POS, "l1pos")
        gam_d = kb.const_dram(GAM, "l1gam")
        eps_sb = kb.const_sb(st, ch, np.full((128, 1), 1e-5, np.float32), "eps1")
        ident_f = kb.const_sb(st, ch, np.eye(128, dtype=np.float32), "idf1")
        posw_b = kb.sb(st, "poswb", [128, 32], F32)
        ch.dma(nc.sync, lambda: nc.sync.dma_start(
            out=posw_b[:, :], in_=rpw_f.broadcast_to([128, 32])))
        lnw_b = kb.sb(st, "lnwb", [128, 3 * 32], F32)
        lnb_b = kb.sb(st, "lnbb", [128, 3 * 32], F32)
        for i in range(3):
            ch.dma(nc.sync, lambda i=i: nc.sync.dma_start(
                out=lnw_b[:, i * 32:(i + 1) * 32],
                in_=rlw_f[i:i + 1, :].broadcast_to([128, 32])))
            ch.dma(nc.sync, lambda i=i: nc.sync.dma_start(
                out=lnb_b[:, i * 32:(i + 1) * 32],
                in_=rlb_f[i:i + 1, :].broadcast_to([128, 32])))
        olw_b = kb.sb(st, "olwb", [128, 32], F32)
        olb_b = kb.sb(st, "olbb", [128, 32], F32)
        ch.dma(nc.sync, lambda: nc.sync.dma_start(
            out=olw_b[:, :], in_=rolw_f.unsqueeze(0).broadcast_to([128, 32])))
        ch.dma(nc.sync, lambda: nc.sync.dma_start(
            out=olb_b[:, :], in_=rolb_f.unsqueeze(0).broadcast_to([128, 32])))
        fc_sb = kb.sb(st, "fcsb", [32, 3 * 32], BF16)
        for i in range(3):
            ch.dma(nc.gpsimd, lambda i=i: nc.gpsimd.dma_start(
                out=fc_sb[:, i * 32:(i + 1) * 32], in_=rfc_f[i]))
        outw_bf = kb.sb(st, "outwb", [32, 1024], BF16)
        ch.dma(nc.gpsimd, lambda: nc.gpsimd.dma_start(out=outw_bf[:, :], in_=row_f))
        pos_col = kb.sb(st, "posc", [128, 1], F32)
        gam_col = kb.sb(st, "gamc", [128, 1], F32)
        h_sb = kb.sb(st, "h", [128, 32], F32)
        hc_sb = kb.sb(st, "hc", [128, 32], F32)
        hsq = kb.sb(st, "hsq", [128, 32], F32)
        s1 = kb.sb(st, "s1", [128, 1], F32)
        rstd = kb.sb(st, "rstd", [128, 1], F32)
        hT = kb.sb(st, "hT", [32, 128], BF16)
        tb = kb.sb(st, "tb", [128, 1024], BF16)
        ps_t = kb.ps(st, "rpe_pt", [128, 128])
        ps_h = kb.ps(st, "rpe_ph", [128, 128])
        ps_o = kb.ps(st, "rpe_po", [128, 1024])

        def rpe_ln_silu_T(wb, bb, io):
            ch.op(nc.vector, lambda: nc.vector.reduce_sum(s1[:, :], h_sb[:, :], AX.X))
            ch.op(nc.vector, lambda: nc.vector.tensor_scalar_mul(
                s1[:, :], s1[:, :], 1.0 / 32))
            ch.op(nc.vector, lambda: nc.vector.tensor_single_scalar(
                hc_sb[:, :], h_sb[:, :], s1[:, :], ALU.subtract))
            ch.op(nc.scalar, lambda: nc.scalar.activation(
                hsq[:, :], hc_sb[:, :], AF.Square))
            ch.op(nc.vector, lambda: nc.vector.reduce_sum(
                rstd[:, :], hsq[:, :], AX.X))
            ch.op(nc.scalar, lambda: nc.scalar.activation(
                rstd[:, :], rstd[:, :], AF.Ln, bias=eps_sb[:, :], scale=1.0 / 32))
            ch.op(nc.scalar, lambda: nc.scalar.activation(
                rstd[:, :], rstd[:, :], AF.Exp, scale=-0.5))
            ch.op(nc.vector, lambda: nc.vector.tensor_single_scalar(
                hc_sb[:, :], hc_sb[:, :], rstd[:, :], ALU.mult))
            ch.op(nc.vector, lambda wb=wb: nc.vector.tensor_mul(
                hc_sb[:, :], hc_sb[:, :], wb))
            ch.op(nc.vector, lambda bb=bb: nc.vector.tensor_add(
                hc_sb[:, :], hc_sb[:, :], bb))
            ch.op(nc.scalar, lambda: nc.scalar.activation(
                hc_sb[:, :], hc_sb[:, :], AF.Silu))
            ch.op(nc.tensor, lambda: nc.tensor.transpose(
                ps_t[0:32, 0:128], hc_sb[:, 0:32], ident_f[:, :]))
            ch.op(nc.vector, lambda: nc.vector.tensor_copy(
                hT[:, :], ps_t[0:32, 0:128]))

        for t in range(64):
            ch.dma(nc.sync, lambda t=t: nc.sync.dma_start(
                out=pos_col[:, :], in_=pos_d[t:t + 1, :].rearrange("a c -> c a")))
            ch.dma(nc.sync, lambda t=t: nc.sync.dma_start(
                out=gam_col[:, :], in_=gam_d[t:t + 1, :].rearrange("a c -> c a")))
            ch.op(nc.vector, lambda: nc.vector.tensor_mul(
                h_sb[:, :], pos_col[:, :].broadcast_to([128, 32]), posw_b[:, :]))
            for i in range(3):
                rpe_ln_silu_T(lnw_b[:, i * 32:(i + 1) * 32],
                              lnb_b[:, i * 32:(i + 1) * 32], i)
                ch.op(nc.tensor, lambda i=i: nc.tensor.matmul(
                    ps_h[:, 0:32], hT[:, :], fc_sb[:, i * 32:(i + 1) * 32],
                    start=True, stop=True))
                ch.op(nc.vector, lambda: nc.vector.tensor_copy(
                    h_sb[:, :], ps_h[:, 0:32]))
            rpe_ln_silu_T(olw_b[:, :], olb_b[:, :], 3)
            for q in range(2):
                ch.op(nc.tensor, lambda q=q: nc.tensor.matmul(
                    ps_o[:, q * 512:(q + 1) * 512], hT[:, :],
                    outw_bf[:, q * 512:(q + 1) * 512],
                    start=True, stop=True), inc=(q == 1))
            ch.op(nc.vector, lambda: nc.vector.tensor_single_scalar(
                tb[:, :], ps_o[:, :], gam_col[:, :], ALU.mult))
            ch.dma(nc.gpsimd, lambda t=t: nc.gpsimd.dma_start(
                out=t_scr[t * 128:(t + 1) * 128, :], in_=tb[:, :]))
        ch.seal()
    barrier(kb)

    dft = _dft_consts()
    with ExitStack() as st:
        ch = Chain(kb, st, "l1_d")
        c1v = kb.const_sb(st, ch, dft["c1v"], "c1v")
        s1v = kb.const_sb(st, ch, dft["s1v"], "s1v")
        c1t = kb.const_sb(st, ch, dft["c1t"], "c1t")
        s1t = kb.const_sb(st, ch, dft["s1t"], "s1t")
        v1 = kb.sb(st, "v1", [64, 64 * 512], BF16)
        t1 = kb.sb(st, "t1", [128, 64 * 512], BF16)
        ev = kb.sb(st, "ev", [128, 1024], BF16)
        ps_re = kb.ps(st, "s1_re", [128, 512])
        ps_im = kb.ps(st, "s1_im", [128, 512])
        vv = v_scr.rearrange("(b a) ch -> b a ch", a=64)
        tt = t_scr.rearrange("(b a) ch -> b a ch", a=64)
        for cc in range(2):
            ch.dma(nc.sync, lambda cc=cc: nc.sync.dma_start(
                out=sb3(v1, 64, 512), in_=vv[:, :, cc * 512:(cc + 1) * 512]))
            for a in range(64):
                ch.op(nc.tensor, lambda a=a: nc.tensor.matmul(
                    ps_re[:, :], c1v[:, a * 128:(a + 1) * 128],
                    v1[:, a * 512:(a + 1) * 512], start=True, stop=True),
                    inc=False)
                ch.op(nc.tensor, lambda a=a: nc.tensor.matmul(
                    ps_im[:, :], s1v[:, a * 128:(a + 1) * 128],
                    v1[:, a * 512:(a + 1) * 512], start=True, stop=True))
                ch.op(nc.vector, lambda: nc.vector.tensor_copy(
                    ev[:, 0:512], ps_re[:, :]))
                ch.op(nc.scalar, lambda: nc.scalar.copy(
                    ev[:, 512:1024], ps_im[:, :]))
                ch.dma(nc.gpsimd, lambda a=a, cc=cc: nc.gpsimd.dma_start(
                    out=z_scr[a, 0, :, cc * 512:(cc + 1) * 512], in_=ev[:, 0:512]))
                ch.dma(nc.gpsimd, lambda a=a, cc=cc: nc.gpsimd.dma_start(
                    out=z_scr[a, 1, :, cc * 512:(cc + 1) * 512], in_=ev[:, 512:1024]))
        for cc in range(2):
            ch.dma(nc.sync, lambda cc=cc: nc.sync.dma_start(
                out=sb3(t1, 64, 512), in_=tt[:, :, cc * 512:(cc + 1) * 512]))
            for a in range(64):
                ch.op(nc.tensor, lambda a=a: nc.tensor.matmul(
                    ps_re[:, :], c1t[:, a * 128:(a + 1) * 128],
                    t1[:, a * 512:(a + 1) * 512], start=True, stop=True),
                    inc=False)
                ch.op(nc.tensor, lambda a=a: nc.tensor.matmul(
                    ps_im[:, :], s1t[:, a * 128:(a + 1) * 128],
                    t1[:, a * 512:(a + 1) * 512], start=True, stop=True))
                ch.op(nc.vector, lambda: nc.vector.tensor_copy(
                    ev[:, 0:512], ps_re[:, :]))
                ch.op(nc.scalar, lambda: nc.scalar.copy(
                    ev[:, 512:1024], ps_im[:, :]))
                ch.dma(nc.gpsimd, lambda a=a, cc=cc: nc.gpsimd.dma_start(
                    out=zt_scr[a, 0, :, cc * 512:(cc + 1) * 512], in_=ev[:, 0:512]))
                ch.dma(nc.gpsimd, lambda a=a, cc=cc: nc.gpsimd.dma_start(
                    out=zt_scr[a, 1, :, cc * 512:(cc + 1) * 512], in_=ev[:, 512:1024]))
        ch.seal()
    barrier(kb)

    with ExitStack() as st:
        ch = Chain(kb, st, "l1_e")
        ls2a = kb.const_sb(st, ch, dft["ls2a"], "ls2a")
        ls2b = kb.const_sb(st, ch, dft["ls2b"], "ls2b")
        zre = kb.sb(st, "zre", [64, 128 * 256], BF16)
        zim = kb.sb(st, "zim", [64, 128 * 256], BF16)
        ev = kb.sb(st, "ev2", [128, 512], BF16)
        ps = kb.ps(st, "s2_ps", [128, 512])
        for src, dst in ((z_scr, xv_scr), (zt_scr, xt_scr)):
            dst3 = dst.rearrange("p (c ch) -> p c ch", ch=1024)
            for cc in range(4):
                ch.dma(nc.sync, lambda src=src, cc=cc: nc.sync.dma_start(
                    out=zre.rearrange("p (c ch) -> p c ch", ch=256),
                    in_=src[:, 0, :, cc * 256:(cc + 1) * 256]))
                ch.dma(nc.sync, lambda src=src, cc=cc: nc.sync.dma_start(
                    out=zim.rearrange("p (c ch) -> p c ch", ch=256),
                    in_=src[:, 1, :, cc * 256:(cc + 1) * 256]))
                for n in range(64):
                    ch.op(nc.tensor, lambda n=n: nc.tensor.matmul(
                        ps[:, :], ls2a[:, :], zre[:, n * 512:(n + 1) * 512],
                        start=True, stop=False), inc=False)
                    ch.op(nc.tensor, lambda n=n: nc.tensor.matmul(
                        ps[:, :], ls2b[:, :], zim[:, n * 512:(n + 1) * 512],
                        start=False, stop=True))
                    ch.op(nc.vector, lambda: nc.vector.tensor_copy(ev[:, :], ps[:, :]))
                    c0 = n * 2
                    ch.dma(nc.gpsimd, lambda dst3=dst3, c0=c0, cc=cc:
                           nc.gpsimd.dma_start(
                        out=dst3[:, c0:c0 + 2, cc * 256:(cc + 1) * 256],
                        in_=sb3(ev, 2, 256)))
        ch.seal()
    barrier(kb)

    with ExitStack() as st:
        ch = Chain(kb, st, "l1_f")
        xv_sb = kb.sb(st, "xvs", [128, 4096], BF16)
        xt_sb = kb.sb(st, "xts", [128, 4096], BF16)
        pp = kb.sb(st, "pp", [128, 4096], BF16)
        tm1 = kb.sb(st, "tm1", [64, 4096], BF16)
        tm2 = kb.sb(st, "tm2", [64, 4096], BF16)
        vim = kb.sb(st, "vimlo", [64, 4096], BF16)
        tim = kb.sb(st, "timlo", [64, 4096], BF16)
        for ck in range(32):
            sl = slice(ck * 4096, (ck + 1) * 4096)
            ch.dma(nc.sync, lambda sl=sl: nc.sync.dma_start(
                out=xv_sb[:, :], in_=xv_scr[:, sl]))
            ch.dma(nc.sync, lambda sl=sl: nc.sync.dma_start(
                out=xt_sb[:, :], in_=xt_scr[:, sl]))
            ch.op(nc.vector, lambda: nc.vector.tensor_copy(
                vim[:, :], xv_sb[64:128, :]))
            ch.op(nc.vector, lambda: nc.vector.tensor_copy(
                tim[:, :], xt_sb[64:128, :]))
            ch.op(nc.vector, lambda: nc.vector.tensor_mul(
                tm1[:, :], xv_sb[0:64, :], xt_sb[0:64, :]))
            ch.op(nc.vector, lambda: nc.vector.tensor_mul(
                tm2[:, :], vim[:, :], tim[:, :]))
            ch.op(nc.vector, lambda: nc.vector.tensor_sub(
                pp[0:64, :], tm1[:, :], tm2[:, :]))
            ch.op(nc.vector, lambda: nc.vector.tensor_mul(
                tm1[:, :], xv_sb[0:64, :], tim[:, :]))
            ch.op(nc.vector, lambda: nc.vector.tensor_mul(
                tm2[:, :], vim[:, :], xt_sb[0:64, :]))
            ch.op(nc.vector, lambda: nc.vector.tensor_add(
                pp[64:128, :], tm1[:, :], tm2[:, :]))
            ch.dma(nc.gpsimd, lambda sl=sl: nc.gpsimd.dma_start(
                out=p_scr[:, sl], in_=pp[:, :]))
        ch.seal()
    barrier(kb)

    with ExitStack() as st:
        ch = Chain(kb, st, "l1_g")
        li1a = kb.const_sb(st, ch, dft["li1a"], "li1a")
        li1b = kb.const_sb(st, ch, dft["li1b"], "li1b")
        pp = kb.sb(st, "pg", [128, 4096], BF16)
        pim_lo = kb.sb(st, "pimlo", [64, 4096], BF16)
        ev = kb.sb(st, "evg", [128, 512], BF16)
        ps = kb.ps(st, "i1_ps", [128, 512])
        for ck in range(32):
            sl = slice(ck * 4096, (ck + 1) * 4096)
            ch.dma(nc.sync, lambda sl=sl: nc.sync.dma_start(
                out=pp[:, :], in_=p_scr[:, sl]))
            ch.op(nc.vector, lambda: nc.vector.tensor_copy(
                pim_lo[:, :], pp[64:128, :]))
            for n in range(8):
                ch.op(nc.tensor, lambda n=n: nc.tensor.matmul(
                    ps[:, :], li1a[:, :], pp[0:64, n * 512:(n + 1) * 512],
                    start=True, stop=False), inc=False)
                ch.op(nc.tensor, lambda n=n: nc.tensor.matmul(
                    ps[:, :], li1b[:, :], pim_lo[:, n * 512:(n + 1) * 512],
                    start=False, stop=True))
                ch.op(nc.vector, lambda: nc.vector.tensor_copy(ev[:, :], ps[:, :]))
                ch.dma(nc.gpsimd, lambda ck=ck, n=n: nc.gpsimd.dma_start(
                    out=q_scr[:, ck * 4096 + n * 512:ck * 4096 + (n + 1) * 512],
                    in_=ev[:, :]))
        ch.seal()
    barrier(kb)

    with ExitStack() as st:
        ch = Chain(kb, st, "l1_h")
        ci2 = kb.const_sb(st, ch, dft["ci2"], "ci2")
        si2 = kb.const_sb(st, ch, dft["si2"], "si2")
        qre = kb.sb(st, "qre", [128, 64 * 512], BF16)
        qim = kb.sb(st, "qim", [128, 64 * 512], BF16)
        ev = kb.sb(st, "evh", [64, 512], BF16)
        ps = kb.ps(st, "i2_ps", [128, 512])
        q4 = q_scr.rearrange("(pl a) (c ch) -> pl c a ch", pl=2, ch=1024)
        yv = y_scr.rearrange("(b a) ch -> a b ch", a=64)
        for cc in range(2):
            ch.dma(nc.sync, lambda cc=cc: nc.sync.dma_start(
                out=sb3(qre, 64, 512), in_=q4[0, :, :, cc * 512:(cc + 1) * 512]))
            ch.dma(nc.sync, lambda cc=cc: nc.sync.dma_start(
                out=sb3(qim, 64, 512), in_=q4[1, :, :, cc * 512:(cc + 1) * 512]))
            for a in range(64):
                ch.op(nc.tensor, lambda a=a: nc.tensor.matmul(
                    ps[0:64, :], ci2[:, a * 64:(a + 1) * 64],
                    qre[:, a * 512:(a + 1) * 512], start=True, stop=False),
                    inc=False)
                ch.op(nc.tensor, lambda a=a: nc.tensor.matmul(
                    ps[0:64, :], si2[:, a * 64:(a + 1) * 64],
                    qim[:, a * 512:(a + 1) * 512], start=False, stop=True))
                ch.op(nc.vector, lambda: nc.vector.tensor_copy(ev[:, :], ps[0:64, :]))
                ch.dma(nc.gpsimd, lambda a=a, cc=cc: nc.gpsimd.dma_start(
                    out=yv[a, :, cc * 512:(cc + 1) * 512], in_=ev[:, :]))
        ch.seal()
    barrier(kb)

    with ExitStack() as st:
        ch = Chain(kb, st, "l1_i")
        ub = kb.sb(st, "ub", [128, 4 * 1024], BF16)
        yb = kb.sb(st, "ybuf", [128, 4 * 1024], BF16)
        pb = kb.sb(st, "pbuf", [128, 4 * 1024], BF16)
        uu = u_scr.rearrange("(q p) ch -> p q ch", p=128)
        yy = y_scr.rearrange("(q p) ch -> p q ch", p=128)
        pr = prod_scr.rearrange("(q p) ch -> p q ch", p=128)
        for q0 in range(0, 32, 4):
            ch.dma(nc.sync, lambda q0=q0: nc.sync.dma_start(
                out=sb3(ub, 4, 1024), in_=uu[:, q0:q0 + 4, :]))
            ch.dma(nc.sync, lambda q0=q0: nc.sync.dma_start(
                out=sb3(yb, 4, 1024), in_=yy[:, q0:q0 + 4, :]))
            ch.op(nc.vector, lambda: nc.vector.tensor_mul(
                pb[:, :], ub[:, :], yb[:, :]))
            ch.dma(nc.gpsimd, lambda q0=q0: nc.gpsimd.dma_start(
                out=pr[:, q0:q0 + 4, :], in_=sb3(pb, 4, 1024)))
        ch.seal()
    barrier(kb)
    with ExitStack() as st:
        ch = Chain(kb, st, "l1_j")
        prodT = kb.sb(st, "prodT", [128, 8 * 4096], BF16)
        ow_sb = kb.sb(st, "ow", [128, 8 * 1024], BF16)
        xt = kb.sb(st, "xt1", [128, 512], F32)
        yo = kb.sb(st, "yo", [128, 512], F32)
        ps = kb.ps(st, "j_ps", [128, 512])
        for dd in range(8):
            ch.dma(nc.sync, lambda dd=dd: nc.sync.dma_start(
                out=prodT[:, dd * 4096:(dd + 1) * 4096],
                in_=prod_scr[:, dd * 128:(dd + 1) * 128], transpose=True))
        ch.dma(nc.sync, lambda: nc.sync.dma_start(
            out=sb3(ow_sb, 8, 1024),
            in_=ow_bf.rearrange("(d p) c -> p d c", p=128)))
        for m in range(8):
            for oc in range(8):
                ch.dma(nc.sync, lambda oc=oc, m=m: nc.sync.dma_start(
                    out=xt[:, :],
                    in_=x_dram[oc * 128:(oc + 1) * 128, m * 512:(m + 1) * 512]))
                for dd in range(8):
                    ch.op(nc.tensor, lambda oc=oc, dd=dd, m=m: nc.tensor.matmul(
                        ps[:, :], ow_sb[:, dd * 1024 + oc * 128:dd * 1024 + (oc + 1) * 128],
                        prodT[:, dd * 4096 + m * 512:dd * 4096 + (m + 1) * 512],
                        start=(dd == 0), stop=(dd == 7)), inc=(dd == 7))
                ch.op(nc.vector, lambda: nc.vector.tensor_add(
                    yo[:, :], ps[:, :], xt[:, :]))
                ch.dma(nc.gpsimd, lambda oc=oc, m=m: nc.gpsimd.dma_start(
                    out=y_dram[oc * 128:(oc + 1) * 128, m * 512:(m + 1) * 512],
                    in_=yo[:, :]))
        ch.seal()
    barrier(kb)


def emit_quantize_w(kb, ch, st, w_f, wq_bf, rows, cols, name, eps=1e-5):
    nc = kb.nc
    nch = rows // 128
    per = max(1, 4096 // cols)
    wld = kb.sb(st, f"q_w{name}", [128, per * cols], F32)
    junk = kb.sb(st, f"q_j{name}", [128, per * cols], F32)
    acc = kb.sb(st, f"q_a{name}", [128, 1], F32)
    accs = kb.sb(st, f"q_as{name}", [128, 1], F32)
    gam = kb.sb(st, f"q_g{name}", [1, 1], F32)
    rbc = kb.sb(st, f"q_rb{name}", [128, 1], F32)
    tmpf = kb.sb(st, f"q_tf{name}", [128, per * cols], F32)
    tmpi = kb.sb(st, f"q_ti{name}", [128, per * cols], I32)
    wqb = kb.sb(st, f"q_wq{name}", [128, per * cols], BF16)
    ones_f = kb.const_sb(st, ch, np.ones((128, 1), np.float32), f"q_o{name}")
    r_scr = kb.dscr(f"q_rs{name}", [1, 1], F32)
    w3 = w_f.rearrange("(n p) c -> n p c", p=128)
    wq3 = wq_bf.rearrange("(n p) c -> n p c", p=128)
    with ExitStack() as ls:
        ps = kb.ps(ls, f"q_ps{name}", [128, 512])
        ch.op(nc.vector, lambda: nc.vector.memset(accs[:, :], 0.0))
        for it in range(nch // per):
            ch.dma(nc.sync, lambda it=it: nc.sync.dma_start(
                out=sb3(wld, per, cols), in_=w3[it * per:(it + 1) * per]))
            ch.op(nc.scalar, lambda: nc.scalar.activation(
                junk[:, :], wld[:, :], AF.Abs, accum_out=acc[:, :]))
            ch.op(nc.vector, lambda: nc.vector.tensor_add(
                accs[:, :], accs[:, :], acc[:, :]))
        ch.op(nc.tensor, lambda: nc.tensor.matmul(
            ps[0:1, 0:1], ones_f[:, :], accs[:, :], start=True, stop=True))
        ch.op(nc.vector, lambda: nc.vector.tensor_scalar(
            gam[:, :], ps[0:1, 0:1], 1.0 / (rows * cols), eps, ALU.mult, ALU.add))
        ch.op(nc.vector, lambda: nc.vector.reciprocal(gam[:, :], gam[:, :]))
        ch.dma(nc.gpsimd, lambda: nc.gpsimd.dma_start(out=r_scr, in_=gam[:, :]))
        ch.dma(nc.sync, lambda: nc.sync.dma_start(
            out=rbc[:, :], in_=r_scr.broadcast_to([128, 1])))
        for it in range(nch // per):
            ch.dma(nc.sync, lambda it=it: nc.sync.dma_start(
                out=sb3(wld, per, cols), in_=w3[it * per:(it + 1) * per]))
            ch.op(nc.vector, lambda: nc.vector.tensor_single_scalar(
                tmpf[:, :], wld[:, :], rbc[:, :], ALU.mult))
            ch.op(nc.vector, lambda: nc.vector.tensor_copy(tmpi[:, :], tmpf[:, :]))
            ch.op(nc.vector, lambda: nc.vector.tensor_scalar(
                wqb[:, :], tmpi[:, :], -1.0, 1.0, ALU.max, ALU.min))
            ch.dma(nc.gpsimd, lambda it=it: nc.gpsimd.dma_start(
                out=wq3[it * per:(it + 1) * per], in_=sb3(wqb, per, cols)))


def emit_l1_ffn(kb, x_dram, w1_f, b1t_f, w2_f, b2t_f, y_dram):
    nc = kb.nc
    ln_scr = kb.dscr("l1f_ln", [1024, 4096], BF16)
    wq1 = kb.dscr("l1f_wq1", [4096, 1024], BF16)
    wq2 = kb.dscr("l1f_wq2", [1024, 4096], BF16)
    with ExitStack() as st:
        ch = Chain(kb, st, "l1f_q")
        emit_quantize_w(kb, ch, st, w1_f, wq1, 4096, 1024, "w1")
        emit_quantize_w(kb, ch, st, w2_f, wq2, 1024, 4096, "w2")
        emit_ln(kb, ch, st, x_dram, ln_scr)
        ch.seal()
    barrier(kb)
    with ExitStack() as st:
        ch = Chain(kb, st, "l1f_m")
        w1T = kb.sb(st, "w1T", [128, 8 * 4096], BF16)
        w2T = kb.sb(st, "w2T", [128, 32 * 1024], BF16)
        lnm = kb.sb(st, "lnm1f", [128, 8 * 512], BF16)
        hblk = kb.sb(st, "hblk1f", [128, 32 * 512], BF16)
        b1t = kb.sb(st, "b1t", [128, 32], F32)
        b2t = kb.sb(st, "b2t", [128, 8], F32)
        xt = kb.sb(st, "xt1f", [128, 512], F32)
        yb = kb.sb(st, "yb1f", [128, 512], F32)
        ps1 = kb.ps(st, "1f_ps1", [128, 512])
        ps2 = kb.ps(st, "1f_ps2", [128, 512])
        ch.dma(nc.sync, lambda: nc.sync.dma_start(out=b1t[:, :], in_=b1t_f))
        ch.dma(nc.sync, lambda: nc.sync.dma_start(out=b2t[:, :], in_=b2t_f))
        for dd in range(8):
            ch.dma(nc.sync, lambda dd=dd: nc.sync.dma_start(
                out=w1T[:, dd * 4096:(dd + 1) * 4096],
                in_=wq1[:, dd * 128:(dd + 1) * 128], transpose=True))
        for fc in range(32):
            ch.dma(nc.sync, lambda fc=fc: nc.sync.dma_start(
                out=w2T[:, fc * 1024:(fc + 1) * 1024],
                in_=wq2[:, fc * 128:(fc + 1) * 128], transpose=True))
        for m in range(8):
            ch.dma(nc.sync, lambda m=m: nc.sync.dma_start(
                out=sb3(lnm, 8, 512), in_=fm(ln_scr, m * 512, (m + 1) * 512)))
            for fc in range(32):
                for dd in range(8):
                    ch.op(nc.tensor, lambda fc=fc, dd=dd: nc.tensor.matmul(
                        ps1[:, :],
                        w1T[:, dd * 4096 + fc * 128:dd * 4096 + (fc + 1) * 128],
                        lnm[:, dd * 512:(dd + 1) * 512],
                        start=(dd == 0), stop=(dd == 7)), inc=(dd == 7))
                ch.op(nc.scalar, lambda fc=fc: nc.scalar.activation(
                    hblk[:, fc * 512:(fc + 1) * 512], ps1[:, :], AF.Relu,
                    bias=b1t[:, fc:fc + 1]))
            for oc in range(8):
                ch.dma(nc.sync, lambda oc=oc, m=m: nc.sync.dma_start(
                    out=xt[:, :],
                    in_=x_dram[oc * 128:(oc + 1) * 128, m * 512:(m + 1) * 512]))
                for fc in range(32):
                    ch.op(nc.tensor, lambda oc=oc, fc=fc: nc.tensor.matmul(
                        ps2[:, :],
                        w2T[:, fc * 1024 + oc * 128:fc * 1024 + (oc + 1) * 128],
                        hblk[:, fc * 512:(fc + 1) * 512],
                        start=(fc == 0), stop=(fc == 31)), inc=(fc == 31))
                ch.op(nc.vector, lambda oc=oc: nc.vector.scalar_tensor_tensor(
                    yb[:, :], ps2[:, :], b2t[:, oc:oc + 1], xt[:, :],
                    ALU.add, ALU.add))
                ch.dma(nc.gpsimd, lambda oc=oc, m=m: nc.gpsimd.dma_start(
                    out=y_dram[oc * 128:(oc + 1) * 128, m * 512:(m + 1) * 512],
                    in_=yb[:, :]))
        ch.seal()
    barrier(kb)


def emit_l2_attn(kb, x_dram, w_f, bt_f, brow_f, y_dram):
    nc = kb.nc
    w_bf = kb.dscr("l2_w", [1024, 1024], BF16)
    ln_scr = kb.dscr("l2_ln", [1024, 4096], BF16)
    f_scr = kb.dscr("l2_f", [1024, 4096], BF16)
    ft_scr = kb.dscr("l2_ft", [4096, 1024], BF16)
    avn_scr = kb.dscr("l2_avn", [1024, 4096], BF16)
    with ExitStack() as st:
        ch = Chain(kb, st, "l2_a")
        emit_cast(kb, ch, w_f, w_bf)
        emit_ln(kb, ch, st, x_dram, ln_scr)
        ch.seal()
    barrier(kb)
    with ExitStack() as st:
        ch = Chain(kb, st, "l2_f")
        ws = kb.sb(st, "l2ws", [128, 8 * 1024], BF16)
        lnm = kb.sb(st, "l2lnm", [128, 8 * 512], BF16)
        bt = kb.sb(st, "l2bt", [128, 8], F32)
        bb = kb.sb(st, "l2bb", [128, 1024], F32)
        ob = kb.sb(st, "l2ob", [128, 1024], BF16)
        ps0 = kb.ps(st, "l2_ps0", [128, 512])
        ps1 = kb.ps(st, "l2_ps1", [128, 1024])
        ch.dma(nc.sync, lambda: nc.sync.dma_start(
            out=sb3(ws, 8, 1024), in_=w_bf.rearrange("(d p) c -> p d c", p=128)))
        ch.dma(nc.sync, lambda: nc.sync.dma_start(out=bt[:, :], in_=bt_f))
        ch.dma(nc.sync, lambda: nc.sync.dma_start(
            out=bb[:, :], in_=brow_f.broadcast_to([128, 1024])))
        for m in range(8):
            ch.dma(nc.sync, lambda m=m: nc.sync.dma_start(
                out=sb3(lnm, 8, 512), in_=fm(ln_scr, m * 512, (m + 1) * 512)))
            for oc in range(8):
                for dd in range(8):
                    ch.op(nc.tensor, lambda oc=oc, dd=dd: nc.tensor.matmul(
                        ps0[:, :], ws[:, dd * 1024 + oc * 128:dd * 1024 + (oc + 1) * 128],
                        lnm[:, dd * 512:(dd + 1) * 512],
                        start=(dd == 0), stop=(dd == 7)), inc=(dd == 7))
                ch.op(nc.scalar, lambda oc=oc: nc.scalar.activation(
                    ob[:, 0:512], ps0[:, :], AF.Identity, bias=bt[:, oc:oc + 1]))
                ch.dma(nc.gpsimd, lambda oc=oc, m=m: nc.gpsimd.dma_start(
                    out=f_scr[oc * 128:(oc + 1) * 128, m * 512:(m + 1) * 512],
                    in_=ob[:, 0:512]))
            for mt in range(4):
                for dd in range(8):
                    for half in range(2):
                        ch.op(nc.tensor, lambda mt=mt, dd=dd, half=half: nc.tensor.matmul(
                            ps1[:, half * 512:(half + 1) * 512],
                            lnm[:, dd * 512 + mt * 128:dd * 512 + (mt + 1) * 128],
                            ws[:, dd * 1024 + half * 512:dd * 1024 + (half + 1) * 512],
                            start=(dd == 0), stop=(dd == 7)),
                            inc=(dd == 7 and half == 1))
                ch.op(nc.vector, lambda: nc.vector.tensor_add(
                    ob[:, 0:1024], ps1[:, :], bb[:, :]))
                r0 = m * 512 + mt * 128
                ch.dma(nc.gpsimd, lambda r0=r0: nc.gpsimd.dma_start(
                    out=ft_scr[r0:r0 + 128, :], in_=ob[:, 0:1024]))
        ch.seal()
    barrier(kb)
    with ExitStack() as st:
        ch = Chain(kb, st, "l2_att")
        ident_bf = kb.const_sb(st, ch, np.eye(128, dtype=np.float32).astype(BF), "idb2")
        ident_f = kb.const_sb(st, ch, np.eye(128, dtype=np.float32), "idf2")
        ones128f = kb.const_sb(st, ch, np.ones((1, 128), np.float32), "o128f")
        p_ = np.arange(128)[:, None]
        c_ = np.arange(148)[None, :]
        band = (0.01 * (np.minimum(np.abs(p_ - c_ + 10), 10) - 10)).astype(np.float32)
        band_sb = kb.const_sb(st, ch, band, "band")
        fT = kb.sb(st, "l2fT", [128, 8 * 4096], BF16)
        ftk = kb.sb(st, "l2ftk", [128, 32 * 1024], BF16)
        p_sb = kb.sb(st, "l2p", [128, 1536], BF16)
        pt_sb = kb.sb(st, "l2pt", [128, 1536], BF16)
        mneg = kb.sb(st, "l2mn", [128, 3], F32)
        lg = kb.sb(st, "l2lg", [128, 3], F32)
        sm1 = kb.sb(st, "l2sm1", [128, 1], F32)
        sm2 = kb.sb(st, "l2sm2", [128, 3], F32)
        arow = kb.sb(st, "l2ar", [1, 384], F32)
        avg_sb = kb.sb(st, "l2avg", [128, 3 * 1024], F32)
        acc1 = kb.sb(st, "l2acc", [128, 1024], F32)
        tmu = kb.sb(st, "l2tmu", [128, 1024], F32)
        alb = kb.sb(st, "l2alb", [128, 128], F32)
        avnb = kb.sb(st, "l2avnb", [128, 8 * 128], BF16)
        lrow = kb.sb(st, "l2lrow", [1, 128], F32)
        ps_s = kb.ps(st, "l2_s", [128, 1536])
        pst = kb.ps(st, "l2_t", [128, 1024], BF16)
        ps_av = kb.ps(st, "l2_av", [128, 1024])
        ps_m = kb.ps(st, "l2_m", [128, 512])
        for dd in range(8):
            ch.dma(nc.sync, lambda dd=dd: nc.sync.dma_start(
                out=fT[:, dd * 4096:(dd + 1) * 4096],
                in_=f_scr[dd * 128:(dd + 1) * 128, :]))
        ch.dma(nc.sync, lambda: nc.sync.dma_start(
            out=sb3(ftk, 32, 1024),
            in_=ft_scr.rearrange("(t p) c -> p t c", p=128)))
        GROUPS = [(0, 3), (3, 6), (6, 8)]
        for i in range(32):
            i0 = i * 128
            jlo, jhi = max(0, i0 - 10), min(4096, i0 + 138)
            for g, (c0, c1) in enumerate(GROUPS):
                gw = (c1 - c0) * 512
                glo, ghi = c0 * 512, c1 * 512
                for jc in range(c0, c1):
                    for dd in range(8):
                        ch.op(nc.tensor, lambda i=i, jc=jc, c0=c0, dd=dd:
                              nc.tensor.matmul(
                            ps_s[:, (jc - c0) * 512:(jc - c0 + 1) * 512],
                            fT[:, dd * 4096 + i * 128:dd * 4096 + (i + 1) * 128],
                            fT[:, dd * 4096 + jc * 512:dd * 4096 + (jc + 1) * 512],
                            start=(dd == 0), stop=(dd == 7)),
                            inc=(dd == 7 and jc == c1 - 1))
                blo, bhi = max(jlo, glo), min(jhi, ghi)
                if blo < bhi:
                    coff = blo - (i0 - 10)
                    ch.op(nc.vector, lambda blo=blo, bhi=bhi, coff=coff, glo=glo:
                          nc.vector.tensor_add(
                        ps_s[:, blo - glo:bhi - glo],
                        ps_s[:, blo - glo:bhi - glo],
                        band_sb[:, coff:coff + (bhi - blo)]))
                ch.op(nc.vector, lambda g=g, gw=gw: nc.vector.reduce_max(
                    mneg[:, g:g + 1], ps_s[:, 0:gw], AX.X, negate=True))
                ch.op(nc.scalar, lambda g=g, gw=gw: nc.scalar.activation(
                    p_sb[:, 0:gw], ps_s[:, 0:gw], AF.Exp,
                    bias=mneg[:, g:g + 1], accum_out=lg[:, g:g + 1]))
                njc = gw // 128
                for b0 in range(0, njc, 4):
                    bn = min(4, njc - b0)
                    for sj in range(bn):
                        ch.op(nc.tensor, lambda b0=b0, sj=sj: nc.tensor.transpose(
                            pst[:, sj * 128:(sj + 1) * 128],
                            p_sb[:, (b0 + sj) * 128:(b0 + sj + 1) * 128],
                            ident_bf[:, :]), inc=(sj == bn - 1))
                    ch.op(nc.vector, lambda b0=b0, bn=bn: nc.vector.tensor_copy(
                        pt_sb[:, b0 * 128:(b0 + bn) * 128], pst[:, 0:bn * 128]))
                for oc in range(8):
                    for jtl in range(njc):
                        jt = c0 * 4 + jtl
                        ch.op(nc.tensor, lambda jt=jt, jtl=jtl, oc=oc, njc=njc:
                              nc.tensor.matmul(
                            ps_av[:, oc * 128:(oc + 1) * 128],
                            ftk[:, jt * 1024 + oc * 128:jt * 1024 + (oc + 1) * 128],
                            pt_sb[:, jtl * 128:(jtl + 1) * 128],
                            start=(jtl == 0), stop=(jtl == njc - 1)),
                            inc=(jtl == njc - 1))
                ch.op(nc.vector, lambda g=g: nc.vector.tensor_copy(
                    avg_sb[:, g * 1024:(g + 1) * 1024], ps_av[:, :]))
            ch.op(nc.vector, lambda: nc.vector.tensor_tensor(
                sm1[:, :], mneg[:, 0:1], mneg[:, 1:2], ALU.min))
            ch.op(nc.vector, lambda: nc.vector.tensor_tensor(
                sm1[:, :], sm1[:, :], mneg[:, 2:3], ALU.min))
            ch.op(nc.vector, lambda: nc.vector.tensor_sub(
                sm2[:, 0:3], sm1[:, :].broadcast_to([128, 3]), mneg[:, 0:3]))
            ch.op(nc.scalar, lambda: nc.scalar.activation(
                sm2[:, 0:3], sm2[:, 0:3], AF.Exp))
            ch.op(nc.vector, lambda: nc.vector.tensor_mul(
                lg[:, 0:3], lg[:, 0:3], sm2[:, 0:3]))
            ch.op(nc.vector, lambda: nc.vector.reduce_sum(
                sm1[:, :], lg[:, 0:3], AX.X))
            ch.op(nc.vector, lambda: nc.vector.reciprocal(sm1[:, :], sm1[:, :]))
            for g in range(3):
                ch.op(nc.tensor, lambda g=g: nc.tensor.transpose(
                    ps_m[0:1, g * 128:(g + 1) * 128], sm2[:, g:g + 1],
                    ident_f[:, :]), inc=(g == 2))
            ch.op(nc.tensor, lambda: nc.tensor.transpose(
                ps_m[0:1, 384:512], sm1[:, :], ident_f[:, :]))
            ch.op(nc.vector, lambda: nc.vector.tensor_copy(
                arow[0:1, 0:384], ps_m[0:1, 0:384]))
            ch.op(nc.vector, lambda: nc.vector.tensor_copy(
                lrow[0:1, :], ps_m[0:1, 384:512]))
            for g in range(3):
                ch.op(nc.tensor, lambda g=g: nc.tensor.matmul(
                    ps_m[:, 0:128], ones128f[:, :],
                    arow[0:1, g * 128:(g + 1) * 128], start=True, stop=True))
                ch.op(nc.vector, lambda: nc.vector.tensor_copy(
                    alb[:, :], ps_m[:, 0:128]))
                if g == 0:
                    ch.op(nc.vector, lambda g=g: nc.vector.tensor_mul(
                        acc1[:, :].rearrange("p (o c) -> p o c", o=8),
                        avg_sb[:, g * 1024:(g + 1) * 1024].rearrange(
                            "p (o c) -> p o c", o=8),
                        alb[:, :].unsqueeze(1).broadcast_to([128, 8, 128])))
                else:
                    ch.op(nc.vector, lambda g=g: nc.vector.tensor_mul(
                        tmu[:, :].rearrange("p (o c) -> p o c", o=8),
                        avg_sb[:, g * 1024:(g + 1) * 1024].rearrange(
                            "p (o c) -> p o c", o=8),
                        alb[:, :].unsqueeze(1).broadcast_to([128, 8, 128])))
                    ch.op(nc.vector, lambda: nc.vector.tensor_add(
                        acc1[:, :], acc1[:, :], tmu[:, :]))
            ch.op(nc.tensor, lambda: nc.tensor.matmul(
                ps_m[:, 0:128], ones128f[:, :], lrow[0:1, :],
                start=True, stop=True))
            ch.op(nc.vector, lambda: nc.vector.tensor_copy(
                alb[:, :], ps_m[:, 0:128]))
            ch.op(nc.vector, lambda: nc.vector.tensor_mul(
                avnb[:, :].rearrange("p (o c) -> p o c", o=8),
                acc1[:, :].rearrange("p (o c) -> p o c", o=8),
                alb[:, :].unsqueeze(1).broadcast_to([128, 8, 128])))
            ch.dma(nc.gpsimd, lambda i=i: nc.gpsimd.dma_start(
                out=avn_scr[:, i * 128:(i + 1) * 128].rearrange(
                    "(o p) c -> p o c", p=128), in_=sb3(avnb, 8, 128)))
        ch.seal()
    barrier(kb)
    with ExitStack() as st:
        ch = Chain(kb, st, "l2_res")
        am = kb.sb(st, "l2am", [128, 8 * 512], BF16)
        xm = kb.sb(st, "l2xm", [128, 8 * 512], F32)
        ym = kb.sb(st, "l2ym", [128, 8 * 512], F32)
        for m in range(8):
            ch.dma(nc.sync, lambda m=m: nc.sync.dma_start(
                out=sb3(am, 8, 512), in_=fm(avn_scr, m * 512, (m + 1) * 512)))
            ch.dma(nc.sync, lambda m=m: nc.sync.dma_start(
                out=sb3(xm, 8, 512), in_=fm(x_dram, m * 512, (m + 1) * 512)))
            ch.op(nc.vector, lambda: nc.vector.tensor_add(
                ym[:, :], xm[:, :], am[:, :]))
            ch.dma(nc.gpsimd, lambda m=m: nc.gpsimd.dma_start(
                out=fm(y_dram, m * 512, (m + 1) * 512), in_=sb3(ym, 8, 512)))
        ch.seal()
    barrier(kb)


def emit_l2_ffn(kb, x_dram, gw_f, egt_f, eut_f, edt_f, sgt_f, sut_f, sdt_f,
                y_dram):
    nc = kb.nc
    ln_scr = kb.dscr("l2f_ln", [1024, 4096], BF16)
    cw_scr = kb.dscr("l2f_cw", [8, 4096], F32)
    eg_bf = kb.dscr("l2f_eg", [8, 1024, 2816], BF16)
    eu_bf = kb.dscr("l2f_eu", [8, 1024, 2816], BF16)
    ed_bf = kb.dscr("l2f_ed", [8, 2816, 1024], BF16)
    sg_bf = kb.dscr("l2f_sg", [1024, 2816], BF16)
    su_bf = kb.dscr("l2f_su", [1024, 2816], BF16)
    sd_bf = kb.dscr("l2f_sd", [2816, 1024], BF16)
    with ExitStack() as st:
        ch = Chain(kb, st, "l2f_pre")
        for e in range(8):
            ch.dma(nc.gpsimd, lambda e=e: nc.gpsimd.dma_start(out=eg_bf[e], in_=egt_f[e]))
            ch.dma(nc.gpsimd, lambda e=e: nc.gpsimd.dma_start(out=eu_bf[e], in_=eut_f[e]))
            ch.dma(nc.gpsimd, lambda e=e: nc.gpsimd.dma_start(out=ed_bf[e], in_=edt_f[e]))
        ch.dma(nc.gpsimd, lambda: nc.gpsimd.dma_start(out=sg_bf, in_=sgt_f))
        ch.dma(nc.gpsimd, lambda: nc.gpsimd.dma_start(out=su_bf, in_=sut_f))
        ch.dma(nc.gpsimd, lambda: nc.gpsimd.dma_start(out=sd_bf, in_=sdt_f))
        ch.dma(nc.gpsimd, lambda: nc.gpsimd.dma_start(out=y_dram, in_=x_dram))
        emit_ln(kb, ch, st, x_dram, ln_scr)
        ch.seal()
    barrier(kb)
    with ExitStack() as st:
        ch = Chain(kb, st, "l2f_gate")
        emit_gating_topk(kb, ch, st, ln_scr, gw_f, cw_scr, 8, topk_norm_eps=1e-20)
        ch.seal()
    barrier(kb)
    with ExitStack() as st:
        df = DF(kb, st, "l2f_ex")
        gwT = kb.sb(st, "gwT", [128, 8 * 2816], BF16)
        uwT = kb.sb(st, "uwT", [128, 8 * 2816], BF16)
        dwT = kb.sb(st, "dwT", [128, 22 * 1024], BF16)
        lnm = [kb.sb(st, f"lnm2f{i}", [128, 8 * 512], BF16) for i in range(2)]
        hblk = kb.sb(st, "hblk2f", [128, 22 * 512], BF16)
        sgb = [kb.sb(st, f"sgb{i}", [128, 512], BF16) for i in range(2)]
        gub = [kb.sb(st, f"gub{i}", [128, 512], BF16) for i in range(2)]
        cwbe = kb.sb(st, "cwbe2", [128, 4096], BF16)
        tmpo = [kb.sb(st, f"tmpo2{i}", [128, 512], F32) for i in range(2)]
        ps_g = [kb.ps(st, f"2f_psg{i}", [128, 512]) for i in range(2)]
        ps_u = [kb.ps(st, f"2f_psu{i}", [128, 512]) for i in range(2)]
        ps2 = [kb.ps(st, f"2f_ps2{i}", [128, 512]) for i in range(2)]
        for e in range(9):
            shared = (e == 8)
            g_src = sg_bf if shared else eg_bf[e]
            u_src = su_bf if shared else eu_bf[e]
            d_src = sd_bf if shared else ed_bf[e]
            df.dma("sync", lambda g_src=g_src: nc.sync.dma_start(
                out=sb3(gwT, 8, 2816),
                in_=g_src.rearrange("(d p) c -> p d c", p=128)), writes=[gwT])
            df.dma("sync", lambda u_src=u_src: nc.sync.dma_start(
                out=sb3(uwT, 8, 2816),
                in_=u_src.rearrange("(d p) c -> p d c", p=128)), writes=[uwT])
            df.dma("sync", lambda d_src=d_src: nc.sync.dma_start(
                out=sb3(dwT, 22, 1024),
                in_=d_src.rearrange("(h p) c -> p h c", p=128)), writes=[dwT])
            if not shared:
                df.dma("gpsimd", lambda e=e: nc.gpsimd.dma_start(
                    out=cwbe[:, :],
                    in_=cw_scr[e:e + 1, :].broadcast_to([128, 4096])),
                    writes=[cwbe])
            for m in range(8):
                lb = lnm[m % 2]
                df.dma("sync", lambda m=m, lb=lb: nc.sync.dma_start(
                    out=sb3(lb, 8, 512), in_=fm(ln_scr, m * 512, (m + 1) * 512)),
                    writes=[lb])
                for hc in range(22):
                    pg, pu = ps_g[hc % 2], ps_u[hc % 2]
                    for dd in range(8):
                        df.op("tensor", lambda hc=hc, dd=dd, pg=pg, lb=lb:
                              nc.tensor.matmul(
                            pg[:, :],
                            gwT[:, dd * 2816 + hc * 128:dd * 2816 + (hc + 1) * 128],
                            lb[:, dd * 512:(dd + 1) * 512],
                            start=(dd == 0), stop=(dd == 7)),
                            reads=[gwT, lb], writes=[pg], inc=False)
                        df.op("tensor", lambda hc=hc, dd=dd, pu=pu, lb=lb:
                              nc.tensor.matmul(
                            pu[:, :],
                            uwT[:, dd * 2816 + hc * 128:dd * 2816 + (hc + 1) * 128],
                            lb[:, dd * 512:(dd + 1) * 512],
                            start=(dd == 0), stop=(dd == 7)),
                            reads=[uwT, lb], writes=[pu], inc=(dd == 7))
                    sg_ = sgb[hc % 2]
                    gu_ = gub[hc % 2]
                    df.op("scalar", lambda pg=pg, sg_=sg_: nc.scalar.activation(
                        sg_[:, :], pg[:, :], AF.Silu), reads=[pg], writes=[sg_])
                    df.op("vector", lambda pu=pu, sg_=sg_, gu_=gu_:
                          nc.vector.tensor_mul(
                        gu_[:, :], sg_[:, :], pu[:, :]),
                        reads=[sg_, pu], writes=[gu_])
                    if shared:
                        df.op("vector", lambda hc=hc, gu_=gu_: nc.vector.tensor_copy(
                            hblk[:, hc * 512:(hc + 1) * 512], gu_[:, :]),
                            reads=[gu_], writes=[(hblk, hc)])
                    else:
                        df.op("vector", lambda hc=hc, m=m, gu_=gu_:
                              nc.vector.tensor_mul(
                            hblk[:, hc * 512:(hc + 1) * 512], gu_[:, :],
                            cwbe[:, m * 512:(m + 1) * 512]),
                            reads=[gu_, cwbe], writes=[(hblk, hc)])
                for oc in range(8):
                    qb = ps2[oc % 2]
                    for hc in range(22):
                        df.op("tensor", lambda oc=oc, hc=hc, qb=qb:
                              nc.tensor.matmul(
                            qb[:, :],
                            dwT[:, hc * 1024 + oc * 128:hc * 1024 + (oc + 1) * 128],
                            hblk[:, hc * 512:(hc + 1) * 512],
                            start=(hc == 0), stop=(hc == 21)),
                            reads=[dwT, (hblk, hc)], writes=[qb], inc=(hc == 21))
                    tb = tmpo[oc % 2]
                    df.op("vector", lambda qb=qb, tb=tb: nc.vector.tensor_copy(
                        tb[:, :], qb[:, :]), reads=[qb], writes=[tb])
                    df.dma("gpsimd", lambda oc=oc, m=m, tb=tb: nc.gpsimd.dma_start(
                        out=y_dram[oc * 128:(oc + 1) * 128, m * 512:(m + 1) * 512],
                        in_=tb[:, :], accum_op=ALU.add),
                        reads=[tb, ("y", oc, m)], writes=[("y", oc, m)])
        df.seal()
    barrier(kb)


def emit_l3_attn(kb, x_dram, ipw_f, ipbt_f, ipbv_f, opw_f, opbt_f, y_dram):
    nc = kb.nc
    ipw_bf = kb.dscr("l3_ipw", [1024, 3072], BF16)
    opw_bf = kb.dscr("l3_opw", [1024, 1024], BF16)
    ln_scr = kb.dscr("l3_ln", [1024, 4096], BF16)
    q_scr = kb.dscr("l3_q", [1024, 4096], BF16)
    k_scr = kb.dscr("l3_k", [1024, 4096], BF16)
    v_scr = kb.dscr("l3_v", [4096, 1024], BF16)
    avn_scr = kb.dscr("l3_avn", [1024, 4096], BF16)
    with ExitStack() as st:
        ch = Chain(kb, st, "l3a_pre")
        emit_cast(kb, ch, ipw_f, ipw_bf)
        emit_cast(kb, ch, opw_f, opw_bf)
        emit_ln(kb, ch, st, x_dram, ln_scr)
        ch.seal()
    barrier(kb)
    with ExitStack() as st:
        ch = Chain(kb, st, "l3a_qkv")
        wT = kb.sb(st, "l3wT", [128, 8 * 3072], BF16)
        lnm = kb.sb(st, "l3lnm", [128, 8 * 512], BF16)
        bt = kb.sb(st, "l3bt", [128, 24], F32)
        btq = kb.sb(st, "l3btq", [128, 8], F32)
        bvb = kb.sb(st, "l3bvb", [128, 1024], F32)
        ob = kb.sb(st, "l3ob", [128, 1024], BF16)
        ps0 = kb.ps(st, "l3_ps0", [128, 512])
        ps_v2 = kb.ps(st, "l3_psv", [128, 1024])
        ch.dma(nc.sync, lambda: nc.sync.dma_start(
            out=sb3(wT, 8, 3072),
            in_=ipw_bf.rearrange("(d p) c -> p d c", p=128)))
        ch.dma(nc.sync, lambda: nc.sync.dma_start(out=bt[:, :], in_=ipbt_f))
        ch.dma(nc.sync, lambda: nc.sync.dma_start(
            out=bvb[:, :], in_=ipbv_f.broadcast_to([128, 1024])))
        ch.op(nc.vector, lambda: nc.vector.tensor_scalar_mul(
            btq[:, :], bt[:, 0:8], 0.125))
        for m in range(8):
            ch.dma(nc.sync, lambda m=m: nc.sync.dma_start(
                out=sb3(lnm, 8, 512), in_=fm(ln_scr, m * 512, (m + 1) * 512)))
            for part, dst in ((0, q_scr), (1, k_scr)):
                for oc in range(8):
                    ocg = part * 8 + oc
                    for dd in range(8):
                        ch.op(nc.tensor, lambda ocg=ocg, dd=dd: nc.tensor.matmul(
                            ps0[:, :],
                            wT[:, dd * 3072 + ocg * 128:dd * 3072 + (ocg + 1) * 128],
                            lnm[:, dd * 512:(dd + 1) * 512],
                            start=(dd == 0), stop=(dd == 7)), inc=(dd == 7))
                    if part == 0:
                        ch.op(nc.scalar, lambda oc=oc: nc.scalar.activation(
                            ob[:, 0:512], ps0[:, :], AF.Identity,
                            bias=btq[:, oc:oc + 1], scale=0.125))
                    else:
                        ch.op(nc.scalar, lambda oc=oc: nc.scalar.activation(
                            ob[:, 0:512], ps0[:, :], AF.Identity,
                            bias=bt[:, 8 + oc:9 + oc]))
                    ch.dma(nc.gpsimd, lambda dst=dst, oc=oc, m=m: nc.gpsimd.dma_start(
                        out=dst[oc * 128:(oc + 1) * 128, m * 512:(m + 1) * 512],
                        in_=ob[:, 0:512]))
            for mt in range(4):
                for dd in range(8):
                    for half in range(2):
                        ch.op(nc.tensor, lambda mt=mt, dd=dd, half=half: nc.tensor.matmul(
                            ps_v2[:, half * 512:(half + 1) * 512],
                            lnm[:, dd * 512 + mt * 128:dd * 512 + (mt + 1) * 128],
                            wT[:, dd * 3072 + 2048 + half * 512:
                               dd * 3072 + 2048 + (half + 1) * 512],
                            start=(dd == 0), stop=(dd == 7)),
                            inc=(dd == 7 and half == 1))
                ch.op(nc.vector, lambda: nc.vector.tensor_add(
                    ob[:, 0:1024], ps_v2[:, :], bvb[:, :]))
                r0 = m * 512 + mt * 128
                ch.dma(nc.gpsimd, lambda r0=r0: nc.gpsimd.dma_start(
                    out=v_scr[r0:r0 + 128, :], in_=ob[:, 0:1024]))
        ch.seal()
    barrier(kb)
    with ExitStack() as st:
        df = DF(kb, st, "l3a_att")
        nc_ = nc
        ident_bf = kb.sb(st, "idb3", [128, 128], BF16)
        ident_f = kb.sb(st, "idf3", [128, 128], F32)
        ones64f = kb.sb(st, "o64f3", [1, 64], F32)
        dmask = kb.sb(st, "dmask3", [128, 8 * 128], F32)
        dm = np.where(np.eye(128) > 0, 0.0, NEG).astype(np.float32)
        for t_, arr in ((ident_bf, np.eye(128, dtype=np.float32).astype(BF)),
                        (ident_f, np.eye(128, dtype=np.float32)),
                        (ones64f, np.ones((1, 64), np.float32)),
                        (dmask, np.tile(dm, (1, 8)))):
            dr = kb.const_dram(arr)
            df.dma("sync", lambda t_=t_, dr=dr: nc_.sync.dma_start(
                out=t_[:, :], in_=dr), writes=[t_])
        qh = [kb.sb(st, f"l3qh{i}", [64, 4096], BF16) for i in range(2)]
        kh = [kb.sb(st, f"l3kh{i}", [64, 4096], BF16) for i in range(2)]
        vh = [kb.sb(st, f"l3vh{i}", [128, 32 * 64], BF16) for i in range(2)]
        p_sb = [kb.sb(st, f"l3p{i}", [128, 1024], BF16) for i in range(2)]
        pt_sb = [kb.sb(st, f"l3pt{i}", [128, 1024], BF16) for i in range(2)]
        avh = kb.sb(st, "l3avh", [64, 4096], F32)
        avn = kb.sb(st, "l3avn", [64, 4096], BF16)
        lrec = kb.sb(st, "l3lrec", [128, 32], F32)
        mneg = kb.sb(st, "l3mneg", [128, 2], F32)
        lrT = kb.sb(st, "l3lrT", [1, 512], F32)
        ps_s = [kb.ps(st, f"l3_s{i}", [128, 1024]) for i in range(2)]
        pst = kb.ps(st, "l3_t", [128, 1024], BF16)
        ps_av = kb.ps(st, "l3_av", [128, 256])
        ps_m = kb.ps(st, "l3_m", [128, 512])
        vh3 = [sb3(v, 32, 64) for v in vh]
        PMALL = [("pml", 0), ("pml", 1)]
        for h in range(16):
            hb = h % 2
            df.dma("sync", lambda h=h, hb=hb: nc_.sync.dma_start(
                out=qh[hb][:, :], in_=q_scr[64 * h:64 * h + 64, :]),
                writes=[qh[hb]])
            df.dma("sync", lambda h=h, hb=hb: nc_.sync.dma_start(
                out=kh[hb][:, :], in_=k_scr[64 * h:64 * h + 64, :]),
                writes=[kh[hb]])
            df.dma("sync", lambda h=h, hb=hb: nc_.sync.dma_start(
                out=vh3[hb], in_=v_scr[:, 64 * h:64 * h + 64].rearrange(
                    "(t p) c -> p t c", p=128)), writes=[vh[hb]])
            for i in range(32):
                ib = i % 2
                ss = ps_s[ib]
                pb = p_sb[ib]
                gneg = i % 2
                for t in range(8):
                    j0 = t * 512 + (i % 4) * 128
                    df.op("tensor", lambda i=i, t=t, j0=j0, ss=ss, hb=hb:
                          nc_.tensor.matmul(
                        ss[:, t * 128:(t + 1) * 128],
                        qh[hb][:, i * 128:(i + 1) * 128], kh[hb][:, j0:j0 + 128],
                        start=True, stop=True),
                        reads=[qh[hb], kh[hb]], writes=[("ss", ib)],
                        inc=(t == 7))
                df.op("vector", lambda ss=ss: nc_.vector.tensor_add(
                    ss[:, :], ss[:, :], dmask[:, :]),
                    reads=[("ss", ib), dmask], writes=[("ss", ib)])
                df.op("vector", lambda ss=ss, gneg=gneg: nc_.vector.reduce_max(
                    mneg[:, gneg:gneg + 1], ss[:, :], AX.X, negate=True),
                    reads=[("ss", ib)], writes=[("mn", gneg)])
                df.op("scalar", lambda i=i, ss=ss, pb=pb, gneg=gneg:
                      nc_.scalar.activation(
                    pb[:, :], ss[:, :], AF.Exp, bias=mneg[:, gneg:gneg + 1],
                    accum_out=lrec[:, i:i + 1]),
                    reads=[("ss", ib), ("mn", gneg)],
                    writes=[pb, ("lr", i), lrec])
                ptb = pt_sb[ib]
                for b0 in (0, 4):
                    for sj in range(4):
                        df.op("tensor", lambda b0=b0, sj=sj, pb=pb:
                              nc_.tensor.transpose(
                            pst[:, sj * 128:(sj + 1) * 128],
                            pb[:, (b0 + sj) * 128:(b0 + sj + 1) * 128],
                            ident_bf[:, :]),
                            reads=[pb, ident_bf], writes=[pst], inc=(sj == 3))
                    df.op("vector", lambda b0=b0, ptb=ptb: nc_.vector.tensor_copy(
                        ptb[:, b0 * 128:(b0 + 4) * 128], pst[:, 0:512]),
                        reads=[pst], writes=[(id(ptb), b0)])
                for t in range(8):
                    jt = t * 4 + (i % 4)
                    df.op("tensor", lambda t=t, jt=jt, hb=hb, ptb=ptb, ib=ib:
                          nc_.tensor.matmul(
                        ps_av[0:64, ib * 128:(ib + 1) * 128], vh3[hb][:, jt, :],
                        ptb[:, t * 128:(t + 1) * 128],
                        start=(t == 0), stop=(t == 7)),
                        reads=[vh[hb], (id(ptb), (t // 4) * 4)],
                        writes=[("av", ib)], inc=(t == 7))
                df.op("vector", lambda i=i, ib=ib: nc_.vector.tensor_copy(
                    avh[:, i * 128:(i + 1) * 128], ps_av[0:64, ib * 128:(ib + 1) * 128]),
                    reads=[("av", ib)], writes=[("avh", i)])
            df.op("vector", lambda: nc_.vector.reciprocal(lrec[:, :], lrec[:, :]),
                  reads=[("lr", i2) for i2 in range(32)] + [lrec], writes=[lrec])
            for it0 in range(0, 32, 4):
                for sj in range(4):
                    df.op("tensor", lambda it0=it0, sj=sj: nc_.tensor.transpose(
                        ps_m[0:1, sj * 128:(sj + 1) * 128],
                        lrec[:, it0 + sj:it0 + sj + 1], ident_f[:, :]),
                        reads=[lrec, ident_f], writes=PMALL, inc=(sj == 3))
                df.op("vector", lambda: nc_.vector.tensor_copy(
                    lrT[0:1, :], ps_m[0:1, 0:512]),
                    reads=[("pml", 0)], writes=[lrT])
                for sj in range(4):
                    df.op("tensor", lambda sj=sj: nc_.tensor.matmul(
                        ps_m[0:64, sj * 128:(sj + 1) * 128], ones64f[:, :],
                        lrT[0:1, sj * 128:(sj + 1) * 128], start=True, stop=True),
                        reads=[lrT, ones64f], writes=PMALL, inc=(sj == 3))
                df.op("vector", lambda it0=it0: nc_.vector.tensor_mul(
                    avn[:, it0 * 128:(it0 + 4) * 128],
                    avh[:, it0 * 128:(it0 + 4) * 128], ps_m[0:64, 0:512]),
                    reads=[("pml", 1)] + [("avh", it0 + k) for k in range(4)],
                    writes=[("avn", it0)])
            df.dma("gpsimd", lambda h=h: nc_.gpsimd.dma_start(
                out=avn_scr[64 * h:64 * h + 64, :], in_=avn[:, :]),
                reads=[("avn", it0) for it0 in range(0, 32, 4)], writes=[avn])
        df.seal()
    barrier(kb)

    with ExitStack() as st:
        ch = Chain(kb, st, "l3a_proj")
        opT = kb.sb(st, "opT", [128, 8 * 1024], BF16)
        avm = kb.sb(st, "l3avm", [128, 8 * 512], BF16)
        obt = kb.sb(st, "l3obt", [128, 8], F32)
        xt = kb.sb(st, "l3xt", [128, 512], F32)
        yb = kb.sb(st, "l3yb", [128, 512], F32)
        ps0 = kb.ps(st, "l3p_ps", [128, 512])
        ch.dma(nc.sync, lambda: nc.sync.dma_start(
            out=sb3(opT, 8, 1024),
            in_=opw_bf.rearrange("(d p) c -> p d c", p=128)))
        ch.dma(nc.sync, lambda: nc.sync.dma_start(out=obt[:, :], in_=opbt_f))
        for m in range(8):
            ch.dma(nc.sync, lambda m=m: nc.sync.dma_start(
                out=sb3(avm, 8, 512), in_=fm(avn_scr, m * 512, (m + 1) * 512)))
            for oc in range(8):
                ch.dma(nc.sync, lambda oc=oc, m=m: nc.sync.dma_start(
                    out=xt[:, :],
                    in_=x_dram[oc * 128:(oc + 1) * 128, m * 512:(m + 1) * 512]))
                for dd in range(8):
                    ch.op(nc.tensor, lambda oc=oc, dd=dd: nc.tensor.matmul(
                        ps0[:, :], opT[:, dd * 1024 + oc * 128:dd * 1024 + (oc + 1) * 128],
                        avm[:, dd * 512:(dd + 1) * 512],
                        start=(dd == 0), stop=(dd == 7)), inc=(dd == 7))
                ch.op(nc.vector, lambda oc=oc: nc.vector.scalar_tensor_tensor(
                    yb[:, :], ps0[:, :], obt[:, oc:oc + 1], xt[:, :],
                    ALU.add, ALU.add))
                ch.dma(nc.gpsimd, lambda oc=oc, m=m: nc.gpsimd.dma_start(
                    out=y_dram[oc * 128:(oc + 1) * 128, m * 512:(m + 1) * 512],
                    in_=yb[:, :]))
        ch.seal()
    barrier(kb)


def emit_l3_ffn(kb, x_dram, bw_f, bbt_f, sa_f, sb_f, ra_f, rb_f, rw_f, rbrow_f,
                y_dram):
    nc = kb.nc
    ln_scr = kb.dscr("l3f_ln", [1024, 4096], BF16)
    bw_bf = kb.dscr("l3f_bw", [1024, 1024], BF16)
    sa_bf = kb.dscr("l3f_sa", [1024, 16], BF16)
    sbw_bf = kb.dscr("l3f_sb", [16, 1024], BF16)
    ra_bf = kb.dscr("l3f_ra", [1024, 96], BF16)
    rb_bf = kb.dscr("l3f_rb", [96, 1024], BF16)
    sw_scr = kb.dscr("l3f_sw", [6, 4096], F32)
    with ExitStack() as st:
        ch = Chain(kb, st, "l3f_pre")
        emit_cast(kb, ch, bw_f, bw_bf)
        emit_cast(kb, ch, sa_f, sa_bf)
        emit_cast(kb, ch, sb_f, sbw_bf)
        emit_cast(kb, ch, ra_f, ra_bf)
        emit_cast(kb, ch, rb_f, rb_bf)
        emit_ln(kb, ch, st, x_dram, ln_scr)
        ch.seal()
    barrier(kb)
    with ExitStack() as st:
        ch = Chain(kb, st, "l3f_gate")
        emit_gating_topk(kb, ch, st, ln_scr, rw_f, sw_scr, 6,
                         topk_norm_eps=1e-6, bias_row=rbrow_f)
        ch.seal()
    barrier(kb)
    with ExitStack() as st:
        ch = Chain(kb, st, "l3f_m")
        bwT = kb.sb(st, "bwT", [128, 8 * 1024], BF16)
        saT = kb.sb(st, "saT", [128, 8 * 16], BF16)
        sbT = kb.sb(st, "sbT", [16, 1024], BF16)
        raT = kb.sb(st, "raT", [128, 8 * 96], BF16)
        rbT = kb.sb(st, "rbT", [96, 1024], BF16)
        lnm = kb.sb(st, "lnm3f", [128, 8 * 512], BF16)
        h1b = kb.sb(st, "h1b", [16, 512], BF16)
        hrb = kb.sb(st, "hrb", [96, 512], BF16)
        swb = kb.sb(st, "swb", [96, 512], F32)
        bbt = kb.sb(st, "bbt", [128, 8], F32)
        xt = kb.sb(st, "xt3f", [128, 512], F32)
        yb = kb.sb(st, "yb3f", [128, 512], F32)
        ps_h1 = kb.ps(st, "3f_ph1", [128, 512])
        ps_hr = kb.ps(st, "3f_phr", [128, 512])
        ps2 = kb.ps(st, "3f_ps2", [128, 512])
        ch.dma(nc.sync, lambda: nc.sync.dma_start(
            out=sb3(bwT, 8, 1024),
            in_=bw_bf.rearrange("(d p) c -> p d c", p=128)))
        ch.dma(nc.sync, lambda: nc.sync.dma_start(
            out=sb3(saT, 8, 16),
            in_=sa_bf.rearrange("(d p) c -> p d c", p=128)))
        ch.dma(nc.sync, lambda: nc.sync.dma_start(
            out=sb3(raT, 8, 96),
            in_=ra_bf.rearrange("(d p) c -> p d c", p=128)))
        ch.dma(nc.sync, lambda: nc.sync.dma_start(out=sbT[:, :], in_=sbw_bf))
        ch.dma(nc.sync, lambda: nc.sync.dma_start(out=rbT[:, :], in_=rb_bf))
        ch.dma(nc.sync, lambda: nc.sync.dma_start(out=bbt[:, :], in_=bbt_f))
        for m in range(8):
            ch.dma(nc.sync, lambda m=m: nc.sync.dma_start(
                out=sb3(lnm, 8, 512), in_=fm(ln_scr, m * 512, (m + 1) * 512)))
            for dd in range(8):
                ch.op(nc.tensor, lambda dd=dd: nc.tensor.matmul(
                    ps_h1[0:16, :], saT[:, dd * 16:(dd + 1) * 16],
                    lnm[:, dd * 512:(dd + 1) * 512],
                    start=(dd == 0), stop=(dd == 7)), inc=False)
                ch.op(nc.tensor, lambda dd=dd: nc.tensor.matmul(
                    ps_hr[0:96, :], raT[:, dd * 96:(dd + 1) * 96],
                    lnm[:, dd * 512:(dd + 1) * 512],
                    start=(dd == 0), stop=(dd == 7)), inc=(dd == 7))
            ch.op(nc.vector, lambda: nc.vector.tensor_copy(
                h1b[:, :], ps_h1[0:16, :]))
            for e in range(6):
                ch.dma(nc.sync, lambda m=m, e=e: nc.sync.dma_start(
                    out=swb[e * 16:(e + 1) * 16, :],
                    in_=sw_scr[e:e + 1, m * 512:(m + 1) * 512].broadcast_to(
                        [16, 512])))
            ch.op(nc.vector, lambda: nc.vector.tensor_mul(
                hrb[:, :], ps_hr[0:96, :], swb[:, :]))
            for oc in range(8):
                ch.dma(nc.sync, lambda oc=oc, m=m: nc.sync.dma_start(
                    out=xt[:, :],
                    in_=x_dram[oc * 128:(oc + 1) * 128, m * 512:(m + 1) * 512]))
                for dd in range(8):
                    ch.op(nc.tensor, lambda oc=oc, dd=dd: nc.tensor.matmul(
                        ps2[:, :],
                        bwT[:, dd * 1024 + oc * 128:dd * 1024 + (oc + 1) * 128],
                        lnm[:, dd * 512:(dd + 1) * 512],
                        start=(dd == 0), stop=False), inc=False)
                ch.op(nc.tensor, lambda oc=oc: nc.tensor.matmul(
                    ps2[:, :], sbT[:, oc * 128:(oc + 1) * 128], h1b[:, :],
                    start=False, stop=False), inc=False)
                ch.op(nc.tensor, lambda oc=oc: nc.tensor.matmul(
                    ps2[:, :], rbT[:, oc * 128:(oc + 1) * 128], hrb[:, :],
                    start=False, stop=True))
                ch.op(nc.vector, lambda oc=oc: nc.vector.scalar_tensor_tensor(
                    yb[:, :], ps2[:, :], bbt[:, oc:oc + 1], xt[:, :],
                    ALU.add, ALU.add))
                ch.dma(nc.gpsimd, lambda oc=oc, m=m: nc.gpsimd.dma_start(
                    out=y_dram[oc * 128:(oc + 1) * 128, m * 512:(m + 1) * 512],
                    in_=yb[:, :]))
        ch.seal()
    barrier(kb)



def _build_l0a():
    kb = KB("k_l0a")
    x = kb.din("x", [1024, 4096], F32)
    wqkv = kb.din("wqkv", [1024, 3072], F32)
    uq = kb.din("uq", [3072], F32)
    sq = kb.din("sq", [1], F32)
    wproj = kb.din("wproj", [1024, 1024], F32)
    up = kb.din("up", [1024], F32)
    sp = kb.din("sp", [1], F32)
    y = kb.dout("y", [1024, 4096], F32)
    emit_l0_attn(kb, x, wqkv, uq, sq, wproj, up, sp, y)
    kb.st.close()
    return kb.nc


def _build_l0f():
    kb = KB("k_l0f")
    x = kb.din("x", [1024, 4096], F32)
    wg = kb.din("wg", [8, 1024], F32)
    w1 = kb.din("w1", [8, 1024, 4096], F32)
    b1t = kb.din("b1t", [8, 128, 32], F32)
    w2 = kb.din("w2", [8, 4096, 1024], F32)
    b2 = kb.din("b2", [8, 1024], F32)
    y = kb.dout("y", [1024, 4096], F32)
    emit_l0_ffn(kb, x, wg, w1, b1t, w2, b2, y)
    kb.st.close()
    return kb.nc


def _build_l1a():
    kb = KB("k_l1a")
    x = kb.din("x", [1024, 4096], F32)
    uvw = kb.din("uvw", [1024, 2048], F32)
    ow = kb.din("ow", [1024, 1024], F32)
    rpw = kb.din("rpw", [1, 32], F32)
    rlw = kb.din("rlw", [3, 32], F32)
    rlb = kb.din("rlb", [3, 32], F32)
    rfc = kb.din("rfc", [3, 32, 32], F32)
    rolw = kb.din("rolw", [32], F32)
    rolb = kb.din("rolb", [32], F32)
    row = kb.din("row", [32, 1024], F32)
    y = kb.dout("y", [1024, 4096], F32)
    emit_l1_attn(kb, x, uvw, ow, rpw, rlw, rlb, rfc, rolw, rolb, row, y)
    kb.st.close()
    return kb.nc


def _build_l1f():
    kb = KB("k_l1f")
    x = kb.din("x", [1024, 4096], F32)
    w1 = kb.din("w1", [4096, 1024], F32)
    b1t = kb.din("b1t", [128, 32], F32)
    w2 = kb.din("w2", [1024, 4096], F32)
    b2t = kb.din("b2t", [128, 8], F32)
    y = kb.dout("y", [1024, 4096], F32)
    emit_l1_ffn(kb, x, w1, b1t, w2, b2t, y)
    kb.st.close()
    return kb.nc


def _build_l2a():
    kb = KB("k_l2a")
    x = kb.din("x", [1024, 4096], F32)
    w = kb.din("w", [1024, 1024], F32)
    bt = kb.din("bt", [128, 8], F32)
    br = kb.din("br", [1, 1024], F32)
    y = kb.dout("y", [1024, 4096], F32)
    emit_l2_attn(kb, x, w, bt, br, y)
    kb.st.close()
    return kb.nc


def _build_l2f():
    kb = KB("k_l2f")
    x = kb.din("x", [1024, 4096], F32)
    gw = kb.din("gw", [8, 1024], F32)
    eg = kb.din("eg", [8, 1024, 2816], F32)
    eu = kb.din("eu", [8, 1024, 2816], F32)
    ed = kb.din("ed", [8, 2816, 1024], F32)
    sg = kb.din("sg", [1024, 2816], F32)
    su = kb.din("su", [1024, 2816], F32)
    sd = kb.din("sd", [2816, 1024], F32)
    y = kb.dout("y", [1024, 4096], F32)
    emit_l2_ffn(kb, x, gw, eg, eu, ed, sg, su, sd, y)
    kb.st.close()
    return kb.nc


def _build_l3():
    kb = KB("k_l3")
    x = kb.din("x", [1024, 4096], F32)
    ipw = kb.din("ipw", [1024, 3072], F32)
    ipbt = kb.din("ipbt", [128, 24], F32)
    ipbv = kb.din("ipbv", [1, 1024], F32)
    opw = kb.din("opw", [1024, 1024], F32)
    opbt = kb.din("opbt", [128, 8], F32)
    xa = kb.dscr("xa", [1024, 4096], F32)
    emit_l3_attn(kb, x, ipw, ipbt, ipbv, opw, opbt, xa)
    bw = kb.din("bw", [1024, 1024], F32)
    bbt = kb.din("bbt", [128, 8], F32)
    sa = kb.din("sa", [1024, 16], F32)
    sbw = kb.din("sbw", [16, 1024], F32)
    ra = kb.din("ra", [1024, 96], F32)
    rb = kb.din("rb", [96, 1024], F32)
    rw = kb.din("rw", [6, 1024], F32)
    rbr = kb.din("rbr", [1, 6], F32)
    y = kb.dout("y", [1024, 4096], F32)
    emit_l3_ffn(kb, xa, bw, bbt, sa, sbw, ra, rb, rw, rbr, y)
    kb.st.close()
    return kb.nc


def _run(nc, common, xs):
    in_maps = [dict(common, x=np.ascontiguousarray(x)) for x in xs]
    res = run_bass_kernel_spmd(nc, in_maps, core_ids=list(range(NCORES)))
    return [np.asarray(r["y"]) for r in res.results]


def kernel_unfused(**inp):
    cg = np.ascontiguousarray
    x = inp["L0_attn_batch"]
    xs = [cg(x[c].T) for c in range(NCORES)]

    xs = _run(_build_l0a(), {
        "wqkv": inp["L0_attn_W_qkv"], "uq": inp["L0_attn_u_qkv"],
        "sq": inp["L0_attn_sigma_qkv"], "wproj": inp["L0_attn_W_proj"],
        "up": inp["L0_attn_u_proj"], "sp": inp["L0_attn_sigma_proj"]}, xs)

    xs = _run(_build_l0f(), {
        "wg": inp["L0_ffn_w_gate"], "w1": inp["L0_ffn_W1"],
        "b1t": cg(inp["L0_ffn_b1"].reshape(8, 32, 128).transpose(0, 2, 1)),
        "w2": inp["L0_ffn_W2"], "b2": inp["L0_ffn_b2"]}, xs)

    xs = _run(_build_l1a(), {
        "uvw": inp["L1_attn_uv_w"], "ow": inp["L1_attn_o_w"],
        "rpw": inp["L1_attn_rpe_pos_w"], "rlw": inp["L1_attn_rpe_ln_w"],
        "rlb": inp["L1_attn_rpe_ln_b"], "rfc": inp["L1_attn_rpe_fc_w"],
        "rolw": inp["L1_attn_rpe_out_ln_w"], "rolb": inp["L1_attn_rpe_out_ln_b"],
        "row": inp["L1_attn_rpe_out_w"]}, xs)

    xs = _run(_build_l1f(), {
        "w1": inp["L1_ffn_W1"],
        "b1t": cg(inp["L1_ffn_b1"].reshape(32, 128).T),
        "w2": inp["L1_ffn_W2"],
        "b2t": cg(inp["L1_ffn_b2"].reshape(8, 128).T)}, xs)

    xs = _run(_build_l2a(), {
        "w": inp["L2_attn_W"],
        "bt": cg(inp["L2_attn_b"].reshape(8, 128).T),
        "br": inp["L2_attn_b"].reshape(1, 1024)}, xs)

    xs = _run(_build_l2f(), {
        "gw": inp["L2_ffn_gate_w"],
        "eg": cg(inp["L2_ffn_expert_gate"].transpose(0, 2, 1)),
        "eu": cg(inp["L2_ffn_expert_up"].transpose(0, 2, 1)),
        "ed": cg(inp["L2_ffn_expert_down"].transpose(0, 2, 1)),
        "sg": cg(inp["L2_ffn_shared_gate"].T),
        "su": cg(inp["L2_ffn_shared_up"].T),
        "sd": cg(inp["L2_ffn_shared_down"].T)}, xs)

    full = np.stack([r.T for r in xs])
    xrs = [cg(full[:, 512 * j:512 * (j + 1), :].reshape(4096, 1024).T)
           for j in range(NCORES)]

    ys = _run(_build_l3(), {
        "ipw": cg(inp["L3_attn_in_proj_w"].T),
        "ipbt": cg(inp["L3_attn_in_proj_b"].reshape(24, 128).T),
        "ipbv": inp["L3_attn_in_proj_b"][2048:].reshape(1, 1024),
        "opw": cg(inp["L3_attn_out_proj_w"].T),
        "opbt": cg(inp["L3_attn_out_proj_b"].reshape(8, 128).T),
        "bw": cg(inp["L3_ffn_base_W"].T),
        "bbt": cg(inp["L3_ffn_base_b"].reshape(8, 128).T),
        "sa": cg(inp["L3_ffn_shared_A"].T),
        "sbw": cg(inp["L3_ffn_shared_B"].T),
        "ra": cg(inp["L3_ffn_routed_A"].reshape(96, 1024).T),
        "rb": cg(inp["L3_ffn_routed_B"].transpose(0, 2, 1).reshape(96, 1024)),
        "rw": inp["L3_ffn_router_W"],
        "rbr": inp["L3_ffn_router_b"].reshape(1, 6)}, xrs)

    out = np.empty((8, 4096, 1024), np.float32)
    for j in range(NCORES):
        out[:, 512 * j:512 * (j + 1), :] = ys[j].T.reshape(8, 512, 1024)
    return out



def emit_reshard_a2a(kb, src_dram, dst_dram):
    from concourse.aot_env import allow_ici_all_to_all
    nc = kb.nc
    a2a_in = kb.nc.dram_tensor("a2a_in", [8, 1024, 512], F32, kind="Internal")
    a2a_out = kb.nc.dram_tensor("a2a_out", [8, 1024, 512], F32, kind="Internal")
    with ExitStack() as st:
        ch = Chain(kb, st, "a2a")
        for j in range(8):
            ch.dma(nc.gpsimd, lambda j=j: nc.gpsimd.dma_start(
                out=a2a_in.ap()[j], in_=src_dram[:, j * 512:(j + 1) * 512]))
        with allow_ici_all_to_all():
            ch.op(nc.gpsimd, lambda: nc.gpsimd.collective_compute(
                "AllToAll", ALU.bypass,
                replica_groups=[list(range(NCORES))],
                ins=[a2a_in.ap().opt()], outs=[a2a_out.ap().opt()]))
        for j in range(8):
            ch.dma(nc.gpsimd, lambda j=j: nc.gpsimd.dma_start(
                out=dst_dram[:, j * 512:(j + 1) * 512], in_=a2a_out.ap()[j]))
        ch.seal()
    barrier(kb)


def _build_fused(inp_shapes_only=False):
    kb = KB("mk_fused")
    x = kb.din("x", [1024, 4096], F32)
    wqkv = kb.din("wqkv", [1024, 3072], F32)
    uq = kb.din("uq", [3072], F32)
    sq = kb.din("sq", [1], F32)
    wproj = kb.din("wproj", [1024, 1024], F32)
    up = kb.din("up", [1024], F32)
    sp = kb.din("sp", [1], F32)
    wg0 = kb.din("wg0", [8, 1024], F32)
    w10 = kb.din("w10", [8, 1024, 4096], F32)
    b1t0 = kb.din("b1t0", [8, 128, 32], F32)
    w20 = kb.din("w20", [8, 4096, 1024], F32)
    b20 = kb.din("b20", [8, 1024], F32)
    uvw = kb.din("uvw", [1024, 2048], F32)
    ow = kb.din("ow", [1024, 1024], F32)
    rpw = kb.din("rpw", [1, 32], F32)
    rlw = kb.din("rlw", [3, 32], F32)
    rlb = kb.din("rlb", [3, 32], F32)
    rfc = kb.din("rfc", [3, 32, 32], F32)
    rolw = kb.din("rolw", [32], F32)
    rolb = kb.din("rolb", [32], F32)
    row = kb.din("row", [32, 1024], F32)
    w11 = kb.din("w11", [4096, 1024], F32)
    b1t1 = kb.din("b1t1", [128, 32], F32)
    w21 = kb.din("w21", [1024, 4096], F32)
    b2t1 = kb.din("b2t1", [128, 8], F32)
    w2a = kb.din("w2a", [1024, 1024], F32)
    bt2a = kb.din("bt2a", [128, 8], F32)
    br2a = kb.din("br2a", [1, 1024], F32)
    gw2 = kb.din("gw2", [8, 1024], F32)
    eg2 = kb.din("eg2", [8, 1024, 2816], F32)
    eu2 = kb.din("eu2", [8, 1024, 2816], F32)
    ed2 = kb.din("ed2", [8, 2816, 1024], F32)
    sg2 = kb.din("sg2", [1024, 2816], F32)
    su2 = kb.din("su2", [1024, 2816], F32)
    sd2 = kb.din("sd2", [2816, 1024], F32)
    ipw = kb.din("ipw", [1024, 3072], F32)
    ipbt = kb.din("ipbt", [128, 24], F32)
    ipbv = kb.din("ipbv", [1, 1024], F32)
    opw = kb.din("opw", [1024, 1024], F32)
    opbt = kb.din("opbt", [128, 8], F32)
    bw3 = kb.din("bw3", [1024, 1024], F32)
    bbt3 = kb.din("bbt3", [128, 8], F32)
    sa3 = kb.din("sa3", [1024, 16], F32)
    sbw3 = kb.din("sbw3", [16, 1024], F32)
    ra3 = kb.din("ra3", [1024, 96], F32)
    rb3 = kb.din("rb3", [96, 1024], F32)
    rw3 = kb.din("rw3", [6, 1024], F32)
    rbr3 = kb.din("rbr3", [1, 6], F32)
    y = kb.dout("y", [1024, 4096], F32)

    xA = kb.dscr("xA", [1024, 4096], F32)
    xB = kb.dscr("xB", [1024, 4096], F32)
    xR = kb.dscr("xR", [1024, 4096], F32)
    xRA = kb.dscr("xRA", [1024, 4096], F32)

    phases = [
        lambda: emit_l0_attn(kb, x, wqkv, uq, sq, wproj, up, sp, xA),
        lambda: emit_l0_ffn(kb, xA, wg0, w10, b1t0, w20, b20, xB),
        lambda: emit_l1_attn(kb, xB, uvw, ow, rpw, rlw, rlb, rfc, rolw, rolb,
                             row, xA),
        lambda: emit_l1_ffn(kb, xA, w11, b1t1, w21, b2t1, xB),
        lambda: emit_l2_attn(kb, xB, w2a, bt2a, br2a, xA),
        lambda: emit_l2_ffn(kb, xA, gw2, eg2, eu2, ed2, sg2, su2, sd2, xB),
        lambda: emit_reshard_a2a(kb, xB, xR),
        lambda: emit_l3_attn(kb, xR, ipw, ipbt, ipbv, opw, opbt, xRA),
        lambda: emit_l3_ffn(kb, xRA, bw3, bbt3, sa3, sbw3, ra3, rb3, rw3,
                            rbr3, y),
    ]
    import os
    upto = int(os.environ.get("MK_UPTO", "9"))
    for ph in phases[:upto]:
        ph()
    if upto < 9:
        srcs = [xA, xB, xA, xB, xA, xB, xR, xRA][upto - 1]
        with ExitStack() as st:
            ch = Chain(kb, st, "fin")
            ch.dma(kb.nc.gpsimd, lambda: kb.nc.gpsimd.dma_start(out=y, in_=srcs))
            ch.seal()
    kb.st.close()
    return kb.nc


def kernel(**inp):
    cg = np.ascontiguousarray
    x = inp["L0_attn_batch"]
    common = {
        "wqkv": inp["L0_attn_W_qkv"], "uq": inp["L0_attn_u_qkv"],
        "sq": inp["L0_attn_sigma_qkv"], "wproj": inp["L0_attn_W_proj"],
        "up": inp["L0_attn_u_proj"], "sp": inp["L0_attn_sigma_proj"],
        "wg0": inp["L0_ffn_w_gate"], "w10": inp["L0_ffn_W1"],
        "b1t0": cg(inp["L0_ffn_b1"].reshape(8, 32, 128).transpose(0, 2, 1)),
        "w20": inp["L0_ffn_W2"], "b20": inp["L0_ffn_b2"],
        "uvw": inp["L1_attn_uv_w"], "ow": inp["L1_attn_o_w"],
        "rpw": inp["L1_attn_rpe_pos_w"], "rlw": inp["L1_attn_rpe_ln_w"],
        "rlb": inp["L1_attn_rpe_ln_b"], "rfc": inp["L1_attn_rpe_fc_w"],
        "rolw": inp["L1_attn_rpe_out_ln_w"], "rolb": inp["L1_attn_rpe_out_ln_b"],
        "row": inp["L1_attn_rpe_out_w"],
        "w11": inp["L1_ffn_W1"],
        "b1t1": cg(inp["L1_ffn_b1"].reshape(32, 128).T),
        "w21": inp["L1_ffn_W2"],
        "b2t1": cg(inp["L1_ffn_b2"].reshape(8, 128).T),
        "w2a": inp["L2_attn_W"],
        "bt2a": cg(inp["L2_attn_b"].reshape(8, 128).T),
        "br2a": inp["L2_attn_b"].reshape(1, 1024),
        "gw2": inp["L2_ffn_gate_w"],
        "eg2": cg(inp["L2_ffn_expert_gate"].transpose(0, 2, 1)),
        "eu2": cg(inp["L2_ffn_expert_up"].transpose(0, 2, 1)),
        "ed2": cg(inp["L2_ffn_expert_down"].transpose(0, 2, 1)),
        "sg2": cg(inp["L2_ffn_shared_gate"].T),
        "su2": cg(inp["L2_ffn_shared_up"].T),
        "sd2": cg(inp["L2_ffn_shared_down"].T),
        "ipw": cg(inp["L3_attn_in_proj_w"].T),
        "ipbt": cg(inp["L3_attn_in_proj_b"].reshape(24, 128).T),
        "ipbv": inp["L3_attn_in_proj_b"][2048:].reshape(1, 1024),
        "opw": cg(inp["L3_attn_out_proj_w"].T),
        "opbt": cg(inp["L3_attn_out_proj_b"].reshape(8, 128).T),
        "bw3": cg(inp["L3_ffn_base_W"].T),
        "bbt3": cg(inp["L3_ffn_base_b"].reshape(8, 128).T),
        "sa3": cg(inp["L3_ffn_shared_A"].T),
        "sbw3": cg(inp["L3_ffn_shared_B"].T),
        "ra3": cg(inp["L3_ffn_routed_A"].reshape(96, 1024).T),
        "rb3": cg(inp["L3_ffn_routed_B"].transpose(0, 2, 1).reshape(96, 1024)),
        "rw3": inp["L3_ffn_router_W"],
        "rbr3": inp["L3_ffn_router_b"].reshape(1, 6),
    }
    nc = _build_fused()
    in_maps = [dict(common, x=cg(x[c].T)) for c in range(NCORES)]
    res = run_bass_kernel_spmd(nc, in_maps, core_ids=list(range(NCORES)))
    out = np.empty((8, 4096, 1024), np.float32)
    for j in range(NCORES):
        out[:, 512 * j:512 * (j + 1), :] = \
            np.asarray(res.results[j]["y"]).T.reshape(8, 512, 1024)
    return out


class DF:

    ENGS = ("tensor", "vector", "scalar", "gpsimd", "sync")

    def __init__(self, kb, st, name):
        self.kb = kb
        nc = kb.nc
        self.sem = {}
        self.cnt = {}
        for e in self.ENGS:
            self.sem[e] = st.enter_context(nc.semaphore(f"{name}_{e}"))
            self.cnt[e] = 0
        for q in ("dsync", "dgp", "dsc"):
            self.sem[q] = st.enter_context(nc.semaphore(f"{name}_{q}"))
            self.cnt[q] = 0
        self._eng = {"tensor": nc.tensor, "vector": nc.vector,
                     "scalar": nc.scalar, "gpsimd": nc.gpsimd, "sync": nc.sync}
        self._dq = {"sync": "dsync", "gpsimd": "dgp", "scalar": "dsc"}
        self.wr = {}
        self.rd = {}
        self.lastw = {}

    def _wait(self, engname, deps):
        eng = self._eng[engname]
        lw = self.lastw.setdefault(engname, {})
        for s, v in sorted(deps.items()):
            if s == engname and engname == "tensor":
                continue
            if v > lw.get(s, 0):
                eng.wait_ge(self.sem[s], v)
                lw[s] = v

    @classmethod
    def _key(cls, k):
        if isinstance(k, tuple):
            return tuple(cls._key(x) for x in k)
        try:
            hash(k)
            return k
        except TypeError:
            return id(k)

    def op(self, engname, emit, reads=(), writes=(), inc=True, dma=False):
        reads = [self._key(k) for k in reads]
        writes = [self._key(k) for k in writes]
        deps = {}
        for k in reads:
            w = self.wr.get(k)
            if w:
                deps[w[0]] = max(deps.get(w[0], 0), w[1])
        for k in writes:
            w = self.wr.get(k)
            if w:
                deps[w[0]] = max(deps.get(w[0], 0), w[1])
            for s, v in self.rd.get(k, {}).items():
                deps[s] = max(deps.get(s, 0), v)
        self._wait(engname, deps)
        ins = emit()
        sname = self._dq[engname] if dma else engname
        if inc:
            ins.then_inc(self.sem[sname], 16 if dma else 1)
            self.cnt[sname] += 16 if dma else 1
        done = (sname, self.cnt[sname] + (16 if dma and not inc else
                                          (1 if not inc else 0)))
        for k in reads:
            self.rd.setdefault(k, {})[sname] = done[1]
        for k in writes:
            self.wr[k] = done
            self.rd[k] = {}
        return ins

    def dma(self, engname, emit, reads=(), writes=(), inc=True):
        return self.op(engname, emit, reads, writes, inc=inc, dma=True)

    def seal(self):
        for engname in self.ENGS:
            deps = {s: v for s, v in self.cnt.items() if v}
            self._wait(engname, deps)


def emit_moe_experts_df(kb, name, ln_scr, cw_scr, x_dram, y_dram, nexp, nh,
                        w_loader, act_kind, has_shared):
    raise NotImplementedError
